# version 17
# baseline (speedup 1.0000x reference)
import sys
import numpy as np

try:
    import concourse.bass as bass
except ImportError:
    sys.path.insert(0, "/opt/trn_rl_repo")
    import concourse.bass as bass
import concourse.mybir as mybir
from concourse.bass_utils import run_bass_kernel_spmd
from contextlib import ExitStack

F32 = mybir.dt.float32
BF16 = mybir.dt.bfloat16
AF = mybir.ActivationFunctionType
ALU = mybir.AluOpType
AX = mybir.AxisListType

T = 2048
DM = 2048
NDC = 16
H = 16
HK = 4
GQ = 4
DH = 128
DFF = 1408
NFC = 11
NE = 18
NRT = 16
WINDOW = 1024
EPS_QK = 1e-6
EPS_LN = 1e-5
RBASE = 10000.0
TA = 512
NTI = T // TA
TF = 1024
NTF = T // TF
INV_SQRT_DH = float(1.0 / np.sqrt(DH))

DMA_SEMS = ("LW", "LX", "ST", "SX", "SKV", "SQA", "SQB", "SOA", "SOB", "S1A", "S1B", "S1C", "S2A", "S2B", "S2C", "SYA", "SYB", "SXB", "STA", "STB", "STC", "STD")
OWN_SEM = {"pe": "PE", "act": "ACT", "dve": "DVE", "pool": "PL", "sp": None}


class Sched:
    def __init__(self):
        self.cur = None

    def start(self, engname, eng, sems):
        self.cur = engname
        self.eng = eng
        self.sems = sems
        self.cnt = {k: 0 for k in sems}
        self.lastwait = {k: 0 for k in sems}

    def op(self, engname, fn, waits=(), inc=None):
        tok = None
        if inc is not None:
            self.cnt[inc] += 16 if inc in DMA_SEMS else 1
            tok = (inc, self.cnt[inc])
        if engname == self.cur:
            own = OWN_SEM.get(engname)
            for w in waits:
                if w is None:
                    continue
                s, v = w
                if s == own or v <= self.lastwait[s]:
                    continue
                self.eng.wait_ge(self.sems[s], v)
                self.lastwait[s] = v
            ins = fn(self.eng)
            if inc is not None:
                ins.then_inc(self.sems[inc], 16 if inc in DMA_SEMS else 1)
        return tok

    def join(self, engname, waits):
        if engname == self.cur:
            own = OWN_SEM.get(engname)
            for w in waits:
                if w is None:
                    continue
                s, v = w
                if s == own or v <= self.lastwait[s]:
                    continue
                self.eng.wait_ge(self.sems[s], v)
                self.lastwait[s] = v


def tmax(*toks):
    best = {}
    for t in toks:
        if t is None:
            continue
        s, v = t
        if v > best.get(s, -1):
            best[s] = v
    return [(s, v) for s, v in best.items()]


class NS:
    pass


def build_program(debug=False):
    nc = bass.Bass("TRN2", target_bir_lowering=False)

    D = NS()
    D.x_fm = nc.dram_tensor("x_fm", [DM, T], F32, kind="ExternalInput").ap()
    D.wq_sl = nc.dram_tensor("wq_sl", [H, 128, NDC, 128], F32, kind="ExternalInput").ap()
    D.wk_sl = nc.dram_tensor("wk_sl", [HK, 128, NDC, 128], F32, kind="ExternalInput").ap()
    D.wvT = nc.dram_tensor("wvT", [DM, HK * DH], F32, kind="ExternalInput").ap()
    D.wo_sl = nc.dram_tensor("wo_sl", [NDC, 128, NDC, 128], F32, kind="ExternalInput").ap()
    D.w1_sl = nc.dram_tensor("w1_sl", [NE, NFC, 128, NDC, 128], F32, kind="ExternalInput").ap()
    D.w2_sl = nc.dram_tensor("w2_sl", [NE, NDC, 128, NFC, 128], F32, kind="ExternalInput").ap()
    D.b1c = nc.dram_tensor("b1c", [128, NE * NFC], F32, kind="ExternalInput").ap()
    D.b2c = nc.dram_tensor("b2c", [128, 2 * NDC], F32, kind="ExternalInput").ap()
    D.b2r = nc.dram_tensor("b2r", [NRT, DM], F32, kind="ExternalInput").ap()
    D.cT = nc.dram_tensor("cT", [DM, NRT], F32, kind="ExternalInput").ap()
    D.bias_bc = nc.dram_tensor("bias_bc", [128, NRT], F32, kind="ExternalInput").ap()
    D.gq = nc.dram_tensor("gq", [128, 1], F32, kind="ExternalInput").ap()
    D.gk = nc.dram_tensor("gk", [128, 1], F32, kind="ExternalInput").ap()
    D.cos_t = nc.dram_tensor("cos_t", [128, T], F32, kind="ExternalInput").ap()
    D.sin_t = nc.dram_tensor("sin_t", [128, T], F32, kind="ExternalInput").ap()
    D.ones_c = nc.dram_tensor("ones_c", [128, 128], F32, kind="ExternalInput").ap()
    D.ident_c = nc.dram_tensor("ident_c", [128, 128], F32, kind="ExternalInput").ap()
    D.mcaus = nc.dram_tensor("mcaus", [128, 128], F32, kind="ExternalInput").ap()
    D.medge = nc.dram_tensor("medge", [128, 128], F32, kind="ExternalInput").ap()
    D.sel_c = nc.dram_tensor("sel_c", [NRT, NRT * 128], F32, kind="ExternalInput").ap()
    D.p64_c = nc.dram_tensor("p64_c", [128, 128], F32, kind="ExternalInput").ap()
    D.identu_c = nc.dram_tensor("identu_c", [128, 128], mybir.dt.uint8, kind="ExternalInput").ap()
    D.y1_sp = nc.dram_tensor("y1_spill", [DM, T], F32, kind="Internal").ap()
    D.out_fm = nc.dram_tensor("out_fm", [DM, T], F32, kind="ExternalOutput").ap()
    D.dbg = {}
    if debug:
        for nm, shape in (("d_lnx", (DM, T)), ("d_q", (DH, T)), ("d_k", (HK * DH, T)),
                          ("d_v", (T, HK * DH)), ("d_att", (DM, T)), ("d_y1", (DM, T)),
                          ("d_u", (DM, T)), ("d_g", (T, NRT)), ("d_h0", (DFF, TF)),
                          ("d_y0", (DH, T)),
                          ("d_acc", (DM, T))):
            D.dbg[nm] = nc.dram_tensor(nm, list(shape), F32, kind="ExternalOutput").ap()

    with ExitStack() as ctx:
        e = ctx.enter_context
        C = NS()
        C.cos_s = e(nc.sbuf_tensor("cos_s", [128, T], BF16))
        C.sin_s = e(nc.sbuf_tensor("sin_s", [128, T], BF16))
        C.mcaus_bf = e(nc.sbuf_tensor("mcaus_bf", [128, 128], BF16))
        C.medge_bf = e(nc.sbuf_tensor("medge_bf", [128, 128], BF16))
        C.id_bf = e(nc.sbuf_tensor("id_bf", [128, 128], BF16))
        C.id_f = e(nc.sbuf_tensor("id_f", [128, 128], F32))
        C.ones_bf = e(nc.sbuf_tensor("ones_bf", [128, 128], BF16))
        C.p64_f = e(nc.sbuf_tensor("p64_f", [128, 128], F32))
        C.gq_s = e(nc.sbuf_tensor("gq_s", [128, 1], F32))
        C.gk_s = e(nc.sbuf_tensor("gk_s", [128, 1], F32))
        C.eps6 = e(nc.sbuf_tensor("eps6", [128, 1], F32))
        C.eps5 = e(nc.sbuf_tensor("eps5", [128, 1], F32))
        C.b1c_s = e(nc.sbuf_tensor("b1c_s", [128, NE * NFC], F32))
        C.b2c_s = e(nc.sbuf_tensor("b2c_s", [128, 2 * NDC], F32))
        C.b2sh_s = e(nc.sbuf_tensor("b2sh_s", [128, NDC], F32))
        C.b2r_bf = e(nc.sbuf_tensor("b2r_bf", [NRT, DM], BF16))
        C.sel_bf = e(nc.sbuf_tensor("sel_bf", [NRT, NRT * 128], BF16))
        C.cT_bf = e(nc.sbuf_tensor("cT_bf", [128, NDC * NRT], BF16))
        C.bias_bc_s = e(nc.sbuf_tensor("bias_bc_s", [128, NRT], F32))
        C.nmax = e(nc.sbuf_tensor("nmax", [128, 1], F32))
        C.sums = e(nc.sbuf_tensor("sums", [128, 1], F32))
        C.lnsum = e(nc.sbuf_tensor("lnsum", [128, 1], F32))
        C.bias2 = e(nc.sbuf_tensor("bias2", [128, 1], F32))
        C.sc16 = e(nc.sbuf_tensor("sc16", [128, NRT], F32))
        C.mx8 = e(nc.sbuf_tensor("mx8", [128, 8], F32))
        C.msk16 = e(nc.sbuf_tensor("msk16", [128, NRT], F32))
        C.g16 = e(nc.sbuf_tensor("g16", [128, NRT], F32))
        C.g_all = e(nc.sbuf_tensor("g_all", [128, 8 * NRT], F32))
        C.rs1 = e(nc.sbuf_tensor("rs1", [128, 1], F32))
        C.thr1 = e(nc.sbuf_tensor("thr1", [128, 1], F32))
        C.rsB = e(nc.sbuf_tensor("rsB", [128, 1], F32))
        C.rsC = e(nc.sbuf_tensor("rsC", [128, 1], F32))
        C.rsb_bf = e(nc.sbuf_tensor("rsb_bf", [128, 2], BF16))
        C.rs2 = e(nc.sbuf_tensor("rs2", [128, 2], F32))
        C.negC = e(nc.sbuf_tensor("negC", [128, 1], F32))
        C.mqA = e(nc.sbuf_tensor("mqA", [128, 1], F32))
        C.gbc_bf = e(nc.sbuf_tensor("gbc_bf", [128, 128], BF16))
        C.eps35 = e(nc.sbuf_tensor("eps35", [128, 1], F32))
        C.diag_bf = e(nc.sbuf_tensor("diag_bf", [128, 2 * 128], BF16))
        C.id_u8 = e(nc.sbuf_tensor("id_u8", [128, 128], mybir.dt.uint8))

        sems = {}
        for s in ("LW", "LX", "ST", "PE", "ACT", "DVE", "PL", "SX", "SKV", "SQA", "SQB", "SOA", "SOB", "S1A", "S1B", "S1C", "S2A", "S2B", "S2C", "SYA", "SYB", "SXB", "STA", "STB", "STC", "STD"):
            sems[s] = e(nc.semaphore(s))

        block = e(nc.Block())
        sch = Sched()
        state = {}

        with ExitStack() as actx:
            ae = actx.enter_context
            A = NS()
            A.x_bf = ae(nc.sbuf_tensor("x_bf", [128, 2 * NDC * TA], BF16))
            A.sq_scr = ae(nc.sbuf_tensor("sq_scr", [128, 2 * TA], BF16))
            A.y1s = ae(nc.sbuf_tensor("y1s", [128, 4 * TA], F32))
            A.mean2_s = ae(nc.sbuf_tensor("mean2_s", [128, TA], F32))
            A.msqv2_s = ae(nc.sbuf_tensor("msqv2_s", [128, TA], F32))
            A.rstd2_s = ae(nc.sbuf_tensor("rstd2_s", [128, TA], F32))
            A.ln_bf = ae(nc.sbuf_tensor("ln_bf", [128, NDC * TA], BF16))
            A.ln_scr = ae(nc.sbuf_tensor("ln_scr", [128, TA], F32))
            A.mean_s = ae(nc.sbuf_tensor("mean_s", [128, TA], F32))
            A.msqv_s = ae(nc.sbuf_tensor("msqv_s", [128, TA], F32))
            A.rstd_s = ae(nc.sbuf_tensor("rstd_s", [128, TA], F32))
            A.k_full = ae(nc.sbuf_tensor("k_full", [128, HK * T], BF16))
            A.v_full = ae(nc.sbuf_tensor("v_full", [128, NDC * TA], BF16))
            A.q_ro = ae(nc.sbuf_tensor("q_ro", [128, TA], F32))
            A.q_fp = ae(nc.sbuf_tensor("q_fp", [128, 2 * TA], F32))
            A.q_sq = ae(nc.sbuf_tensor("q_sq", [128, TA], BF16))
            A.rn_s = ae(nc.sbuf_tensor("rn_s", [128, TA], F32))
            A.q_bf = ae(nc.sbuf_tensor("q_bf", [128, TA], BF16))
            A.p_bf = ae(nc.sbuf_tensor("p_bf", [128, 2 * 1152], BF16))
            A.pT_sb = ae(nc.sbuf_tensor("pT_sb", [128, 1152], BF16))
            A.y_att = ae(nc.sbuf_tensor("y_att", [128, H * TA], BF16))
            A.att_bf = ae(nc.sbuf_tensor("att_bf", [128, NDC * TA], BF16))
            A.wbig = ae(nc.sbuf_tensor("wbig", [128, 4 * DM], BF16))
            A.wq_sb = ae(nc.sbuf_tensor("wq_sb", [128, 2 * DM], BF16))
            A.wo_sb = ae(nc.sbuf_tensor("wo_sb", [128, 2 * DM], BF16))
            A.pA = ae(nc.psum_tensor("pA", [128, 512], F32))
            A.pB = ae(nc.psum_tensor("pB", [128, 512], F32))
            A.pQ = ae(nc.psum_tensor("pQ", [128, 512], F32))
            A.pS = ae(nc.psum_tensor("pS", [128, 1536], F32))
            A.pTT = ae(nc.psum_tensor("pTT", [128, 2048], BF16))

            def attn_pass(engname, eng):
                sch.start(engname, eng, sems)
                ct = _sched_consts(sch, nc, D, C)
                af = _sched_attn(sch, nc, D, C, A, ct, debug)
                state[engname] = (ct, af, dict(sch.cnt), dict(sch.lastwait))

            block.tensor(lambda eng: attn_pass("pe", eng))
            block.scalar(lambda eng: attn_pass("act", eng))
            block.vector(lambda eng: attn_pass("dve", eng))
            block.gpsimd(lambda eng: attn_pass("pool", eng))
            block.sync(lambda eng: attn_pass("sp", eng))

        with ExitStack() as fctx:
            fe = fctx.enter_context
            Fb = NS()
            Fb.u_bf = fe(nc.sbuf_tensor("u_bf", [128, NDC * TF], BF16))
            Fb.acc = fe(nc.sbuf_tensor("acc", [128, NDC * TF], F32))
            Fb.h_bf = fe(nc.sbuf_tensor("h_bf", [128, NFC * TF], BF16))
            Fb.G_bf = fe(nc.sbuf_tensor("G_bf", [128, TF], BF16))
            Fb.gT_bf = fe(nc.sbuf_tensor("gT_bf", [NRT, TF], BF16))
            Fb.y1_scr = fe(nc.sbuf_tensor("y1_scr", [128, 2 * TF], F32))
            Fb.cast_f = fe(nc.sbuf_tensor("cast_f", [128, 2 * TF], BF16))
            Fb.sq_f = fe(nc.sbuf_tensor("sq_f", [128, TF], BF16))
            Fb.ln_f = fe(nc.sbuf_tensor("ln_f", [128, TF], F32))
            Fb.mean_f = fe(nc.sbuf_tensor("mean_f", [128, TF], F32))
            Fb.msqv_f = fe(nc.sbuf_tensor("msqv_f", [128, TF], F32))
            Fb.rstd_f = fe(nc.sbuf_tensor("rstd_f", [128, TF], F32))
            Fb.w1_sb = fe(nc.sbuf_tensor("w1_sb", [128, 3 * NDC * 128], BF16))
            Fb.w2_sb = fe(nc.sbuf_tensor("w2_sb", [128, 2 * NFC * 128], BF16))
            Fb.pHa = fe(nc.psum_tensor("pHa", [128, TF], F32))
            Fb.pHb = fe(nc.psum_tensor("pHb", [128, TF], F32))
            Fb.pYa = fe(nc.psum_tensor("pYa", [128, TF], F32))
            Fb.pYb = fe(nc.psum_tensor("pYb", [128, TF], F32))

            def ffn_pass(engname, eng):
                ct, af, cnt, lastwait = state[engname]
                sch.start(engname, eng, sems)
                sch.cnt = cnt
                sch.lastwait = lastwait
                _sched_ffn(sch, nc, D, C, Fb, ct, af, debug)

            block.tensor(lambda eng: ffn_pass("pe", eng))
            block.scalar(lambda eng: ffn_pass("act", eng))
            block.vector(lambda eng: ffn_pass("dve", eng))
            block.gpsimd(lambda eng: ffn_pass("pool", eng))
            block.sync(lambda eng: ffn_pass("sp", eng))

    return nc


def _sched_consts(sch, nc, D, C):
    ct = {}
    t = None
    for dst, src in ((C.gq_s, D.gq), (C.gk_s, D.gk),
                     (C.b1c_s, D.b1c), (C.bias_bc_s, D.bias_bc), (C.id_f, D.ident_c),
                     (C.p64_f, D.p64_c)):
        t = sch.op("sp", lambda E, d=dst, s=src: E.dma_start(out=d[:], in_=s), inc="LX")
    sch.op("sp", lambda E: E.dma_start(out=C.id_u8[:], in_=D.identu_c), inc="LX")
    t = sch.op("sp", lambda E: E.dma_start(
        out=C.b2c_s[:].rearrange("p (n c) -> p n c", n=2),
        in_=D.b2c.rearrange("p (n c) -> p n c", n=2)), inc="LX")
    ct["lx"] = t
    t = None
    for dst, src in ((C.ones_bf, D.ones_c), (C.id_bf, D.ident_c), (C.b2r_bf, D.b2r),
                     (C.sel_bf, D.sel_c), (C.cos_s, D.cos_t), (C.sin_s, D.sin_t),
                     (C.mcaus_bf, D.mcaus), (C.medge_bf, D.medge)):
        t = sch.op("pool", lambda E, d=dst, s=src: E.dma_start(out=d[:], in_=s), inc="LW")
    t = sch.op("pool", lambda E: E.dma_start(
        out=C.cT_bf[:].rearrange("p (c j) -> p c j", c=NDC),
        in_=D.cT.rearrange("(c p) j -> p c j", p=128)), inc="LW")
    ct["lw"] = t
    sch.op("dve", lambda E: E.memset(C.eps6[:], EPS_QK))
    sch.op("dve", lambda E: E.memset(C.diag_bf[:], 0.0))
    sch.op("dve", lambda E: E.memset(C.eps35[:], 1e-35))
    ct["eps"] = sch.op("dve", lambda E: E.memset(C.eps5[:], EPS_LN), inc="DVE")
    ct["b2sh"] = sch.op("dve", lambda E: E.tensor_add(
        C.b2sh_s[:], C.b2c_s[:, 0:NDC], C.b2c_s[:, NDC:2 * NDC]),
        waits=[ct["lx"]], inc="DVE")
    return ct


def _sched_attn(sch, nc, D, C, A, ct, debug):
    st = NS()
    st.wbig_free = None
    st.wq_free = [None, None]
    st.wo_free = [None, None]
    st.scr_free = [None, None]
    st.pA_free = ct["eps"]
    st.pB_free = None
    st.pQ_free = None
    st.pS_free = None
    st.pTT_free = None
    st.pbf_free = None
    st.pT_free = None
    st.lnbf_free = None
    st.yatt_free = None
    st.attbf_free = []
    st.stat_free = []
    st.qfp_free = [None, None]
    st.xbf_free = [[], []]
    st.xld = [None, None]
    st.y1s_free = []
    st.qro_free = None
    st.proj_tok = None
    st.diag_free = [None, None]
    st.tr_free = [None, None]
    st.pbf_par_free = [None, None]
    st.vts = [None] * NDC
    t_spill = None

    def rope_start(h, par):
        t_ev = sch.op("act", lambda E, par=par: E.copy(
            A.q_fp[:, par * TA:(par + 1) * TA], A.pQ[:]),
            waits=tmax(st.proj_tok, st.qfp_free[par]), inc="ACT")
        st.pQ_free = t_ev
        return t_ev

    def rope_finish(t_ev, par, gamma_s, scale2, dst_fn, ti0):
        t_sw = sch.op("pe", lambda E, par=par: E.matmul(
            A.pA[:], C.p64_f[:], A.q_fp[:, par * TA:(par + 1) * TA],
            start=True, stop=True),
            waits=tmax(t_ev, st.pA_free), inc="PE")
        sch.op("dve", lambda E, par=par: E.tensor_mul(
            A.ln_scr[:], A.q_fp[:, par * TA:(par + 1) * TA], C.cos_s[:, ti0:ti0 + TA]),
            waits=tmax(t_ev))
        sch.op("dve", lambda E: E.tensor_mul(
            A.q_ro[:], A.pA[:], C.sin_s[:, ti0:ti0 + TA]), waits=tmax(t_sw))
        t_rope = sch.op("dve", lambda E: E.tensor_add(
            A.q_ro[:], A.q_ro[:], A.ln_scr[:]), inc="DVE")
        st.pA_free = t_rope
        st.qfp_free[par] = t_rope
        t_sq = sch.op("dve", lambda E: E.tensor_mul(A.q_sq[:], A.q_ro[:], A.q_ro[:]),
                      inc="DVE")
        t_ss = sch.op("pe", lambda E: E.matmul(
            A.pB[:], C.ones_bf[:], A.q_sq[:], start=True, stop=True),
            waits=tmax(t_sq, st.pB_free), inc="PE")
        t_s1 = sch.op("act", lambda E: E.activation(A.rn_s[:], A.pB[:], AF.Sqrt),
                      waits=tmax(t_ss), inc="ACT")
        st.pB_free = t_s1
        sch.op("act", lambda E: E.activation(A.rn_s[:], A.rn_s[:], AF.Ln, bias=C.eps6[:]))
        t_rn = sch.op("act", lambda E: E.activation(A.rn_s[:], A.rn_s[:], AF.Exp, scale=-1.0),
                      inc="ACT")
        sch.op("dve", lambda E: E.tensor_mul(A.ln_scr[:], A.q_ro[:], A.rn_s[:]),
               waits=tmax(t_rn))
        if scale2 is None:
            t_fin = sch.op("dve", lambda E: E.tensor_scalar(
                dst_fn(), A.ln_scr[:], gamma_s[:], None, ALU.mult), inc="DVE")
        else:
            t_fin = sch.op("dve", lambda E: E.tensor_scalar(
                dst_fn(), A.ln_scr[:], gamma_s[:], scale2, ALU.mult, ALU.mult),
                inc="DVE")
        st.qro_free = t_fin
        return t_fin

    t_b1 = sch.op("dve", lambda E: E.tensor_copy(C.gbc_bf[:], C.gq_s[:].to_broadcast([128, 128])),
                  waits=tmax(ct["lx"]), inc="DVE")
    t_t1 = sch.op("pe", lambda E: E.transpose(A.pTT[:, 0:128], C.gbc_bf[:], C.id_bf[:]),
                  waits=tmax(t_b1, ct["lw"]), inc="PE")
    t_ev1 = sch.op("act", lambda E: E.copy(A.ln_scr[:, 0:128], A.pTT[:, 0:128]),
                   waits=tmax(t_t1), inc="ACT")
    t_m1 = sch.op("dve", lambda E: E.reduce_max(C.mqA[:], A.ln_scr[:, 0:128], AX.X,
                                                apply_absolute_value=True),
                  waits=tmax(t_ev1), inc="DVE")
    t_b2 = sch.op("dve", lambda E: E.tensor_copy(C.gbc_bf[:], C.gk_s[:].to_broadcast([128, 128])),
                  inc="DVE")
    t_t2 = sch.op("pe", lambda E: E.transpose(A.pTT[:, 128:256], C.gbc_bf[:], C.id_bf[:]),
                  waits=tmax(t_b2, t_ev1), inc="PE")
    t_ev2 = sch.op("act", lambda E: E.copy(A.ln_scr[:, 128:256], A.pTT[:, 128:256]),
                   waits=tmax(t_t2), inc="ACT")
    t_m2 = sch.op("dve", lambda E: E.reduce_max(C.negC[:], A.ln_scr[:, 128:256], AX.X,
                                                apply_absolute_value=True),
                  waits=tmax(t_ev2), inc="DVE")
    sch.op("dve", lambda E: E.tensor_mul(C.negC[:], C.negC[:], C.mqA[:]))
    t_negC = sch.op("dve", lambda E: E.tensor_scalar(
        C.negC[:], C.negC[:], -INV_SQRT_DH, None, ALU.mult), inc="DVE")
    st.pTT_free = t_ev2
    st.stat_free = [t_m2]

    for i in range(NTI):
        ti0 = i * TA
        par = i % 2
        if i == 0:
            st.xld[0] = sch.op("pool", lambda E: E.dma_start(
                out=A.x_bf[:, 0:NDC * TA].rearrange("p (c t) -> p c t", c=NDC),
                in_=D.x_fm.rearrange("(c p) t -> p c t", p=128)[:, :, 0:TA]),
                inc="SX")
        if i + 1 < NTI:
            pnxt = (i + 1) % 2
            st.xld[pnxt] = sch.op("pool", lambda E, i=i, pnxt=pnxt: E.dma_start(
                out=A.x_bf[:, pnxt * NDC * TA:(pnxt + 1) * NDC * TA].rearrange("p (c t) -> p c t", c=NDC),
                in_=D.x_fm.rearrange("(c p) t -> p c t", p=128)[:, :, (i + 1) * TA:(i + 2) * TA]),
                waits=tmax(*st.xbf_free[pnxt]), inc=("SX" if pnxt == 0 else "SXB"))
        t_x = st.xld[par]
        xb0 = par * NDC * TA
        t_wv = sch.op("pool", lambda E: E.dma_start(
            out=A.wbig[:, 0:NDC * 512].rearrange("p (c v) -> p c v", c=NDC),
            in_=D.wvT.rearrange("(c p) v -> p c v", p=128)),
            waits=tmax(st.wbig_free), inc="SKV")
        t_stats = None
        for dc in range(NDC):
            b = dc % 2
            t_cast = sch.op("act", lambda E, dc=dc, b=b, xb0=xb0: E.activation(
                A.sq_scr[:, b * TA:(b + 1) * TA], A.x_bf[:, xb0 + dc * TA: xb0 + (dc + 1) * TA],
                AF.Square), waits=tmax(t_x, st.scr_free[b]), inc="ACT")
            sch.op("pe", lambda E, b=b, dc=dc, xb0=xb0: E.matmul(
                A.pA[:], C.ones_bf[:], A.x_bf[:, xb0 + dc * TA: xb0 + (dc + 1) * TA],
                start=(dc == 0), stop=(dc == NDC - 1)),
                waits=tmax(t_cast,
                           st.pA_free if dc == 0 else None,
                           st.pB_free if dc == 0 else None,
                           ct["lw"] if dc == 0 else None))
            t_stats = sch.op("pe", lambda E, b=b, dc=dc: E.matmul(
                A.pB[:], C.ones_bf[:], A.sq_scr[:, b * TA:(b + 1) * TA],
                start=(dc == 0), stop=(dc == NDC - 1)), inc="PE")
            st.scr_free[b] = t_stats
        t_mean = sch.op("act", lambda E: E.mul(A.mean_s[:], A.pA[:], 1.0 / DM),
                        waits=tmax(t_stats, *st.stat_free), inc="ACT")
        t_msq = sch.op("act", lambda E: E.activation(A.msqv_s[:], A.mean_s[:], AF.Square),
                       inc="ACT")
        t_var = sch.op("dve", lambda E: E.scalar_tensor_tensor(
            A.msqv_s[:], A.pB[:], 1.0 / DM, A.msqv_s[:], ALU.mult, ALU.subtract),
            waits=tmax(t_msq, t_stats), inc="DVE")
        sch.op("act", lambda E: E.activation(
            A.rstd_s[:], A.msqv_s[:], AF.Ln, bias=C.eps5[:]),
            waits=tmax(t_var, ct["eps"]))
        t_rstd = sch.op("act", lambda E: E.activation(
            A.rstd_s[:], A.rstd_s[:], AF.Exp, scale=-0.5), inc="ACT")
        st.pA_free = t_mean
        st.pB_free = t_var
        t_ln = None
        t_ln_pl = None
        t_ln_dc = [None] * NDC
        for dc in range(NDC):
            if dc % 3 == 2:
                sch.op("pool", lambda E, dc=dc, xb0=xb0: E.tensor_sub(
                    A.q_ro[:], A.x_bf[:, xb0 + dc * TA: xb0 + (dc + 1) * TA], A.mean_s[:]),
                    waits=tmax(t_rstd, t_mean, st.lnbf_free, st.qro_free))
                t_ln_pl = sch.op("pool", lambda E, dc=dc: E.tensor_mul(
                    A.ln_bf[:, dc * TA:(dc + 1) * TA], A.q_ro[:], A.rstd_s[:]),
                    inc="PL")
                t_ln_dc[dc] = t_ln_pl
            else:
                sch.op("dve", lambda E, dc=dc, xb0=xb0: E.tensor_sub(
                    A.ln_scr[:], A.x_bf[:, xb0 + dc * TA: xb0 + (dc + 1) * TA], A.mean_s[:]),
                    waits=tmax(t_rstd, t_mean, st.lnbf_free))
                t_ln = sch.op("dve", lambda E, dc=dc: E.tensor_mul(
                    A.ln_bf[:, dc * TA:(dc + 1) * TA], A.ln_scr[:], A.rstd_s[:]),
                    inc="DVE")
                t_ln_dc[dc] = t_ln
        if debug:
            sch.op("pool", lambda E, ti0=ti0: E.dma_start(
                out=D.dbg["d_lnx"].rearrange("(c p) t -> p c t", p=128)[:, :, ti0:ti0 + TA],
                in_=A.ln_bf[:].rearrange("p (c t) -> p c t", c=NDC)),
                waits=tmax(t_ln), inc="LW")
        st.stat_free = [t_ln, t_ln_pl]
        t_vp = None
        for ts_ in range(4):
            gt = 4 * i + ts_

            def vproj(E, ts_=ts_):
                last = None
                for dc in range(NDC):
                    last = E.matmul(
                        A.pQ[:], A.ln_bf[:, dc * TA + ts_ * 128: dc * TA + ts_ * 128 + 128],
                        A.wbig[:, dc * 512:(dc + 1) * 512],
                        start=(dc == 0), stop=(dc == NDC - 1))
                return last
            t_vp = sch.op("pe", vproj,
                          waits=tmax(t_wv, st.pQ_free, t_ln, t_ln_pl,
                                     *([t_ln_dc[dc] for dc in range(NDC)] if ts_ == 0 else [])),
                          inc="PE")
            t_ve = sch.op("act", lambda E, gt=gt: E.copy(
                A.v_full[:, gt * 512:(gt + 1) * 512], A.pQ[:]),
                waits=tmax(t_vp), inc="ACT")
            st.vts[gt] = t_ve
            st.pQ_free = t_ve
        t_wk = sch.op("pool", lambda E: E.dma_start(
            out=A.wbig[:].rearrange("p (k c q) -> p k c q", k=HK, c=NDC),
            in_=D.wk_sl.rearrange("k p c q -> p k c q")),
            waits=tmax(t_vp), inc="SKV")
        t_kfin = None
        for hk in range(HK):
            def kproj(E, hk=hk):
                last = None
                for dc in range(NDC):
                    last = E.matmul(
                        A.pQ[:], A.wbig[:, hk * DM + dc * 128: hk * DM + (dc + 1) * 128],
                        A.ln_bf[:, dc * TA:(dc + 1) * TA],
                        start=(dc == 0), stop=(dc == NDC - 1))
                return last
            st.proj_tok = sch.op("pe", kproj,
                                 waits=tmax(t_ln, t_ln_pl, t_wk, st.pQ_free), inc="PE")
            t_ev = rope_start(hk, hk % 2)
            t_kfin = rope_finish(
                t_ev, hk % 2, C.gk_s, None,
                lambda hk=hk, ti0=ti0: A.k_full[:, hk * T + ti0: hk * T + ti0 + TA], ti0)
        if debug:
            sch.op("pool", lambda E, ti0=ti0: E.dma_start(
                out=D.dbg["d_k"].rearrange("(c p) t -> p c t", p=128)[:, :, ti0:ti0 + TA],
                in_=A.k_full[:].rearrange("p (c t) -> p c t", c=HK)[:, :, ti0:ti0 + TA]),
                waits=tmax(t_kfin), inc="LW")
            sch.op("pool", lambda E, i=i: E.dma_start(
                out=D.dbg["d_v"].rearrange("(c p) v -> p c v", p=128)[:, 4 * i:4 * i + 4, :],
                in_=A.v_full[:].rearrange("p (c v) -> p c v", c=NDC)[:, 4 * i:4 * i + 4, :]),
                waits=tmax(st.vts[4 * i + 3]), inc="LW")
        t_wq = [None] * H

        def load_wq(h):
            return sch.op("pool", lambda E, h=h: E.dma_start(
                out=A.wq_sb[:, (h % 2) * DM:(h % 2 + 1) * DM].rearrange("p (c q) -> p c q", c=NDC),
                in_=D.wq_sl[h]),
                waits=tmax(st.wq_free[h % 2]), inc=("SQA" if h % 2 == 0 else "SQB"))

        t_wq[0] = load_wq(0)
        t_wq[1] = load_wq(1)
        t_qfin = [None] * H
        pend = None

        def q_dst(h, ti0=ti0):
            return lambda h=h: A.wbig[:, h * TA:(h + 1) * TA]

        for h in range(H):
            def qproj(E, h=h):
                last = None
                for dc in range(NDC):
                    last = E.matmul(
                        A.pQ[:], A.wq_sb[:, (h % 2) * DM + dc * 128:(h % 2) * DM + (dc + 1) * 128],
                        A.ln_bf[:, dc * TA:(dc + 1) * TA],
                        start=(dc == 0), stop=(dc == NDC - 1))
                return last
            st.proj_tok = sch.op("pe", qproj,
                                 waits=tmax(t_wq[h], st.pQ_free, t_ln, t_ln_pl,
                                            st.wbig_free if h == 0 else None),
                                 inc="PE")
            t_qp = st.proj_tok
            st.wq_free[h % 2] = t_qp
            if h + 2 < H:
                t_wq[h + 2] = load_wq(h + 2)
            t_ev = rope_start(h, h % 2)
            if pend is not None:
                ph, pev = pend
                t_qfin[ph] = rope_finish(pev, ph % 2, C.gq_s, INV_SQRT_DH, q_dst(ph), ti0)
            pend = (h, t_ev)
        ph, pev = pend
        t_qfin[ph] = rope_finish(pev, ph % 2, C.gq_s, INV_SQRT_DH, q_dst(ph), ti0)
        if debug:
            sch.op("pool", lambda E, ti0=ti0: E.dma_start(
                out=D.dbg["d_q"][:, ti0:ti0 + TA], in_=A.wbig[:, 0:TA]),
                waits=tmax(t_qfin[0]), inc="LW")

        core = []
        for h in range(H):
            for sub in range(4):
                core.append((h, sub))
        pend_tp = None
        t_e1 = None
        t_sc_last = None
        for idx, (h, sub) in enumerate(core):
            hk = h // GQ
            gsub = 4 * i + sub
            c_lo = max(0, gsub - 8)
            nch = gsub - c_lo + 1
            W = 128 * nch

            def scmm(E, h=h, sub=sub, c_lo=c_lo, W=W, hk=hk, has_edge=(gsub >= 8)):
                last = None
                for off in range(0, W, 512):
                    n = min(512, W - off)
                    has_mask = (has_edge and off == 0) or (W - 128 >= off and W - 128 < off + n)
                    last = E.matmul(
                        A.pS[:, off:off + n],
                        A.wbig[:, h * TA + sub * 128: h * TA + sub * 128 + 128],
                        A.k_full[:, hk * T + c_lo * 128 + off: hk * T + c_lo * 128 + off + n],
                        start=True, stop=not has_mask)
                    if has_edge and off == 0:
                        last = E.matmul(A.pS[:, 0:128], C.id_bf[:], C.medge_bf[:],
                                        start=False,
                                        stop=not (W - 128 < off + n))
                    if W - 128 >= off and W - 128 < off + n:
                        last = E.matmul(A.pS[:, W - 128:W], C.id_bf[:], C.mcaus_bf[:],
                                        start=False, stop=True)
                return last
            t_sc = sch.op("pe", scmm, waits=tmax(t_qfin[h], st.pS_free, t_kfin, ct["lw"]),
                          inc="PE")
            t_sc_last = t_sc
            par = idx % 2
            t_e1 = sch.op("act", lambda E, W=W, par=par: E.activation(
                A.p_bf[:, par * 1152: par * 1152 + W], A.pS[:, 0:W], AF.Exp,
                bias=C.negC[:], accum_out=C.sums[:]),
                waits=tmax(t_sc, t_negC, st.pbf_par_free[par]), inc="ACT")
            st.pS_free = t_e1
            sch.op("act", lambda E: E.activation(C.lnsum[:], C.sums[:], AF.Ln, bias=C.eps35[:]))
            t_ds = sch.op("act", lambda E, par=par: E.activation(
                C.rs2[:, par:par + 1], C.lnsum[:], AF.Exp, scale=-1.0),
                waits=tmax(st.diag_free[par]), inc="ACT")
            t_diag = sch.op("dve", lambda E, par=par, W=W: E.tensor_scalar(
                A.p_bf[:, par * 1152: par * 1152 + W],
                A.p_bf[:, par * 1152: par * 1152 + W],
                C.rs2[:, par:par + 1], None, ALU.mult),
                waits=tmax(t_ds), inc="DVE")
            st.diag_free[par] = t_diag
            def emit_tp(ph, psub, pnch, pc_lo, phk, pW, t_pe1, t_pdiag, ppar):
                def trs(E, nch=pnch, par=ppar):
                    last = None
                    for c in range(nch):
                        last = E.transpose(A.pTT[:, c * 128:(c + 1) * 128],
                                           A.p_bf[:, par * 1152 + c * 128: par * 1152 + (c + 1) * 128],
                                           C.id_bf[:])
                    return last
                t_tr = sch.op("pe", trs, waits=tmax(t_pe1, t_pdiag, st.pTT_free), inc="PE")
                st.pbf_par_free[ppar] = t_tr
                st.tr_free[ppar] = t_tr
                t_pte = sch.op("dve", lambda E, W=pW: E.tensor_copy(A.pT_sb[:, 0:W], A.pTT[:, 0:W]),
                               waits=tmax(t_tr, st.pT_free), inc="DVE")
                st.pTT_free = t_pte

                def pv(E, nch=pnch, c_lo=pc_lo, hk=phk):
                    last = None
                    for c in range(nch):
                        last = E.matmul(
                            A.pB[:, 0:128],
                            A.v_full[:, (c_lo + c) * 512 + hk * 128: (c_lo + c) * 512 + hk * 128 + 128],
                            A.pT_sb[:, c * 128:(c + 1) * 128],
                            start=(c == 0), stop=(c == nch - 1))
                    return last
                t_pv = sch.op("pe", pv,
                              waits=tmax(t_pte, st.pB_free, st.vts[4 * i + psub]), inc="PE")
                st.pT_free = t_pv
                t_ye = sch.op("dve", lambda E, h=ph, sub=psub: E.tensor_copy(
                    A.y_att[:, h * TA + sub * 128: h * TA + sub * 128 + 128], A.pB[:, 0:128]),
                    waits=tmax(t_pv, st.yatt_free if (ph == 0 and psub == 0) else None),
                    inc="DVE")
                st.pB_free = t_ye
                return t_ye
            if pend_tp is not None:
                t_ye = emit_tp(*pend_tp)
            pend_tp = (h, sub, nch, c_lo, hk, W, t_e1, t_diag, par)
        t_ye = emit_tp(*pend_tp)
        t_ye_last = t_ye
        st.wbig_free = t_sc_last
        if debug:
            sch.op("pool", lambda E, ti0=ti0: E.dma_start(
                out=D.dbg["d_y0"][:, ti0:ti0 + TA], in_=A.y_att[:, 0:TA]),
                waits=tmax(t_ye), inc="LW")
        t_wo = {}

        def load_wo(dcp):
            t_wo[dcp] = sch.op("pool", lambda E, dcp=dcp: E.dma_start(
                out=A.wo_sb[:, (dcp % 2) * DM:(dcp % 2 + 1) * DM].rearrange("p (c q) -> p c q", c=NDC),
                in_=D.wo_sl[dcp]),
                waits=tmax(st.wo_free[dcp % 2]), inc=("SOA" if dcp % 2 == 0 else "SOB"))

        load_wo(0)
        load_wo(1)
        t_stats2 = None
        t_wmm = None
        for dcp in range(NDC):
            def womm(E, dcp=dcp):
                last = None
                for ec in range(NDC):
                    last = E.matmul(
                        A.pQ[:], A.wo_sb[:, (dcp % 2) * DM + ec * 128:(dcp % 2) * DM + (ec + 1) * 128],
                        A.y_att[:, ec * TA:(ec + 1) * TA],
                        start=(ec == 0), stop=(ec == NDC - 1))
                return last
            t_wmm = sch.op("pe", womm,
                           waits=tmax(t_wo[dcp], t_ye_last, st.pQ_free), inc="PE")
            st.wo_free[dcp % 2] = t_wmm
            if dcp + 2 < NDC:
                load_wo(dcp + 2)
            t_ae = sch.op("act", lambda E, dcp=dcp: E.copy(
                A.att_bf[:, dcp * TA:(dcp + 1) * TA], A.pQ[:]),
                waits=tmax(t_wmm, *(st.attbf_free if dcp == 0 else [])), inc="ACT")
            st.pQ_free = t_ae
            b = dcp % 2
            t_sqc = sch.op("act", lambda E, dcp=dcp, b=b: E.activation(
                A.sq_scr[:, b * TA:(b + 1) * TA], A.att_bf[:, dcp * TA:(dcp + 1) * TA],
                AF.Square),
                waits=tmax(st.scr_free[b]), inc="ACT")
            sch.op("pe", lambda E, dcp=dcp: E.matmul(
                A.pA[:], C.ones_bf[:], A.att_bf[:, dcp * TA:(dcp + 1) * TA],
                start=(dcp == 0), stop=(dcp == NDC - 1)),
                waits=tmax(t_sqc,
                           st.pA_free if dcp == 0 else None,
                           st.pB_free if dcp == 0 else None))
            t_stats2 = sch.op("pe", lambda E, dcp=dcp, b=b: E.matmul(
                A.pB[:], C.ones_bf[:], A.sq_scr[:, b * TA:(b + 1) * TA],
                start=(dcp == 0), stop=(dcp == NDC - 1)), inc="PE")
            st.scr_free[b] = t_stats2
        if debug:
            sch.op("pool", lambda E, ti0=ti0: E.dma_start(
                out=D.dbg["d_att"].rearrange("(c p) t -> p c t", p=128)[:, :, ti0:ti0 + TA],
                in_=A.att_bf[:].rearrange("p (c t) -> p c t", c=NDC)),
                waits=tmax(t_stats2), inc="LW")
        t_mean2 = sch.op("act", lambda E: E.mul(A.mean2_s[:], A.pA[:], 1.0 / DM),
                         waits=tmax(t_stats2, t_ln, t_ln_pl), inc="ACT")
        t_msq2 = sch.op("act", lambda E: E.activation(A.msqv2_s[:], A.mean2_s[:], AF.Square),
                        inc="ACT")
        t_var2 = sch.op("dve", lambda E: E.scalar_tensor_tensor(
            A.msqv2_s[:], A.pB[:], 1.0 / DM, A.msqv2_s[:], ALU.mult, ALU.subtract),
            waits=tmax(t_msq2, t_stats2), inc="DVE")
        sch.op("act", lambda E: E.activation(
            A.rstd2_s[:], A.msqv2_s[:], AF.Ln, bias=C.eps5[:]),
            waits=tmax(t_var2))
        t_rstd2 = sch.op("act", lambda E: E.activation(
            A.rstd2_s[:], A.rstd2_s[:], AF.Exp, scale=-0.5), inc="ACT")
        st.pA_free = t_mean2
        st.pB_free = t_var2
        t_res = None
        t_res_pl = None
        t_sp_dc = [None] * NDC
        for dc in range(NDC):
            b = dc % 4
            warslot = t_sp_dc[dc - 4] if dc >= 4 else (st.y1s_free[dc] if dc < len(st.y1s_free) else None)
            if dc % 2 == 1:
                sch.op("pool", lambda E, dc=dc: E.tensor_sub(
                    A.q_ro[:], A.att_bf[:, dc * TA:(dc + 1) * TA], A.mean2_s[:]),
                    waits=tmax(t_rstd2, t_mean2, st.qro_free))
                sch.op("pool", lambda E: E.tensor_mul(A.q_ro[:], A.q_ro[:], A.rstd2_s[:]))
                t_res_pl = sch.op("pool", lambda E, dc=dc, b=b, xb0=xb0: E.tensor_add(
                    A.y1s[:, b * TA:(b + 1) * TA], A.q_ro[:],
                    A.x_bf[:, xb0 + dc * TA: xb0 + (dc + 1) * TA]),
                    waits=tmax(warslot), inc="PL")
                t_last = t_res_pl
            else:
                sch.op("dve", lambda E, dc=dc: E.tensor_sub(
                    A.ln_scr[:], A.att_bf[:, dc * TA:(dc + 1) * TA], A.mean2_s[:]),
                    waits=tmax(t_rstd2, t_mean2))
                sch.op("dve", lambda E: E.tensor_mul(A.ln_scr[:], A.ln_scr[:], A.rstd2_s[:]))
                t_res = sch.op("dve", lambda E, dc=dc, b=b, xb0=xb0: E.tensor_add(
                    A.y1s[:, b * TA:(b + 1) * TA], A.ln_scr[:],
                    A.x_bf[:, xb0 + dc * TA: xb0 + (dc + 1) * TA]),
                    waits=tmax(warslot), inc="DVE")
                t_last = t_res
            t_sp_dc[dc] = sch.op("sp", lambda E, dc=dc, b=b, ti0=ti0: E.dma_start(
                out=D.y1_sp[dc * 128:(dc + 1) * 128, ti0:ti0 + TA],
                in_=A.y1s[:, b * TA:(b + 1) * TA]),
                waits=tmax(t_last), inc=("STA", "STB", "STC", "STD")[b])
        st.y1s_free = [t_sp_dc[NDC - 4], t_sp_dc[NDC - 3], t_sp_dc[NDC - 2], t_sp_dc[NDC - 1]]
        t_spill = st.y1s_free
        st.xbf_free[par] = [t_res, t_res_pl]
        st.attbf_free = [t_res, t_res_pl]
        st.qro_free = t_res_pl
        st.lnbf_free = t_qp
        st.yatt_free = t_wmm

    return {"pe": ("PE", sch.cnt["PE"]), "act": ("ACT", sch.cnt["ACT"]),
            "dve": ("DVE", sch.cnt["DVE"]), "spill": t_spill}


def _sched_ffn(sch, nc, D, C, Fb, ct, af, debug):
    t_spill_all = af["spill"]
    if not isinstance(t_spill_all, list):
        t_spill_all = [t_spill_all]
    allf = [af["pe"], af["act"], af["dve"]]
    for en in ("pe", "act", "dve", "pool", "sp"):
        sch.join(en, tmax(*allf))

    st = NS()
    st.y1_free = [None, None]
    st.cast_free = [None, None]
    st.u_free = None
    st.acc_free = None
    st.h_free = [None] * NFC
    st.pH_free = [None, None]
    st.pY_free = [None, None]
    st.G_free = None
    st.gT_free = None
    st.stat_free = None
    st.w1free = [None, None, None]
    st.w2free = [None, None]
    st.gall_free = [None] * 8
    st.sqf_free = None
    st.g16_free = None
    t_w1 = {}
    t_w2 = {}

    def load_w1(j, fc):
        idx = (j * NFC + fc) % 3
        t_w1[(j, fc)] = sch.op("pool", lambda E, j=j, fc=fc, idx=idx: E.dma_start(
            out=Fb.w1_sb[:, idx * DM:(idx + 1) * DM].rearrange("p (c q) -> p c q", c=NDC),
            in_=D.w1_sl[j, fc]),
            waits=tmax(st.w1free[idx]), inc=("S1A", "S1B", "S1C")[idx])

    def load_w2(j, dcp):
        idx = (j * NDC + dcp) % 2
        t_w2[(j, dcp)] = sch.op("pool", lambda E, j=j, dcp=dcp, idx=idx: E.dma_start(
            out=Fb.w2_sb[:, idx * NFC * 128:(idx + 1) * NFC * 128].rearrange("p (c q) -> p c q", c=NFC),
            in_=D.w2_sl[j, dcp]),
            waits=tmax(st.w2free[idx]), inc=("S2A", "S2B")[idx])

    for f in range(NTF):
        tf0 = f * TF
        load_w1(0, 0)
        load_w1(0, 1)
        load_w1(0, 2)
        load_w2(0, 0)
        load_w2(0, 1)
        t_st3 = None
        for dc in range(NDC):
            b = dc % 2
            t_l = sch.op("sp", lambda E, dc=dc, b=b, tf0=tf0: E.dma_start(
                out=Fb.y1_scr[:, b * TF:(b + 1) * TF],
                in_=D.y1_sp[dc * 128:(dc + 1) * 128, tf0:tf0 + TF]),
                waits=tmax(st.y1_free[b], *t_spill_all), inc=("SYA" if b == 0 else "SYB"))
            sch.op("act", lambda E, b=b: E.copy(
                Fb.cast_f[:, b * TF:(b + 1) * TF], Fb.y1_scr[:, b * TF:(b + 1) * TF]),
                waits=tmax(t_l, st.cast_free[b]))
            t_cf = sch.op("act", lambda E, b=b: E.activation(
                Fb.sq_f[:], Fb.y1_scr[:, b * TF:(b + 1) * TF],
                AF.Square), waits=tmax(st.sqf_free), inc="ACT")
            st.y1_free[b] = t_cf

            def stats3(E, dc=dc, b=b):
                last = None
                for hh in range(2):
                    last = E.matmul(Fb.pHa[:, hh * 512:(hh + 1) * 512], C.ones_bf[:],
                                    Fb.cast_f[:, b * TF + hh * 512: b * TF + (hh + 1) * 512],
                                    start=(dc == 0), stop=(dc == NDC - 1))
                for hh in range(2):
                    last = E.matmul(Fb.pHb[:, hh * 512:(hh + 1) * 512], C.ones_bf[:],
                                    Fb.sq_f[:, hh * 512:(hh + 1) * 512],
                                    start=(dc == 0), stop=(dc == NDC - 1))
                return last
            t_st3 = sch.op("pe", stats3,
                           waits=tmax(t_cf,
                                      st.pH_free[0] if dc == 0 else None,
                                      st.pH_free[1] if dc == 0 else None),
                           inc="PE")
            st.cast_free[b] = t_st3
            st.sqf_free = t_st3
        t_mean3 = sch.op("act", lambda E: E.mul(Fb.mean_f[:], Fb.pHa[:], 1.0 / DM),
                         waits=tmax(t_st3, st.stat_free), inc="ACT")
        t_msq3 = sch.op("act", lambda E: E.activation(Fb.msqv_f[:], Fb.mean_f[:], AF.Square),
                        inc="ACT")
        t_var3 = sch.op("dve", lambda E: E.scalar_tensor_tensor(
            Fb.msqv_f[:], Fb.pHb[:], 1.0 / DM, Fb.msqv_f[:], ALU.mult, ALU.subtract),
            waits=tmax(t_msq3, t_st3), inc="DVE")
        sch.op("act", lambda E: E.activation(
            Fb.rstd_f[:], Fb.msqv_f[:], AF.Ln, bias=C.eps5[:]),
            waits=tmax(t_var3))
        t_rstd3 = sch.op("act", lambda E: E.activation(
            Fb.rstd_f[:], Fb.rstd_f[:], AF.Exp, scale=-0.5), inc="ACT")
        st.pH_free = [t_mean3, t_var3]
        t_u = None
        t_u_pl = None
        for dc in range(NDC):
            b = dc % 2
            t_l = sch.op("sp", lambda E, dc=dc, b=b, tf0=tf0: E.dma_start(
                out=Fb.y1_scr[:, b * TF:(b + 1) * TF],
                in_=D.y1_sp[dc * 128:(dc + 1) * 128, tf0:tf0 + TF]),
                waits=tmax(st.y1_free[b], *t_spill_all), inc=("SYA" if b == 0 else "SYB"))
            if dc % 3 == 2:
                lnp = Fb.cast_f[:].bitcast(F32)
                sch.op("pool", lambda E, b=b, lnp=lnp: E.tensor_sub(
                    lnp, Fb.y1_scr[:, b * TF:(b + 1) * TF], Fb.mean_f[:]),
                    waits=tmax(t_l, t_rstd3, t_mean3, st.u_free, st.cast_free[0],
                               st.cast_free[1]))
                t_u_pl = sch.op("pool", lambda E, dc=dc, lnp=lnp: E.tensor_mul(
                    Fb.u_bf[:, dc * TF:(dc + 1) * TF], lnp, Fb.rstd_f[:]),
                    inc="PL")
                st.y1_free[b] = t_u_pl
            else:
                sch.op("dve", lambda E, b=b: E.tensor_sub(
                    Fb.ln_f[:], Fb.y1_scr[:, b * TF:(b + 1) * TF], Fb.mean_f[:]),
                    waits=tmax(t_l, t_rstd3, t_mean3, st.u_free))
                t_u = sch.op("dve", lambda E, dc=dc: E.tensor_mul(
                    Fb.u_bf[:, dc * TF:(dc + 1) * TF], Fb.ln_f[:], Fb.rstd_f[:]),
                    inc="DVE")
                st.y1_free[b] = t_u
        if debug:
            sch.op("pool", lambda E, tf0=tf0: E.dma_start(
                out=D.dbg["d_u"].rearrange("(c p) t -> p c t", p=128)[:, :, tf0:tf0 + TF],
                in_=Fb.u_bf[:].rearrange("p (c t) -> p c t", c=NDC)),
                waits=tmax(t_u), inc="LW")
        t_rmm = [None] * 8
        t_sg_prev = None
        t_msk_prev = None
        t_gmul_prev = None
        t_ln_prev = None
        t_final_prev = None
        t_gfin = [None] * 8
        for ts_ in range(8):
            def rmm(E, ts_=ts_):
                last = None
                for dc in range(NDC):
                    last = E.matmul(
                        Fb.pYa[:, ts_ * 64: ts_ * 64 + NRT],
                        Fb.u_bf[:, dc * TF + ts_ * 128: dc * TF + ts_ * 128 + 128],
                        C.cT_bf[:, dc * NRT:(dc + 1) * NRT],
                        start=(dc == 0), stop=(dc == NDC - 1))
                return last
            t_rmm[ts_] = sch.op("pe", rmm,
                                waits=tmax(t_u, t_u_pl, st.pY_free[0] if ts_ == 0 else None,
                                           ct["lw"]),
                                inc="PE")
        for ts_ in range(8):
            t_sg = sch.op("act", lambda E, ts_=ts_: E.activation(
                C.sc16[:], Fb.pYa[:, ts_ * 64: ts_ * 64 + NRT], AF.Sigmoid),
                waits=tmax(t_rmm[7], t_gmul_prev, t_msk_prev), inc="ACT")
            t_bias = sch.op("pool", lambda E: E.tensor_add(C.sc16[:], C.sc16[:], C.bias_bc_s[:]),
                            waits=tmax(t_sg, ct["lx"]), inc="PL")
            t_mx = sch.op("dve", lambda E: E.max(out=C.mx8[:], in_=C.sc16[:]),
                          waits=tmax(t_bias), inc="DVE")
            t_thr = sch.op("pool", lambda E: E.tensor_copy(C.thr1[:], C.mx8[:, 3:4]),
                           waits=tmax(t_mx), inc="PL")
            t_msk = sch.op("dve", lambda E: E.tensor_tensor(
                C.msk16[:], C.sc16[:], C.thr1[:].to_broadcast([128, NRT]), ALU.is_ge),
                waits=tmax(t_thr), inc="DVE")
            t_msk_prev = t_msk
            t_gmul = sch.op("pool", lambda E, ts_=ts_: E.tensor_mul(
                C.g_all[:, ts_ * NRT:(ts_ + 1) * NRT], C.sc16[:], C.msk16[:]),
                waits=tmax(t_msk, st.gall_free[ts_], t_final_prev), inc="PL")
            t_gmul_prev = t_gmul
            t_red = sch.op("dve", lambda E, ts_=ts_: E.reduce_sum(
                C.rs1[:], C.g_all[:, ts_ * NRT:(ts_ + 1) * NRT], AX.X),
                waits=tmax(t_gmul), inc="DVE")
            t_rsc = sch.op("pool", lambda E: E.tensor_copy(C.rsB[:], C.rs1[:]),
                           waits=tmax(t_red, t_ln_prev), inc="PL")
            t_eps = sch.op("dve", lambda E: E.tensor_scalar_max(C.rsB[:], C.rsB[:], 1e-8),
                           waits=tmax(t_rsc), inc="DVE")
            t_ln = sch.op("act", lambda E: E.activation(C.rsC[:], C.rsB[:], AF.Ln),
                          waits=tmax(t_eps, t_final_prev), inc="ACT")
            t_ln_prev = t_ln
            t_exp = sch.op("act", lambda E: E.activation(C.rsC[:], C.rsC[:], AF.Exp, scale=-1.0),
                           inc="ACT")
            t_final = sch.op("dve", lambda E, ts_=ts_: E.tensor_tensor(
                C.g_all[:, ts_ * NRT:(ts_ + 1) * NRT], C.g_all[:, ts_ * NRT:(ts_ + 1) * NRT],
                C.rsC[:].to_broadcast([128, NRT]), ALU.mult),
                waits=tmax(t_exp), inc="DVE")
            t_final_prev = t_final
            t_gfin[ts_] = t_final
            if debug:
                t_gd = sch.op("sp", lambda E, ts_=ts_, tf0=tf0: E.dma_start(
                    out=D.dbg["d_g"][tf0 + ts_ * 128: tf0 + (ts_ + 1) * 128, :],
                    in_=C.g_all[:, ts_ * NRT:(ts_ + 1) * NRT]),
                    waits=tmax(t_final), inc="LX")
        st.pY_free[0] = t_sg
        t_hm = None
        t_ym = None
        t_hev = [None] * NFC
        t_ac17 = [None] * NDC
        for j in range(NE):
            routed = j >= 2
            jr = j - 2
            t_ge = None
            if routed:
                def gmm(E, jr=jr):
                    last = None
                    for hh in range(2):
                        last = E.matmul(Fb.pHa[:, hh * 512:(hh + 1) * 512],
                                        C.sel_bf[:, jr * 128:(jr + 1) * 128],
                                        Fb.gT_bf[:, hh * 512:(hh + 1) * 512],
                                        start=True, stop=True)
                    return last
                t_gm = sch.op("pe", gmm, waits=tmax(t_gT, st.pH_free[0]), inc="PE")
                t_ge = sch.op("act", lambda E: E.copy(Fb.G_bf[:], Fb.pHa[:]),
                              waits=tmax(t_gm, st.G_free), inc="ACT")
                st.pH_free[0] = t_ge
            for fc in range(NFC):
                pH = Fb.pHa if fc % 2 == 0 else Fb.pHb
                pidx = fc % 2
                widx = (j * NFC + fc) % 3

                def hmm(E, pH=pH, widx=widx):
                    last = None
                    for dc in range(NDC):
                        for hh in range(2):
                            last = E.matmul(
                                pH[:, hh * 512:(hh + 1) * 512],
                                Fb.w1_sb[:, widx * DM + dc * 128: widx * DM + (dc + 1) * 128],
                                Fb.u_bf[:, dc * TF + hh * 512: dc * TF + (hh + 1) * 512],
                                start=(dc == 0), stop=(dc == NDC - 1))
                    return last
                t_hm = sch.op("pe", hmm,
                              waits=tmax(t_w1[(j, fc)], t_u, t_u_pl, st.pH_free[pidx]), inc="PE")
                st.w1free[widx] = t_hm
                nj, nfc = (j, fc + 3) if fc + 3 < NFC else (j + 1, fc + 3 - NFC)
                if nj < NE:
                    load_w1(nj, nfc)
                if j == 0 and fc < 8:
                    ts_ = fc
                    t_gtr = sch.op("pe", lambda E, ts_=ts_: E.transpose(
                        Fb.pYb[0:NRT, ts_ * 128:(ts_ + 1) * 128],
                        C.g_all[:, ts_ * NRT:(ts_ + 1) * NRT], C.id_f[:]),
                        waits=tmax(t_gfin[ts_], st.pY_free[1] if ts_ == 0 else None),
                        inc="PE")
                    st.gall_free[ts_] = t_gtr
                    if ts_ == 7:
                        t_gT = sch.op("act", lambda E: E.copy(Fb.gT_bf[:], Fb.pYb[0:NRT, :]),
                                      waits=tmax(t_gtr, st.gT_free), inc="ACT")
                        st.pY_free[1] = t_gT
                t_he = sch.op("act", lambda E, fc=fc, pH=pH, j=j: E.activation(
                    Fb.h_bf[:, fc * TF:(fc + 1) * TF], pH[:], AF.Relu,
                    bias=C.b1c_s[:, j * NFC + fc: j * NFC + fc + 1]),
                    waits=tmax(t_hm, st.h_free[fc], ct["lx"]), inc="ACT")
                st.pH_free[pidx] = t_he
                t_hev[fc] = t_he
                if routed:
                    t_hg = sch.op("dve", lambda E, fc=fc: E.tensor_mul(
                        Fb.h_bf[:, fc * TF:(fc + 1) * TF],
                        Fb.h_bf[:, fc * TF:(fc + 1) * TF], Fb.G_bf[:]),
                        waits=tmax(t_he, t_ge), inc="DVE")
                    t_hev[fc] = t_hg
            if routed:
                st.G_free = t_hev[NFC - 1]
            if debug and j == 0 and f == 0:
                sch.op("pool", lambda E: E.dma_start(
                    out=D.dbg["d_h0"].rearrange("(c p) t -> p c t", p=128),
                    in_=Fb.h_bf[:].rearrange("p (c t) -> p c t", c=NFC)),
                    waits=tmax(t_hev[NFC - 1]), inc="LW")
            for dcp in range(NDC):
                pY = Fb.pYa if dcp % 2 == 0 else Fb.pYb
                pidx = dcp % 2
                widx = (j * NDC + dcp) % 2

                def ymm(E, dcp=dcp, pY=pY, widx=widx, j=j):
                    last = None
                    islast = (j == NE - 1)
                    for fc in range(NFC):
                        for hh in range(2):
                            last = E.matmul(
                                pY[:, hh * 512:(hh + 1) * 512],
                                Fb.w2_sb[:, widx * NFC * 128 + fc * 128: widx * NFC * 128 + (fc + 1) * 128],
                                Fb.h_bf[:, fc * TF + hh * 512: fc * TF + (hh + 1) * 512],
                                start=(fc == 0), stop=(fc == NFC - 1 and not islast))
                    if islast:
                        for hh in range(2):
                            last = E.matmul(
                                pY[:, hh * 512:(hh + 1) * 512],
                                C.b2r_bf[:, dcp * 128:(dcp + 1) * 128],
                                Fb.gT_bf[:, hh * 512:(hh + 1) * 512],
                                start=False, stop=True)
                    return last
                t_ym = sch.op("pe", ymm,
                              waits=tmax(t_hev[NFC - 1], t_w2[(j, dcp)], st.pY_free[pidx]),
                              inc="PE")
                st.w2free[widx] = t_ym
                nj, ndcp = (j, dcp + 2) if dcp + 2 < NDC else (j + 1, dcp + 2 - NDC)
                if nj < NE:
                    load_w2(nj, ndcp)
                if j == 0:
                    def acc0(E, dcp=dcp, pY=pY):
                        E.tensor_tensor(Fb.acc[:, dcp * TF:(dcp + 1) * TF], pY[:],
                                        C.b2sh_s[:, dcp:dcp + 1].to_broadcast([128, TF]),
                                        ALU.add)
                        return E.tensor_add(Fb.acc[:, dcp * TF:(dcp + 1) * TF],
                                            Fb.acc[:, dcp * TF:(dcp + 1) * TF],
                                            Fb.u_bf[:, dcp * TF:(dcp + 1) * TF])
                    t_ac = sch.op("dve", acc0,
                                  waits=tmax(t_ym, ct["b2sh"], st.acc_free), inc="DVE")
                else:
                    t_ac = sch.op("dve", lambda E, dcp=dcp, pY=pY: E.tensor_add(
                        Fb.acc[:, dcp * TF:(dcp + 1) * TF],
                        Fb.acc[:, dcp * TF:(dcp + 1) * TF], pY[:]),
                        waits=tmax(t_ym), inc="DVE")
                st.pY_free[pidx] = t_ac
                if j == NE - 1:
                    t_ac17[dcp] = t_ac
            for fc in range(NFC):
                st.h_free[fc] = t_ym
        st.u_free = t_hm
        st.gT_free = t_ym
        t_acc_all = t_ac
        t_st4 = None
        for dc in range(NDC):
            b = dc % 2
            sch.op("act", lambda E, dc=dc, b=b: E.copy(
                Fb.cast_f[:, b * TF:(b + 1) * TF], Fb.acc[:, dc * TF:(dc + 1) * TF]),
                waits=tmax(t_ac17[dc], st.cast_free[b], t_u_pl))
            t_c4 = sch.op("act", lambda E, dc=dc, b=b: E.activation(
                Fb.sq_f[:], Fb.acc[:, dc * TF:(dc + 1) * TF],
                AF.Square), waits=tmax(st.sqf_free), inc="ACT")

            def stats4(E, dc=dc, b=b):
                last = None
                for hh in range(2):
                    last = E.matmul(Fb.pHa[:, hh * 512:(hh + 1) * 512], C.ones_bf[:],
                                    Fb.cast_f[:, b * TF + hh * 512: b * TF + (hh + 1) * 512],
                                    start=(dc == 0), stop=(dc == NDC - 1))
                for hh in range(2):
                    last = E.matmul(Fb.pHb[:, hh * 512:(hh + 1) * 512], C.ones_bf[:],
                                    Fb.sq_f[:, hh * 512:(hh + 1) * 512],
                                    start=(dc == 0), stop=(dc == NDC - 1))
                return last
            t_st4 = sch.op("pe", stats4,
                           waits=tmax(t_c4,
                                      st.pH_free[0] if dc == 0 else None,
                                      st.pH_free[1] if dc == 0 else None),
                           inc="PE")
            st.cast_free[b] = t_st4
            st.sqf_free = t_st4
        t_mean4 = sch.op("act", lambda E: E.mul(Fb.mean_f[:], Fb.pHa[:], 1.0 / DM),
                         waits=tmax(t_st4, t_u, t_u_pl), inc="ACT")
        t_msq4 = sch.op("act", lambda E: E.activation(Fb.msqv_f[:], Fb.mean_f[:], AF.Square),
                        inc="ACT")
        t_var4 = sch.op("dve", lambda E: E.scalar_tensor_tensor(
            Fb.msqv_f[:], Fb.pHb[:], 1.0 / DM, Fb.msqv_f[:], ALU.mult, ALU.subtract),
            waits=tmax(t_msq4, t_st4), inc="DVE")
        sch.op("act", lambda E: E.activation(
            Fb.rstd_f[:], Fb.msqv_f[:], AF.Ln, bias=C.eps5[:]),
            waits=tmax(t_var4))
        t_rstd4 = sch.op("act", lambda E: E.activation(
            Fb.rstd_f[:], Fb.rstd_f[:], AF.Exp, scale=-0.5), inc="ACT")
        st.pH_free = [t_mean4, t_var4]
        st.stat_free = t_rstd4
        if debug:
            sch.op("sp", lambda E, tf0=tf0: E.dma_start(
                out=D.dbg["d_acc"].rearrange("(c p) t -> p c t", p=128)[:, :, tf0:tf0 + TF],
                in_=Fb.acc[:].rearrange("p (c t) -> p c t", c=NDC)),
                waits=tmax(t_acc_all), inc="LX")
        t_out = None
        for dc in range(NDC):
            b = dc % 2
            t_l = sch.op("sp", lambda E, dc=dc, b=b, tf0=tf0: E.dma_start(
                out=Fb.y1_scr[:, b * TF:(b + 1) * TF],
                in_=D.y1_sp[dc * 128:(dc + 1) * 128, tf0:tf0 + TF]),
                waits=tmax(st.y1_free[b], *t_spill_all), inc=("SYA" if b == 0 else "SYB"))
            sch.op("dve", lambda E, dc=dc: E.tensor_sub(
                Fb.ln_f[:], Fb.acc[:, dc * TF:(dc + 1) * TF], Fb.mean_f[:]),
                waits=tmax(t_rstd4, t_mean4))
            sch.op("dve", lambda E: E.tensor_mul(Fb.ln_f[:], Fb.ln_f[:], Fb.rstd_f[:]))
            t_out = sch.op("dve", lambda E, dc=dc, b=b: E.tensor_add(
                Fb.acc[:, dc * TF:(dc + 1) * TF], Fb.ln_f[:],
                Fb.y1_scr[:, b * TF:(b + 1) * TF]),
                waits=tmax(t_l), inc="DVE")
            st.y1_free[b] = t_out
        t_store = sch.op("sp", lambda E, tf0=tf0: E.dma_start(
            out=D.out_fm.rearrange("(c p) t -> p c t", p=128)[:, :, tf0:tf0 + TF],
            in_=Fb.acc[:].rearrange("p (c t) -> p c t", c=NDC)),
            waits=tmax(t_out), inc="ST")
        st.acc_free = t_store

    sch.join("sp", [("ST", sch.cnt["ST"])])



def _host_layout(inputs):
    x = np.asarray(inputs["L0_attn_x"], dtype=np.float32)
    wq = np.asarray(inputs["L0_attn_wq"], dtype=np.float32)
    wk = np.asarray(inputs["L0_attn_wk"], dtype=np.float32)
    wv = np.asarray(inputs["L0_attn_wv"], dtype=np.float32)
    wo = np.asarray(inputs["L0_attn_wo"], dtype=np.float32)
    gq = np.asarray(inputs["L0_attn_gamma_q"], dtype=np.float32)
    gk = np.asarray(inputs["L0_attn_gamma_k"], dtype=np.float32)
    sw1 = np.asarray(inputs["L0_ffn_shared_w1"], dtype=np.float32)
    sb1 = np.asarray(inputs["L0_ffn_shared_b1"], dtype=np.float32)
    sw2 = np.asarray(inputs["L0_ffn_shared_w2"], dtype=np.float32)
    sb2 = np.asarray(inputs["L0_ffn_shared_b2"], dtype=np.float32)
    rw1 = np.asarray(inputs["L0_ffn_routed_w1"], dtype=np.float32)
    rb1 = np.asarray(inputs["L0_ffn_routed_b1"], dtype=np.float32)
    rw2 = np.asarray(inputs["L0_ffn_routed_w2"], dtype=np.float32)
    rb2 = np.asarray(inputs["L0_ffn_routed_b2"], dtype=np.float32)
    cent = np.asarray(inputs["L0_ffn_centroids"], dtype=np.float32)
    bias = np.asarray(inputs["L0_ffn_bias"], dtype=np.float32)

    com = {}
    com["wq_sl"] = np.ascontiguousarray(
        wq.reshape(H, 128, NDC, 128).transpose(0, 3, 2, 1))
    com["wk_sl"] = np.ascontiguousarray(
        wk.reshape(HK, 128, NDC, 128).transpose(0, 3, 2, 1))
    com["wvT"] = np.ascontiguousarray(wv.T)
    com["wo_sl"] = np.ascontiguousarray(
        wo.reshape(NDC, 128, NDC, 128).transpose(0, 3, 2, 1))
    w1_all = np.concatenate([sw1, rw1], axis=0)
    com["w1_sl"] = np.ascontiguousarray(
        w1_all.reshape(NE, NDC, 128, NFC, 128).transpose(0, 3, 2, 1, 4))
    w2_all = np.concatenate([sw2, rw2], axis=0)
    com["w2_sl"] = np.ascontiguousarray(
        w2_all.reshape(NE, NFC, 128, NDC, 128).transpose(0, 3, 2, 1, 4))
    b1_all = np.concatenate([sb1, rb1], axis=0)
    com["b1c"] = np.ascontiguousarray(b1_all.reshape(NE, NFC, 128).transpose(2, 0, 1)
                                      .reshape(128, NE * NFC))
    com["b2c"] = np.ascontiguousarray(sb2.reshape(2, NDC, 128).transpose(2, 0, 1)
                                      .reshape(128, 2 * NDC))
    com["b2r"] = np.ascontiguousarray(rb2)
    com["cT"] = np.ascontiguousarray(cent.T)
    com["bias_bc"] = np.ascontiguousarray(np.tile(bias[None, :], (128, 1)))
    com["gq"] = np.ascontiguousarray(gq.reshape(128, 1))
    com["gk"] = np.ascontiguousarray(gk.reshape(128, 1))

    inv_freq = (1.0 / RBASE ** (np.arange(0, DH, 2, dtype=np.float64) / DH))
    ang = np.arange(T, dtype=np.float64)[:, None] * inv_freq[None, :]
    cos = np.cos(ang)
    sin = np.sin(ang)
    com["cos_t"] = np.ascontiguousarray(
        np.concatenate([cos, cos], axis=1).T.astype(np.float32))
    com["sin_t"] = np.ascontiguousarray(
        np.concatenate([-sin, sin], axis=1).T.astype(np.float32))
    com["ones_c"] = np.ones((128, 128), np.float32)
    com["ident_c"] = np.eye(128, dtype=np.float32)
    il = np.arange(128)[:, None]
    jl = np.arange(128)[None, :]
    com["mcaus"] = np.where(jl <= il, 0.0, -1e9).astype(np.float32)
    com["medge"] = np.where(jl >= il, 0.0, -1e9).astype(np.float32)
    sel = np.zeros((NRT, NRT * 128), np.float32)
    for j in range(NRT):
        sel[j, j * 128:(j + 1) * 128] = 1.0
    com["sel_c"] = sel
    com["identu_c"] = np.eye(128, dtype=np.uint8)
    kk = np.arange(128)[:, None]
    mm = np.arange(128)[None, :]
    com["p64_c"] = (np.abs(kk - mm) == 64).astype(np.float32)

    in_maps = []
    for b in range(8):
        m = dict(com)
        m["x_fm"] = np.ascontiguousarray(x[b].T)
        in_maps.append(m)
    return in_maps


_NC_CACHE = {}


def kernel(**inputs):
    in_maps = _host_layout(inputs)
    key = False
    if key not in _NC_CACHE:
        _NC_CACHE[key] = build_program(debug=False)
    nc = _NC_CACHE[key]
    res = run_bass_kernel_spmd(nc, in_maps, core_ids=list(range(8)))
    out = np.stack([np.asarray(r["out_fm"]).T for r in res.results], axis=0)
    return np.ascontiguousarray(out.astype(np.float32))


# version 18
# speedup vs baseline: 1.0390x; 1.0390x over previous
import sys
import numpy as np

try:
    import concourse.bass as bass
except ImportError:
    sys.path.insert(0, "/opt/trn_rl_repo")
    import concourse.bass as bass
import concourse.mybir as mybir
from concourse.bass_utils import run_bass_kernel_spmd
from contextlib import ExitStack

F32 = mybir.dt.float32
BF16 = mybir.dt.bfloat16
AF = mybir.ActivationFunctionType
ALU = mybir.AluOpType
AX = mybir.AxisListType

T = 2048
DM = 2048
NDC = 16
H = 16
HK = 4
GQ = 4
DH = 128
DFF = 1408
NFC = 11
NE = 18
NRT = 16
WINDOW = 1024
EPS_QK = 1e-6
EPS_LN = 1e-5
RBASE = 10000.0
TA = 512
NTI = T // TA
TF = 1024
NTF = T // TF
INV_SQRT_DH = float(1.0 / np.sqrt(DH))

DMA_SEMS = ("LW", "LX", "ST", "SX", "SKV", "SQA", "SQB", "SOA", "SOB", "S1A", "S1B", "S1C", "S2A", "S2B", "S2C", "SYA", "SYB", "SXB", "STA", "STB", "STC", "STD")
OWN_SEM = {"pe": "PE", "act": "ACT", "dve": "DVE", "pool": "PL", "sp": None}


class Sched:
    def __init__(self):
        self.cur = None

    def start(self, engname, eng, sems):
        self.cur = engname
        self.eng = eng
        self.sems = sems
        self.cnt = {k: 0 for k in sems}
        self.lastwait = {k: 0 for k in sems}

    def op(self, engname, fn, waits=(), inc=None):
        tok = None
        if inc is not None:
            self.cnt[inc] += 16 if inc in DMA_SEMS else 1
            tok = (inc, self.cnt[inc])
        if engname == self.cur:
            own = OWN_SEM.get(engname)
            for w in waits:
                if w is None:
                    continue
                s, v = w
                if s == own or v <= self.lastwait[s]:
                    continue
                self.eng.wait_ge(self.sems[s], v)
                self.lastwait[s] = v
            ins = fn(self.eng)
            if inc is not None:
                ins.then_inc(self.sems[inc], 16 if inc in DMA_SEMS else 1)
        return tok

    def join(self, engname, waits):
        if engname == self.cur:
            own = OWN_SEM.get(engname)
            for w in waits:
                if w is None:
                    continue
                s, v = w
                if s == own or v <= self.lastwait[s]:
                    continue
                self.eng.wait_ge(self.sems[s], v)
                self.lastwait[s] = v


def tmax(*toks):
    best = {}
    for t in toks:
        if t is None:
            continue
        s, v = t
        if v > best.get(s, -1):
            best[s] = v
    return [(s, v) for s, v in best.items()]


class NS:
    pass


def build_program(debug=False):
    nc = bass.Bass("TRN2", target_bir_lowering=False)

    D = NS()
    D.x_fm = nc.dram_tensor("x_fm", [DM, T], F32, kind="ExternalInput").ap()
    D.wq_sl = nc.dram_tensor("wq_sl", [H, 128, NDC, 128], F32, kind="ExternalInput").ap()
    D.wk_sl = nc.dram_tensor("wk_sl", [HK, 128, NDC, 128], F32, kind="ExternalInput").ap()
    D.wvT = nc.dram_tensor("wvT", [DM, HK * DH], F32, kind="ExternalInput").ap()
    D.wo_sl = nc.dram_tensor("wo_sl", [NDC, 128, NDC, 128], F32, kind="ExternalInput").ap()
    D.w1_sl = nc.dram_tensor("w1_sl", [NE, NFC, 128, NDC, 128], F32, kind="ExternalInput").ap()
    D.w2_sl = nc.dram_tensor("w2_sl", [NE, NDC, 128, NFC, 128], F32, kind="ExternalInput").ap()
    D.b1c = nc.dram_tensor("b1c", [128, NE * NFC], F32, kind="ExternalInput").ap()
    D.b2c = nc.dram_tensor("b2c", [128, 2 * NDC], F32, kind="ExternalInput").ap()
    D.b2r = nc.dram_tensor("b2r", [NRT, DM], F32, kind="ExternalInput").ap()
    D.cT = nc.dram_tensor("cT", [DM, NRT], F32, kind="ExternalInput").ap()
    D.bias_bc = nc.dram_tensor("bias_bc", [128, NRT], F32, kind="ExternalInput").ap()
    D.gq = nc.dram_tensor("gq", [128, 1], F32, kind="ExternalInput").ap()
    D.gk = nc.dram_tensor("gk", [128, 1], F32, kind="ExternalInput").ap()
    D.cos_t = nc.dram_tensor("cos_t", [128, T], F32, kind="ExternalInput").ap()
    D.sin_t = nc.dram_tensor("sin_t", [128, T], F32, kind="ExternalInput").ap()
    D.ones_c = nc.dram_tensor("ones_c", [128, 128], F32, kind="ExternalInput").ap()
    D.ident_c = nc.dram_tensor("ident_c", [128, 128], F32, kind="ExternalInput").ap()
    D.mcaus = nc.dram_tensor("mcaus", [128, 128], F32, kind="ExternalInput").ap()
    D.medge = nc.dram_tensor("medge", [128, 128], F32, kind="ExternalInput").ap()
    D.sel_c = nc.dram_tensor("sel_c", [NRT, NRT * 128], F32, kind="ExternalInput").ap()
    D.p64_c = nc.dram_tensor("p64_c", [128, 128], F32, kind="ExternalInput").ap()
    D.identu_c = nc.dram_tensor("identu_c", [128, 128], mybir.dt.uint8, kind="ExternalInput").ap()
    D.y1_sp = nc.dram_tensor("y1_spill", [DM, T], F32, kind="Internal").ap()
    D.out_fm = nc.dram_tensor("out_fm", [DM, T], F32, kind="ExternalOutput").ap()
    D.dbg = {}
    if debug:
        for nm, shape in (("d_lnx", (DM, T)), ("d_q", (DH, T)), ("d_k", (HK * DH, T)),
                          ("d_v", (T, HK * DH)), ("d_att", (DM, T)), ("d_y1", (DM, T)),
                          ("d_u", (DM, T)), ("d_g", (T, NRT)), ("d_h0", (DFF, TF)),
                          ("d_y0", (DH, T)),
                          ("d_acc", (DM, T))):
            D.dbg[nm] = nc.dram_tensor(nm, list(shape), F32, kind="ExternalOutput").ap()

    with ExitStack() as ctx:
        e = ctx.enter_context
        C = NS()
        C.cos_s = e(nc.sbuf_tensor("cos_s", [128, T], BF16))
        C.sin_s = e(nc.sbuf_tensor("sin_s", [128, T], BF16))
        C.mcaus_bf = e(nc.sbuf_tensor("mcaus_bf", [128, 128], BF16))
        C.medge_bf = e(nc.sbuf_tensor("medge_bf", [128, 128], BF16))
        C.id_bf = e(nc.sbuf_tensor("id_bf", [128, 128], BF16))
        C.id_f = e(nc.sbuf_tensor("id_f", [128, 128], F32))
        C.ones_bf = e(nc.sbuf_tensor("ones_bf", [128, 128], BF16))
        C.p64_f = e(nc.sbuf_tensor("p64_f", [128, 128], F32))
        C.gq_s = e(nc.sbuf_tensor("gq_s", [128, 1], F32))
        C.gk_s = e(nc.sbuf_tensor("gk_s", [128, 1], F32))
        C.eps6 = e(nc.sbuf_tensor("eps6", [128, 1], F32))
        C.eps5 = e(nc.sbuf_tensor("eps5", [128, 1], F32))
        C.b1c_s = e(nc.sbuf_tensor("b1c_s", [128, NE * NFC], F32))
        C.b2c_s = e(nc.sbuf_tensor("b2c_s", [128, 2 * NDC], F32))
        C.b2sh_s = e(nc.sbuf_tensor("b2sh_s", [128, NDC], F32))
        C.b2r_bf = e(nc.sbuf_tensor("b2r_bf", [NRT, DM], BF16))
        C.sel_bf = e(nc.sbuf_tensor("sel_bf", [NRT, NRT * 128], BF16))
        C.cT_bf = e(nc.sbuf_tensor("cT_bf", [128, NDC * NRT], BF16))
        C.bias_bc_s = e(nc.sbuf_tensor("bias_bc_s", [128, NRT], F32))
        C.nmax = e(nc.sbuf_tensor("nmax", [128, 1], F32))
        C.sums = e(nc.sbuf_tensor("sums", [128, 1], F32))
        C.lnsum = e(nc.sbuf_tensor("lnsum", [128, 1], F32))
        C.bias2 = e(nc.sbuf_tensor("bias2", [128, 1], F32))
        C.sc16 = e(nc.sbuf_tensor("sc16", [128, NRT], F32))
        C.mx8 = e(nc.sbuf_tensor("mx8", [128, 8], F32))
        C.msk16 = e(nc.sbuf_tensor("msk16", [128, NRT], F32))
        C.g16 = e(nc.sbuf_tensor("g16", [128, NRT], F32))
        C.g_all = e(nc.sbuf_tensor("g_all", [128, 8 * NRT], F32))
        C.rs1 = e(nc.sbuf_tensor("rs1", [128, 1], F32))
        C.thr1 = e(nc.sbuf_tensor("thr1", [128, 1], F32))
        C.rsB = e(nc.sbuf_tensor("rsB", [128, 1], F32))
        C.rsC = e(nc.sbuf_tensor("rsC", [128, 1], F32))
        C.rsb_bf = e(nc.sbuf_tensor("rsb_bf", [128, 2], BF16))
        C.rs2 = e(nc.sbuf_tensor("rs2", [128, 2], F32))
        C.negC = e(nc.sbuf_tensor("negC", [128, 1], F32))
        C.mqA = e(nc.sbuf_tensor("mqA", [128, 1], F32))
        C.gbc_bf = e(nc.sbuf_tensor("gbc_bf", [128, 128], BF16))
        C.eps35 = e(nc.sbuf_tensor("eps35", [128, 1], F32))
        C.diag_bf = e(nc.sbuf_tensor("diag_bf", [128, 2 * 128], BF16))
        C.id_u8 = e(nc.sbuf_tensor("id_u8", [128, 128], mybir.dt.uint8))

        sems = {}
        for s in ("LW", "LX", "ST", "PE", "ACT", "DVE", "PL", "SX", "SKV", "SQA", "SQB", "SOA", "SOB", "S1A", "S1B", "S1C", "S2A", "S2B", "S2C", "SYA", "SYB", "SXB", "STA", "STB", "STC", "STD"):
            sems[s] = e(nc.semaphore(s))

        block = e(nc.Block())
        sch = Sched()
        state = {}

        with ExitStack() as actx:
            ae = actx.enter_context
            A = NS()
            A.x_bf = ae(nc.sbuf_tensor("x_bf", [128, 2 * NDC * TA], BF16))
            A.sq_scr = ae(nc.sbuf_tensor("sq_scr", [128, 2 * TA], BF16))
            A.y1s = ae(nc.sbuf_tensor("y1s", [128, 4 * TA], F32))
            A.mean2_s = ae(nc.sbuf_tensor("mean2_s", [128, TA], F32))
            A.msqv2_s = ae(nc.sbuf_tensor("msqv2_s", [128, TA], F32))
            A.rstd2_s = ae(nc.sbuf_tensor("rstd2_s", [128, TA], F32))
            A.ln_bf = ae(nc.sbuf_tensor("ln_bf", [128, NDC * TA], BF16))
            A.ln_scr = ae(nc.sbuf_tensor("ln_scr", [128, TA], F32))
            A.mean_s = ae(nc.sbuf_tensor("mean_s", [128, TA], F32))
            A.msqv_s = ae(nc.sbuf_tensor("msqv_s", [128, TA], F32))
            A.rstd_s = ae(nc.sbuf_tensor("rstd_s", [128, TA], F32))
            A.k_full = ae(nc.sbuf_tensor("k_full", [128, HK * T], BF16))
            A.v_full = ae(nc.sbuf_tensor("v_full", [128, NDC * TA], BF16))
            A.q_ro = ae(nc.sbuf_tensor("q_ro", [128, TA], F32))
            A.q_fp = ae(nc.sbuf_tensor("q_fp", [128, 2 * TA], F32))
            A.q_sq = ae(nc.sbuf_tensor("q_sq", [128, TA], BF16))
            A.rn_s = ae(nc.sbuf_tensor("rn_s", [128, TA], F32))
            A.q_bf = ae(nc.sbuf_tensor("q_bf", [128, TA], BF16))
            A.p_bf = ae(nc.sbuf_tensor("p_bf", [128, 2 * 1152], BF16))
            A.pT_sb = ae(nc.sbuf_tensor("pT_sb", [128, 1152], BF16))
            A.y_att = ae(nc.sbuf_tensor("y_att", [128, H * TA], BF16))
            A.att_bf = ae(nc.sbuf_tensor("att_bf", [128, NDC * TA], BF16))
            A.wbig = ae(nc.sbuf_tensor("wbig", [128, 4 * DM], BF16))
            A.wq_sb = ae(nc.sbuf_tensor("wq_sb", [128, 2 * DM], BF16))
            A.wo_sb = ae(nc.sbuf_tensor("wo_sb", [128, 2 * DM], BF16))
            A.pA = ae(nc.psum_tensor("pA", [128, 512], F32))
            A.pB = ae(nc.psum_tensor("pB", [128, 512], F32))
            A.pQ = ae(nc.psum_tensor("pQ", [128, 512], F32))
            A.pS = ae(nc.psum_tensor("pS", [128, 1536], F32))
            A.pTT = ae(nc.psum_tensor("pTT", [128, 2048], BF16))

            def attn_pass(engname, eng):
                sch.start(engname, eng, sems)
                ct = _sched_consts(sch, nc, D, C)
                af = _sched_attn(sch, nc, D, C, A, ct, debug)
                state[engname] = (ct, af, dict(sch.cnt), dict(sch.lastwait))

            block.tensor(lambda eng: attn_pass("pe", eng))
            block.scalar(lambda eng: attn_pass("act", eng))
            block.vector(lambda eng: attn_pass("dve", eng))
            block.gpsimd(lambda eng: attn_pass("pool", eng))
            block.sync(lambda eng: attn_pass("sp", eng))

        with ExitStack() as fctx:
            fe = fctx.enter_context
            Fb = NS()
            Fb.u_bf = fe(nc.sbuf_tensor("u_bf", [128, NDC * TF], BF16))
            Fb.acc = fe(nc.sbuf_tensor("acc", [128, NDC * TF], F32))
            Fb.h_bf = fe(nc.sbuf_tensor("h_bf", [128, NFC * TF], BF16))
            Fb.G_bf = fe(nc.sbuf_tensor("G_bf", [128, TF], BF16))
            Fb.gT_bf = fe(nc.sbuf_tensor("gT_bf", [NRT, TF], BF16))
            Fb.y1_scr = fe(nc.sbuf_tensor("y1_scr", [128, 2 * TF], F32))
            Fb.cast_f = fe(nc.sbuf_tensor("cast_f", [128, 2 * TF], BF16))
            Fb.sq_f = fe(nc.sbuf_tensor("sq_f", [128, TF], BF16))
            Fb.ln_f = fe(nc.sbuf_tensor("ln_f", [128, TF], F32))
            Fb.mean_f = fe(nc.sbuf_tensor("mean_f", [128, TF], F32))
            Fb.msqv_f = fe(nc.sbuf_tensor("msqv_f", [128, TF], F32))
            Fb.rstd_f = fe(nc.sbuf_tensor("rstd_f", [128, TF], F32))
            Fb.w1_sb = fe(nc.sbuf_tensor("w1_sb", [128, 3 * NDC * 128], BF16))
            Fb.w2_sb = fe(nc.sbuf_tensor("w2_sb", [128, 2 * NFC * 128], BF16))
            Fb.pHa = fe(nc.psum_tensor("pHa", [128, TF], F32))
            Fb.pHb = fe(nc.psum_tensor("pHb", [128, TF], F32))
            Fb.pYa = fe(nc.psum_tensor("pYa", [128, TF], F32))
            Fb.pYb = fe(nc.psum_tensor("pYb", [128, TF], F32))

            def ffn_pass(engname, eng):
                ct, af, cnt, lastwait = state[engname]
                sch.start(engname, eng, sems)
                sch.cnt = cnt
                sch.lastwait = lastwait
                _sched_ffn(sch, nc, D, C, Fb, ct, af, debug)

            block.tensor(lambda eng: ffn_pass("pe", eng))
            block.scalar(lambda eng: ffn_pass("act", eng))
            block.vector(lambda eng: ffn_pass("dve", eng))
            block.gpsimd(lambda eng: ffn_pass("pool", eng))
            block.sync(lambda eng: ffn_pass("sp", eng))

    return nc


def _sched_consts(sch, nc, D, C):
    ct = {}
    t = None
    for dst, src in ((C.gq_s, D.gq), (C.gk_s, D.gk),
                     (C.b1c_s, D.b1c), (C.bias_bc_s, D.bias_bc), (C.id_f, D.ident_c),
                     (C.p64_f, D.p64_c)):
        t = sch.op("sp", lambda E, d=dst, s=src: E.dma_start(out=d[:], in_=s), inc="LX")
    sch.op("sp", lambda E: E.dma_start(out=C.id_u8[:], in_=D.identu_c), inc="LX")
    t = sch.op("sp", lambda E: E.dma_start(
        out=C.b2c_s[:].rearrange("p (n c) -> p n c", n=2),
        in_=D.b2c.rearrange("p (n c) -> p n c", n=2)), inc="LX")
    ct["lx"] = t
    t = None
    for dst, src in ((C.ones_bf, D.ones_c), (C.id_bf, D.ident_c), (C.b2r_bf, D.b2r),
                     (C.sel_bf, D.sel_c), (C.cos_s, D.cos_t), (C.sin_s, D.sin_t),
                     (C.mcaus_bf, D.mcaus), (C.medge_bf, D.medge)):
        t = sch.op("pool", lambda E, d=dst, s=src: E.dma_start(out=d[:], in_=s), inc="LW")
    t = sch.op("pool", lambda E: E.dma_start(
        out=C.cT_bf[:].rearrange("p (c j) -> p c j", c=NDC),
        in_=D.cT.rearrange("(c p) j -> p c j", p=128)), inc="LW")
    ct["lw"] = t
    sch.op("dve", lambda E: E.memset(C.eps6[:], EPS_QK))
    sch.op("dve", lambda E: E.memset(C.diag_bf[:], 0.0))
    sch.op("dve", lambda E: E.memset(C.eps35[:], 1e-35))
    ct["eps"] = sch.op("dve", lambda E: E.memset(C.eps5[:], EPS_LN), inc="DVE")
    ct["b2sh"] = sch.op("dve", lambda E: E.tensor_add(
        C.b2sh_s[:], C.b2c_s[:, 0:NDC], C.b2c_s[:, NDC:2 * NDC]),
        waits=[ct["lx"]], inc="DVE")
    return ct


def _sched_attn(sch, nc, D, C, A, ct, debug):
    st = NS()
    st.wbig_free = None
    st.wq_free = [None, None]
    st.wo_free = [None, None]
    st.scr_free = [None, None]
    st.pA_free = ct["eps"]
    st.pB_free = None
    st.pQ_free = None
    st.pS_free = None
    st.pTT_free = None
    st.pbf_free = None
    st.pT_free = None
    st.lnbf_free = None
    st.yatt_free = None
    st.attbf_free = []
    st.stat_free = []
    st.qfp_free = [None, None]
    st.xbf_free = [[], []]
    st.xld = [None, None]
    st.y1s_free = []
    st.qro_free = None
    st.proj_tok = None
    st.diag_free = [None, None]
    st.tr_free = [None, None]
    st.pbf_par_free = [None, None]
    st.vts = [None] * NDC
    t_spill = None

    def rope_start(h, par):
        t_ev = sch.op("act", lambda E, par=par: E.copy(
            A.q_fp[:, par * TA:(par + 1) * TA], A.pQ[:]),
            waits=tmax(st.proj_tok, st.qfp_free[par]), inc="ACT")
        st.pQ_free = t_ev
        return t_ev

    def rope_finish(t_ev, par, gamma_s, scale2, dst_fn, ti0):
        t_sw = sch.op("pe", lambda E, par=par: E.matmul(
            A.pA[:], C.p64_f[:], A.q_fp[:, par * TA:(par + 1) * TA],
            start=True, stop=True),
            waits=tmax(t_ev, st.pA_free), inc="PE")
        sch.op("dve", lambda E, par=par: E.tensor_mul(
            A.ln_scr[:], A.q_fp[:, par * TA:(par + 1) * TA], C.cos_s[:, ti0:ti0 + TA]),
            waits=tmax(t_ev))
        sch.op("dve", lambda E: E.tensor_mul(
            A.q_ro[:], A.pA[:], C.sin_s[:, ti0:ti0 + TA]), waits=tmax(t_sw))
        t_rope = sch.op("dve", lambda E: E.tensor_add(
            A.q_ro[:], A.q_ro[:], A.ln_scr[:]), inc="DVE")
        st.pA_free = t_rope
        st.qfp_free[par] = t_rope
        t_sq = sch.op("dve", lambda E: E.tensor_mul(A.q_sq[:], A.q_ro[:], A.q_ro[:]),
                      inc="DVE")
        t_ss = sch.op("pe", lambda E: E.matmul(
            A.pB[:], C.ones_bf[:], A.q_sq[:], start=True, stop=True),
            waits=tmax(t_sq, st.pB_free), inc="PE")
        t_s1 = sch.op("act", lambda E: E.activation(A.rn_s[:], A.pB[:], AF.Sqrt),
                      waits=tmax(t_ss), inc="ACT")
        st.pB_free = t_s1
        sch.op("act", lambda E: E.activation(A.rn_s[:], A.rn_s[:], AF.Ln, bias=C.eps6[:]))
        t_rn = sch.op("act", lambda E: E.activation(A.rn_s[:], A.rn_s[:], AF.Exp, scale=-1.0),
                      inc="ACT")
        sch.op("dve", lambda E: E.tensor_mul(A.ln_scr[:], A.q_ro[:], A.rn_s[:]),
               waits=tmax(t_rn))
        if scale2 is None:
            t_fin = sch.op("dve", lambda E: E.tensor_scalar(
                dst_fn(), A.ln_scr[:], gamma_s[:], None, ALU.mult), inc="DVE")
        else:
            t_fin = sch.op("dve", lambda E: E.tensor_scalar(
                dst_fn(), A.ln_scr[:], gamma_s[:], scale2, ALU.mult, ALU.mult),
                inc="DVE")
        st.qro_free = t_fin
        return t_fin

    t_b1 = sch.op("dve", lambda E: E.tensor_copy(C.gbc_bf[:], C.gq_s[:].to_broadcast([128, 128])),
                  waits=tmax(ct["lx"]), inc="DVE")
    t_t1 = sch.op("pe", lambda E: E.transpose(A.pTT[:, 0:128], C.gbc_bf[:], C.id_bf[:]),
                  waits=tmax(t_b1, ct["lw"]), inc="PE")
    t_ev1 = sch.op("act", lambda E: E.copy(A.ln_scr[:, 0:128], A.pTT[:, 0:128]),
                   waits=tmax(t_t1), inc="ACT")
    t_m1 = sch.op("dve", lambda E: E.reduce_max(C.mqA[:], A.ln_scr[:, 0:128], AX.X,
                                                apply_absolute_value=True),
                  waits=tmax(t_ev1), inc="DVE")
    t_b2 = sch.op("dve", lambda E: E.tensor_copy(C.gbc_bf[:], C.gk_s[:].to_broadcast([128, 128])),
                  waits=tmax(t_t1), inc="DVE")
    t_t2 = sch.op("pe", lambda E: E.transpose(A.pTT[:, 128:256], C.gbc_bf[:], C.id_bf[:]),
                  waits=tmax(t_b2, t_ev1), inc="PE")
    t_ev2 = sch.op("act", lambda E: E.copy(A.ln_scr[:, 128:256], A.pTT[:, 128:256]),
                   waits=tmax(t_t2), inc="ACT")
    t_m2 = sch.op("dve", lambda E: E.reduce_max(C.negC[:], A.ln_scr[:, 128:256], AX.X,
                                                apply_absolute_value=True),
                  waits=tmax(t_ev2), inc="DVE")
    sch.op("dve", lambda E: E.tensor_mul(C.negC[:], C.negC[:], C.mqA[:]))
    t_negC = sch.op("dve", lambda E: E.tensor_scalar(
        C.negC[:], C.negC[:], -INV_SQRT_DH, None, ALU.mult), inc="DVE")
    st.pTT_free = t_ev2
    st.stat_free = [t_m2]

    for i in range(NTI):
        ti0 = i * TA
        par = i % 2
        if i == 0:
            st.xld[0] = sch.op("pool", lambda E: E.dma_start(
                out=A.x_bf[:, 0:NDC * TA].rearrange("p (c t) -> p c t", c=NDC),
                in_=D.x_fm.rearrange("(c p) t -> p c t", p=128)[:, :, 0:TA]),
                inc="SX")
        if i + 1 < NTI:
            pnxt = (i + 1) % 2
            st.xld[pnxt] = sch.op("pool", lambda E, i=i, pnxt=pnxt: E.dma_start(
                out=A.x_bf[:, pnxt * NDC * TA:(pnxt + 1) * NDC * TA].rearrange("p (c t) -> p c t", c=NDC),
                in_=D.x_fm.rearrange("(c p) t -> p c t", p=128)[:, :, (i + 1) * TA:(i + 2) * TA]),
                waits=tmax(*st.xbf_free[pnxt]), inc=("SX" if pnxt == 0 else "SXB"))
        t_x = st.xld[par]
        xb0 = par * NDC * TA
        t_wv = sch.op("pool", lambda E: E.dma_start(
            out=A.wbig[:, 0:NDC * 512].rearrange("p (c v) -> p c v", c=NDC),
            in_=D.wvT.rearrange("(c p) v -> p c v", p=128)),
            waits=tmax(st.wbig_free), inc="SKV")
        t_stats = None
        for dc in range(NDC):
            b = dc % 2
            t_cast = sch.op("act", lambda E, dc=dc, b=b, xb0=xb0: E.activation(
                A.sq_scr[:, b * TA:(b + 1) * TA], A.x_bf[:, xb0 + dc * TA: xb0 + (dc + 1) * TA],
                AF.Square), waits=tmax(t_x, st.scr_free[b]), inc="ACT")
            sch.op("pe", lambda E, b=b, dc=dc, xb0=xb0: E.matmul(
                A.pA[:], C.ones_bf[:], A.x_bf[:, xb0 + dc * TA: xb0 + (dc + 1) * TA],
                start=(dc == 0), stop=(dc == NDC - 1)),
                waits=tmax(t_cast,
                           st.pA_free if dc == 0 else None,
                           st.pB_free if dc == 0 else None,
                           ct["lw"] if dc == 0 else None))
            t_stats = sch.op("pe", lambda E, b=b, dc=dc: E.matmul(
                A.pB[:], C.ones_bf[:], A.sq_scr[:, b * TA:(b + 1) * TA],
                start=(dc == 0), stop=(dc == NDC - 1)), inc="PE")
            st.scr_free[b] = t_stats
        t_mean = sch.op("act", lambda E: E.mul(A.mean_s[:], A.pA[:], 1.0 / DM),
                        waits=tmax(t_stats, *st.stat_free), inc="ACT")
        t_msq = sch.op("act", lambda E: E.activation(A.msqv_s[:], A.mean_s[:], AF.Square),
                       inc="ACT")
        t_var = sch.op("dve", lambda E: E.scalar_tensor_tensor(
            A.msqv_s[:], A.pB[:], 1.0 / DM, A.msqv_s[:], ALU.mult, ALU.subtract),
            waits=tmax(t_msq, t_stats), inc="DVE")
        sch.op("act", lambda E: E.activation(
            A.rstd_s[:], A.msqv_s[:], AF.Ln, bias=C.eps5[:]),
            waits=tmax(t_var, ct["eps"]))
        t_rstd = sch.op("act", lambda E: E.activation(
            A.rstd_s[:], A.rstd_s[:], AF.Exp, scale=-0.5), inc="ACT")
        st.pA_free = t_mean
        st.pB_free = t_var
        t_ln = None
        t_ln_pl = None
        t_ln_dc = [None] * NDC
        for dc in range(NDC):
            if dc % 3 == 2:
                sch.op("pool", lambda E, dc=dc, xb0=xb0: E.tensor_sub(
                    A.q_ro[:], A.x_bf[:, xb0 + dc * TA: xb0 + (dc + 1) * TA], A.mean_s[:]),
                    waits=tmax(t_rstd, t_mean, st.lnbf_free, st.qro_free))
                t_ln_pl = sch.op("pool", lambda E, dc=dc: E.tensor_mul(
                    A.ln_bf[:, dc * TA:(dc + 1) * TA], A.q_ro[:], A.rstd_s[:]),
                    inc="PL")
                t_ln_dc[dc] = t_ln_pl
            else:
                sch.op("dve", lambda E, dc=dc, xb0=xb0: E.tensor_sub(
                    A.ln_scr[:], A.x_bf[:, xb0 + dc * TA: xb0 + (dc + 1) * TA], A.mean_s[:]),
                    waits=tmax(t_rstd, t_mean, st.lnbf_free))
                t_ln = sch.op("dve", lambda E, dc=dc: E.tensor_mul(
                    A.ln_bf[:, dc * TA:(dc + 1) * TA], A.ln_scr[:], A.rstd_s[:]),
                    inc="DVE")
                t_ln_dc[dc] = t_ln
        if debug:
            sch.op("pool", lambda E, ti0=ti0: E.dma_start(
                out=D.dbg["d_lnx"].rearrange("(c p) t -> p c t", p=128)[:, :, ti0:ti0 + TA],
                in_=A.ln_bf[:].rearrange("p (c t) -> p c t", c=NDC)),
                waits=tmax(t_ln), inc="LW")
        st.stat_free = [t_ln, t_ln_pl]
        t_vp = None
        for ts_ in range(4):
            gt = 4 * i + ts_

            def vproj(E, ts_=ts_):
                last = None
                for dc in range(NDC):
                    last = E.matmul(
                        A.pQ[:], A.ln_bf[:, dc * TA + ts_ * 128: dc * TA + ts_ * 128 + 128],
                        A.wbig[:, dc * 512:(dc + 1) * 512],
                        start=(dc == 0), stop=(dc == NDC - 1))
                return last
            t_vp = sch.op("pe", vproj,
                          waits=tmax(t_wv, st.pQ_free, t_ln, t_ln_pl,
                                     *([t_ln_dc[dc] for dc in range(NDC)] if ts_ == 0 else [])),
                          inc="PE")
            t_ve = sch.op("act", lambda E, gt=gt: E.copy(
                A.v_full[:, gt * 512:(gt + 1) * 512], A.pQ[:]),
                waits=tmax(t_vp), inc="ACT")
            st.vts[gt] = t_ve
            st.pQ_free = t_ve
        t_wk = sch.op("pool", lambda E: E.dma_start(
            out=A.wbig[:].rearrange("p (k c q) -> p k c q", k=HK, c=NDC),
            in_=D.wk_sl.rearrange("k p c q -> p k c q")),
            waits=tmax(t_vp), inc="SKV")
        t_kfin = None
        for hk in range(HK):
            def kproj(E, hk=hk):
                last = None
                for dc in range(NDC):
                    last = E.matmul(
                        A.pQ[:], A.wbig[:, hk * DM + dc * 128: hk * DM + (dc + 1) * 128],
                        A.ln_bf[:, dc * TA:(dc + 1) * TA],
                        start=(dc == 0), stop=(dc == NDC - 1))
                return last
            st.proj_tok = sch.op("pe", kproj,
                                 waits=tmax(t_ln, t_ln_pl, t_wk, st.pQ_free), inc="PE")
            t_ev = rope_start(hk, hk % 2)
            t_kfin = rope_finish(
                t_ev, hk % 2, C.gk_s, None,
                lambda hk=hk, ti0=ti0: A.k_full[:, hk * T + ti0: hk * T + ti0 + TA], ti0)
        if debug:
            sch.op("pool", lambda E, ti0=ti0: E.dma_start(
                out=D.dbg["d_k"].rearrange("(c p) t -> p c t", p=128)[:, :, ti0:ti0 + TA],
                in_=A.k_full[:].rearrange("p (c t) -> p c t", c=HK)[:, :, ti0:ti0 + TA]),
                waits=tmax(t_kfin), inc="LW")
            sch.op("pool", lambda E, i=i: E.dma_start(
                out=D.dbg["d_v"].rearrange("(c p) v -> p c v", p=128)[:, 4 * i:4 * i + 4, :],
                in_=A.v_full[:].rearrange("p (c v) -> p c v", c=NDC)[:, 4 * i:4 * i + 4, :]),
                waits=tmax(st.vts[4 * i + 3]), inc="LW")
        t_wq = [None] * H

        def load_wq(h):
            return sch.op("pool", lambda E, h=h: E.dma_start(
                out=A.wq_sb[:, (h % 2) * DM:(h % 2 + 1) * DM].rearrange("p (c q) -> p c q", c=NDC),
                in_=D.wq_sl[h]),
                waits=tmax(st.wq_free[h % 2]), inc=("SQA" if h % 2 == 0 else "SQB"))

        t_wq[0] = load_wq(0)
        t_wq[1] = load_wq(1)
        t_qfin = [None] * H
        pend = None

        def q_dst(h, ti0=ti0):
            return lambda h=h: A.wbig[:, h * TA:(h + 1) * TA]

        for h in range(H):
            def qproj(E, h=h):
                last = None
                for dc in range(NDC):
                    last = E.matmul(
                        A.pQ[:], A.wq_sb[:, (h % 2) * DM + dc * 128:(h % 2) * DM + (dc + 1) * 128],
                        A.ln_bf[:, dc * TA:(dc + 1) * TA],
                        start=(dc == 0), stop=(dc == NDC - 1))
                return last
            st.proj_tok = sch.op("pe", qproj,
                                 waits=tmax(t_wq[h], st.pQ_free, t_ln, t_ln_pl,
                                            st.wbig_free if h == 0 else None),
                                 inc="PE")
            t_qp = st.proj_tok
            st.wq_free[h % 2] = t_qp
            if h + 2 < H:
                t_wq[h + 2] = load_wq(h + 2)
            t_ev = rope_start(h, h % 2)
            if pend is not None:
                ph, pev = pend
                t_qfin[ph] = rope_finish(pev, ph % 2, C.gq_s, INV_SQRT_DH, q_dst(ph), ti0)
            pend = (h, t_ev)
        ph, pev = pend
        t_qfin[ph] = rope_finish(pev, ph % 2, C.gq_s, INV_SQRT_DH, q_dst(ph), ti0)
        if debug:
            sch.op("pool", lambda E, ti0=ti0: E.dma_start(
                out=D.dbg["d_q"][:, ti0:ti0 + TA], in_=A.wbig[:, 0:TA]),
                waits=tmax(t_qfin[0]), inc="LW")

        core = []
        for h in range(H):
            for sub in range(4):
                core.append((h, sub))
        pend_tp = None
        t_e1 = None
        t_sc_last = None
        for idx, (h, sub) in enumerate(core):
            hk = h // GQ
            gsub = 4 * i + sub
            c_lo = max(0, gsub - 8)
            nch = gsub - c_lo + 1
            W = 128 * nch

            def scmm(E, h=h, sub=sub, c_lo=c_lo, W=W, hk=hk, has_edge=(gsub >= 8)):
                last = None
                for off in range(0, W, 512):
                    n = min(512, W - off)
                    has_mask = (has_edge and off == 0) or (W - 128 >= off and W - 128 < off + n)
                    last = E.matmul(
                        A.pS[:, off:off + n],
                        A.wbig[:, h * TA + sub * 128: h * TA + sub * 128 + 128],
                        A.k_full[:, hk * T + c_lo * 128 + off: hk * T + c_lo * 128 + off + n],
                        start=True, stop=not has_mask)
                    if has_edge and off == 0:
                        last = E.matmul(A.pS[:, 0:128], C.id_bf[:], C.medge_bf[:],
                                        start=False,
                                        stop=not (W - 128 < off + n))
                    if W - 128 >= off and W - 128 < off + n:
                        last = E.matmul(A.pS[:, W - 128:W], C.id_bf[:], C.mcaus_bf[:],
                                        start=False, stop=True)
                return last
            t_sc = sch.op("pe", scmm, waits=tmax(t_qfin[h], st.pS_free, t_kfin, ct["lw"]),
                          inc="PE")
            t_sc_last = t_sc
            par = idx % 2
            t_e1 = sch.op("act", lambda E, W=W, par=par: E.activation(
                A.p_bf[:, par * 1152: par * 1152 + W], A.pS[:, 0:W], AF.Exp,
                bias=C.negC[:], accum_out=C.sums[:]),
                waits=tmax(t_sc, t_negC, st.pbf_par_free[par]), inc="ACT")
            st.pS_free = t_e1
            sch.op("act", lambda E: E.activation(C.lnsum[:], C.sums[:], AF.Ln, bias=C.eps35[:]))
            t_ds = sch.op("act", lambda E, par=par: E.activation(
                C.rs2[:, par:par + 1], C.lnsum[:], AF.Exp, scale=-1.0),
                waits=tmax(st.diag_free[par]), inc="ACT")
            t_diag = sch.op("dve", lambda E, par=par, W=W: E.tensor_scalar(
                A.p_bf[:, par * 1152: par * 1152 + W],
                A.p_bf[:, par * 1152: par * 1152 + W],
                C.rs2[:, par:par + 1], None, ALU.mult),
                waits=tmax(t_ds), inc="DVE")
            st.diag_free[par] = t_diag
            def emit_tp(ph, psub, pnch, pc_lo, phk, pW, t_pe1, t_pdiag, ppar):
                def trs(E, nch=pnch, par=ppar):
                    last = None
                    for c in range(nch):
                        last = E.transpose(A.pTT[:, c * 128:(c + 1) * 128],
                                           A.p_bf[:, par * 1152 + c * 128: par * 1152 + (c + 1) * 128],
                                           C.id_bf[:])
                    return last
                t_tr = sch.op("pe", trs, waits=tmax(t_pe1, t_pdiag, st.pTT_free), inc="PE")
                st.pbf_par_free[ppar] = t_tr
                st.tr_free[ppar] = t_tr
                t_pte = sch.op("dve", lambda E, W=pW: E.tensor_copy(A.pT_sb[:, 0:W], A.pTT[:, 0:W]),
                               waits=tmax(t_tr, st.pT_free), inc="DVE")
                st.pTT_free = t_pte

                def pv(E, nch=pnch, c_lo=pc_lo, hk=phk):
                    last = None
                    for c in range(nch):
                        last = E.matmul(
                            A.pB[:, 0:128],
                            A.v_full[:, (c_lo + c) * 512 + hk * 128: (c_lo + c) * 512 + hk * 128 + 128],
                            A.pT_sb[:, c * 128:(c + 1) * 128],
                            start=(c == 0), stop=(c == nch - 1))
                    return last
                t_pv = sch.op("pe", pv,
                              waits=tmax(t_pte, st.pB_free, st.vts[4 * i + psub]), inc="PE")
                st.pT_free = t_pv
                t_ye = sch.op("dve", lambda E, h=ph, sub=psub: E.tensor_copy(
                    A.y_att[:, h * TA + sub * 128: h * TA + sub * 128 + 128], A.pB[:, 0:128]),
                    waits=tmax(t_pv, st.yatt_free if (ph == 0 and psub == 0) else None),
                    inc="DVE")
                st.pB_free = t_ye
                return t_ye
            if pend_tp is not None:
                t_ye = emit_tp(*pend_tp)
            pend_tp = (h, sub, nch, c_lo, hk, W, t_e1, t_diag, par)
        t_ye = emit_tp(*pend_tp)
        t_ye_last = t_ye
        st.wbig_free = t_sc_last
        if debug:
            sch.op("pool", lambda E, ti0=ti0: E.dma_start(
                out=D.dbg["d_y0"][:, ti0:ti0 + TA], in_=A.y_att[:, 0:TA]),
                waits=tmax(t_ye), inc="LW")
        t_wo = {}

        def load_wo(dcp):
            t_wo[dcp] = sch.op("pool", lambda E, dcp=dcp: E.dma_start(
                out=A.wo_sb[:, (dcp % 2) * DM:(dcp % 2 + 1) * DM].rearrange("p (c q) -> p c q", c=NDC),
                in_=D.wo_sl[dcp]),
                waits=tmax(st.wo_free[dcp % 2]), inc=("SOA" if dcp % 2 == 0 else "SOB"))

        load_wo(0)
        load_wo(1)
        t_stats2 = None
        t_wmm = None
        for dcp in range(NDC):
            def womm(E, dcp=dcp):
                last = None
                for ec in range(NDC):
                    last = E.matmul(
                        A.pQ[:], A.wo_sb[:, (dcp % 2) * DM + ec * 128:(dcp % 2) * DM + (ec + 1) * 128],
                        A.y_att[:, ec * TA:(ec + 1) * TA],
                        start=(ec == 0), stop=(ec == NDC - 1))
                return last
            t_wmm = sch.op("pe", womm,
                           waits=tmax(t_wo[dcp], t_ye_last, st.pQ_free), inc="PE")
            st.wo_free[dcp % 2] = t_wmm
            if dcp + 2 < NDC:
                load_wo(dcp + 2)
            t_ae = sch.op("act", lambda E, dcp=dcp: E.copy(
                A.att_bf[:, dcp * TA:(dcp + 1) * TA], A.pQ[:]),
                waits=tmax(t_wmm, *(st.attbf_free if dcp == 0 else [])), inc="ACT")
            st.pQ_free = t_ae
            b = dcp % 2
            t_sqc = sch.op("act", lambda E, dcp=dcp, b=b: E.activation(
                A.sq_scr[:, b * TA:(b + 1) * TA], A.att_bf[:, dcp * TA:(dcp + 1) * TA],
                AF.Square),
                waits=tmax(st.scr_free[b]), inc="ACT")
            sch.op("pe", lambda E, dcp=dcp: E.matmul(
                A.pA[:], C.ones_bf[:], A.att_bf[:, dcp * TA:(dcp + 1) * TA],
                start=(dcp == 0), stop=(dcp == NDC - 1)),
                waits=tmax(t_sqc,
                           st.pA_free if dcp == 0 else None,
                           st.pB_free if dcp == 0 else None))
            t_stats2 = sch.op("pe", lambda E, dcp=dcp, b=b: E.matmul(
                A.pB[:], C.ones_bf[:], A.sq_scr[:, b * TA:(b + 1) * TA],
                start=(dcp == 0), stop=(dcp == NDC - 1)), inc="PE")
            st.scr_free[b] = t_stats2
        if debug:
            sch.op("pool", lambda E, ti0=ti0: E.dma_start(
                out=D.dbg["d_att"].rearrange("(c p) t -> p c t", p=128)[:, :, ti0:ti0 + TA],
                in_=A.att_bf[:].rearrange("p (c t) -> p c t", c=NDC)),
                waits=tmax(t_stats2), inc="LW")
        t_mean2 = sch.op("act", lambda E: E.mul(A.mean2_s[:], A.pA[:], 1.0 / DM),
                         waits=tmax(t_stats2, t_ln, t_ln_pl), inc="ACT")
        t_msq2 = sch.op("act", lambda E: E.activation(A.msqv2_s[:], A.mean2_s[:], AF.Square),
                        inc="ACT")
        t_var2 = sch.op("dve", lambda E: E.scalar_tensor_tensor(
            A.msqv2_s[:], A.pB[:], 1.0 / DM, A.msqv2_s[:], ALU.mult, ALU.subtract),
            waits=tmax(t_msq2, t_stats2), inc="DVE")
        sch.op("act", lambda E: E.activation(
            A.rstd2_s[:], A.msqv2_s[:], AF.Ln, bias=C.eps5[:]),
            waits=tmax(t_var2))
        t_rstd2 = sch.op("act", lambda E: E.activation(
            A.rstd2_s[:], A.rstd2_s[:], AF.Exp, scale=-0.5), inc="ACT")
        st.pA_free = t_mean2
        st.pB_free = t_var2
        t_res = None
        t_res_pl = None
        t_sp_dc = [None] * NDC
        for dc in range(NDC):
            b = dc % 4
            warslot = t_sp_dc[dc - 4] if dc >= 4 else (st.y1s_free[dc] if dc < len(st.y1s_free) else None)
            if dc % 2 == 1:
                sch.op("pool", lambda E, dc=dc: E.tensor_sub(
                    A.q_ro[:], A.att_bf[:, dc * TA:(dc + 1) * TA], A.mean2_s[:]),
                    waits=tmax(t_rstd2, t_mean2, st.qro_free))
                sch.op("pool", lambda E: E.tensor_mul(A.q_ro[:], A.q_ro[:], A.rstd2_s[:]))
                t_res_pl = sch.op("pool", lambda E, dc=dc, b=b, xb0=xb0: E.tensor_add(
                    A.y1s[:, b * TA:(b + 1) * TA], A.q_ro[:],
                    A.x_bf[:, xb0 + dc * TA: xb0 + (dc + 1) * TA]),
                    waits=tmax(warslot), inc="PL")
                t_last = t_res_pl
            else:
                sch.op("dve", lambda E, dc=dc: E.tensor_sub(
                    A.ln_scr[:], A.att_bf[:, dc * TA:(dc + 1) * TA], A.mean2_s[:]),
                    waits=tmax(t_rstd2, t_mean2))
                sch.op("dve", lambda E: E.tensor_mul(A.ln_scr[:], A.ln_scr[:], A.rstd2_s[:]))
                t_res = sch.op("dve", lambda E, dc=dc, b=b, xb0=xb0: E.tensor_add(
                    A.y1s[:, b * TA:(b + 1) * TA], A.ln_scr[:],
                    A.x_bf[:, xb0 + dc * TA: xb0 + (dc + 1) * TA]),
                    waits=tmax(warslot), inc="DVE")
                t_last = t_res
            t_sp_dc[dc] = sch.op("sp", lambda E, dc=dc, b=b, ti0=ti0: E.dma_start(
                out=D.y1_sp[dc * 128:(dc + 1) * 128, ti0:ti0 + TA],
                in_=A.y1s[:, b * TA:(b + 1) * TA]),
                waits=tmax(t_last), inc=("STA", "STB", "STC", "STD")[b])
        st.y1s_free = [t_sp_dc[NDC - 4], t_sp_dc[NDC - 3], t_sp_dc[NDC - 2], t_sp_dc[NDC - 1]]
        t_spill = st.y1s_free
        st.xbf_free[par] = [t_res, t_res_pl]
        st.attbf_free = [t_res, t_res_pl]
        st.qro_free = t_res_pl
        st.lnbf_free = t_qp
        st.yatt_free = t_wmm

    return {"pe": ("PE", sch.cnt["PE"]), "act": ("ACT", sch.cnt["ACT"]),
            "dve": ("DVE", sch.cnt["DVE"]), "spill": t_spill}


def _sched_ffn(sch, nc, D, C, Fb, ct, af, debug):
    t_spill_all = af["spill"]
    if not isinstance(t_spill_all, list):
        t_spill_all = [t_spill_all]
    allf = [af["pe"], af["act"], af["dve"]]
    for en in ("pe", "act", "dve", "pool", "sp"):
        sch.join(en, tmax(*allf))

    st = NS()
    st.y1_free = [None, None]
    st.cast_free = [None, None]
    st.u_free = None
    st.acc_free = None
    st.h_free = [None] * NFC
    st.pH_free = [None, None]
    st.pY_free = [None, None]
    st.G_free = None
    st.gT_free = None
    st.stat_free = None
    st.w1free = [None, None, None]
    st.w2free = [None, None]
    st.gall_free = [None] * 8
    st.sqf_free = None
    st.g16_free = None
    t_w1 = {}
    t_w2 = {}

    def load_w1(j, fc):
        idx = (j * NFC + fc) % 3
        t_w1[(j, fc)] = sch.op("pool", lambda E, j=j, fc=fc, idx=idx: E.dma_start(
            out=Fb.w1_sb[:, idx * DM:(idx + 1) * DM].rearrange("p (c q) -> p c q", c=NDC),
            in_=D.w1_sl[j, fc]),
            waits=tmax(st.w1free[idx]), inc=("S1A", "S1B", "S1C")[idx])

    def load_w2(j, dcp):
        idx = (j * NDC + dcp) % 2
        t_w2[(j, dcp)] = sch.op("pool", lambda E, j=j, dcp=dcp, idx=idx: E.dma_start(
            out=Fb.w2_sb[:, idx * NFC * 128:(idx + 1) * NFC * 128].rearrange("p (c q) -> p c q", c=NFC),
            in_=D.w2_sl[j, dcp]),
            waits=tmax(st.w2free[idx]), inc=("S2A", "S2B")[idx])

    for f in range(NTF):
        tf0 = f * TF
        load_w1(0, 0)
        load_w1(0, 1)
        load_w1(0, 2)
        load_w2(0, 0)
        load_w2(0, 1)
        t_st3 = None
        for dc in range(NDC):
            b = dc % 2
            t_l = sch.op("sp", lambda E, dc=dc, b=b, tf0=tf0: E.dma_start(
                out=Fb.y1_scr[:, b * TF:(b + 1) * TF],
                in_=D.y1_sp[dc * 128:(dc + 1) * 128, tf0:tf0 + TF]),
                waits=tmax(st.y1_free[b], *t_spill_all), inc=("SYA" if b == 0 else "SYB"))
            sch.op("act", lambda E, b=b: E.copy(
                Fb.cast_f[:, b * TF:(b + 1) * TF], Fb.y1_scr[:, b * TF:(b + 1) * TF]),
                waits=tmax(t_l, st.cast_free[b]))
            t_cf = sch.op("act", lambda E, b=b: E.activation(
                Fb.sq_f[:], Fb.y1_scr[:, b * TF:(b + 1) * TF],
                AF.Square), waits=tmax(st.sqf_free), inc="ACT")
            st.y1_free[b] = t_cf

            def stats3(E, dc=dc, b=b):
                last = None
                for hh in range(2):
                    last = E.matmul(Fb.pHa[:, hh * 512:(hh + 1) * 512], C.ones_bf[:],
                                    Fb.cast_f[:, b * TF + hh * 512: b * TF + (hh + 1) * 512],
                                    start=(dc == 0), stop=(dc == NDC - 1))
                for hh in range(2):
                    last = E.matmul(Fb.pHb[:, hh * 512:(hh + 1) * 512], C.ones_bf[:],
                                    Fb.sq_f[:, hh * 512:(hh + 1) * 512],
                                    start=(dc == 0), stop=(dc == NDC - 1))
                return last
            t_st3 = sch.op("pe", stats3,
                           waits=tmax(t_cf,
                                      st.pH_free[0] if dc == 0 else None,
                                      st.pH_free[1] if dc == 0 else None),
                           inc="PE")
            st.cast_free[b] = t_st3
            st.sqf_free = t_st3
        t_mean3 = sch.op("act", lambda E: E.mul(Fb.mean_f[:], Fb.pHa[:], 1.0 / DM),
                         waits=tmax(t_st3, st.stat_free), inc="ACT")
        t_msq3 = sch.op("act", lambda E: E.activation(Fb.msqv_f[:], Fb.mean_f[:], AF.Square),
                        inc="ACT")
        t_var3 = sch.op("dve", lambda E: E.scalar_tensor_tensor(
            Fb.msqv_f[:], Fb.pHb[:], 1.0 / DM, Fb.msqv_f[:], ALU.mult, ALU.subtract),
            waits=tmax(t_msq3, t_st3), inc="DVE")
        sch.op("act", lambda E: E.activation(
            Fb.rstd_f[:], Fb.msqv_f[:], AF.Ln, bias=C.eps5[:]),
            waits=tmax(t_var3))
        t_rstd3 = sch.op("act", lambda E: E.activation(
            Fb.rstd_f[:], Fb.rstd_f[:], AF.Exp, scale=-0.5), inc="ACT")
        st.pH_free = [t_mean3, t_var3]
        t_u = None
        t_u_pl = None
        for dc in range(NDC):
            b = dc % 2
            t_l = sch.op("sp", lambda E, dc=dc, b=b, tf0=tf0: E.dma_start(
                out=Fb.y1_scr[:, b * TF:(b + 1) * TF],
                in_=D.y1_sp[dc * 128:(dc + 1) * 128, tf0:tf0 + TF]),
                waits=tmax(st.y1_free[b], *t_spill_all), inc=("SYA" if b == 0 else "SYB"))
            if dc % 3 == 2:
                lnp = Fb.cast_f[:].bitcast(F32)
                sch.op("pool", lambda E, b=b, lnp=lnp: E.tensor_sub(
                    lnp, Fb.y1_scr[:, b * TF:(b + 1) * TF], Fb.mean_f[:]),
                    waits=tmax(t_l, t_rstd3, t_mean3, st.u_free, st.cast_free[0],
                               st.cast_free[1]))
                t_u_pl = sch.op("pool", lambda E, dc=dc, lnp=lnp: E.tensor_mul(
                    Fb.u_bf[:, dc * TF:(dc + 1) * TF], lnp, Fb.rstd_f[:]),
                    inc="PL")
                st.y1_free[b] = t_u_pl
            else:
                sch.op("dve", lambda E, b=b: E.tensor_sub(
                    Fb.ln_f[:], Fb.y1_scr[:, b * TF:(b + 1) * TF], Fb.mean_f[:]),
                    waits=tmax(t_l, t_rstd3, t_mean3, st.u_free))
                t_u = sch.op("dve", lambda E, dc=dc: E.tensor_mul(
                    Fb.u_bf[:, dc * TF:(dc + 1) * TF], Fb.ln_f[:], Fb.rstd_f[:]),
                    inc="DVE")
                st.y1_free[b] = t_u
        if debug:
            sch.op("pool", lambda E, tf0=tf0: E.dma_start(
                out=D.dbg["d_u"].rearrange("(c p) t -> p c t", p=128)[:, :, tf0:tf0 + TF],
                in_=Fb.u_bf[:].rearrange("p (c t) -> p c t", c=NDC)),
                waits=tmax(t_u), inc="LW")
        t_rmm = [None] * 8
        t_sg_prev = None
        t_msk_prev = None
        t_gmul_prev = None
        t_ln_prev = None
        t_final_prev = None
        t_gfin = [None] * 8
        for ts_ in range(8):
            def rmm(E, ts_=ts_):
                last = None
                for dc in range(NDC):
                    last = E.matmul(
                        Fb.pYa[:, ts_ * 64: ts_ * 64 + NRT],
                        Fb.u_bf[:, dc * TF + ts_ * 128: dc * TF + ts_ * 128 + 128],
                        C.cT_bf[:, dc * NRT:(dc + 1) * NRT],
                        start=(dc == 0), stop=(dc == NDC - 1))
                return last
            t_rmm[ts_] = sch.op("pe", rmm,
                                waits=tmax(t_u, t_u_pl, st.pY_free[0] if ts_ == 0 else None,
                                           ct["lw"]),
                                inc="PE")
        for ts_ in range(8):
            t_sg = sch.op("act", lambda E, ts_=ts_: E.activation(
                C.sc16[:], Fb.pYa[:, ts_ * 64: ts_ * 64 + NRT], AF.Sigmoid),
                waits=tmax(t_rmm[7], t_gmul_prev, t_msk_prev), inc="ACT")
            t_bias = sch.op("pool", lambda E: E.tensor_add(C.sc16[:], C.sc16[:], C.bias_bc_s[:]),
                            waits=tmax(t_sg, ct["lx"]), inc="PL")
            t_mx = sch.op("dve", lambda E: E.max(out=C.mx8[:], in_=C.sc16[:]),
                          waits=tmax(t_bias), inc="DVE")
            t_thr = sch.op("pool", lambda E: E.tensor_copy(C.thr1[:], C.mx8[:, 3:4]),
                           waits=tmax(t_mx), inc="PL")
            t_msk = sch.op("dve", lambda E: E.tensor_tensor(
                C.msk16[:], C.sc16[:], C.thr1[:].to_broadcast([128, NRT]), ALU.is_ge),
                waits=tmax(t_thr), inc="DVE")
            t_msk_prev = t_msk
            t_gmul = sch.op("pool", lambda E, ts_=ts_: E.tensor_mul(
                C.g_all[:, ts_ * NRT:(ts_ + 1) * NRT], C.sc16[:], C.msk16[:]),
                waits=tmax(t_msk, st.gall_free[ts_], t_final_prev), inc="PL")
            t_gmul_prev = t_gmul
            t_red = sch.op("dve", lambda E, ts_=ts_: E.reduce_sum(
                C.rs1[:], C.g_all[:, ts_ * NRT:(ts_ + 1) * NRT], AX.X),
                waits=tmax(t_gmul), inc="DVE")
            t_rsc = sch.op("pool", lambda E: E.tensor_copy(C.rsB[:], C.rs1[:]),
                           waits=tmax(t_red, t_ln_prev), inc="PL")
            t_eps = sch.op("dve", lambda E: E.tensor_scalar_max(C.rsB[:], C.rsB[:], 1e-8),
                           waits=tmax(t_rsc), inc="DVE")
            t_ln = sch.op("act", lambda E: E.activation(C.rsC[:], C.rsB[:], AF.Ln),
                          waits=tmax(t_eps, t_final_prev), inc="ACT")
            t_ln_prev = t_ln
            t_exp = sch.op("act", lambda E: E.activation(C.rsC[:], C.rsC[:], AF.Exp, scale=-1.0),
                           inc="ACT")
            t_final = sch.op("dve", lambda E, ts_=ts_: E.tensor_tensor(
                C.g_all[:, ts_ * NRT:(ts_ + 1) * NRT], C.g_all[:, ts_ * NRT:(ts_ + 1) * NRT],
                C.rsC[:].to_broadcast([128, NRT]), ALU.mult),
                waits=tmax(t_exp), inc="DVE")
            t_final_prev = t_final
            t_gfin[ts_] = t_final
            if debug:
                t_gd = sch.op("sp", lambda E, ts_=ts_, tf0=tf0: E.dma_start(
                    out=D.dbg["d_g"][tf0 + ts_ * 128: tf0 + (ts_ + 1) * 128, :],
                    in_=C.g_all[:, ts_ * NRT:(ts_ + 1) * NRT]),
                    waits=tmax(t_final), inc="LX")
        st.pY_free[0] = t_sg
        t_hm = None
        t_ym = None
        t_hev = [None] * NFC
        t_ac17 = [None] * NDC
        for j in range(NE):
            routed = j >= 2
            jr = j - 2
            t_ge = None
            if routed:
                def gmm(E, jr=jr):
                    last = None
                    for hh in range(2):
                        last = E.matmul(Fb.pHa[:, hh * 512:(hh + 1) * 512],
                                        C.sel_bf[:, jr * 128:(jr + 1) * 128],
                                        Fb.gT_bf[:, hh * 512:(hh + 1) * 512],
                                        start=True, stop=True)
                    return last
                t_gm = sch.op("pe", gmm, waits=tmax(t_gT, st.pH_free[0]), inc="PE")
                t_ge = sch.op("act", lambda E: E.copy(Fb.G_bf[:], Fb.pHa[:]),
                              waits=tmax(t_gm, st.G_free), inc="ACT")
                st.pH_free[0] = t_ge
            for fc in range(NFC):
                pH = Fb.pHa if fc % 2 == 0 else Fb.pHb
                pidx = fc % 2
                widx = (j * NFC + fc) % 3

                def hmm(E, pH=pH, widx=widx):
                    last = None
                    for dc in range(NDC):
                        for hh in range(2):
                            last = E.matmul(
                                pH[:, hh * 512:(hh + 1) * 512],
                                Fb.w1_sb[:, widx * DM + dc * 128: widx * DM + (dc + 1) * 128],
                                Fb.u_bf[:, dc * TF + hh * 512: dc * TF + (hh + 1) * 512],
                                start=(dc == 0), stop=(dc == NDC - 1))
                    return last
                t_hm = sch.op("pe", hmm,
                              waits=tmax(t_w1[(j, fc)], t_u, t_u_pl, st.pH_free[pidx]), inc="PE")
                st.w1free[widx] = t_hm
                nj, nfc = (j, fc + 3) if fc + 3 < NFC else (j + 1, fc + 3 - NFC)
                if nj < NE:
                    load_w1(nj, nfc)
                if j == 0 and fc < 8:
                    ts_ = fc
                    t_gtr = sch.op("pe", lambda E, ts_=ts_: E.transpose(
                        Fb.pYb[0:NRT, ts_ * 128:(ts_ + 1) * 128],
                        C.g_all[:, ts_ * NRT:(ts_ + 1) * NRT], C.id_f[:]),
                        waits=tmax(t_gfin[ts_], st.pY_free[1] if ts_ == 0 else None),
                        inc="PE")
                    st.gall_free[ts_] = t_gtr
                    if ts_ == 7:
                        t_gT = sch.op("act", lambda E: E.copy(Fb.gT_bf[:], Fb.pYb[0:NRT, :]),
                                      waits=tmax(t_gtr, st.gT_free), inc="ACT")
                        st.pY_free[1] = t_gT
                t_he = sch.op("act", lambda E, fc=fc, pH=pH, j=j: E.activation(
                    Fb.h_bf[:, fc * TF:(fc + 1) * TF], pH[:], AF.Relu,
                    bias=C.b1c_s[:, j * NFC + fc: j * NFC + fc + 1]),
                    waits=tmax(t_hm, st.h_free[fc], ct["lx"]), inc="ACT")
                st.pH_free[pidx] = t_he
                t_hev[fc] = t_he
                if routed:
                    t_hg = sch.op("dve", lambda E, fc=fc: E.tensor_mul(
                        Fb.h_bf[:, fc * TF:(fc + 1) * TF],
                        Fb.h_bf[:, fc * TF:(fc + 1) * TF], Fb.G_bf[:]),
                        waits=tmax(t_he, t_ge), inc="DVE")
                    t_hev[fc] = t_hg
            if routed:
                st.G_free = t_hev[NFC - 1]
            if debug and j == 0 and f == 0:
                sch.op("pool", lambda E: E.dma_start(
                    out=D.dbg["d_h0"].rearrange("(c p) t -> p c t", p=128),
                    in_=Fb.h_bf[:].rearrange("p (c t) -> p c t", c=NFC)),
                    waits=tmax(t_hev[NFC - 1]), inc="LW")
            for dcp in range(NDC):
                pY = Fb.pYa if dcp % 2 == 0 else Fb.pYb
                pidx = dcp % 2
                widx = (j * NDC + dcp) % 2

                def ymm(E, dcp=dcp, pY=pY, widx=widx, j=j):
                    last = None
                    islast = (j == NE - 1)
                    for fc in range(NFC):
                        for hh in range(2):
                            last = E.matmul(
                                pY[:, hh * 512:(hh + 1) * 512],
                                Fb.w2_sb[:, widx * NFC * 128 + fc * 128: widx * NFC * 128 + (fc + 1) * 128],
                                Fb.h_bf[:, fc * TF + hh * 512: fc * TF + (hh + 1) * 512],
                                start=(fc == 0), stop=(fc == NFC - 1 and not islast))
                    if islast:
                        for hh in range(2):
                            last = E.matmul(
                                pY[:, hh * 512:(hh + 1) * 512],
                                C.b2r_bf[:, dcp * 128:(dcp + 1) * 128],
                                Fb.gT_bf[:, hh * 512:(hh + 1) * 512],
                                start=False, stop=True)
                    return last
                t_ym = sch.op("pe", ymm,
                              waits=tmax(t_hev[NFC - 1], t_w2[(j, dcp)], st.pY_free[pidx]),
                              inc="PE")
                st.w2free[widx] = t_ym
                nj, ndcp = (j, dcp + 2) if dcp + 2 < NDC else (j + 1, dcp + 2 - NDC)
                if nj < NE:
                    load_w2(nj, ndcp)
                if j == 0:
                    def acc0(E, dcp=dcp, pY=pY):
                        E.tensor_tensor(Fb.acc[:, dcp * TF:(dcp + 1) * TF], pY[:],
                                        C.b2sh_s[:, dcp:dcp + 1].to_broadcast([128, TF]),
                                        ALU.add)
                        return E.tensor_add(Fb.acc[:, dcp * TF:(dcp + 1) * TF],
                                            Fb.acc[:, dcp * TF:(dcp + 1) * TF],
                                            Fb.u_bf[:, dcp * TF:(dcp + 1) * TF])
                    t_ac = sch.op("dve", acc0,
                                  waits=tmax(t_ym, ct["b2sh"], st.acc_free), inc="DVE")
                else:
                    t_ac = sch.op("dve", lambda E, dcp=dcp, pY=pY: E.tensor_add(
                        Fb.acc[:, dcp * TF:(dcp + 1) * TF],
                        Fb.acc[:, dcp * TF:(dcp + 1) * TF], pY[:]),
                        waits=tmax(t_ym), inc="DVE")
                st.pY_free[pidx] = t_ac
                if j == NE - 1:
                    t_ac17[dcp] = t_ac
            for fc in range(NFC):
                st.h_free[fc] = t_ym
        st.u_free = t_hm
        st.gT_free = t_ym
        t_acc_all = t_ac
        t_st4 = None
        for dc in range(NDC):
            b = dc % 2
            sch.op("act", lambda E, dc=dc, b=b: E.copy(
                Fb.cast_f[:, b * TF:(b + 1) * TF], Fb.acc[:, dc * TF:(dc + 1) * TF]),
                waits=tmax(t_ac17[dc], st.cast_free[b], t_u_pl))
            t_c4 = sch.op("act", lambda E, dc=dc, b=b: E.activation(
                Fb.sq_f[:], Fb.acc[:, dc * TF:(dc + 1) * TF],
                AF.Square), waits=tmax(st.sqf_free), inc="ACT")

            def stats4(E, dc=dc, b=b):
                last = None
                for hh in range(2):
                    last = E.matmul(Fb.pHa[:, hh * 512:(hh + 1) * 512], C.ones_bf[:],
                                    Fb.cast_f[:, b * TF + hh * 512: b * TF + (hh + 1) * 512],
                                    start=(dc == 0), stop=(dc == NDC - 1))
                for hh in range(2):
                    last = E.matmul(Fb.pHb[:, hh * 512:(hh + 1) * 512], C.ones_bf[:],
                                    Fb.sq_f[:, hh * 512:(hh + 1) * 512],
                                    start=(dc == 0), stop=(dc == NDC - 1))
                return last
            t_st4 = sch.op("pe", stats4,
                           waits=tmax(t_c4,
                                      st.pH_free[0] if dc == 0 else None,
                                      st.pH_free[1] if dc == 0 else None),
                           inc="PE")
            st.cast_free[b] = t_st4
            st.sqf_free = t_st4
        t_mean4 = sch.op("act", lambda E: E.mul(Fb.mean_f[:], Fb.pHa[:], 1.0 / DM),
                         waits=tmax(t_st4, t_u, t_u_pl), inc="ACT")
        t_msq4 = sch.op("act", lambda E: E.activation(Fb.msqv_f[:], Fb.mean_f[:], AF.Square),
                        inc="ACT")
        t_var4 = sch.op("dve", lambda E: E.scalar_tensor_tensor(
            Fb.msqv_f[:], Fb.pHb[:], 1.0 / DM, Fb.msqv_f[:], ALU.mult, ALU.subtract),
            waits=tmax(t_msq4, t_st4), inc="DVE")
        sch.op("act", lambda E: E.activation(
            Fb.rstd_f[:], Fb.msqv_f[:], AF.Ln, bias=C.eps5[:]),
            waits=tmax(t_var4))
        t_rstd4 = sch.op("act", lambda E: E.activation(
            Fb.rstd_f[:], Fb.rstd_f[:], AF.Exp, scale=-0.5), inc="ACT")
        st.pH_free = [t_mean4, t_var4]
        st.stat_free = t_rstd4
        if debug:
            sch.op("sp", lambda E, tf0=tf0: E.dma_start(
                out=D.dbg["d_acc"].rearrange("(c p) t -> p c t", p=128)[:, :, tf0:tf0 + TF],
                in_=Fb.acc[:].rearrange("p (c t) -> p c t", c=NDC)),
                waits=tmax(t_acc_all), inc="LX")
        t_out = None
        for dc in range(NDC):
            b = dc % 2
            t_l = sch.op("sp", lambda E, dc=dc, b=b, tf0=tf0: E.dma_start(
                out=Fb.y1_scr[:, b * TF:(b + 1) * TF],
                in_=D.y1_sp[dc * 128:(dc + 1) * 128, tf0:tf0 + TF]),
                waits=tmax(st.y1_free[b], *t_spill_all), inc=("SYA" if b == 0 else "SYB"))
            sch.op("dve", lambda E, dc=dc: E.tensor_sub(
                Fb.ln_f[:], Fb.acc[:, dc * TF:(dc + 1) * TF], Fb.mean_f[:]),
                waits=tmax(t_rstd4, t_mean4))
            sch.op("dve", lambda E: E.tensor_mul(Fb.ln_f[:], Fb.ln_f[:], Fb.rstd_f[:]))
            t_out = sch.op("dve", lambda E, dc=dc, b=b: E.tensor_add(
                Fb.acc[:, dc * TF:(dc + 1) * TF], Fb.ln_f[:],
                Fb.y1_scr[:, b * TF:(b + 1) * TF]),
                waits=tmax(t_l), inc="DVE")
            st.y1_free[b] = t_out
        t_store = sch.op("sp", lambda E, tf0=tf0: E.dma_start(
            out=D.out_fm.rearrange("(c p) t -> p c t", p=128)[:, :, tf0:tf0 + TF],
            in_=Fb.acc[:].rearrange("p (c t) -> p c t", c=NDC)),
            waits=tmax(t_out), inc="ST")
        st.acc_free = t_store

    sch.join("sp", [("ST", sch.cnt["ST"])])



def _host_layout(inputs):
    x = np.asarray(inputs["L0_attn_x"], dtype=np.float32)
    wq = np.asarray(inputs["L0_attn_wq"], dtype=np.float32)
    wk = np.asarray(inputs["L0_attn_wk"], dtype=np.float32)
    wv = np.asarray(inputs["L0_attn_wv"], dtype=np.float32)
    wo = np.asarray(inputs["L0_attn_wo"], dtype=np.float32)
    gq = np.asarray(inputs["L0_attn_gamma_q"], dtype=np.float32)
    gk = np.asarray(inputs["L0_attn_gamma_k"], dtype=np.float32)
    sw1 = np.asarray(inputs["L0_ffn_shared_w1"], dtype=np.float32)
    sb1 = np.asarray(inputs["L0_ffn_shared_b1"], dtype=np.float32)
    sw2 = np.asarray(inputs["L0_ffn_shared_w2"], dtype=np.float32)
    sb2 = np.asarray(inputs["L0_ffn_shared_b2"], dtype=np.float32)
    rw1 = np.asarray(inputs["L0_ffn_routed_w1"], dtype=np.float32)
    rb1 = np.asarray(inputs["L0_ffn_routed_b1"], dtype=np.float32)
    rw2 = np.asarray(inputs["L0_ffn_routed_w2"], dtype=np.float32)
    rb2 = np.asarray(inputs["L0_ffn_routed_b2"], dtype=np.float32)
    cent = np.asarray(inputs["L0_ffn_centroids"], dtype=np.float32)
    bias = np.asarray(inputs["L0_ffn_bias"], dtype=np.float32)

    com = {}
    com["wq_sl"] = np.ascontiguousarray(
        wq.reshape(H, 128, NDC, 128).transpose(0, 3, 2, 1))
    com["wk_sl"] = np.ascontiguousarray(
        wk.reshape(HK, 128, NDC, 128).transpose(0, 3, 2, 1))
    com["wvT"] = np.ascontiguousarray(wv.T)
    com["wo_sl"] = np.ascontiguousarray(
        wo.reshape(NDC, 128, NDC, 128).transpose(0, 3, 2, 1))
    w1_all = np.concatenate([sw1, rw1], axis=0)
    com["w1_sl"] = np.ascontiguousarray(
        w1_all.reshape(NE, NDC, 128, NFC, 128).transpose(0, 3, 2, 1, 4))
    w2_all = np.concatenate([sw2, rw2], axis=0)
    com["w2_sl"] = np.ascontiguousarray(
        w2_all.reshape(NE, NFC, 128, NDC, 128).transpose(0, 3, 2, 1, 4))
    b1_all = np.concatenate([sb1, rb1], axis=0)
    com["b1c"] = np.ascontiguousarray(b1_all.reshape(NE, NFC, 128).transpose(2, 0, 1)
                                      .reshape(128, NE * NFC))
    com["b2c"] = np.ascontiguousarray(sb2.reshape(2, NDC, 128).transpose(2, 0, 1)
                                      .reshape(128, 2 * NDC))
    com["b2r"] = np.ascontiguousarray(rb2)
    com["cT"] = np.ascontiguousarray(cent.T)
    com["bias_bc"] = np.ascontiguousarray(np.tile(bias[None, :], (128, 1)))
    com["gq"] = np.ascontiguousarray(gq.reshape(128, 1))
    com["gk"] = np.ascontiguousarray(gk.reshape(128, 1))

    inv_freq = (1.0 / RBASE ** (np.arange(0, DH, 2, dtype=np.float64) / DH))
    ang = np.arange(T, dtype=np.float64)[:, None] * inv_freq[None, :]
    cos = np.cos(ang)
    sin = np.sin(ang)
    com["cos_t"] = np.ascontiguousarray(
        np.concatenate([cos, cos], axis=1).T.astype(np.float32))
    com["sin_t"] = np.ascontiguousarray(
        np.concatenate([-sin, sin], axis=1).T.astype(np.float32))
    com["ones_c"] = np.ones((128, 128), np.float32)
    com["ident_c"] = np.eye(128, dtype=np.float32)
    il = np.arange(128)[:, None]
    jl = np.arange(128)[None, :]
    com["mcaus"] = np.where(jl <= il, 0.0, -1e9).astype(np.float32)
    com["medge"] = np.where(jl >= il, 0.0, -1e9).astype(np.float32)
    sel = np.zeros((NRT, NRT * 128), np.float32)
    for j in range(NRT):
        sel[j, j * 128:(j + 1) * 128] = 1.0
    com["sel_c"] = sel
    com["identu_c"] = np.eye(128, dtype=np.uint8)
    kk = np.arange(128)[:, None]
    mm = np.arange(128)[None, :]
    com["p64_c"] = (np.abs(kk - mm) == 64).astype(np.float32)

    in_maps = []
    for b in range(8):
        m = dict(com)
        m["x_fm"] = np.ascontiguousarray(x[b].T)
        in_maps.append(m)
    return in_maps


_NC_CACHE = {}


def kernel(**inputs):
    in_maps = _host_layout(inputs)
    key = False
    if key not in _NC_CACHE:
        _NC_CACHE[key] = build_program(debug=False)
    nc = _NC_CACHE[key]
    res = run_bass_kernel_spmd(nc, in_maps, core_ids=list(range(8)))
    out = np.stack([np.asarray(r["out_fm"]).T for r in res.results], axis=0)
    return np.ascontiguousarray(out.astype(np.float32))


# version 19
# speedup vs baseline: 1.0486x; 1.0092x over previous
import sys
import numpy as np

try:
    import concourse.bass as bass
except ImportError:
    sys.path.insert(0, "/opt/trn_rl_repo")
    import concourse.bass as bass
import concourse.mybir as mybir
from concourse.bass_utils import run_bass_kernel_spmd
from contextlib import ExitStack

F32 = mybir.dt.float32
BF16 = mybir.dt.bfloat16
AF = mybir.ActivationFunctionType
ALU = mybir.AluOpType
AX = mybir.AxisListType

T = 2048
DM = 2048
NDC = 16
H = 16
HK = 4
GQ = 4
DH = 128
DFF = 1408
NFC = 11
NE = 18
NRT = 16
WINDOW = 1024
EPS_QK = 1e-6
EPS_LN = 1e-5
RBASE = 10000.0
TA = 512
NTI = T // TA
TF = 1024
NTF = T // TF
INV_SQRT_DH = float(1.0 / np.sqrt(DH))

DMA_SEMS = ("LW", "LX", "ST", "SX", "SKV", "SQA", "SQB", "SOA", "SOB", "S1A", "S1B", "S1C", "S2A", "S2B", "S2C", "SYA", "SYB", "SXB", "STA", "STB", "STC", "STD")
OWN_SEM = {"pe": "PE", "act": "ACT", "dve": "DVE", "pool": "PL", "sp": None}


class Sched:
    def __init__(self):
        self.cur = None

    def start(self, engname, eng, sems):
        self.cur = engname
        self.eng = eng
        self.sems = sems
        self.cnt = {k: 0 for k in sems}
        self.lastwait = {k: 0 for k in sems}

    def op(self, engname, fn, waits=(), inc=None):
        tok = None
        if inc is not None:
            self.cnt[inc] += 16 if inc in DMA_SEMS else 1
            tok = (inc, self.cnt[inc])
        if engname == self.cur:
            own = OWN_SEM.get(engname)
            for w in waits:
                if w is None:
                    continue
                s, v = w
                if s == own or v <= self.lastwait[s]:
                    continue
                self.eng.wait_ge(self.sems[s], v)
                self.lastwait[s] = v
            ins = fn(self.eng)
            if inc is not None:
                ins.then_inc(self.sems[inc], 16 if inc in DMA_SEMS else 1)
        return tok

    def join(self, engname, waits):
        if engname == self.cur:
            own = OWN_SEM.get(engname)
            for w in waits:
                if w is None:
                    continue
                s, v = w
                if s == own or v <= self.lastwait[s]:
                    continue
                self.eng.wait_ge(self.sems[s], v)
                self.lastwait[s] = v


def tmax(*toks):
    best = {}
    for t in toks:
        if t is None:
            continue
        s, v = t
        if v > best.get(s, -1):
            best[s] = v
    return [(s, v) for s, v in best.items()]


class NS:
    pass


def build_program(debug=False):
    nc = bass.Bass("TRN2", target_bir_lowering=False)

    D = NS()
    D.x_fm = nc.dram_tensor("x_fm", [DM, T], F32, kind="ExternalInput").ap()
    D.wq_sl = nc.dram_tensor("wq_sl", [H, 128, NDC, 128], F32, kind="ExternalInput").ap()
    D.wk_sl = nc.dram_tensor("wk_sl", [HK, 128, NDC, 128], F32, kind="ExternalInput").ap()
    D.wvT = nc.dram_tensor("wvT", [DM, HK * DH], F32, kind="ExternalInput").ap()
    D.wo_sl = nc.dram_tensor("wo_sl", [NDC, 128, NDC, 128], F32, kind="ExternalInput").ap()
    D.w1_sl = nc.dram_tensor("w1_sl", [NE, NFC, 128, NDC, 128], F32, kind="ExternalInput").ap()
    D.w2_sl = nc.dram_tensor("w2_sl", [NE, NDC, 128, NFC, 128], F32, kind="ExternalInput").ap()
    D.b1c = nc.dram_tensor("b1c", [128, NE * NFC], F32, kind="ExternalInput").ap()
    D.b2c = nc.dram_tensor("b2c", [128, 2 * NDC], F32, kind="ExternalInput").ap()
    D.b2r = nc.dram_tensor("b2r", [NRT, DM], F32, kind="ExternalInput").ap()
    D.cT = nc.dram_tensor("cT", [DM, NRT], F32, kind="ExternalInput").ap()
    D.bias_bc = nc.dram_tensor("bias_bc", [128, NRT], F32, kind="ExternalInput").ap()
    D.gq = nc.dram_tensor("gq", [128, 1], F32, kind="ExternalInput").ap()
    D.gk = nc.dram_tensor("gk", [128, 1], F32, kind="ExternalInput").ap()
    D.cos_t = nc.dram_tensor("cos_t", [128, T], F32, kind="ExternalInput").ap()
    D.sin_t = nc.dram_tensor("sin_t", [128, T], F32, kind="ExternalInput").ap()
    D.ones_c = nc.dram_tensor("ones_c", [128, 128], F32, kind="ExternalInput").ap()
    D.ident_c = nc.dram_tensor("ident_c", [128, 128], F32, kind="ExternalInput").ap()
    D.mcaus = nc.dram_tensor("mcaus", [128, 128], F32, kind="ExternalInput").ap()
    D.medge = nc.dram_tensor("medge", [128, 128], F32, kind="ExternalInput").ap()
    D.sel_c = nc.dram_tensor("sel_c", [NRT, NRT * 128], F32, kind="ExternalInput").ap()
    D.p64_c = nc.dram_tensor("p64_c", [128, 128], F32, kind="ExternalInput").ap()
    D.identu_c = nc.dram_tensor("identu_c", [128, 128], mybir.dt.uint8, kind="ExternalInput").ap()
    D.y1_sp = nc.dram_tensor("y1_spill", [DM, T], F32, kind="Internal").ap()
    D.out_fm = nc.dram_tensor("out_fm", [DM, T], F32, kind="ExternalOutput").ap()
    D.dbg = {}
    if debug:
        for nm, shape in (("d_lnx", (DM, T)), ("d_q", (DH, T)), ("d_k", (HK * DH, T)),
                          ("d_v", (T, HK * DH)), ("d_att", (DM, T)), ("d_y1", (DM, T)),
                          ("d_u", (DM, T)), ("d_g", (T, NRT)), ("d_h0", (DFF, TF)),
                          ("d_y0", (DH, T)),
                          ("d_acc", (DM, T))):
            D.dbg[nm] = nc.dram_tensor(nm, list(shape), F32, kind="ExternalOutput").ap()

    with ExitStack() as ctx:
        e = ctx.enter_context
        C = NS()
        C.cos_s = e(nc.sbuf_tensor("cos_s", [128, T], BF16))
        C.sin_s = e(nc.sbuf_tensor("sin_s", [128, T], BF16))
        C.mcaus_bf = e(nc.sbuf_tensor("mcaus_bf", [128, 128], BF16))
        C.medge_bf = e(nc.sbuf_tensor("medge_bf", [128, 128], BF16))
        C.id_bf = e(nc.sbuf_tensor("id_bf", [128, 128], BF16))
        C.id_f = e(nc.sbuf_tensor("id_f", [128, 128], F32))
        C.ones_bf = e(nc.sbuf_tensor("ones_bf", [128, 128], BF16))
        C.p64_f = e(nc.sbuf_tensor("p64_f", [128, 128], F32))
        C.gq_s = e(nc.sbuf_tensor("gq_s", [128, 1], F32))
        C.gk_s = e(nc.sbuf_tensor("gk_s", [128, 1], F32))
        C.eps6 = e(nc.sbuf_tensor("eps6", [128, 1], F32))
        C.eps5 = e(nc.sbuf_tensor("eps5", [128, 1], F32))
        C.b1c_s = e(nc.sbuf_tensor("b1c_s", [128, NE * NFC], F32))
        C.b2c_s = e(nc.sbuf_tensor("b2c_s", [128, 2 * NDC], F32))
        C.b2sh_s = e(nc.sbuf_tensor("b2sh_s", [128, NDC], F32))
        C.b2r_bf = e(nc.sbuf_tensor("b2r_bf", [NRT, DM], BF16))
        C.sel_bf = e(nc.sbuf_tensor("sel_bf", [NRT, NRT * 128], BF16))
        C.cT_bf = e(nc.sbuf_tensor("cT_bf", [128, NDC * NRT], BF16))
        C.bias_bc_s = e(nc.sbuf_tensor("bias_bc_s", [128, NRT], F32))
        C.nmax = e(nc.sbuf_tensor("nmax", [128, 1], F32))
        C.sums = e(nc.sbuf_tensor("sums", [128, 1], F32))
        C.lnsum = e(nc.sbuf_tensor("lnsum", [128, 1], F32))
        C.bias2 = e(nc.sbuf_tensor("bias2", [128, 1], F32))
        C.sc16 = e(nc.sbuf_tensor("sc16", [128, NRT], F32))
        C.mx8 = e(nc.sbuf_tensor("mx8", [128, 8], F32))
        C.msk16 = e(nc.sbuf_tensor("msk16", [128, NRT], F32))
        C.g16 = e(nc.sbuf_tensor("g16", [128, NRT], F32))
        C.g_all = e(nc.sbuf_tensor("g_all", [128, 8 * NRT], F32))
        C.rs1 = e(nc.sbuf_tensor("rs1", [128, 1], F32))
        C.thr1 = e(nc.sbuf_tensor("thr1", [128, 1], F32))
        C.rsB = e(nc.sbuf_tensor("rsB", [128, 1], F32))
        C.rsC = e(nc.sbuf_tensor("rsC", [128, 1], F32))
        C.rsb_bf = e(nc.sbuf_tensor("rsb_bf", [128, 2], BF16))
        C.rs2 = e(nc.sbuf_tensor("rs2", [128, 2], F32))
        C.negC = e(nc.sbuf_tensor("negC", [128, 1], F32))
        C.mqA = e(nc.sbuf_tensor("mqA", [128, 1], F32))
        C.gbc_bf = e(nc.sbuf_tensor("gbc_bf", [128, 128], BF16))
        C.eps35 = e(nc.sbuf_tensor("eps35", [128, 1], F32))
        C.diag_bf = e(nc.sbuf_tensor("diag_bf", [128, 2 * 128], BF16))
        C.id_u8 = e(nc.sbuf_tensor("id_u8", [128, 128], mybir.dt.uint8))

        sems = {}
        for s in ("LW", "LX", "ST", "PE", "ACT", "DVE", "PL", "SX", "SKV", "SQA", "SQB", "SOA", "SOB", "S1A", "S1B", "S1C", "S2A", "S2B", "S2C", "SYA", "SYB", "SXB", "STA", "STB", "STC", "STD"):
            sems[s] = e(nc.semaphore(s))

        block = e(nc.Block())
        sch = Sched()
        state = {}

        with ExitStack() as actx:
            ae = actx.enter_context
            A = NS()
            A.x_bf = ae(nc.sbuf_tensor("x_bf", [128, 2 * NDC * TA], BF16))
            A.sq_scr = ae(nc.sbuf_tensor("sq_scr", [128, 2 * TA], BF16))
            A.y1s = ae(nc.sbuf_tensor("y1s", [128, 4 * TA], F32))
            A.mean2_s = ae(nc.sbuf_tensor("mean2_s", [128, TA], F32))
            A.msqv2_s = ae(nc.sbuf_tensor("msqv2_s", [128, TA], F32))
            A.rstd2_s = ae(nc.sbuf_tensor("rstd2_s", [128, TA], F32))
            A.ln_bf = ae(nc.sbuf_tensor("ln_bf", [128, NDC * TA], BF16))
            A.ln_scr = ae(nc.sbuf_tensor("ln_scr", [128, TA], F32))
            A.mean_s = ae(nc.sbuf_tensor("mean_s", [128, TA], F32))
            A.msqv_s = ae(nc.sbuf_tensor("msqv_s", [128, TA], F32))
            A.rstd_s = ae(nc.sbuf_tensor("rstd_s", [128, TA], F32))
            A.k_full = ae(nc.sbuf_tensor("k_full", [128, HK * T], BF16))
            A.v_full = ae(nc.sbuf_tensor("v_full", [128, NDC * TA], BF16))
            A.q_ro = ae(nc.sbuf_tensor("q_ro", [128, TA], F32))
            A.q_fp = ae(nc.sbuf_tensor("q_fp", [128, 2 * TA], F32))
            A.q_sq = ae(nc.sbuf_tensor("q_sq", [128, TA], BF16))
            A.rn_s = ae(nc.sbuf_tensor("rn_s", [128, TA], F32))
            A.q_bf = ae(nc.sbuf_tensor("q_bf", [128, TA], BF16))
            A.p_bf = ae(nc.sbuf_tensor("p_bf", [128, 2 * 1152], BF16))
            A.pT_sb = ae(nc.sbuf_tensor("pT_sb", [128, 1152], BF16))
            A.y_att = ae(nc.sbuf_tensor("y_att", [128, H * TA], BF16))
            A.att_bf = ae(nc.sbuf_tensor("att_bf", [128, NDC * TA], BF16))
            A.wbig = ae(nc.sbuf_tensor("wbig", [128, 4 * DM], BF16))
            A.wq_sb = ae(nc.sbuf_tensor("wq_sb", [128, 2 * DM], BF16))
            A.wo_sb = ae(nc.sbuf_tensor("wo_sb", [128, 2 * DM], BF16))
            A.pA = ae(nc.psum_tensor("pA", [128, 512], F32))
            A.pB = ae(nc.psum_tensor("pB", [128, 512], F32))
            A.pQ = ae(nc.psum_tensor("pQ", [128, 512], F32))
            A.pS = ae(nc.psum_tensor("pS", [128, 1536], F32))
            A.pTT = ae(nc.psum_tensor("pTT", [128, 2048], BF16))

            def attn_pass(engname, eng):
                sch.start(engname, eng, sems)
                ct = _sched_consts(sch, nc, D, C)
                af = _sched_attn(sch, nc, D, C, A, ct, debug)
                state[engname] = (ct, af, dict(sch.cnt), dict(sch.lastwait))

            block.tensor(lambda eng: attn_pass("pe", eng))
            block.scalar(lambda eng: attn_pass("act", eng))
            block.vector(lambda eng: attn_pass("dve", eng))
            block.gpsimd(lambda eng: attn_pass("pool", eng))
            block.sync(lambda eng: attn_pass("sp", eng))

        with ExitStack() as fctx:
            fe = fctx.enter_context
            Fb = NS()
            Fb.u_bf = fe(nc.sbuf_tensor("u_bf", [128, NDC * TF], BF16))
            Fb.acc = fe(nc.sbuf_tensor("acc", [128, NDC * TF], F32))
            Fb.h_bf = fe(nc.sbuf_tensor("h_bf", [128, NFC * TF], BF16))
            Fb.G_bf = fe(nc.sbuf_tensor("G_bf", [128, TF], BF16))
            Fb.gT_bf = fe(nc.sbuf_tensor("gT_bf", [NRT, TF], BF16))
            Fb.y1_scr = fe(nc.sbuf_tensor("y1_scr", [128, 2 * TF], F32))
            Fb.cast_f = fe(nc.sbuf_tensor("cast_f", [128, 2 * TF], BF16))
            Fb.sq_f = fe(nc.sbuf_tensor("sq_f", [128, TF], BF16))
            Fb.ln_f = fe(nc.sbuf_tensor("ln_f", [128, TF], F32))
            Fb.mean_f = fe(nc.sbuf_tensor("mean_f", [128, TF], F32))
            Fb.msqv_f = fe(nc.sbuf_tensor("msqv_f", [128, TF], F32))
            Fb.rstd_f = fe(nc.sbuf_tensor("rstd_f", [128, TF], F32))
            Fb.w1_sb = fe(nc.sbuf_tensor("w1_sb", [128, 3 * NDC * 128], BF16))
            Fb.w2_sb = fe(nc.sbuf_tensor("w2_sb", [128, 2 * NFC * 128], BF16))
            Fb.pHa = fe(nc.psum_tensor("pHa", [128, TF], F32))
            Fb.pHb = fe(nc.psum_tensor("pHb", [128, TF], F32))
            Fb.pYa = fe(nc.psum_tensor("pYa", [128, TF], F32))
            Fb.pYb = fe(nc.psum_tensor("pYb", [128, TF], F32))

            def ffn_pass(engname, eng):
                ct, af, cnt, lastwait = state[engname]
                sch.start(engname, eng, sems)
                sch.cnt = cnt
                sch.lastwait = lastwait
                _sched_ffn(sch, nc, D, C, Fb, ct, af, debug)

            block.tensor(lambda eng: ffn_pass("pe", eng))
            block.scalar(lambda eng: ffn_pass("act", eng))
            block.vector(lambda eng: ffn_pass("dve", eng))
            block.gpsimd(lambda eng: ffn_pass("pool", eng))
            block.sync(lambda eng: ffn_pass("sp", eng))

    return nc


def _sched_consts(sch, nc, D, C):
    ct = {}
    t = None
    for dst, src in ((C.gq_s, D.gq), (C.gk_s, D.gk),
                     (C.b1c_s, D.b1c), (C.bias_bc_s, D.bias_bc), (C.id_f, D.ident_c),
                     (C.p64_f, D.p64_c)):
        t = sch.op("sp", lambda E, d=dst, s=src: E.dma_start(out=d[:], in_=s), inc="LX")
    sch.op("sp", lambda E: E.dma_start(out=C.id_u8[:], in_=D.identu_c), inc="LX")
    t = sch.op("sp", lambda E: E.dma_start(
        out=C.b2c_s[:].rearrange("p (n c) -> p n c", n=2),
        in_=D.b2c.rearrange("p (n c) -> p n c", n=2)), inc="LX")
    ct["lx"] = t
    t = None
    for dst, src in ((C.ones_bf, D.ones_c), (C.id_bf, D.ident_c), (C.b2r_bf, D.b2r),
                     (C.sel_bf, D.sel_c), (C.cos_s, D.cos_t), (C.sin_s, D.sin_t),
                     (C.mcaus_bf, D.mcaus), (C.medge_bf, D.medge)):
        t = sch.op("pool", lambda E, d=dst, s=src: E.dma_start(out=d[:], in_=s), inc="LW")
    t = sch.op("pool", lambda E: E.dma_start(
        out=C.cT_bf[:].rearrange("p (c j) -> p c j", c=NDC),
        in_=D.cT.rearrange("(c p) j -> p c j", p=128)), inc="LW")
    ct["lw"] = t
    sch.op("dve", lambda E: E.memset(C.eps6[:], EPS_QK))
    sch.op("dve", lambda E: E.memset(C.diag_bf[:], 0.0))
    sch.op("dve", lambda E: E.memset(C.eps35[:], 1e-35))
    ct["eps"] = sch.op("dve", lambda E: E.memset(C.eps5[:], EPS_LN), inc="DVE")
    ct["b2sh"] = sch.op("dve", lambda E: E.tensor_add(
        C.b2sh_s[:], C.b2c_s[:, 0:NDC], C.b2c_s[:, NDC:2 * NDC]),
        waits=[ct["lx"]], inc="DVE")
    return ct


def _sched_attn(sch, nc, D, C, A, ct, debug):
    st = NS()
    st.wbig_free = None
    st.wq_free = [None, None]
    st.wo_free = [None, None]
    st.scr_free = [None, None]
    st.pA_free = ct["eps"]
    st.pB_free = None
    st.pQ_free = None
    st.pS_free = None
    st.pTT_free = None
    st.pbf_free = None
    st.pT_free = None
    st.lnbf_free = None
    st.yatt_free = None
    st.attbf_free = []
    st.stat_free = []
    st.qfp_free = [None, None]
    st.xbf_free = [[], []]
    st.xld = [None, None]
    st.y1s_free = []
    st.qro_free = None
    st.proj_tok = None
    st.diag_free = [None, None]
    st.tr_free = [None, None]
    st.pbf_par_free = [None, None]
    st.vts = [None] * NDC
    t_spill = None

    def rope_start(h, par):
        t_ev = sch.op("act", lambda E, par=par: E.copy(
            A.q_fp[:, par * TA:(par + 1) * TA], A.pQ[:]),
            waits=tmax(st.proj_tok, st.qfp_free[par]), inc="ACT")
        st.pQ_free = t_ev
        return t_ev

    def rope_finish(t_ev, par, gamma_s, scale2, dst_fn, ti0):
        t_sw = sch.op("pe", lambda E, par=par: E.matmul(
            A.pA[:], C.p64_f[:], A.q_fp[:, par * TA:(par + 1) * TA],
            start=True, stop=True),
            waits=tmax(t_ev, st.pA_free), inc="PE")
        sch.op("dve", lambda E, par=par: E.tensor_mul(
            A.ln_scr[:], A.q_fp[:, par * TA:(par + 1) * TA], C.cos_s[:, ti0:ti0 + TA]),
            waits=tmax(t_ev))
        sch.op("dve", lambda E: E.tensor_mul(
            A.q_ro[:], A.pA[:], C.sin_s[:, ti0:ti0 + TA]), waits=tmax(t_sw))
        t_rope = sch.op("dve", lambda E: E.tensor_add(
            A.q_ro[:], A.q_ro[:], A.ln_scr[:]), inc="DVE")
        st.pA_free = t_rope
        st.qfp_free[par] = t_rope
        t_sq = sch.op("dve", lambda E: E.tensor_mul(A.q_sq[:], A.q_ro[:], A.q_ro[:]),
                      inc="DVE")
        t_ss = sch.op("pe", lambda E: E.matmul(
            A.pB[:], C.ones_bf[:], A.q_sq[:], start=True, stop=True),
            waits=tmax(t_sq, st.pB_free), inc="PE")
        t_s1 = sch.op("act", lambda E: E.activation(A.rn_s[:], A.pB[:], AF.Sqrt),
                      waits=tmax(t_ss), inc="ACT")
        st.pB_free = t_s1
        sch.op("act", lambda E: E.activation(A.rn_s[:], A.rn_s[:], AF.Ln, bias=C.eps6[:]))
        t_rn = sch.op("act", lambda E: E.activation(A.rn_s[:], A.rn_s[:], AF.Exp, scale=-1.0),
                      inc="ACT")
        sch.op("dve", lambda E: E.tensor_mul(A.ln_scr[:], A.q_ro[:], A.rn_s[:]),
               waits=tmax(t_rn))
        if scale2 is None:
            t_fin = sch.op("dve", lambda E: E.tensor_scalar(
                dst_fn(), A.ln_scr[:], gamma_s[:], None, ALU.mult), inc="DVE")
        else:
            t_fin = sch.op("dve", lambda E: E.tensor_scalar(
                dst_fn(), A.ln_scr[:], gamma_s[:], scale2, ALU.mult, ALU.mult),
                inc="DVE")
        st.qro_free = t_fin
        return t_fin

    t_b1 = sch.op("dve", lambda E: E.tensor_copy(C.gbc_bf[:], C.gq_s[:].to_broadcast([128, 128])),
                  waits=tmax(ct["lx"]), inc="DVE")
    t_t1 = sch.op("pe", lambda E: E.transpose(A.pTT[:, 0:128], C.gbc_bf[:], C.id_bf[:]),
                  waits=tmax(t_b1, ct["lw"]), inc="PE")
    t_ev1 = sch.op("act", lambda E: E.copy(A.ln_scr[:, 0:128], A.pTT[:, 0:128]),
                   waits=tmax(t_t1), inc="ACT")
    t_m1 = sch.op("dve", lambda E: E.reduce_max(C.mqA[:], A.ln_scr[:, 0:128], AX.X,
                                                apply_absolute_value=True),
                  waits=tmax(t_ev1), inc="DVE")
    t_b2 = sch.op("dve", lambda E: E.tensor_copy(C.gbc_bf[:], C.gk_s[:].to_broadcast([128, 128])),
                  waits=tmax(t_t1), inc="DVE")
    t_t2 = sch.op("pe", lambda E: E.transpose(A.pTT[:, 128:256], C.gbc_bf[:], C.id_bf[:]),
                  waits=tmax(t_b2, t_ev1), inc="PE")
    t_ev2 = sch.op("act", lambda E: E.copy(A.ln_scr[:, 128:256], A.pTT[:, 128:256]),
                   waits=tmax(t_t2), inc="ACT")
    sch.op("dve", lambda E: E.reduce_max(C.negC[:], A.ln_scr[:, 128:256], AX.X,
                                         apply_absolute_value=True),
           waits=tmax(t_ev2))
    sch.op("dve", lambda E: E.tensor_mul(C.negC[:], C.negC[:], C.mqA[:]))
    t_negC = sch.op("dve", lambda E: E.tensor_scalar(
        C.negC[:], C.negC[:], -INV_SQRT_DH, None, ALU.mult), inc="DVE")
    t_m2 = t_negC
    st.pTT_free = t_ev2
    st.stat_free = [t_m2]

    for i in range(NTI):
        ti0 = i * TA
        par = i % 2
        if i == 0:
            st.xld[0] = sch.op("pool", lambda E: E.dma_start(
                out=A.x_bf[:, 0:NDC * TA].rearrange("p (c t) -> p c t", c=NDC),
                in_=D.x_fm.rearrange("(c p) t -> p c t", p=128)[:, :, 0:TA]),
                inc="SX")
        if i + 1 < NTI:
            pnxt = (i + 1) % 2
            st.xld[pnxt] = sch.op("pool", lambda E, i=i, pnxt=pnxt: E.dma_start(
                out=A.x_bf[:, pnxt * NDC * TA:(pnxt + 1) * NDC * TA].rearrange("p (c t) -> p c t", c=NDC),
                in_=D.x_fm.rearrange("(c p) t -> p c t", p=128)[:, :, (i + 1) * TA:(i + 2) * TA]),
                waits=tmax(*st.xbf_free[pnxt]), inc=("SX" if pnxt == 0 else "SXB"))
        t_x = st.xld[par]
        xb0 = par * NDC * TA
        t_wv = sch.op("pool", lambda E: E.dma_start(
            out=A.wbig[:, 0:NDC * 512].rearrange("p (c v) -> p c v", c=NDC),
            in_=D.wvT.rearrange("(c p) v -> p c v", p=128)),
            waits=tmax(st.wbig_free), inc="SKV")
        t_stats = None
        for dc in range(NDC):
            b = dc % 2
            t_cast = sch.op("act", lambda E, dc=dc, b=b, xb0=xb0: E.activation(
                A.sq_scr[:, b * TA:(b + 1) * TA], A.x_bf[:, xb0 + dc * TA: xb0 + (dc + 1) * TA],
                AF.Square), waits=tmax(t_x, st.scr_free[b]), inc="ACT")
            sch.op("pe", lambda E, b=b, dc=dc, xb0=xb0: E.matmul(
                A.pA[:], C.ones_bf[:], A.x_bf[:, xb0 + dc * TA: xb0 + (dc + 1) * TA],
                start=(dc == 0), stop=(dc == NDC - 1)),
                waits=tmax(t_cast,
                           st.pA_free if dc == 0 else None,
                           st.pB_free if dc == 0 else None,
                           ct["lw"] if dc == 0 else None))
            t_stats = sch.op("pe", lambda E, b=b, dc=dc: E.matmul(
                A.pB[:], C.ones_bf[:], A.sq_scr[:, b * TA:(b + 1) * TA],
                start=(dc == 0), stop=(dc == NDC - 1)), inc="PE")
            st.scr_free[b] = t_stats
        t_mean = sch.op("act", lambda E: E.mul(A.mean_s[:], A.pA[:], 1.0 / DM),
                        waits=tmax(t_stats, *st.stat_free), inc="ACT")
        t_msq = sch.op("act", lambda E: E.activation(A.msqv_s[:], A.mean_s[:], AF.Square),
                       inc="ACT")
        t_var = sch.op("dve", lambda E: E.scalar_tensor_tensor(
            A.msqv_s[:], A.pB[:], 1.0 / DM, A.msqv_s[:], ALU.mult, ALU.subtract),
            waits=tmax(t_msq, t_stats), inc="DVE")
        sch.op("act", lambda E: E.activation(
            A.rstd_s[:], A.msqv_s[:], AF.Ln, bias=C.eps5[:]),
            waits=tmax(t_var, ct["eps"]))
        t_rstd = sch.op("act", lambda E: E.activation(
            A.rstd_s[:], A.rstd_s[:], AF.Exp, scale=-0.5), inc="ACT")
        st.pA_free = t_mean
        st.pB_free = t_var
        t_ln = None
        t_ln_pl = None
        t_ln_dc = [None] * NDC
        for dc in range(NDC):
            if dc % 3 == 2:
                sch.op("pool", lambda E, dc=dc, xb0=xb0: E.tensor_sub(
                    A.q_ro[:], A.x_bf[:, xb0 + dc * TA: xb0 + (dc + 1) * TA], A.mean_s[:]),
                    waits=tmax(t_rstd, t_mean, st.lnbf_free, st.qro_free))
                t_ln_pl = sch.op("pool", lambda E, dc=dc: E.tensor_mul(
                    A.ln_bf[:, dc * TA:(dc + 1) * TA], A.q_ro[:], A.rstd_s[:]),
                    inc="PL")
                t_ln_dc[dc] = t_ln_pl
            else:
                sch.op("dve", lambda E, dc=dc, xb0=xb0: E.tensor_sub(
                    A.ln_scr[:], A.x_bf[:, xb0 + dc * TA: xb0 + (dc + 1) * TA], A.mean_s[:]),
                    waits=tmax(t_rstd, t_mean, st.lnbf_free))
                t_ln = sch.op("dve", lambda E, dc=dc: E.tensor_mul(
                    A.ln_bf[:, dc * TA:(dc + 1) * TA], A.ln_scr[:], A.rstd_s[:]),
                    inc="DVE")
                t_ln_dc[dc] = t_ln
        if debug:
            sch.op("pool", lambda E, ti0=ti0: E.dma_start(
                out=D.dbg["d_lnx"].rearrange("(c p) t -> p c t", p=128)[:, :, ti0:ti0 + TA],
                in_=A.ln_bf[:].rearrange("p (c t) -> p c t", c=NDC)),
                waits=tmax(t_ln), inc="LW")
        st.stat_free = [t_ln, t_ln_pl]
        t_vp = None
        for ts_ in range(4):
            gt = 4 * i + ts_

            def vproj(E, ts_=ts_):
                last = None
                for dc in range(NDC):
                    last = E.matmul(
                        A.pQ[:], A.ln_bf[:, dc * TA + ts_ * 128: dc * TA + ts_ * 128 + 128],
                        A.wbig[:, dc * 512:(dc + 1) * 512],
                        start=(dc == 0), stop=(dc == NDC - 1))
                return last
            t_vp = sch.op("pe", vproj,
                          waits=tmax(t_wv, st.pQ_free, t_ln, t_ln_pl,
                                     *([t_ln_dc[dc] for dc in range(NDC)] if ts_ == 0 else [])),
                          inc="PE")
            t_ve = sch.op("act", lambda E, gt=gt: E.copy(
                A.v_full[:, gt * 512:(gt + 1) * 512], A.pQ[:]),
                waits=tmax(t_vp), inc="ACT")
            st.vts[gt] = t_ve
            st.pQ_free = t_ve
        t_wk = sch.op("pool", lambda E: E.dma_start(
            out=A.wbig[:].rearrange("p (k c q) -> p k c q", k=HK, c=NDC),
            in_=D.wk_sl.rearrange("k p c q -> p k c q")),
            waits=tmax(t_vp), inc="SKV")
        t_kfin = None
        for hk in range(HK):
            def kproj(E, hk=hk):
                last = None
                for dc in range(NDC):
                    last = E.matmul(
                        A.pQ[:], A.wbig[:, hk * DM + dc * 128: hk * DM + (dc + 1) * 128],
                        A.ln_bf[:, dc * TA:(dc + 1) * TA],
                        start=(dc == 0), stop=(dc == NDC - 1))
                return last
            st.proj_tok = sch.op("pe", kproj,
                                 waits=tmax(t_ln, t_ln_pl, t_wk, st.pQ_free), inc="PE")
            t_ev = rope_start(hk, hk % 2)
            t_kfin = rope_finish(
                t_ev, hk % 2, C.gk_s, None,
                lambda hk=hk, ti0=ti0: A.k_full[:, hk * T + ti0: hk * T + ti0 + TA], ti0)
        if debug:
            sch.op("pool", lambda E, ti0=ti0: E.dma_start(
                out=D.dbg["d_k"].rearrange("(c p) t -> p c t", p=128)[:, :, ti0:ti0 + TA],
                in_=A.k_full[:].rearrange("p (c t) -> p c t", c=HK)[:, :, ti0:ti0 + TA]),
                waits=tmax(t_kfin), inc="LW")
            sch.op("pool", lambda E, i=i: E.dma_start(
                out=D.dbg["d_v"].rearrange("(c p) v -> p c v", p=128)[:, 4 * i:4 * i + 4, :],
                in_=A.v_full[:].rearrange("p (c v) -> p c v", c=NDC)[:, 4 * i:4 * i + 4, :]),
                waits=tmax(st.vts[4 * i + 3]), inc="LW")
        t_wq = [None] * H

        def load_wq(h):
            return sch.op("pool", lambda E, h=h: E.dma_start(
                out=A.wq_sb[:, (h % 2) * DM:(h % 2 + 1) * DM].rearrange("p (c q) -> p c q", c=NDC),
                in_=D.wq_sl[h]),
                waits=tmax(st.wq_free[h % 2]), inc=("SQA" if h % 2 == 0 else "SQB"))

        t_wq[0] = load_wq(0)
        t_wq[1] = load_wq(1)
        t_qfin = [None] * H
        pend = None

        def q_dst(h, ti0=ti0):
            return lambda h=h: A.wbig[:, h * TA:(h + 1) * TA]

        for h in range(H):
            def qproj(E, h=h):
                last = None
                for dc in range(NDC):
                    last = E.matmul(
                        A.pQ[:], A.wq_sb[:, (h % 2) * DM + dc * 128:(h % 2) * DM + (dc + 1) * 128],
                        A.ln_bf[:, dc * TA:(dc + 1) * TA],
                        start=(dc == 0), stop=(dc == NDC - 1))
                return last
            st.proj_tok = sch.op("pe", qproj,
                                 waits=tmax(t_wq[h], st.pQ_free, t_ln, t_ln_pl,
                                            st.wbig_free if h == 0 else None),
                                 inc="PE")
            t_qp = st.proj_tok
            st.wq_free[h % 2] = t_qp
            if h + 2 < H:
                t_wq[h + 2] = load_wq(h + 2)
            t_ev = rope_start(h, h % 2)
            if pend is not None:
                ph, pev = pend
                t_qfin[ph] = rope_finish(pev, ph % 2, C.gq_s, INV_SQRT_DH, q_dst(ph), ti0)
            pend = (h, t_ev)
        ph, pev = pend
        t_qfin[ph] = rope_finish(pev, ph % 2, C.gq_s, INV_SQRT_DH, q_dst(ph), ti0)
        if debug:
            sch.op("pool", lambda E, ti0=ti0: E.dma_start(
                out=D.dbg["d_q"][:, ti0:ti0 + TA], in_=A.wbig[:, 0:TA]),
                waits=tmax(t_qfin[0]), inc="LW")

        core = []
        for h in range(H):
            for sub in range(4):
                core.append((h, sub))
        pend_tp = None
        t_e1 = None
        t_sc_last = None
        for idx, (h, sub) in enumerate(core):
            hk = h // GQ
            gsub = 4 * i + sub
            c_lo = max(0, gsub - 8)
            nch = gsub - c_lo + 1
            W = 128 * nch

            def scmm(E, h=h, sub=sub, c_lo=c_lo, W=W, hk=hk, has_edge=(gsub >= 8)):
                last = None
                for off in range(0, W, 512):
                    n = min(512, W - off)
                    has_mask = (has_edge and off == 0) or (W - 128 >= off and W - 128 < off + n)
                    last = E.matmul(
                        A.pS[:, off:off + n],
                        A.wbig[:, h * TA + sub * 128: h * TA + sub * 128 + 128],
                        A.k_full[:, hk * T + c_lo * 128 + off: hk * T + c_lo * 128 + off + n],
                        start=True, stop=not has_mask)
                    if has_edge and off == 0:
                        last = E.matmul(A.pS[:, 0:128], C.id_bf[:], C.medge_bf[:],
                                        start=False,
                                        stop=not (W - 128 < off + n))
                    if W - 128 >= off and W - 128 < off + n:
                        last = E.matmul(A.pS[:, W - 128:W], C.id_bf[:], C.mcaus_bf[:],
                                        start=False, stop=True)
                return last
            t_sc = sch.op("pe", scmm, waits=tmax(t_qfin[h], st.pS_free, t_kfin, ct["lw"]),
                          inc="PE")
            t_sc_last = t_sc
            par = idx % 2
            t_e1 = sch.op("act", lambda E, W=W, par=par: E.activation(
                A.p_bf[:, par * 1152: par * 1152 + W], A.pS[:, 0:W], AF.Exp,
                bias=C.negC[:], accum_out=C.sums[:]),
                waits=tmax(t_sc, t_negC, st.pbf_par_free[par]), inc="ACT")
            st.pS_free = t_e1
            sch.op("act", lambda E: E.activation(C.lnsum[:], C.sums[:], AF.Ln, bias=C.eps35[:]))
            t_ds = sch.op("act", lambda E, par=par: E.activation(
                C.rs2[:, par:par + 1], C.lnsum[:], AF.Exp, scale=-1.0),
                waits=tmax(st.diag_free[par]), inc="ACT")
            t_diag = sch.op("dve", lambda E, par=par, W=W: E.tensor_scalar(
                A.p_bf[:, par * 1152: par * 1152 + W],
                A.p_bf[:, par * 1152: par * 1152 + W],
                C.rs2[:, par:par + 1], None, ALU.mult),
                waits=tmax(t_ds), inc="DVE")
            st.diag_free[par] = t_diag
            def emit_tp(ph, psub, pnch, pc_lo, phk, pW, t_pe1, t_pdiag, ppar):
                def trs(E, nch=pnch, par=ppar):
                    last = None
                    for c in range(nch):
                        last = E.transpose(A.pTT[:, c * 128:(c + 1) * 128],
                                           A.p_bf[:, par * 1152 + c * 128: par * 1152 + (c + 1) * 128],
                                           C.id_bf[:])
                    return last
                t_tr = sch.op("pe", trs, waits=tmax(t_pe1, t_pdiag, st.pTT_free), inc="PE")
                st.pbf_par_free[ppar] = t_tr
                st.tr_free[ppar] = t_tr
                t_pte = sch.op("dve", lambda E, W=pW: E.tensor_copy(A.pT_sb[:, 0:W], A.pTT[:, 0:W]),
                               waits=tmax(t_tr, st.pT_free), inc="DVE")
                st.pTT_free = t_pte

                def pv(E, nch=pnch, c_lo=pc_lo, hk=phk):
                    last = None
                    for c in range(nch):
                        last = E.matmul(
                            A.pB[:, 0:128],
                            A.v_full[:, (c_lo + c) * 512 + hk * 128: (c_lo + c) * 512 + hk * 128 + 128],
                            A.pT_sb[:, c * 128:(c + 1) * 128],
                            start=(c == 0), stop=(c == nch - 1))
                    return last
                t_pv = sch.op("pe", pv,
                              waits=tmax(t_pte, st.pB_free, st.vts[4 * i + psub]), inc="PE")
                st.pT_free = t_pv
                t_ye = sch.op("dve", lambda E, h=ph, sub=psub: E.tensor_copy(
                    A.y_att[:, h * TA + sub * 128: h * TA + sub * 128 + 128], A.pB[:, 0:128]),
                    waits=tmax(t_pv, st.yatt_free if (ph == 0 and psub == 0) else None),
                    inc="DVE")
                st.pB_free = t_ye
                return t_ye
            if pend_tp is not None:
                t_ye = emit_tp(*pend_tp)
            pend_tp = (h, sub, nch, c_lo, hk, W, t_e1, t_diag, par)
        t_ye = emit_tp(*pend_tp)
        t_ye_last = t_ye
        st.wbig_free = t_sc_last
        if debug:
            sch.op("pool", lambda E, ti0=ti0: E.dma_start(
                out=D.dbg["d_y0"][:, ti0:ti0 + TA], in_=A.y_att[:, 0:TA]),
                waits=tmax(t_ye), inc="LW")
        t_wo = {}

        def load_wo(dcp):
            t_wo[dcp] = sch.op("pool", lambda E, dcp=dcp: E.dma_start(
                out=A.wo_sb[:, (dcp % 2) * DM:(dcp % 2 + 1) * DM].rearrange("p (c q) -> p c q", c=NDC),
                in_=D.wo_sl[dcp]),
                waits=tmax(st.wo_free[dcp % 2]), inc=("SOA" if dcp % 2 == 0 else "SOB"))

        load_wo(0)
        load_wo(1)
        t_stats2 = None
        t_wmm = None
        for dcp in range(NDC):
            def womm(E, dcp=dcp):
                last = None
                for ec in range(NDC):
                    last = E.matmul(
                        A.pQ[:], A.wo_sb[:, (dcp % 2) * DM + ec * 128:(dcp % 2) * DM + (ec + 1) * 128],
                        A.y_att[:, ec * TA:(ec + 1) * TA],
                        start=(ec == 0), stop=(ec == NDC - 1))
                return last
            t_wmm = sch.op("pe", womm,
                           waits=tmax(t_wo[dcp], t_ye_last, st.pQ_free), inc="PE")
            st.wo_free[dcp % 2] = t_wmm
            if dcp + 2 < NDC:
                load_wo(dcp + 2)
            t_ae = sch.op("act", lambda E, dcp=dcp: E.copy(
                A.att_bf[:, dcp * TA:(dcp + 1) * TA], A.pQ[:]),
                waits=tmax(t_wmm, *(st.attbf_free if dcp == 0 else [])), inc="ACT")
            st.pQ_free = t_ae
            b = dcp % 2
            t_sqc = sch.op("act", lambda E, dcp=dcp, b=b: E.activation(
                A.sq_scr[:, b * TA:(b + 1) * TA], A.att_bf[:, dcp * TA:(dcp + 1) * TA],
                AF.Square),
                waits=tmax(st.scr_free[b]), inc="ACT")
            sch.op("pe", lambda E, dcp=dcp: E.matmul(
                A.pA[:], C.ones_bf[:], A.att_bf[:, dcp * TA:(dcp + 1) * TA],
                start=(dcp == 0), stop=(dcp == NDC - 1)),
                waits=tmax(t_sqc,
                           st.pA_free if dcp == 0 else None,
                           st.pB_free if dcp == 0 else None))
            t_stats2 = sch.op("pe", lambda E, dcp=dcp, b=b: E.matmul(
                A.pB[:], C.ones_bf[:], A.sq_scr[:, b * TA:(b + 1) * TA],
                start=(dcp == 0), stop=(dcp == NDC - 1)), inc="PE")
            st.scr_free[b] = t_stats2
        if debug:
            sch.op("pool", lambda E, ti0=ti0: E.dma_start(
                out=D.dbg["d_att"].rearrange("(c p) t -> p c t", p=128)[:, :, ti0:ti0 + TA],
                in_=A.att_bf[:].rearrange("p (c t) -> p c t", c=NDC)),
                waits=tmax(t_stats2), inc="LW")
        t_mean2 = sch.op("act", lambda E: E.mul(A.mean2_s[:], A.pA[:], 1.0 / DM),
                         waits=tmax(t_stats2, t_ln, t_ln_pl), inc="ACT")
        t_msq2 = sch.op("act", lambda E: E.activation(A.msqv2_s[:], A.mean2_s[:], AF.Square),
                        inc="ACT")
        t_var2 = sch.op("dve", lambda E: E.scalar_tensor_tensor(
            A.msqv2_s[:], A.pB[:], 1.0 / DM, A.msqv2_s[:], ALU.mult, ALU.subtract),
            waits=tmax(t_msq2, t_stats2), inc="DVE")
        sch.op("act", lambda E: E.activation(
            A.rstd2_s[:], A.msqv2_s[:], AF.Ln, bias=C.eps5[:]),
            waits=tmax(t_var2))
        t_rstd2 = sch.op("act", lambda E: E.activation(
            A.rstd2_s[:], A.rstd2_s[:], AF.Exp, scale=-0.5), inc="ACT")
        st.pA_free = t_mean2
        st.pB_free = t_var2
        t_res = None
        t_res_pl = None
        t_sp_dc = [None] * NDC
        for dc in range(NDC):
            b = dc % 4
            warslot = t_sp_dc[dc - 4] if dc >= 4 else (st.y1s_free[dc] if dc < len(st.y1s_free) else None)
            if dc % 2 == 1:
                sch.op("pool", lambda E, dc=dc: E.tensor_sub(
                    A.q_ro[:], A.att_bf[:, dc * TA:(dc + 1) * TA], A.mean2_s[:]),
                    waits=tmax(t_rstd2, t_mean2, st.qro_free))
                sch.op("pool", lambda E: E.tensor_mul(A.q_ro[:], A.q_ro[:], A.rstd2_s[:]))
                t_res_pl = sch.op("pool", lambda E, dc=dc, b=b, xb0=xb0: E.tensor_add(
                    A.y1s[:, b * TA:(b + 1) * TA], A.q_ro[:],
                    A.x_bf[:, xb0 + dc * TA: xb0 + (dc + 1) * TA]),
                    waits=tmax(warslot), inc="PL")
                t_last = t_res_pl
            else:
                sch.op("dve", lambda E, dc=dc: E.tensor_sub(
                    A.ln_scr[:], A.att_bf[:, dc * TA:(dc + 1) * TA], A.mean2_s[:]),
                    waits=tmax(t_rstd2, t_mean2))
                sch.op("dve", lambda E: E.tensor_mul(A.ln_scr[:], A.ln_scr[:], A.rstd2_s[:]))
                t_res = sch.op("dve", lambda E, dc=dc, b=b, xb0=xb0: E.tensor_add(
                    A.y1s[:, b * TA:(b + 1) * TA], A.ln_scr[:],
                    A.x_bf[:, xb0 + dc * TA: xb0 + (dc + 1) * TA]),
                    waits=tmax(warslot), inc="DVE")
                t_last = t_res
            t_sp_dc[dc] = sch.op("sp", lambda E, dc=dc, b=b, ti0=ti0: E.dma_start(
                out=D.y1_sp[dc * 128:(dc + 1) * 128, ti0:ti0 + TA],
                in_=A.y1s[:, b * TA:(b + 1) * TA]),
                waits=tmax(t_last), inc=("STA", "STB", "STC", "STD")[b])
        st.y1s_free = [t_sp_dc[NDC - 4], t_sp_dc[NDC - 3], t_sp_dc[NDC - 2], t_sp_dc[NDC - 1]]
        t_spill = st.y1s_free
        st.xbf_free[par] = [t_res, t_res_pl]
        st.attbf_free = [t_res, t_res_pl]
        st.qro_free = t_res_pl
        st.lnbf_free = t_qp
        st.yatt_free = t_wmm

    return {"pe": ("PE", sch.cnt["PE"]), "act": ("ACT", sch.cnt["ACT"]),
            "dve": ("DVE", sch.cnt["DVE"]), "spill": t_spill}


def _sched_ffn(sch, nc, D, C, Fb, ct, af, debug):
    t_spill_all = af["spill"]
    if not isinstance(t_spill_all, list):
        t_spill_all = [t_spill_all]
    allf = [af["pe"], af["act"], af["dve"]]
    for en in ("pe", "act", "dve", "pool", "sp"):
        sch.join(en, tmax(*allf))

    st = NS()
    st.y1_free = [None, None]
    st.cast_free = [None, None]
    st.u_free = None
    st.acc_free = None
    st.h_free = [None] * NFC
    st.pH_free = [None, None]
    st.pY_free = [None, None]
    st.G_free = None
    st.gT_free = None
    st.stat_free = None
    st.w1free = [None, None, None]
    st.w2free = [None, None]
    st.gall_free = [None] * 8
    st.sqf_free = None
    st.g16_free = None
    t_w1 = {}
    t_w2 = {}

    def load_w1(j, fc):
        idx = (j * NFC + fc) % 3
        t_w1[(j, fc)] = sch.op("pool", lambda E, j=j, fc=fc, idx=idx: E.dma_start(
            out=Fb.w1_sb[:, idx * DM:(idx + 1) * DM].rearrange("p (c q) -> p c q", c=NDC),
            in_=D.w1_sl[j, fc]),
            waits=tmax(st.w1free[idx]), inc=("S1A", "S1B", "S1C")[idx])

    def load_w2(j, dcp):
        idx = (j * NDC + dcp) % 2
        t_w2[(j, dcp)] = sch.op("pool", lambda E, j=j, dcp=dcp, idx=idx: E.dma_start(
            out=Fb.w2_sb[:, idx * NFC * 128:(idx + 1) * NFC * 128].rearrange("p (c q) -> p c q", c=NFC),
            in_=D.w2_sl[j, dcp]),
            waits=tmax(st.w2free[idx]), inc=("S2A", "S2B")[idx])

    for f in range(NTF):
        tf0 = f * TF
        load_w1(0, 0)
        load_w1(0, 1)
        load_w1(0, 2)
        load_w2(0, 0)
        load_w2(0, 1)
        t_st3 = None
        for dc in range(NDC):
            b = dc % 2
            t_l = sch.op("sp", lambda E, dc=dc, b=b, tf0=tf0: E.dma_start(
                out=Fb.y1_scr[:, b * TF:(b + 1) * TF],
                in_=D.y1_sp[dc * 128:(dc + 1) * 128, tf0:tf0 + TF]),
                waits=tmax(st.y1_free[b], *t_spill_all), inc=("SYA" if b == 0 else "SYB"))
            sch.op("act", lambda E, b=b: E.copy(
                Fb.cast_f[:, b * TF:(b + 1) * TF], Fb.y1_scr[:, b * TF:(b + 1) * TF]),
                waits=tmax(t_l, st.cast_free[b]))
            t_cf = sch.op("act", lambda E, b=b: E.activation(
                Fb.sq_f[:], Fb.y1_scr[:, b * TF:(b + 1) * TF],
                AF.Square), waits=tmax(st.sqf_free), inc="ACT")
            st.y1_free[b] = t_cf

            def stats3(E, dc=dc, b=b):
                last = None
                for hh in range(2):
                    last = E.matmul(Fb.pHa[:, hh * 512:(hh + 1) * 512], C.ones_bf[:],
                                    Fb.cast_f[:, b * TF + hh * 512: b * TF + (hh + 1) * 512],
                                    start=(dc == 0), stop=(dc == NDC - 1))
                for hh in range(2):
                    last = E.matmul(Fb.pHb[:, hh * 512:(hh + 1) * 512], C.ones_bf[:],
                                    Fb.sq_f[:, hh * 512:(hh + 1) * 512],
                                    start=(dc == 0), stop=(dc == NDC - 1))
                return last
            t_st3 = sch.op("pe", stats3,
                           waits=tmax(t_cf,
                                      st.pH_free[0] if dc == 0 else None,
                                      st.pH_free[1] if dc == 0 else None),
                           inc="PE")
            st.cast_free[b] = t_st3
            st.sqf_free = t_st3
        t_mean3 = sch.op("act", lambda E: E.mul(Fb.mean_f[:], Fb.pHa[:], 1.0 / DM),
                         waits=tmax(t_st3, st.stat_free), inc="ACT")
        t_msq3 = sch.op("act", lambda E: E.activation(Fb.msqv_f[:], Fb.mean_f[:], AF.Square),
                        inc="ACT")
        t_var3 = sch.op("dve", lambda E: E.scalar_tensor_tensor(
            Fb.msqv_f[:], Fb.pHb[:], 1.0 / DM, Fb.msqv_f[:], ALU.mult, ALU.subtract),
            waits=tmax(t_msq3, t_st3), inc="DVE")
        sch.op("act", lambda E: E.activation(
            Fb.rstd_f[:], Fb.msqv_f[:], AF.Ln, bias=C.eps5[:]),
            waits=tmax(t_var3))
        t_rstd3 = sch.op("act", lambda E: E.activation(
            Fb.rstd_f[:], Fb.rstd_f[:], AF.Exp, scale=-0.5), inc="ACT")
        st.pH_free = [t_mean3, t_var3]
        t_u = None
        t_u_pl = None
        for dc in range(NDC):
            b = dc % 2
            t_l = sch.op("sp", lambda E, dc=dc, b=b, tf0=tf0: E.dma_start(
                out=Fb.y1_scr[:, b * TF:(b + 1) * TF],
                in_=D.y1_sp[dc * 128:(dc + 1) * 128, tf0:tf0 + TF]),
                waits=tmax(st.y1_free[b], *t_spill_all), inc=("SYA" if b == 0 else "SYB"))
            if dc % 3 == 2:
                lnp = Fb.cast_f[:].bitcast(F32)
                sch.op("pool", lambda E, b=b, lnp=lnp: E.tensor_sub(
                    lnp, Fb.y1_scr[:, b * TF:(b + 1) * TF], Fb.mean_f[:]),
                    waits=tmax(t_l, t_rstd3, t_mean3, st.u_free, st.cast_free[0],
                               st.cast_free[1]))
                t_u_pl = sch.op("pool", lambda E, dc=dc, lnp=lnp: E.tensor_mul(
                    Fb.u_bf[:, dc * TF:(dc + 1) * TF], lnp, Fb.rstd_f[:]),
                    inc="PL")
                st.y1_free[b] = t_u_pl
            else:
                sch.op("dve", lambda E, b=b: E.tensor_sub(
                    Fb.ln_f[:], Fb.y1_scr[:, b * TF:(b + 1) * TF], Fb.mean_f[:]),
                    waits=tmax(t_l, t_rstd3, t_mean3, st.u_free))
                t_u = sch.op("dve", lambda E, dc=dc: E.tensor_mul(
                    Fb.u_bf[:, dc * TF:(dc + 1) * TF], Fb.ln_f[:], Fb.rstd_f[:]),
                    inc="DVE")
                st.y1_free[b] = t_u
        if debug:
            sch.op("pool", lambda E, tf0=tf0: E.dma_start(
                out=D.dbg["d_u"].rearrange("(c p) t -> p c t", p=128)[:, :, tf0:tf0 + TF],
                in_=Fb.u_bf[:].rearrange("p (c t) -> p c t", c=NDC)),
                waits=tmax(t_u), inc="LW")
        t_rmm = [None] * 8
        t_sg_prev = None
        t_msk_prev = None
        t_gmul_prev = None
        t_ln_prev = None
        t_final_prev = None
        t_gfin = [None] * 8
        for ts_ in range(8):
            def rmm(E, ts_=ts_):
                last = None
                for dc in range(NDC):
                    last = E.matmul(
                        Fb.pYa[:, ts_ * 64: ts_ * 64 + NRT],
                        Fb.u_bf[:, dc * TF + ts_ * 128: dc * TF + ts_ * 128 + 128],
                        C.cT_bf[:, dc * NRT:(dc + 1) * NRT],
                        start=(dc == 0), stop=(dc == NDC - 1))
                return last
            t_rmm[ts_] = sch.op("pe", rmm,
                                waits=tmax(t_u, t_u_pl, st.pY_free[0] if ts_ == 0 else None,
                                           ct["lw"]),
                                inc="PE")
        for ts_ in range(8):
            t_sg = sch.op("act", lambda E, ts_=ts_: E.activation(
                C.sc16[:], Fb.pYa[:, ts_ * 64: ts_ * 64 + NRT], AF.Sigmoid),
                waits=tmax(t_rmm[7], t_gmul_prev, t_msk_prev), inc="ACT")
            t_bias = sch.op("pool", lambda E: E.tensor_add(C.sc16[:], C.sc16[:], C.bias_bc_s[:]),
                            waits=tmax(t_sg, ct["lx"]), inc="PL")
            t_mx = sch.op("dve", lambda E: E.max(out=C.mx8[:], in_=C.sc16[:]),
                          waits=tmax(t_bias), inc="DVE")
            t_thr = sch.op("pool", lambda E: E.tensor_copy(C.thr1[:], C.mx8[:, 3:4]),
                           waits=tmax(t_mx), inc="PL")
            t_msk = sch.op("dve", lambda E: E.tensor_tensor(
                C.msk16[:], C.sc16[:], C.thr1[:].to_broadcast([128, NRT]), ALU.is_ge),
                waits=tmax(t_thr), inc="DVE")
            t_msk_prev = t_msk
            t_gmul = sch.op("pool", lambda E, ts_=ts_: E.tensor_mul(
                C.g_all[:, ts_ * NRT:(ts_ + 1) * NRT], C.sc16[:], C.msk16[:]),
                waits=tmax(t_msk, st.gall_free[ts_], t_final_prev), inc="PL")
            t_gmul_prev = t_gmul
            t_red = sch.op("dve", lambda E, ts_=ts_: E.reduce_sum(
                C.rs1[:], C.g_all[:, ts_ * NRT:(ts_ + 1) * NRT], AX.X),
                waits=tmax(t_gmul), inc="DVE")
            t_rsc = sch.op("pool", lambda E: E.tensor_copy(C.rsB[:], C.rs1[:]),
                           waits=tmax(t_red, t_ln_prev), inc="PL")
            t_eps = sch.op("dve", lambda E: E.tensor_scalar_max(C.rsB[:], C.rsB[:], 1e-8),
                           waits=tmax(t_rsc), inc="DVE")
            t_ln = sch.op("act", lambda E: E.activation(C.rsC[:], C.rsB[:], AF.Ln),
                          waits=tmax(t_eps, t_final_prev), inc="ACT")
            t_ln_prev = t_ln
            t_exp = sch.op("act", lambda E: E.activation(C.rsC[:], C.rsC[:], AF.Exp, scale=-1.0),
                           inc="ACT")
            t_final = sch.op("dve", lambda E, ts_=ts_: E.tensor_tensor(
                C.g_all[:, ts_ * NRT:(ts_ + 1) * NRT], C.g_all[:, ts_ * NRT:(ts_ + 1) * NRT],
                C.rsC[:].to_broadcast([128, NRT]), ALU.mult),
                waits=tmax(t_exp), inc="DVE")
            t_final_prev = t_final
            t_gfin[ts_] = t_final
            if debug:
                t_gd = sch.op("sp", lambda E, ts_=ts_, tf0=tf0: E.dma_start(
                    out=D.dbg["d_g"][tf0 + ts_ * 128: tf0 + (ts_ + 1) * 128, :],
                    in_=C.g_all[:, ts_ * NRT:(ts_ + 1) * NRT]),
                    waits=tmax(t_final), inc="LX")
        st.pY_free[0] = t_sg
        t_hm = None
        t_ym = None
        t_hev = [None] * NFC
        t_ac17 = [None] * NDC
        for j in range(NE):
            routed = j >= 2
            jr = j - 2
            t_ge = None
            if routed:
                def gmm(E, jr=jr):
                    last = None
                    for hh in range(2):
                        last = E.matmul(Fb.pHa[:, hh * 512:(hh + 1) * 512],
                                        C.sel_bf[:, jr * 128:(jr + 1) * 128],
                                        Fb.gT_bf[:, hh * 512:(hh + 1) * 512],
                                        start=True, stop=True)
                    return last
                t_gm = sch.op("pe", gmm, waits=tmax(t_gT, st.pH_free[0]), inc="PE")
                t_ge = sch.op("act", lambda E: E.copy(Fb.G_bf[:], Fb.pHa[:]),
                              waits=tmax(t_gm, st.G_free), inc="ACT")
                st.pH_free[0] = t_ge
            for fc in range(NFC):
                pH = Fb.pHa if fc % 2 == 0 else Fb.pHb
                pidx = fc % 2
                widx = (j * NFC + fc) % 3

                def hmm(E, pH=pH, widx=widx):
                    last = None
                    for dc in range(NDC):
                        for hh in range(2):
                            last = E.matmul(
                                pH[:, hh * 512:(hh + 1) * 512],
                                Fb.w1_sb[:, widx * DM + dc * 128: widx * DM + (dc + 1) * 128],
                                Fb.u_bf[:, dc * TF + hh * 512: dc * TF + (hh + 1) * 512],
                                start=(dc == 0), stop=(dc == NDC - 1))
                    return last
                t_hm = sch.op("pe", hmm,
                              waits=tmax(t_w1[(j, fc)], t_u, t_u_pl, st.pH_free[pidx]), inc="PE")
                st.w1free[widx] = t_hm
                nj, nfc = (j, fc + 3) if fc + 3 < NFC else (j + 1, fc + 3 - NFC)
                if nj < NE:
                    load_w1(nj, nfc)
                if j == 0 and fc < 8:
                    ts_ = fc
                    t_gtr = sch.op("pe", lambda E, ts_=ts_: E.transpose(
                        Fb.pYb[0:NRT, ts_ * 128:(ts_ + 1) * 128],
                        C.g_all[:, ts_ * NRT:(ts_ + 1) * NRT], C.id_f[:]),
                        waits=tmax(t_gfin[ts_], st.pY_free[1] if ts_ == 0 else None),
                        inc="PE")
                    st.gall_free[ts_] = t_gtr
                    if ts_ == 7:
                        t_gT = sch.op("act", lambda E: E.copy(Fb.gT_bf[:], Fb.pYb[0:NRT, :]),
                                      waits=tmax(t_gtr, st.gT_free), inc="ACT")
                        st.pY_free[1] = t_gT
                t_he = sch.op("act", lambda E, fc=fc, pH=pH, j=j: E.activation(
                    Fb.h_bf[:, fc * TF:(fc + 1) * TF], pH[:], AF.Relu,
                    bias=C.b1c_s[:, j * NFC + fc: j * NFC + fc + 1]),
                    waits=tmax(t_hm, st.h_free[fc], ct["lx"]), inc="ACT")
                st.pH_free[pidx] = t_he
                t_hev[fc] = t_he
                if routed:
                    t_hg = sch.op("dve", lambda E, fc=fc: E.tensor_mul(
                        Fb.h_bf[:, fc * TF:(fc + 1) * TF],
                        Fb.h_bf[:, fc * TF:(fc + 1) * TF], Fb.G_bf[:]),
                        waits=tmax(t_he, t_ge), inc="DVE")
                    t_hev[fc] = t_hg
            if routed:
                st.G_free = t_hev[NFC - 1]
            if debug and j == 0 and f == 0:
                sch.op("pool", lambda E: E.dma_start(
                    out=D.dbg["d_h0"].rearrange("(c p) t -> p c t", p=128),
                    in_=Fb.h_bf[:].rearrange("p (c t) -> p c t", c=NFC)),
                    waits=tmax(t_hev[NFC - 1]), inc="LW")
            for dcp in range(NDC):
                pY = Fb.pYa if dcp % 2 == 0 else Fb.pYb
                pidx = dcp % 2
                widx = (j * NDC + dcp) % 2

                def ymm(E, dcp=dcp, pY=pY, widx=widx, j=j):
                    last = None
                    islast = (j == NE - 1)
                    for fc in range(NFC):
                        for hh in range(2):
                            last = E.matmul(
                                pY[:, hh * 512:(hh + 1) * 512],
                                Fb.w2_sb[:, widx * NFC * 128 + fc * 128: widx * NFC * 128 + (fc + 1) * 128],
                                Fb.h_bf[:, fc * TF + hh * 512: fc * TF + (hh + 1) * 512],
                                start=(fc == 0), stop=(fc == NFC - 1 and not islast))
                    if islast:
                        for hh in range(2):
                            last = E.matmul(
                                pY[:, hh * 512:(hh + 1) * 512],
                                C.b2r_bf[:, dcp * 128:(dcp + 1) * 128],
                                Fb.gT_bf[:, hh * 512:(hh + 1) * 512],
                                start=False, stop=True)
                    return last
                t_ym = sch.op("pe", ymm,
                              waits=tmax(t_hev[NFC - 1], t_w2[(j, dcp)], st.pY_free[pidx]),
                              inc="PE")
                st.w2free[widx] = t_ym
                nj, ndcp = (j, dcp + 2) if dcp + 2 < NDC else (j + 1, dcp + 2 - NDC)
                if nj < NE:
                    load_w2(nj, ndcp)
                if j == 0:
                    def acc0(E, dcp=dcp, pY=pY):
                        E.tensor_tensor(Fb.acc[:, dcp * TF:(dcp + 1) * TF], pY[:],
                                        C.b2sh_s[:, dcp:dcp + 1].to_broadcast([128, TF]),
                                        ALU.add)
                        return E.tensor_add(Fb.acc[:, dcp * TF:(dcp + 1) * TF],
                                            Fb.acc[:, dcp * TF:(dcp + 1) * TF],
                                            Fb.u_bf[:, dcp * TF:(dcp + 1) * TF])
                    t_ac = sch.op("dve", acc0,
                                  waits=tmax(t_ym, ct["b2sh"], st.acc_free), inc="DVE")
                else:
                    t_ac = sch.op("dve", lambda E, dcp=dcp, pY=pY: E.tensor_add(
                        Fb.acc[:, dcp * TF:(dcp + 1) * TF],
                        Fb.acc[:, dcp * TF:(dcp + 1) * TF], pY[:]),
                        waits=tmax(t_ym), inc="DVE")
                st.pY_free[pidx] = t_ac
                if j == NE - 1:
                    t_ac17[dcp] = t_ac
            for fc in range(NFC):
                st.h_free[fc] = t_ym
        st.u_free = t_hm
        st.gT_free = t_ym
        t_acc_all = t_ac
        t_st4 = None
        for dc in range(NDC):
            b = dc % 2
            sch.op("act", lambda E, dc=dc, b=b: E.copy(
                Fb.cast_f[:, b * TF:(b + 1) * TF], Fb.acc[:, dc * TF:(dc + 1) * TF]),
                waits=tmax(t_ac17[dc], st.cast_free[b], t_u_pl))
            t_c4 = sch.op("act", lambda E, dc=dc, b=b: E.activation(
                Fb.sq_f[:], Fb.acc[:, dc * TF:(dc + 1) * TF],
                AF.Square), waits=tmax(st.sqf_free), inc="ACT")

            def stats4(E, dc=dc, b=b):
                last = None
                for hh in range(2):
                    last = E.matmul(Fb.pHa[:, hh * 512:(hh + 1) * 512], C.ones_bf[:],
                                    Fb.cast_f[:, b * TF + hh * 512: b * TF + (hh + 1) * 512],
                                    start=(dc == 0), stop=(dc == NDC - 1))
                for hh in range(2):
                    last = E.matmul(Fb.pHb[:, hh * 512:(hh + 1) * 512], C.ones_bf[:],
                                    Fb.sq_f[:, hh * 512:(hh + 1) * 512],
                                    start=(dc == 0), stop=(dc == NDC - 1))
                return last
            t_st4 = sch.op("pe", stats4,
                           waits=tmax(t_c4,
                                      st.pH_free[0] if dc == 0 else None,
                                      st.pH_free[1] if dc == 0 else None),
                           inc="PE")
            st.cast_free[b] = t_st4
            st.sqf_free = t_st4
        t_mean4 = sch.op("act", lambda E: E.mul(Fb.mean_f[:], Fb.pHa[:], 1.0 / DM),
                         waits=tmax(t_st4, t_u, t_u_pl), inc="ACT")
        t_msq4 = sch.op("act", lambda E: E.activation(Fb.msqv_f[:], Fb.mean_f[:], AF.Square),
                        inc="ACT")
        t_var4 = sch.op("dve", lambda E: E.scalar_tensor_tensor(
            Fb.msqv_f[:], Fb.pHb[:], 1.0 / DM, Fb.msqv_f[:], ALU.mult, ALU.subtract),
            waits=tmax(t_msq4, t_st4), inc="DVE")
        sch.op("act", lambda E: E.activation(
            Fb.rstd_f[:], Fb.msqv_f[:], AF.Ln, bias=C.eps5[:]),
            waits=tmax(t_var4))
        t_rstd4 = sch.op("act", lambda E: E.activation(
            Fb.rstd_f[:], Fb.rstd_f[:], AF.Exp, scale=-0.5), inc="ACT")
        st.pH_free = [t_mean4, t_var4]
        st.stat_free = t_rstd4
        if debug:
            sch.op("sp", lambda E, tf0=tf0: E.dma_start(
                out=D.dbg["d_acc"].rearrange("(c p) t -> p c t", p=128)[:, :, tf0:tf0 + TF],
                in_=Fb.acc[:].rearrange("p (c t) -> p c t", c=NDC)),
                waits=tmax(t_acc_all), inc="LX")
        t_out = None
        for dc in range(NDC):
            b = dc % 2
            t_l = sch.op("sp", lambda E, dc=dc, b=b, tf0=tf0: E.dma_start(
                out=Fb.y1_scr[:, b * TF:(b + 1) * TF],
                in_=D.y1_sp[dc * 128:(dc + 1) * 128, tf0:tf0 + TF]),
                waits=tmax(st.y1_free[b], *t_spill_all), inc=("SYA" if b == 0 else "SYB"))
            sch.op("dve", lambda E, dc=dc: E.tensor_sub(
                Fb.ln_f[:], Fb.acc[:, dc * TF:(dc + 1) * TF], Fb.mean_f[:]),
                waits=tmax(t_rstd4, t_mean4))
            sch.op("dve", lambda E: E.tensor_mul(Fb.ln_f[:], Fb.ln_f[:], Fb.rstd_f[:]))
            t_out = sch.op("dve", lambda E, dc=dc, b=b: E.tensor_add(
                Fb.acc[:, dc * TF:(dc + 1) * TF], Fb.ln_f[:],
                Fb.y1_scr[:, b * TF:(b + 1) * TF]),
                waits=tmax(t_l), inc="DVE")
            st.y1_free[b] = t_out
        t_store = sch.op("sp", lambda E, tf0=tf0: E.dma_start(
            out=D.out_fm.rearrange("(c p) t -> p c t", p=128)[:, :, tf0:tf0 + TF],
            in_=Fb.acc[:].rearrange("p (c t) -> p c t", c=NDC)),
            waits=tmax(t_out), inc="ST")
        st.acc_free = t_store

    sch.join("sp", [("ST", sch.cnt["ST"])])



def _host_layout(inputs):
    x = np.asarray(inputs["L0_attn_x"], dtype=np.float32)
    wq = np.asarray(inputs["L0_attn_wq"], dtype=np.float32)
    wk = np.asarray(inputs["L0_attn_wk"], dtype=np.float32)
    wv = np.asarray(inputs["L0_attn_wv"], dtype=np.float32)
    wo = np.asarray(inputs["L0_attn_wo"], dtype=np.float32)
    gq = np.asarray(inputs["L0_attn_gamma_q"], dtype=np.float32)
    gk = np.asarray(inputs["L0_attn_gamma_k"], dtype=np.float32)
    sw1 = np.asarray(inputs["L0_ffn_shared_w1"], dtype=np.float32)
    sb1 = np.asarray(inputs["L0_ffn_shared_b1"], dtype=np.float32)
    sw2 = np.asarray(inputs["L0_ffn_shared_w2"], dtype=np.float32)
    sb2 = np.asarray(inputs["L0_ffn_shared_b2"], dtype=np.float32)
    rw1 = np.asarray(inputs["L0_ffn_routed_w1"], dtype=np.float32)
    rb1 = np.asarray(inputs["L0_ffn_routed_b1"], dtype=np.float32)
    rw2 = np.asarray(inputs["L0_ffn_routed_w2"], dtype=np.float32)
    rb2 = np.asarray(inputs["L0_ffn_routed_b2"], dtype=np.float32)
    cent = np.asarray(inputs["L0_ffn_centroids"], dtype=np.float32)
    bias = np.asarray(inputs["L0_ffn_bias"], dtype=np.float32)

    com = {}
    com["wq_sl"] = np.ascontiguousarray(
        wq.reshape(H, 128, NDC, 128).transpose(0, 3, 2, 1))
    com["wk_sl"] = np.ascontiguousarray(
        wk.reshape(HK, 128, NDC, 128).transpose(0, 3, 2, 1))
    com["wvT"] = np.ascontiguousarray(wv.T)
    com["wo_sl"] = np.ascontiguousarray(
        wo.reshape(NDC, 128, NDC, 128).transpose(0, 3, 2, 1))
    w1_all = np.concatenate([sw1, rw1], axis=0)
    com["w1_sl"] = np.ascontiguousarray(
        w1_all.reshape(NE, NDC, 128, NFC, 128).transpose(0, 3, 2, 1, 4))
    w2_all = np.concatenate([sw2, rw2], axis=0)
    com["w2_sl"] = np.ascontiguousarray(
        w2_all.reshape(NE, NFC, 128, NDC, 128).transpose(0, 3, 2, 1, 4))
    b1_all = np.concatenate([sb1, rb1], axis=0)
    com["b1c"] = np.ascontiguousarray(b1_all.reshape(NE, NFC, 128).transpose(2, 0, 1)
                                      .reshape(128, NE * NFC))
    com["b2c"] = np.ascontiguousarray(sb2.reshape(2, NDC, 128).transpose(2, 0, 1)
                                      .reshape(128, 2 * NDC))
    com["b2r"] = np.ascontiguousarray(rb2)
    com["cT"] = np.ascontiguousarray(cent.T)
    com["bias_bc"] = np.ascontiguousarray(np.tile(bias[None, :], (128, 1)))
    com["gq"] = np.ascontiguousarray(gq.reshape(128, 1))
    com["gk"] = np.ascontiguousarray(gk.reshape(128, 1))

    inv_freq = (1.0 / RBASE ** (np.arange(0, DH, 2, dtype=np.float64) / DH))
    ang = np.arange(T, dtype=np.float64)[:, None] * inv_freq[None, :]
    cos = np.cos(ang)
    sin = np.sin(ang)
    com["cos_t"] = np.ascontiguousarray(
        np.concatenate([cos, cos], axis=1).T.astype(np.float32))
    com["sin_t"] = np.ascontiguousarray(
        np.concatenate([-sin, sin], axis=1).T.astype(np.float32))
    com["ones_c"] = np.ones((128, 128), np.float32)
    com["ident_c"] = np.eye(128, dtype=np.float32)
    il = np.arange(128)[:, None]
    jl = np.arange(128)[None, :]
    com["mcaus"] = np.where(jl <= il, 0.0, -1e9).astype(np.float32)
    com["medge"] = np.where(jl >= il, 0.0, -1e9).astype(np.float32)
    sel = np.zeros((NRT, NRT * 128), np.float32)
    for j in range(NRT):
        sel[j, j * 128:(j + 1) * 128] = 1.0
    com["sel_c"] = sel
    com["identu_c"] = np.eye(128, dtype=np.uint8)
    kk = np.arange(128)[:, None]
    mm = np.arange(128)[None, :]
    com["p64_c"] = (np.abs(kk - mm) == 64).astype(np.float32)

    in_maps = []
    for b in range(8):
        m = dict(com)
        m["x_fm"] = np.ascontiguousarray(x[b].T)
        in_maps.append(m)
    return in_maps


_NC_CACHE = {}


def kernel(**inputs):
    in_maps = _host_layout(inputs)
    key = False
    if key not in _NC_CACHE:
        _NC_CACHE[key] = build_program(debug=False)
    nc = _NC_CACHE[key]
    res = run_bass_kernel_spmd(nc, in_maps, core_ids=list(range(8)))
    out = np.stack([np.asarray(r["out_fm"]).T for r in res.results], axis=0)
    return np.ascontiguousarray(out.astype(np.float32))


# version 20
# speedup vs baseline: 1.0660x; 1.0166x over previous
import sys
import numpy as np

try:
    import concourse.bass as bass
except ImportError:
    sys.path.insert(0, "/opt/trn_rl_repo")
    import concourse.bass as bass
import concourse.mybir as mybir
from concourse.bass_utils import run_bass_kernel_spmd
from contextlib import ExitStack

F32 = mybir.dt.float32
BF16 = mybir.dt.bfloat16
AF = mybir.ActivationFunctionType
ALU = mybir.AluOpType
AX = mybir.AxisListType

T = 2048
DM = 2048
NDC = 16
H = 16
HK = 4
GQ = 4
DH = 128
DFF = 1408
NFC = 11
NE = 18
NRT = 16
WINDOW = 1024
EPS_QK = 1e-6
EPS_LN = 1e-5
RBASE = 10000.0
TA = 512
NTI = T // TA
TF = 1024
NTF = T // TF
INV_SQRT_DH = float(1.0 / np.sqrt(DH))

DMA_SEMS = ("LW", "LX", "ST", "SX", "SKV", "SQA", "SQB", "SOA", "SOB", "S1A", "S1B", "S1C", "S2A", "S2B", "S2C", "SYA", "SYB", "SXB", "STA", "STB", "STC", "STD")
OWN_SEM = {"pe": "PE", "act": "ACT", "dve": "DVE", "pool": "PL", "sp": None}


class Sched:
    def __init__(self):
        self.cur = None

    def start(self, engname, eng, sems):
        self.cur = engname
        self.eng = eng
        self.sems = sems
        self.cnt = {k: 0 for k in sems}
        self.lastwait = {k: 0 for k in sems}

    def op(self, engname, fn, waits=(), inc=None):
        tok = None
        if inc is not None:
            self.cnt[inc] += 16 if inc in DMA_SEMS else 1
            tok = (inc, self.cnt[inc])
        if engname == self.cur:
            own = OWN_SEM.get(engname)
            for w in waits:
                if w is None:
                    continue
                s, v = w
                if s == own or v <= self.lastwait[s]:
                    continue
                self.eng.wait_ge(self.sems[s], v)
                self.lastwait[s] = v
            ins = fn(self.eng)
            if inc is not None:
                ins.then_inc(self.sems[inc], 16 if inc in DMA_SEMS else 1)
        return tok

    def join(self, engname, waits):
        if engname == self.cur:
            own = OWN_SEM.get(engname)
            for w in waits:
                if w is None:
                    continue
                s, v = w
                if s == own or v <= self.lastwait[s]:
                    continue
                self.eng.wait_ge(self.sems[s], v)
                self.lastwait[s] = v


def tmax(*toks):
    best = {}
    for t in toks:
        if t is None:
            continue
        s, v = t
        if v > best.get(s, -1):
            best[s] = v
    return [(s, v) for s, v in best.items()]


class NS:
    pass


def build_program(debug=False):
    nc = bass.Bass("TRN2", target_bir_lowering=False)

    D = NS()
    D.x_fm = nc.dram_tensor("x_fm", [DM, T], F32, kind="ExternalInput").ap()
    D.wq_sl = nc.dram_tensor("wq_sl", [H, 128, NDC, 128], F32, kind="ExternalInput").ap()
    D.wk_sl = nc.dram_tensor("wk_sl", [HK, 128, NDC, 128], F32, kind="ExternalInput").ap()
    D.wvT = nc.dram_tensor("wvT", [DM, HK * DH], F32, kind="ExternalInput").ap()
    D.wo_sl = nc.dram_tensor("wo_sl", [NDC, 128, NDC, 128], F32, kind="ExternalInput").ap()
    D.w1_sl = nc.dram_tensor("w1_sl", [NE, NFC, 128, NDC, 128], F32, kind="ExternalInput").ap()
    D.w2_sl = nc.dram_tensor("w2_sl", [NE, NDC, 128, NFC, 128], F32, kind="ExternalInput").ap()
    D.b1c = nc.dram_tensor("b1c", [128, NE * NFC], F32, kind="ExternalInput").ap()
    D.b2c = nc.dram_tensor("b2c", [128, 2 * NDC], F32, kind="ExternalInput").ap()
    D.b2r = nc.dram_tensor("b2r", [NRT, DM], F32, kind="ExternalInput").ap()
    D.cT = nc.dram_tensor("cT", [DM, NRT], F32, kind="ExternalInput").ap()
    D.bias_bc = nc.dram_tensor("bias_bc", [128, NRT], F32, kind="ExternalInput").ap()
    D.gq = nc.dram_tensor("gq", [128, 1], F32, kind="ExternalInput").ap()
    D.gk = nc.dram_tensor("gk", [128, 1], F32, kind="ExternalInput").ap()
    D.cos_t = nc.dram_tensor("cos_t", [128, T], F32, kind="ExternalInput").ap()
    D.sin_t = nc.dram_tensor("sin_t", [128, T], F32, kind="ExternalInput").ap()
    D.ones_c = nc.dram_tensor("ones_c", [128, 128], F32, kind="ExternalInput").ap()
    D.ident_c = nc.dram_tensor("ident_c", [128, 128], F32, kind="ExternalInput").ap()
    D.mcaus = nc.dram_tensor("mcaus", [128, 128], F32, kind="ExternalInput").ap()
    D.medge = nc.dram_tensor("medge", [128, 128], F32, kind="ExternalInput").ap()
    D.sel_c = nc.dram_tensor("sel_c", [NRT, NRT * 128], F32, kind="ExternalInput").ap()
    D.p64_c = nc.dram_tensor("p64_c", [128, 128], F32, kind="ExternalInput").ap()
    D.identu_c = nc.dram_tensor("identu_c", [128, 128], mybir.dt.uint8, kind="ExternalInput").ap()
    D.y1_sp = nc.dram_tensor("y1_spill", [DM, T], F32, kind="Internal").ap()
    D.out_fm = nc.dram_tensor("out_fm", [DM, T], F32, kind="ExternalOutput").ap()
    D.dbg = {}
    if debug:
        for nm, shape in (("d_lnx", (DM, T)), ("d_q", (DH, T)), ("d_k", (HK * DH, T)),
                          ("d_v", (T, HK * DH)), ("d_att", (DM, T)), ("d_y1", (DM, T)),
                          ("d_u", (DM, T)), ("d_g", (T, NRT)), ("d_h0", (DFF, TF)),
                          ("d_y0", (DH, T)),
                          ("d_acc", (DM, T))):
            D.dbg[nm] = nc.dram_tensor(nm, list(shape), F32, kind="ExternalOutput").ap()

    with ExitStack() as ctx:
        e = ctx.enter_context
        C = NS()
        C.cos_s = e(nc.sbuf_tensor("cos_s", [128, T], BF16))
        C.sin_s = e(nc.sbuf_tensor("sin_s", [128, T], BF16))
        C.mcaus_bf = e(nc.sbuf_tensor("mcaus_bf", [128, 128], BF16))
        C.medge_bf = e(nc.sbuf_tensor("medge_bf", [128, 128], BF16))
        C.id_bf = e(nc.sbuf_tensor("id_bf", [128, 128], BF16))
        C.id_f = e(nc.sbuf_tensor("id_f", [128, 128], F32))
        C.ones_bf = e(nc.sbuf_tensor("ones_bf", [128, 128], BF16))
        C.p64_bf = e(nc.sbuf_tensor("p64_bf", [128, 128], BF16))
        C.gq_s = e(nc.sbuf_tensor("gq_s", [128, 1], F32))
        C.gk_s = e(nc.sbuf_tensor("gk_s", [128, 1], F32))
        C.eps6 = e(nc.sbuf_tensor("eps6", [128, 1], F32))
        C.eps5 = e(nc.sbuf_tensor("eps5", [128, 1], F32))
        C.b1c_s = e(nc.sbuf_tensor("b1c_s", [128, NE * NFC], F32))
        C.b2c_s = e(nc.sbuf_tensor("b2c_s", [128, 2 * NDC], F32))
        C.b2sh_s = e(nc.sbuf_tensor("b2sh_s", [128, NDC], F32))
        C.b2r_bf = e(nc.sbuf_tensor("b2r_bf", [NRT, DM], BF16))
        C.sel_bf = e(nc.sbuf_tensor("sel_bf", [NRT, NRT * 128], BF16))
        C.cT_bf = e(nc.sbuf_tensor("cT_bf", [128, NDC * NRT], BF16))
        C.bias_bc_s = e(nc.sbuf_tensor("bias_bc_s", [128, NRT], F32))
        C.nmax = e(nc.sbuf_tensor("nmax", [128, 1], F32))
        C.sums = e(nc.sbuf_tensor("sums", [128, 1], F32))
        C.lnsum = e(nc.sbuf_tensor("lnsum", [128, 1], F32))
        C.bias2 = e(nc.sbuf_tensor("bias2", [128, 1], F32))
        C.sc16 = e(nc.sbuf_tensor("sc16", [128, NRT], F32))
        C.mx8 = e(nc.sbuf_tensor("mx8", [128, 8], F32))
        C.msk16 = e(nc.sbuf_tensor("msk16", [128, NRT], F32))
        C.g16 = e(nc.sbuf_tensor("g16", [128, NRT], F32))
        C.g_all = e(nc.sbuf_tensor("g_all", [128, 8 * NRT], F32))
        C.rs1 = e(nc.sbuf_tensor("rs1", [128, 1], F32))
        C.thr1 = e(nc.sbuf_tensor("thr1", [128, 1], F32))
        C.rsB = e(nc.sbuf_tensor("rsB", [128, 1], F32))
        C.rsC = e(nc.sbuf_tensor("rsC", [128, 1], F32))
        C.rsb_bf = e(nc.sbuf_tensor("rsb_bf", [128, 2], BF16))
        C.rs2 = e(nc.sbuf_tensor("rs2", [128, 2], F32))
        C.negC = e(nc.sbuf_tensor("negC", [128, 1], F32))
        C.mqA = e(nc.sbuf_tensor("mqA", [128, 1], F32))
        C.gbc_bf = e(nc.sbuf_tensor("gbc_bf", [128, 128], BF16))
        C.eps35 = e(nc.sbuf_tensor("eps35", [128, 1], F32))
        C.diag_bf = e(nc.sbuf_tensor("diag_bf", [128, 2 * 128], BF16))
        C.id_u8 = e(nc.sbuf_tensor("id_u8", [128, 128], mybir.dt.uint8))

        sems = {}
        for s in ("LW", "LX", "ST", "PE", "ACT", "DVE", "PL", "SX", "SKV", "SQA", "SQB", "SOA", "SOB", "S1A", "S1B", "S1C", "S2A", "S2B", "S2C", "SYA", "SYB", "SXB", "STA", "STB", "STC", "STD"):
            sems[s] = e(nc.semaphore(s))

        block = e(nc.Block())
        sch = Sched()
        state = {}

        with ExitStack() as actx:
            ae = actx.enter_context
            A = NS()
            A.x_bf = ae(nc.sbuf_tensor("x_bf", [128, 2 * NDC * TA], BF16))
            A.sq_scr = ae(nc.sbuf_tensor("sq_scr", [128, 2 * TA], BF16))
            A.y1s = ae(nc.sbuf_tensor("y1s", [128, 4 * TA], F32))
            A.mean2_s = ae(nc.sbuf_tensor("mean2_s", [128, TA], F32))
            A.msqv2_s = ae(nc.sbuf_tensor("msqv2_s", [128, TA], F32))
            A.rstd2_s = ae(nc.sbuf_tensor("rstd2_s", [128, TA], F32))
            A.ln_bf = ae(nc.sbuf_tensor("ln_bf", [128, NDC * TA], BF16))
            A.ln_scr = ae(nc.sbuf_tensor("ln_scr", [128, TA], F32))
            A.mean_s = ae(nc.sbuf_tensor("mean_s", [128, TA], F32))
            A.msqv_s = ae(nc.sbuf_tensor("msqv_s", [128, TA], F32))
            A.rstd_s = ae(nc.sbuf_tensor("rstd_s", [128, TA], F32))
            A.k_full = ae(nc.sbuf_tensor("k_full", [128, HK * T], BF16))
            A.v_full = ae(nc.sbuf_tensor("v_full", [128, NDC * TA], BF16))
            A.q_ro = ae(nc.sbuf_tensor("q_ro", [128, TA], F32))
            A.q_fp = ae(nc.sbuf_tensor("q_fp", [128, 2 * TA], BF16))
            A.q_sq = ae(nc.sbuf_tensor("q_sq", [128, TA], BF16))
            A.rn_s = ae(nc.sbuf_tensor("rn_s", [128, TA], F32))
            A.q_bf = ae(nc.sbuf_tensor("q_bf", [128, TA], BF16))
            A.p_bf = ae(nc.sbuf_tensor("p_bf", [128, 2 * 1152], BF16))
            A.pT_sb = ae(nc.sbuf_tensor("pT_sb", [128, 1152], BF16))
            A.y_att = ae(nc.sbuf_tensor("y_att", [128, H * TA], BF16))
            A.att_bf = ae(nc.sbuf_tensor("att_bf", [128, NDC * TA], BF16))
            A.wbig = ae(nc.sbuf_tensor("wbig", [128, 4 * DM], BF16))
            A.wq_sb = ae(nc.sbuf_tensor("wq_sb", [128, 2 * DM], BF16))
            A.wo_sb = ae(nc.sbuf_tensor("wo_sb", [128, 2 * DM], BF16))
            A.pA = ae(nc.psum_tensor("pA", [128, 512], F32))
            A.pB = ae(nc.psum_tensor("pB", [128, 512], F32))
            A.pQ = ae(nc.psum_tensor("pQ", [128, 512], F32))
            A.pS = ae(nc.psum_tensor("pS", [128, 1536], F32))
            A.pTT = ae(nc.psum_tensor("pTT", [128, 2048], BF16))

            def attn_pass(engname, eng):
                sch.start(engname, eng, sems)
                ct = _sched_consts(sch, nc, D, C)
                af = _sched_attn(sch, nc, D, C, A, ct, debug)
                state[engname] = (ct, af, dict(sch.cnt), dict(sch.lastwait))

            block.tensor(lambda eng: attn_pass("pe", eng))
            block.scalar(lambda eng: attn_pass("act", eng))
            block.vector(lambda eng: attn_pass("dve", eng))
            block.gpsimd(lambda eng: attn_pass("pool", eng))
            block.sync(lambda eng: attn_pass("sp", eng))

        with ExitStack() as fctx:
            fe = fctx.enter_context
            Fb = NS()
            Fb.u_bf = fe(nc.sbuf_tensor("u_bf", [128, NDC * TF], BF16))
            Fb.acc = fe(nc.sbuf_tensor("acc", [128, NDC * TF], F32))
            Fb.h_bf = fe(nc.sbuf_tensor("h_bf", [128, NFC * TF], BF16))
            Fb.G_bf = fe(nc.sbuf_tensor("G_bf", [128, TF], BF16))
            Fb.gT_bf = fe(nc.sbuf_tensor("gT_bf", [NRT, TF], BF16))
            Fb.y1_scr = fe(nc.sbuf_tensor("y1_scr", [128, 2 * TF], F32))
            Fb.cast_f = fe(nc.sbuf_tensor("cast_f", [128, 2 * TF], BF16))
            Fb.sq_f = fe(nc.sbuf_tensor("sq_f", [128, TF], BF16))
            Fb.ln_f = fe(nc.sbuf_tensor("ln_f", [128, TF], F32))
            Fb.mean_f = fe(nc.sbuf_tensor("mean_f", [128, TF], F32))
            Fb.msqv_f = fe(nc.sbuf_tensor("msqv_f", [128, TF], F32))
            Fb.rstd_f = fe(nc.sbuf_tensor("rstd_f", [128, TF], F32))
            Fb.w1_sb = fe(nc.sbuf_tensor("w1_sb", [128, 3 * NDC * 128], BF16))
            Fb.w2_sb = fe(nc.sbuf_tensor("w2_sb", [128, 2 * NFC * 128], BF16))
            Fb.pHa = fe(nc.psum_tensor("pHa", [128, TF], F32))
            Fb.pHb = fe(nc.psum_tensor("pHb", [128, TF], F32))
            Fb.pYa = fe(nc.psum_tensor("pYa", [128, TF], F32))
            Fb.pYb = fe(nc.psum_tensor("pYb", [128, TF], F32))

            def ffn_pass(engname, eng):
                ct, af, cnt, lastwait = state[engname]
                sch.start(engname, eng, sems)
                sch.cnt = cnt
                sch.lastwait = lastwait
                _sched_ffn(sch, nc, D, C, Fb, ct, af, debug)

            block.tensor(lambda eng: ffn_pass("pe", eng))
            block.scalar(lambda eng: ffn_pass("act", eng))
            block.vector(lambda eng: ffn_pass("dve", eng))
            block.gpsimd(lambda eng: ffn_pass("pool", eng))
            block.sync(lambda eng: ffn_pass("sp", eng))

    return nc


def _sched_consts(sch, nc, D, C):
    ct = {}
    t = None
    for dst, src in ((C.gq_s, D.gq), (C.gk_s, D.gk),
                     (C.b1c_s, D.b1c), (C.bias_bc_s, D.bias_bc), (C.id_f, D.ident_c)):
        t = sch.op("sp", lambda E, d=dst, s=src: E.dma_start(out=d[:], in_=s), inc="LX")
    sch.op("sp", lambda E: E.dma_start(out=C.id_u8[:], in_=D.identu_c), inc="LX")
    t = sch.op("sp", lambda E: E.dma_start(
        out=C.b2c_s[:].rearrange("p (n c) -> p n c", n=2),
        in_=D.b2c.rearrange("p (n c) -> p n c", n=2)), inc="LX")
    ct["lx"] = t
    t = None
    for dst, src in ((C.ones_bf, D.ones_c), (C.id_bf, D.ident_c), (C.b2r_bf, D.b2r),
                     (C.sel_bf, D.sel_c), (C.cos_s, D.cos_t), (C.sin_s, D.sin_t),
                     (C.mcaus_bf, D.mcaus), (C.medge_bf, D.medge), (C.p64_bf, D.p64_c)):
        t = sch.op("pool", lambda E, d=dst, s=src: E.dma_start(out=d[:], in_=s), inc="LW")
    t = sch.op("pool", lambda E: E.dma_start(
        out=C.cT_bf[:].rearrange("p (c j) -> p c j", c=NDC),
        in_=D.cT.rearrange("(c p) j -> p c j", p=128)), inc="LW")
    ct["lw"] = t
    sch.op("dve", lambda E: E.memset(C.eps6[:], EPS_QK))
    sch.op("dve", lambda E: E.memset(C.diag_bf[:], 0.0))
    sch.op("dve", lambda E: E.memset(C.eps35[:], 1e-35))
    ct["eps"] = sch.op("dve", lambda E: E.memset(C.eps5[:], EPS_LN), inc="DVE")
    ct["b2sh"] = sch.op("dve", lambda E: E.tensor_add(
        C.b2sh_s[:], C.b2c_s[:, 0:NDC], C.b2c_s[:, NDC:2 * NDC]),
        waits=[ct["lx"]], inc="DVE")
    return ct


def _sched_attn(sch, nc, D, C, A, ct, debug):
    st = NS()
    st.wbig_free = None
    st.wq_free = [None, None]
    st.wo_free = [None, None]
    st.scr_free = [None, None]
    st.pA_free = ct["eps"]
    st.pB_free = None
    st.pQ_free = None
    st.pS_free = None
    st.pTT_free = None
    st.pbf_free = None
    st.pT_free = None
    st.lnbf_free = None
    st.yatt_free = None
    st.attbf_free = []
    st.stat_free = []
    st.qfp_free = [None, None]
    st.xbf_free = [[], []]
    st.xld = [None, None]
    st.y1s_free = []
    st.qro_free = None
    st.proj_tok = None
    st.diag_free = [None, None]
    st.tr_free = [None, None]
    st.pbf_par_free = [None, None]
    st.vts = [None] * NDC
    t_spill = None

    def rope_start(h, par):
        t_ev = sch.op("act", lambda E, par=par: E.copy(
            A.q_fp[:, par * TA:(par + 1) * TA], A.pQ[:]),
            waits=tmax(st.proj_tok, st.qfp_free[par]), inc="ACT")
        st.pQ_free = t_ev
        return t_ev

    def rope_finish(t_ev, par, gamma_s, scale2, dst_fn, ti0):
        t_sw = sch.op("pe", lambda E, par=par: E.matmul(
            A.pA[:], C.p64_bf[:], A.q_fp[:, par * TA:(par + 1) * TA],
            start=True, stop=True),
            waits=tmax(t_ev, st.pA_free), inc="PE")
        sch.op("dve", lambda E, par=par: E.tensor_mul(
            A.ln_scr[:], A.q_fp[:, par * TA:(par + 1) * TA], C.cos_s[:, ti0:ti0 + TA]),
            waits=tmax(t_ev))
        sch.op("dve", lambda E: E.tensor_mul(
            A.q_ro[:], A.pA[:], C.sin_s[:, ti0:ti0 + TA]), waits=tmax(t_sw))
        t_rope = sch.op("dve", lambda E: E.tensor_add(
            A.q_ro[:], A.q_ro[:], A.ln_scr[:]), inc="DVE")
        st.pA_free = t_rope
        st.qfp_free[par] = t_rope
        t_sq = sch.op("dve", lambda E: E.tensor_mul(A.q_sq[:], A.q_ro[:], A.q_ro[:]),
                      inc="DVE")
        t_ss = sch.op("pe", lambda E: E.matmul(
            A.pB[:], C.ones_bf[:], A.q_sq[:], start=True, stop=True),
            waits=tmax(t_sq, st.pB_free), inc="PE")
        t_s1 = sch.op("act", lambda E: E.activation(A.rn_s[:], A.pB[:], AF.Sqrt),
                      waits=tmax(t_ss), inc="ACT")
        st.pB_free = t_s1
        sch.op("act", lambda E: E.activation(A.rn_s[:], A.rn_s[:], AF.Ln, bias=C.eps6[:]))
        t_rn = sch.op("act", lambda E: E.activation(A.rn_s[:], A.rn_s[:], AF.Exp, scale=-1.0),
                      inc="ACT")
        sch.op("dve", lambda E: E.tensor_mul(A.ln_scr[:], A.q_ro[:], A.rn_s[:]),
               waits=tmax(t_rn))
        if scale2 is None:
            t_fin = sch.op("dve", lambda E: E.tensor_scalar(
                dst_fn(), A.ln_scr[:], gamma_s[:], None, ALU.mult), inc="DVE")
        else:
            t_fin = sch.op("dve", lambda E: E.tensor_scalar(
                dst_fn(), A.ln_scr[:], gamma_s[:], scale2, ALU.mult, ALU.mult),
                inc="DVE")
        st.qro_free = t_fin
        return t_fin

    t_b1 = sch.op("dve", lambda E: E.tensor_copy(C.gbc_bf[:], C.gq_s[:].to_broadcast([128, 128])),
                  waits=tmax(ct["lx"]), inc="DVE")
    t_t1 = sch.op("pe", lambda E: E.transpose(A.pTT[:, 0:128], C.gbc_bf[:], C.id_bf[:]),
                  waits=tmax(t_b1, ct["lw"]), inc="PE")
    t_ev1 = sch.op("act", lambda E: E.copy(A.ln_scr[:, 0:128], A.pTT[:, 0:128]),
                   waits=tmax(t_t1), inc="ACT")
    t_m1 = sch.op("dve", lambda E: E.reduce_max(C.mqA[:], A.ln_scr[:, 0:128], AX.X,
                                                apply_absolute_value=True),
                  waits=tmax(t_ev1), inc="DVE")
    t_b2 = sch.op("dve", lambda E: E.tensor_copy(C.gbc_bf[:], C.gk_s[:].to_broadcast([128, 128])),
                  waits=tmax(t_t1), inc="DVE")
    t_t2 = sch.op("pe", lambda E: E.transpose(A.pTT[:, 128:256], C.gbc_bf[:], C.id_bf[:]),
                  waits=tmax(t_b2, t_ev1), inc="PE")
    t_ev2 = sch.op("act", lambda E: E.copy(A.ln_scr[:, 128:256], A.pTT[:, 128:256]),
                   waits=tmax(t_t2), inc="ACT")
    sch.op("dve", lambda E: E.reduce_max(C.negC[:], A.ln_scr[:, 128:256], AX.X,
                                         apply_absolute_value=True),
           waits=tmax(t_ev2))
    sch.op("dve", lambda E: E.tensor_mul(C.negC[:], C.negC[:], C.mqA[:]))
    t_negC = sch.op("dve", lambda E: E.tensor_scalar(
        C.negC[:], C.negC[:], -INV_SQRT_DH, None, ALU.mult), inc="DVE")
    t_m2 = t_negC
    st.pTT_free = t_ev2
    st.stat_free = [t_m2]

    for i in range(NTI):
        ti0 = i * TA
        par = i % 2
        if i == 0:
            st.xld[0] = sch.op("pool", lambda E: E.dma_start(
                out=A.x_bf[:, 0:NDC * TA].rearrange("p (c t) -> p c t", c=NDC),
                in_=D.x_fm.rearrange("(c p) t -> p c t", p=128)[:, :, 0:TA]),
                inc="SX")
        if i + 1 < NTI:
            pnxt = (i + 1) % 2
            st.xld[pnxt] = sch.op("pool", lambda E, i=i, pnxt=pnxt: E.dma_start(
                out=A.x_bf[:, pnxt * NDC * TA:(pnxt + 1) * NDC * TA].rearrange("p (c t) -> p c t", c=NDC),
                in_=D.x_fm.rearrange("(c p) t -> p c t", p=128)[:, :, (i + 1) * TA:(i + 2) * TA]),
                waits=tmax(*st.xbf_free[pnxt]), inc=("SX" if pnxt == 0 else "SXB"))
        t_x = st.xld[par]
        xb0 = par * NDC * TA
        t_wv = sch.op("pool", lambda E: E.dma_start(
            out=A.wbig[:, 0:NDC * 512].rearrange("p (c v) -> p c v", c=NDC),
            in_=D.wvT.rearrange("(c p) v -> p c v", p=128)),
            waits=tmax(st.wbig_free), inc="SKV")
        t_stats = None
        for dc in range(NDC):
            b = dc % 2
            t_cast = sch.op("act", lambda E, dc=dc, b=b, xb0=xb0: E.activation(
                A.sq_scr[:, b * TA:(b + 1) * TA], A.x_bf[:, xb0 + dc * TA: xb0 + (dc + 1) * TA],
                AF.Square), waits=tmax(t_x, st.scr_free[b]), inc="ACT")
            sch.op("pe", lambda E, b=b, dc=dc, xb0=xb0: E.matmul(
                A.pA[:], C.ones_bf[:], A.x_bf[:, xb0 + dc * TA: xb0 + (dc + 1) * TA],
                start=(dc == 0), stop=(dc == NDC - 1)),
                waits=tmax(t_cast,
                           st.pA_free if dc == 0 else None,
                           st.pB_free if dc == 0 else None,
                           ct["lw"] if dc == 0 else None))
            t_stats = sch.op("pe", lambda E, b=b, dc=dc: E.matmul(
                A.pB[:], C.ones_bf[:], A.sq_scr[:, b * TA:(b + 1) * TA],
                start=(dc == 0), stop=(dc == NDC - 1)), inc="PE")
            st.scr_free[b] = t_stats
        t_mean = sch.op("act", lambda E: E.mul(A.mean_s[:], A.pA[:], 1.0 / DM),
                        waits=tmax(t_stats, *st.stat_free), inc="ACT")
        t_msq = sch.op("act", lambda E: E.activation(A.msqv_s[:], A.mean_s[:], AF.Square),
                       inc="ACT")
        t_var = sch.op("dve", lambda E: E.scalar_tensor_tensor(
            A.msqv_s[:], A.pB[:], 1.0 / DM, A.msqv_s[:], ALU.mult, ALU.subtract),
            waits=tmax(t_msq, t_stats), inc="DVE")
        sch.op("act", lambda E: E.activation(
            A.rstd_s[:], A.msqv_s[:], AF.Ln, bias=C.eps5[:]),
            waits=tmax(t_var, ct["eps"]))
        t_rstd = sch.op("act", lambda E: E.activation(
            A.rstd_s[:], A.rstd_s[:], AF.Exp, scale=-0.5), inc="ACT")
        st.pA_free = t_mean
        st.pB_free = t_var
        t_ln = None
        t_ln_pl = None
        t_ln_dc = [None] * NDC
        for dc in range(NDC):
            if dc % 3 == 2:
                sch.op("pool", lambda E, dc=dc, xb0=xb0: E.tensor_sub(
                    A.q_ro[:], A.x_bf[:, xb0 + dc * TA: xb0 + (dc + 1) * TA], A.mean_s[:]),
                    waits=tmax(t_rstd, t_mean, st.lnbf_free, st.qro_free))
                t_ln_pl = sch.op("pool", lambda E, dc=dc: E.tensor_mul(
                    A.ln_bf[:, dc * TA:(dc + 1) * TA], A.q_ro[:], A.rstd_s[:]),
                    inc="PL")
                t_ln_dc[dc] = t_ln_pl
            else:
                sch.op("dve", lambda E, dc=dc, xb0=xb0: E.tensor_sub(
                    A.ln_scr[:], A.x_bf[:, xb0 + dc * TA: xb0 + (dc + 1) * TA], A.mean_s[:]),
                    waits=tmax(t_rstd, t_mean, st.lnbf_free))
                t_ln = sch.op("dve", lambda E, dc=dc: E.tensor_mul(
                    A.ln_bf[:, dc * TA:(dc + 1) * TA], A.ln_scr[:], A.rstd_s[:]),
                    inc="DVE")
                t_ln_dc[dc] = t_ln
        if debug:
            sch.op("pool", lambda E, ti0=ti0: E.dma_start(
                out=D.dbg["d_lnx"].rearrange("(c p) t -> p c t", p=128)[:, :, ti0:ti0 + TA],
                in_=A.ln_bf[:].rearrange("p (c t) -> p c t", c=NDC)),
                waits=tmax(t_ln), inc="LW")
        st.stat_free = [t_ln, t_ln_pl]
        t_vp = None
        for ts_ in range(4):
            gt = 4 * i + ts_

            def vproj(E, ts_=ts_):
                last = None
                for dc in range(NDC):
                    last = E.matmul(
                        A.pQ[:], A.ln_bf[:, dc * TA + ts_ * 128: dc * TA + ts_ * 128 + 128],
                        A.wbig[:, dc * 512:(dc + 1) * 512],
                        start=(dc == 0), stop=(dc == NDC - 1))
                return last
            t_vp = sch.op("pe", vproj,
                          waits=tmax(t_wv, st.pQ_free, t_ln, t_ln_pl,
                                     *([t_ln_dc[dc] for dc in range(NDC)] if ts_ == 0 else [])),
                          inc="PE")
            t_ve = sch.op("act", lambda E, gt=gt: E.copy(
                A.v_full[:, gt * 512:(gt + 1) * 512], A.pQ[:]),
                waits=tmax(t_vp), inc="ACT")
            st.vts[gt] = t_ve
            st.pQ_free = t_ve
        t_wk = sch.op("pool", lambda E: E.dma_start(
            out=A.wbig[:].rearrange("p (k c q) -> p k c q", k=HK, c=NDC),
            in_=D.wk_sl.rearrange("k p c q -> p k c q")),
            waits=tmax(t_vp), inc="SKV")
        t_kfin = None
        for hk in range(HK):
            def kproj(E, hk=hk):
                last = None
                for dc in range(NDC):
                    last = E.matmul(
                        A.pQ[:], A.wbig[:, hk * DM + dc * 128: hk * DM + (dc + 1) * 128],
                        A.ln_bf[:, dc * TA:(dc + 1) * TA],
                        start=(dc == 0), stop=(dc == NDC - 1))
                return last
            st.proj_tok = sch.op("pe", kproj,
                                 waits=tmax(t_ln, t_ln_pl, t_wk, st.pQ_free), inc="PE")
            t_ev = rope_start(hk, hk % 2)
            t_kfin = rope_finish(
                t_ev, hk % 2, C.gk_s, None,
                lambda hk=hk, ti0=ti0: A.k_full[:, hk * T + ti0: hk * T + ti0 + TA], ti0)
        if debug:
            sch.op("pool", lambda E, ti0=ti0: E.dma_start(
                out=D.dbg["d_k"].rearrange("(c p) t -> p c t", p=128)[:, :, ti0:ti0 + TA],
                in_=A.k_full[:].rearrange("p (c t) -> p c t", c=HK)[:, :, ti0:ti0 + TA]),
                waits=tmax(t_kfin), inc="LW")
            sch.op("pool", lambda E, i=i: E.dma_start(
                out=D.dbg["d_v"].rearrange("(c p) v -> p c v", p=128)[:, 4 * i:4 * i + 4, :],
                in_=A.v_full[:].rearrange("p (c v) -> p c v", c=NDC)[:, 4 * i:4 * i + 4, :]),
                waits=tmax(st.vts[4 * i + 3]), inc="LW")
        t_wq = [None] * H

        def load_wq(h):
            return sch.op("pool", lambda E, h=h: E.dma_start(
                out=A.wq_sb[:, (h % 2) * DM:(h % 2 + 1) * DM].rearrange("p (c q) -> p c q", c=NDC),
                in_=D.wq_sl[h]),
                waits=tmax(st.wq_free[h % 2]), inc=("SQA" if h % 2 == 0 else "SQB"))

        t_wq[0] = load_wq(0)
        t_wq[1] = load_wq(1)
        t_qfin = [None] * H
        pend = None

        def q_dst(h, ti0=ti0):
            return lambda h=h: A.wbig[:, h * TA:(h + 1) * TA]

        for h in range(H):
            def qproj(E, h=h):
                last = None
                for dc in range(NDC):
                    last = E.matmul(
                        A.pQ[:], A.wq_sb[:, (h % 2) * DM + dc * 128:(h % 2) * DM + (dc + 1) * 128],
                        A.ln_bf[:, dc * TA:(dc + 1) * TA],
                        start=(dc == 0), stop=(dc == NDC - 1))
                return last
            st.proj_tok = sch.op("pe", qproj,
                                 waits=tmax(t_wq[h], st.pQ_free, t_ln, t_ln_pl,
                                            st.wbig_free if h == 0 else None),
                                 inc="PE")
            t_qp = st.proj_tok
            st.wq_free[h % 2] = t_qp
            if h + 2 < H:
                t_wq[h + 2] = load_wq(h + 2)
            t_ev = rope_start(h, h % 2)
            if pend is not None:
                ph, pev = pend
                t_qfin[ph] = rope_finish(pev, ph % 2, C.gq_s, INV_SQRT_DH, q_dst(ph), ti0)
            pend = (h, t_ev)
        ph, pev = pend
        t_qfin[ph] = rope_finish(pev, ph % 2, C.gq_s, INV_SQRT_DH, q_dst(ph), ti0)
        if debug:
            sch.op("pool", lambda E, ti0=ti0: E.dma_start(
                out=D.dbg["d_q"][:, ti0:ti0 + TA], in_=A.wbig[:, 0:TA]),
                waits=tmax(t_qfin[0]), inc="LW")

        core = []
        for h in range(H):
            for sub in range(4):
                core.append((h, sub))
        pend_tp = None
        t_e1 = None
        t_sc_last = None
        for idx, (h, sub) in enumerate(core):
            hk = h // GQ
            gsub = 4 * i + sub
            c_lo = max(0, gsub - 8)
            nch = gsub - c_lo + 1
            W = 128 * nch

            def scmm(E, h=h, sub=sub, c_lo=c_lo, W=W, hk=hk, has_edge=(gsub >= 8)):
                last = None
                for off in range(0, W, 512):
                    n = min(512, W - off)
                    has_mask = (has_edge and off == 0) or (W - 128 >= off and W - 128 < off + n)
                    last = E.matmul(
                        A.pS[:, off:off + n],
                        A.wbig[:, h * TA + sub * 128: h * TA + sub * 128 + 128],
                        A.k_full[:, hk * T + c_lo * 128 + off: hk * T + c_lo * 128 + off + n],
                        start=True, stop=not has_mask)
                    if has_edge and off == 0:
                        last = E.matmul(A.pS[:, 0:128], C.id_bf[:], C.medge_bf[:],
                                        start=False,
                                        stop=not (W - 128 < off + n))
                    if W - 128 >= off and W - 128 < off + n:
                        last = E.matmul(A.pS[:, W - 128:W], C.id_bf[:], C.mcaus_bf[:],
                                        start=False, stop=True)
                return last
            t_sc = sch.op("pe", scmm, waits=tmax(t_qfin[h], st.pS_free, t_kfin, ct["lw"]),
                          inc="PE")
            t_sc_last = t_sc
            par = idx % 2
            t_e1 = sch.op("act", lambda E, W=W, par=par: E.activation(
                A.p_bf[:, par * 1152: par * 1152 + W], A.pS[:, 0:W], AF.Exp,
                bias=C.negC[:], accum_out=C.sums[:]),
                waits=tmax(t_sc, t_negC, st.pbf_par_free[par]), inc="ACT")
            st.pS_free = t_e1
            sch.op("act", lambda E: E.activation(C.lnsum[:], C.sums[:], AF.Ln, bias=C.eps35[:]))
            t_ds = sch.op("act", lambda E, par=par: E.activation(
                C.rs2[:, par:par + 1], C.lnsum[:], AF.Exp, scale=-1.0),
                waits=tmax(st.diag_free[par]), inc="ACT")
            t_diag = sch.op("dve", lambda E, par=par, W=W: E.tensor_scalar(
                A.p_bf[:, par * 1152: par * 1152 + W],
                A.p_bf[:, par * 1152: par * 1152 + W],
                C.rs2[:, par:par + 1], None, ALU.mult),
                waits=tmax(t_ds), inc="DVE")
            st.diag_free[par] = t_diag
            def emit_tp(ph, psub, pnch, pc_lo, phk, pW, t_pe1, t_pdiag, ppar):
                def trs(E, nch=pnch, par=ppar):
                    last = None
                    for c in range(nch):
                        last = E.transpose(A.pTT[:, c * 128:(c + 1) * 128],
                                           A.p_bf[:, par * 1152 + c * 128: par * 1152 + (c + 1) * 128],
                                           C.id_bf[:])
                    return last
                t_tr = sch.op("pe", trs, waits=tmax(t_pe1, t_pdiag, st.pTT_free), inc="PE")
                st.pbf_par_free[ppar] = t_tr
                st.tr_free[ppar] = t_tr
                t_pte = sch.op("dve", lambda E, W=pW: E.tensor_copy(A.pT_sb[:, 0:W], A.pTT[:, 0:W]),
                               waits=tmax(t_tr, st.pT_free), inc="DVE")
                st.pTT_free = t_pte

                def pv(E, nch=pnch, c_lo=pc_lo, hk=phk):
                    last = None
                    for c in range(nch):
                        last = E.matmul(
                            A.pB[:, 0:128],
                            A.v_full[:, (c_lo + c) * 512 + hk * 128: (c_lo + c) * 512 + hk * 128 + 128],
                            A.pT_sb[:, c * 128:(c + 1) * 128],
                            start=(c == 0), stop=(c == nch - 1))
                    return last
                t_pv = sch.op("pe", pv,
                              waits=tmax(t_pte, st.pB_free, st.vts[4 * i + psub]), inc="PE")
                st.pT_free = t_pv
                t_ye = sch.op("dve", lambda E, h=ph, sub=psub: E.tensor_copy(
                    A.y_att[:, h * TA + sub * 128: h * TA + sub * 128 + 128], A.pB[:, 0:128]),
                    waits=tmax(t_pv, st.yatt_free if (ph == 0 and psub == 0) else None),
                    inc="DVE")
                st.pB_free = t_ye
                return t_ye
            if pend_tp is not None:
                t_ye = emit_tp(*pend_tp)
            pend_tp = (h, sub, nch, c_lo, hk, W, t_e1, t_diag, par)
        t_ye = emit_tp(*pend_tp)
        t_ye_last = t_ye
        st.wbig_free = t_sc_last
        if debug:
            sch.op("pool", lambda E, ti0=ti0: E.dma_start(
                out=D.dbg["d_y0"][:, ti0:ti0 + TA], in_=A.y_att[:, 0:TA]),
                waits=tmax(t_ye), inc="LW")
        t_wo = {}

        def load_wo(dcp):
            t_wo[dcp] = sch.op("pool", lambda E, dcp=dcp: E.dma_start(
                out=A.wo_sb[:, (dcp % 2) * DM:(dcp % 2 + 1) * DM].rearrange("p (c q) -> p c q", c=NDC),
                in_=D.wo_sl[dcp]),
                waits=tmax(st.wo_free[dcp % 2]), inc=("SOA" if dcp % 2 == 0 else "SOB"))

        load_wo(0)
        load_wo(1)
        t_stats2 = None
        t_wmm = None
        for dcp in range(NDC):
            def womm(E, dcp=dcp):
                last = None
                for ec in range(NDC):
                    last = E.matmul(
                        A.pQ[:], A.wo_sb[:, (dcp % 2) * DM + ec * 128:(dcp % 2) * DM + (ec + 1) * 128],
                        A.y_att[:, ec * TA:(ec + 1) * TA],
                        start=(ec == 0), stop=(ec == NDC - 1))
                return last
            t_wmm = sch.op("pe", womm,
                           waits=tmax(t_wo[dcp], t_ye_last, st.pQ_free), inc="PE")
            st.wo_free[dcp % 2] = t_wmm
            if dcp + 2 < NDC:
                load_wo(dcp + 2)
            t_ae = sch.op("act", lambda E, dcp=dcp: E.copy(
                A.att_bf[:, dcp * TA:(dcp + 1) * TA], A.pQ[:]),
                waits=tmax(t_wmm, *(st.attbf_free if dcp == 0 else [])), inc="ACT")
            st.pQ_free = t_ae
            b = dcp % 2
            t_sqc = sch.op("act", lambda E, dcp=dcp, b=b: E.activation(
                A.sq_scr[:, b * TA:(b + 1) * TA], A.att_bf[:, dcp * TA:(dcp + 1) * TA],
                AF.Square),
                waits=tmax(st.scr_free[b]), inc="ACT")
            sch.op("pe", lambda E, dcp=dcp: E.matmul(
                A.pA[:], C.ones_bf[:], A.att_bf[:, dcp * TA:(dcp + 1) * TA],
                start=(dcp == 0), stop=(dcp == NDC - 1)),
                waits=tmax(t_sqc,
                           st.pA_free if dcp == 0 else None,
                           st.pB_free if dcp == 0 else None))
            t_stats2 = sch.op("pe", lambda E, dcp=dcp, b=b: E.matmul(
                A.pB[:], C.ones_bf[:], A.sq_scr[:, b * TA:(b + 1) * TA],
                start=(dcp == 0), stop=(dcp == NDC - 1)), inc="PE")
            st.scr_free[b] = t_stats2
        if debug:
            sch.op("pool", lambda E, ti0=ti0: E.dma_start(
                out=D.dbg["d_att"].rearrange("(c p) t -> p c t", p=128)[:, :, ti0:ti0 + TA],
                in_=A.att_bf[:].rearrange("p (c t) -> p c t", c=NDC)),
                waits=tmax(t_stats2), inc="LW")
        t_mean2 = sch.op("act", lambda E: E.mul(A.mean2_s[:], A.pA[:], 1.0 / DM),
                         waits=tmax(t_stats2, t_ln, t_ln_pl), inc="ACT")
        t_msq2 = sch.op("act", lambda E: E.activation(A.msqv2_s[:], A.mean2_s[:], AF.Square),
                        inc="ACT")
        t_var2 = sch.op("dve", lambda E: E.scalar_tensor_tensor(
            A.msqv2_s[:], A.pB[:], 1.0 / DM, A.msqv2_s[:], ALU.mult, ALU.subtract),
            waits=tmax(t_msq2, t_stats2), inc="DVE")
        sch.op("act", lambda E: E.activation(
            A.rstd2_s[:], A.msqv2_s[:], AF.Ln, bias=C.eps5[:]),
            waits=tmax(t_var2))
        t_rstd2 = sch.op("act", lambda E: E.activation(
            A.rstd2_s[:], A.rstd2_s[:], AF.Exp, scale=-0.5), inc="ACT")
        st.pA_free = t_mean2
        st.pB_free = t_var2
        t_res = None
        t_res_pl = None
        t_sp_dc = [None] * NDC
        for dc in range(NDC):
            b = dc % 4
            warslot = t_sp_dc[dc - 4] if dc >= 4 else (st.y1s_free[dc] if dc < len(st.y1s_free) else None)
            if dc % 2 == 1:
                sch.op("pool", lambda E, dc=dc: E.tensor_sub(
                    A.q_ro[:], A.att_bf[:, dc * TA:(dc + 1) * TA], A.mean2_s[:]),
                    waits=tmax(t_rstd2, t_mean2, st.qro_free))
                sch.op("pool", lambda E: E.tensor_mul(A.q_ro[:], A.q_ro[:], A.rstd2_s[:]))
                t_res_pl = sch.op("pool", lambda E, dc=dc, b=b, xb0=xb0: E.tensor_add(
                    A.y1s[:, b * TA:(b + 1) * TA], A.q_ro[:],
                    A.x_bf[:, xb0 + dc * TA: xb0 + (dc + 1) * TA]),
                    waits=tmax(warslot), inc="PL")
                t_last = t_res_pl
            else:
                sch.op("dve", lambda E, dc=dc: E.tensor_sub(
                    A.ln_scr[:], A.att_bf[:, dc * TA:(dc + 1) * TA], A.mean2_s[:]),
                    waits=tmax(t_rstd2, t_mean2))
                sch.op("dve", lambda E: E.tensor_mul(A.ln_scr[:], A.ln_scr[:], A.rstd2_s[:]))
                t_res = sch.op("dve", lambda E, dc=dc, b=b, xb0=xb0: E.tensor_add(
                    A.y1s[:, b * TA:(b + 1) * TA], A.ln_scr[:],
                    A.x_bf[:, xb0 + dc * TA: xb0 + (dc + 1) * TA]),
                    waits=tmax(warslot), inc="DVE")
                t_last = t_res
            t_sp_dc[dc] = sch.op("sp", lambda E, dc=dc, b=b, ti0=ti0: E.dma_start(
                out=D.y1_sp[dc * 128:(dc + 1) * 128, ti0:ti0 + TA],
                in_=A.y1s[:, b * TA:(b + 1) * TA]),
                waits=tmax(t_last), inc=("STA", "STB", "STC", "STD")[b])
        st.y1s_free = [t_sp_dc[NDC - 4], t_sp_dc[NDC - 3], t_sp_dc[NDC - 2], t_sp_dc[NDC - 1]]
        t_spill = st.y1s_free
        st.xbf_free[par] = [t_res, t_res_pl]
        st.attbf_free = [t_res, t_res_pl]
        st.qro_free = t_res_pl
        st.lnbf_free = t_qp
        st.yatt_free = t_wmm

    return {"pe": ("PE", sch.cnt["PE"]), "act": ("ACT", sch.cnt["ACT"]),
            "dve": ("DVE", sch.cnt["DVE"]), "spill": t_spill}


def _sched_ffn(sch, nc, D, C, Fb, ct, af, debug):
    t_spill_all = af["spill"]
    if not isinstance(t_spill_all, list):
        t_spill_all = [t_spill_all]
    allf = [af["pe"], af["act"], af["dve"]]
    for en in ("pe", "act", "dve", "pool", "sp"):
        sch.join(en, tmax(*allf))

    st = NS()
    st.y1_free = [None, None]
    st.cast_free = [None, None]
    st.u_free = None
    st.acc_free = None
    st.h_free = [None] * NFC
    st.pH_free = [None, None]
    st.pY_free = [None, None]
    st.G_free = None
    st.gT_free = None
    st.stat_free = None
    st.w1free = [None, None, None]
    st.w2free = [None, None]
    st.gall_free = [None] * 8
    st.sqf_free = None
    st.g16_free = None
    t_w1 = {}
    t_w2 = {}

    def load_w1(j, fc):
        idx = (j * NFC + fc) % 3
        t_w1[(j, fc)] = sch.op("pool", lambda E, j=j, fc=fc, idx=idx: E.dma_start(
            out=Fb.w1_sb[:, idx * DM:(idx + 1) * DM].rearrange("p (c q) -> p c q", c=NDC),
            in_=D.w1_sl[j, fc]),
            waits=tmax(st.w1free[idx]), inc=("S1A", "S1B", "S1C")[idx])

    def load_w2(j, dcp):
        idx = (j * NDC + dcp) % 2
        t_w2[(j, dcp)] = sch.op("pool", lambda E, j=j, dcp=dcp, idx=idx: E.dma_start(
            out=Fb.w2_sb[:, idx * NFC * 128:(idx + 1) * NFC * 128].rearrange("p (c q) -> p c q", c=NFC),
            in_=D.w2_sl[j, dcp]),
            waits=tmax(st.w2free[idx]), inc=("S2A", "S2B")[idx])

    for f in range(NTF):
        tf0 = f * TF
        load_w1(0, 0)
        load_w1(0, 1)
        load_w1(0, 2)
        load_w2(0, 0)
        load_w2(0, 1)
        t_st3 = None
        for dc in range(NDC):
            b = dc % 2
            t_l = sch.op("sp", lambda E, dc=dc, b=b, tf0=tf0: E.dma_start(
                out=Fb.y1_scr[:, b * TF:(b + 1) * TF],
                in_=D.y1_sp[dc * 128:(dc + 1) * 128, tf0:tf0 + TF]),
                waits=tmax(st.y1_free[b], *t_spill_all), inc=("SYA" if b == 0 else "SYB"))
            sch.op("act", lambda E, b=b: E.copy(
                Fb.cast_f[:, b * TF:(b + 1) * TF], Fb.y1_scr[:, b * TF:(b + 1) * TF]),
                waits=tmax(t_l, st.cast_free[b]))
            t_cf = sch.op("act", lambda E, b=b: E.activation(
                Fb.sq_f[:], Fb.y1_scr[:, b * TF:(b + 1) * TF],
                AF.Square), waits=tmax(st.sqf_free), inc="ACT")
            st.y1_free[b] = t_cf

            def stats3(E, dc=dc, b=b):
                last = None
                for hh in range(2):
                    last = E.matmul(Fb.pHa[:, hh * 512:(hh + 1) * 512], C.ones_bf[:],
                                    Fb.cast_f[:, b * TF + hh * 512: b * TF + (hh + 1) * 512],
                                    start=(dc == 0), stop=(dc == NDC - 1))
                for hh in range(2):
                    last = E.matmul(Fb.pHb[:, hh * 512:(hh + 1) * 512], C.ones_bf[:],
                                    Fb.sq_f[:, hh * 512:(hh + 1) * 512],
                                    start=(dc == 0), stop=(dc == NDC - 1))
                return last
            t_st3 = sch.op("pe", stats3,
                           waits=tmax(t_cf,
                                      st.pH_free[0] if dc == 0 else None,
                                      st.pH_free[1] if dc == 0 else None),
                           inc="PE")
            st.cast_free[b] = t_st3
            st.sqf_free = t_st3
        t_mean3 = sch.op("act", lambda E: E.mul(Fb.mean_f[:], Fb.pHa[:], 1.0 / DM),
                         waits=tmax(t_st3, st.stat_free), inc="ACT")
        t_msq3 = sch.op("act", lambda E: E.activation(Fb.msqv_f[:], Fb.mean_f[:], AF.Square),
                        inc="ACT")
        t_var3 = sch.op("dve", lambda E: E.scalar_tensor_tensor(
            Fb.msqv_f[:], Fb.pHb[:], 1.0 / DM, Fb.msqv_f[:], ALU.mult, ALU.subtract),
            waits=tmax(t_msq3, t_st3), inc="DVE")
        sch.op("act", lambda E: E.activation(
            Fb.rstd_f[:], Fb.msqv_f[:], AF.Ln, bias=C.eps5[:]),
            waits=tmax(t_var3))
        t_rstd3 = sch.op("act", lambda E: E.activation(
            Fb.rstd_f[:], Fb.rstd_f[:], AF.Exp, scale=-0.5), inc="ACT")
        st.pH_free = [t_mean3, t_var3]
        t_u = None
        t_u_pl = None
        for dc in range(NDC):
            b = dc % 2
            t_l = sch.op("sp", lambda E, dc=dc, b=b, tf0=tf0: E.dma_start(
                out=Fb.y1_scr[:, b * TF:(b + 1) * TF],
                in_=D.y1_sp[dc * 128:(dc + 1) * 128, tf0:tf0 + TF]),
                waits=tmax(st.y1_free[b], *t_spill_all), inc=("SYA" if b == 0 else "SYB"))
            if dc % 3 == 2:
                lnp = Fb.cast_f[:].bitcast(F32)
                sch.op("pool", lambda E, b=b, lnp=lnp: E.tensor_sub(
                    lnp, Fb.y1_scr[:, b * TF:(b + 1) * TF], Fb.mean_f[:]),
                    waits=tmax(t_l, t_rstd3, t_mean3, st.u_free, st.cast_free[0],
                               st.cast_free[1]))
                t_u_pl = sch.op("pool", lambda E, dc=dc, lnp=lnp: E.tensor_mul(
                    Fb.u_bf[:, dc * TF:(dc + 1) * TF], lnp, Fb.rstd_f[:]),
                    inc="PL")
                st.y1_free[b] = t_u_pl
            else:
                sch.op("dve", lambda E, b=b: E.tensor_sub(
                    Fb.ln_f[:], Fb.y1_scr[:, b * TF:(b + 1) * TF], Fb.mean_f[:]),
                    waits=tmax(t_l, t_rstd3, t_mean3, st.u_free))
                t_u = sch.op("dve", lambda E, dc=dc: E.tensor_mul(
                    Fb.u_bf[:, dc * TF:(dc + 1) * TF], Fb.ln_f[:], Fb.rstd_f[:]),
                    inc="DVE")
                st.y1_free[b] = t_u
        if debug:
            sch.op("pool", lambda E, tf0=tf0: E.dma_start(
                out=D.dbg["d_u"].rearrange("(c p) t -> p c t", p=128)[:, :, tf0:tf0 + TF],
                in_=Fb.u_bf[:].rearrange("p (c t) -> p c t", c=NDC)),
                waits=tmax(t_u), inc="LW")
        t_rmm = [None] * 8
        t_sg_prev = None
        t_msk_prev = None
        t_gmul_prev = None
        t_ln_prev = None
        t_final_prev = None
        t_gfin = [None] * 8
        for ts_ in range(8):
            def rmm(E, ts_=ts_):
                last = None
                for dc in range(NDC):
                    last = E.matmul(
                        Fb.pYa[:, ts_ * 64: ts_ * 64 + NRT],
                        Fb.u_bf[:, dc * TF + ts_ * 128: dc * TF + ts_ * 128 + 128],
                        C.cT_bf[:, dc * NRT:(dc + 1) * NRT],
                        start=(dc == 0), stop=(dc == NDC - 1))
                return last
            t_rmm[ts_] = sch.op("pe", rmm,
                                waits=tmax(t_u, t_u_pl, st.pY_free[0] if ts_ == 0 else None,
                                           ct["lw"]),
                                inc="PE")
        for ts_ in range(8):
            t_sg = sch.op("act", lambda E, ts_=ts_: E.activation(
                C.sc16[:], Fb.pYa[:, ts_ * 64: ts_ * 64 + NRT], AF.Sigmoid),
                waits=tmax(t_rmm[7], t_gmul_prev, t_msk_prev), inc="ACT")
            t_bias = sch.op("pool", lambda E: E.tensor_add(C.sc16[:], C.sc16[:], C.bias_bc_s[:]),
                            waits=tmax(t_sg, ct["lx"]), inc="PL")
            t_mx = sch.op("dve", lambda E: E.max(out=C.mx8[:], in_=C.sc16[:]),
                          waits=tmax(t_bias), inc="DVE")
            t_thr = sch.op("pool", lambda E: E.tensor_copy(C.thr1[:], C.mx8[:, 3:4]),
                           waits=tmax(t_mx), inc="PL")
            t_msk = sch.op("dve", lambda E: E.tensor_tensor(
                C.msk16[:], C.sc16[:], C.thr1[:].to_broadcast([128, NRT]), ALU.is_ge),
                waits=tmax(t_thr), inc="DVE")
            t_msk_prev = t_msk
            t_gmul = sch.op("pool", lambda E, ts_=ts_: E.tensor_mul(
                C.g_all[:, ts_ * NRT:(ts_ + 1) * NRT], C.sc16[:], C.msk16[:]),
                waits=tmax(t_msk, st.gall_free[ts_], t_final_prev), inc="PL")
            t_gmul_prev = t_gmul
            t_red = sch.op("dve", lambda E, ts_=ts_: E.reduce_sum(
                C.rs1[:], C.g_all[:, ts_ * NRT:(ts_ + 1) * NRT], AX.X),
                waits=tmax(t_gmul), inc="DVE")
            t_rsc = sch.op("pool", lambda E: E.tensor_copy(C.rsB[:], C.rs1[:]),
                           waits=tmax(t_red, t_ln_prev), inc="PL")
            t_eps = sch.op("dve", lambda E: E.tensor_scalar_max(C.rsB[:], C.rsB[:], 1e-8),
                           waits=tmax(t_rsc), inc="DVE")
            t_ln = sch.op("act", lambda E: E.activation(C.rsC[:], C.rsB[:], AF.Ln),
                          waits=tmax(t_eps, t_final_prev), inc="ACT")
            t_ln_prev = t_ln
            t_exp = sch.op("act", lambda E: E.activation(C.rsC[:], C.rsC[:], AF.Exp, scale=-1.0),
                           inc="ACT")
            t_final = sch.op("dve", lambda E, ts_=ts_: E.tensor_tensor(
                C.g_all[:, ts_ * NRT:(ts_ + 1) * NRT], C.g_all[:, ts_ * NRT:(ts_ + 1) * NRT],
                C.rsC[:].to_broadcast([128, NRT]), ALU.mult),
                waits=tmax(t_exp), inc="DVE")
            t_final_prev = t_final
            t_gfin[ts_] = t_final
            if debug:
                t_gd = sch.op("sp", lambda E, ts_=ts_, tf0=tf0: E.dma_start(
                    out=D.dbg["d_g"][tf0 + ts_ * 128: tf0 + (ts_ + 1) * 128, :],
                    in_=C.g_all[:, ts_ * NRT:(ts_ + 1) * NRT]),
                    waits=tmax(t_final), inc="LX")
        st.pY_free[0] = t_sg
        t_hm = None
        t_ym = None
        t_hev = [None] * NFC
        t_ac17 = [None] * NDC
        for j in range(NE):
            routed = j >= 2
            jr = j - 2
            t_ge = None
            if routed:
                def gmm(E, jr=jr):
                    last = None
                    for hh in range(2):
                        last = E.matmul(Fb.pHa[:, hh * 512:(hh + 1) * 512],
                                        C.sel_bf[:, jr * 128:(jr + 1) * 128],
                                        Fb.gT_bf[:, hh * 512:(hh + 1) * 512],
                                        start=True, stop=True)
                    return last
                t_gm = sch.op("pe", gmm, waits=tmax(t_gT, st.pH_free[0]), inc="PE")
                t_ge = sch.op("act", lambda E: E.copy(Fb.G_bf[:], Fb.pHa[:]),
                              waits=tmax(t_gm, st.G_free), inc="ACT")
                st.pH_free[0] = t_ge
            for fc in range(NFC):
                pH = Fb.pHa if fc % 2 == 0 else Fb.pHb
                pidx = fc % 2
                widx = (j * NFC + fc) % 3

                def hmm(E, pH=pH, widx=widx):
                    last = None
                    for dc in range(NDC):
                        for hh in range(2):
                            last = E.matmul(
                                pH[:, hh * 512:(hh + 1) * 512],
                                Fb.w1_sb[:, widx * DM + dc * 128: widx * DM + (dc + 1) * 128],
                                Fb.u_bf[:, dc * TF + hh * 512: dc * TF + (hh + 1) * 512],
                                start=(dc == 0), stop=(dc == NDC - 1))
                    return last
                t_hm = sch.op("pe", hmm,
                              waits=tmax(t_w1[(j, fc)], t_u, t_u_pl, st.pH_free[pidx]), inc="PE")
                st.w1free[widx] = t_hm
                nj, nfc = (j, fc + 3) if fc + 3 < NFC else (j + 1, fc + 3 - NFC)
                if nj < NE:
                    load_w1(nj, nfc)
                if j == 0 and fc < 8:
                    ts_ = fc
                    t_gtr = sch.op("pe", lambda E, ts_=ts_: E.transpose(
                        Fb.pYb[0:NRT, ts_ * 128:(ts_ + 1) * 128],
                        C.g_all[:, ts_ * NRT:(ts_ + 1) * NRT], C.id_f[:]),
                        waits=tmax(t_gfin[ts_], st.pY_free[1] if ts_ == 0 else None),
                        inc="PE")
                    st.gall_free[ts_] = t_gtr
                    if ts_ == 7:
                        t_gT = sch.op("act", lambda E: E.copy(Fb.gT_bf[:], Fb.pYb[0:NRT, :]),
                                      waits=tmax(t_gtr, st.gT_free), inc="ACT")
                        st.pY_free[1] = t_gT
                t_he = sch.op("act", lambda E, fc=fc, pH=pH, j=j: E.activation(
                    Fb.h_bf[:, fc * TF:(fc + 1) * TF], pH[:], AF.Relu,
                    bias=C.b1c_s[:, j * NFC + fc: j * NFC + fc + 1]),
                    waits=tmax(t_hm, st.h_free[fc], ct["lx"]), inc="ACT")
                st.pH_free[pidx] = t_he
                t_hev[fc] = t_he
                if routed:
                    t_hg = sch.op("dve", lambda E, fc=fc: E.tensor_mul(
                        Fb.h_bf[:, fc * TF:(fc + 1) * TF],
                        Fb.h_bf[:, fc * TF:(fc + 1) * TF], Fb.G_bf[:]),
                        waits=tmax(t_he, t_ge), inc="DVE")
                    t_hev[fc] = t_hg
            if routed:
                st.G_free = t_hev[NFC - 1]
            if debug and j == 0 and f == 0:
                sch.op("pool", lambda E: E.dma_start(
                    out=D.dbg["d_h0"].rearrange("(c p) t -> p c t", p=128),
                    in_=Fb.h_bf[:].rearrange("p (c t) -> p c t", c=NFC)),
                    waits=tmax(t_hev[NFC - 1]), inc="LW")
            for dcp in range(NDC):
                pY = Fb.pYa if dcp % 2 == 0 else Fb.pYb
                pidx = dcp % 2
                widx = (j * NDC + dcp) % 2

                def ymm(E, dcp=dcp, pY=pY, widx=widx, j=j):
                    last = None
                    islast = (j == NE - 1)
                    for fc in range(NFC):
                        for hh in range(2):
                            last = E.matmul(
                                pY[:, hh * 512:(hh + 1) * 512],
                                Fb.w2_sb[:, widx * NFC * 128 + fc * 128: widx * NFC * 128 + (fc + 1) * 128],
                                Fb.h_bf[:, fc * TF + hh * 512: fc * TF + (hh + 1) * 512],
                                start=(fc == 0), stop=(fc == NFC - 1 and not islast))
                    if islast:
                        for hh in range(2):
                            last = E.matmul(
                                pY[:, hh * 512:(hh + 1) * 512],
                                C.b2r_bf[:, dcp * 128:(dcp + 1) * 128],
                                Fb.gT_bf[:, hh * 512:(hh + 1) * 512],
                                start=False, stop=True)
                    return last
                t_ym = sch.op("pe", ymm,
                              waits=tmax(t_hev[NFC - 1], t_w2[(j, dcp)], st.pY_free[pidx]),
                              inc="PE")
                st.w2free[widx] = t_ym
                nj, ndcp = (j, dcp + 2) if dcp + 2 < NDC else (j + 1, dcp + 2 - NDC)
                if nj < NE:
                    load_w2(nj, ndcp)
                if j == 0:
                    def acc0(E, dcp=dcp, pY=pY):
                        E.tensor_tensor(Fb.acc[:, dcp * TF:(dcp + 1) * TF], pY[:],
                                        C.b2sh_s[:, dcp:dcp + 1].to_broadcast([128, TF]),
                                        ALU.add)
                        return E.tensor_add(Fb.acc[:, dcp * TF:(dcp + 1) * TF],
                                            Fb.acc[:, dcp * TF:(dcp + 1) * TF],
                                            Fb.u_bf[:, dcp * TF:(dcp + 1) * TF])
                    t_ac = sch.op("dve", acc0,
                                  waits=tmax(t_ym, ct["b2sh"], st.acc_free), inc="DVE")
                else:
                    t_ac = sch.op("dve", lambda E, dcp=dcp, pY=pY: E.tensor_add(
                        Fb.acc[:, dcp * TF:(dcp + 1) * TF],
                        Fb.acc[:, dcp * TF:(dcp + 1) * TF], pY[:]),
                        waits=tmax(t_ym), inc="DVE")
                st.pY_free[pidx] = t_ac
                if j == NE - 1:
                    t_ac17[dcp] = t_ac
            for fc in range(NFC):
                st.h_free[fc] = t_ym
        st.u_free = t_hm
        st.gT_free = t_ym
        t_acc_all = t_ac
        t_st4 = None
        for dc in range(NDC):
            b = dc % 2
            sch.op("act", lambda E, dc=dc, b=b: E.copy(
                Fb.cast_f[:, b * TF:(b + 1) * TF], Fb.acc[:, dc * TF:(dc + 1) * TF]),
                waits=tmax(t_ac17[dc], st.cast_free[b], t_u_pl))
            t_c4 = sch.op("act", lambda E, dc=dc, b=b: E.activation(
                Fb.sq_f[:], Fb.acc[:, dc * TF:(dc + 1) * TF],
                AF.Square), waits=tmax(st.sqf_free), inc="ACT")

            def stats4(E, dc=dc, b=b):
                last = None
                for hh in range(2):
                    last = E.matmul(Fb.pHa[:, hh * 512:(hh + 1) * 512], C.ones_bf[:],
                                    Fb.cast_f[:, b * TF + hh * 512: b * TF + (hh + 1) * 512],
                                    start=(dc == 0), stop=(dc == NDC - 1))
                for hh in range(2):
                    last = E.matmul(Fb.pHb[:, hh * 512:(hh + 1) * 512], C.ones_bf[:],
                                    Fb.sq_f[:, hh * 512:(hh + 1) * 512],
                                    start=(dc == 0), stop=(dc == NDC - 1))
                return last
            t_st4 = sch.op("pe", stats4,
                           waits=tmax(t_c4,
                                      st.pH_free[0] if dc == 0 else None,
                                      st.pH_free[1] if dc == 0 else None),
                           inc="PE")
            st.cast_free[b] = t_st4
            st.sqf_free = t_st4
        t_mean4 = sch.op("act", lambda E: E.mul(Fb.mean_f[:], Fb.pHa[:], 1.0 / DM),
                         waits=tmax(t_st4, t_u, t_u_pl), inc="ACT")
        t_msq4 = sch.op("act", lambda E: E.activation(Fb.msqv_f[:], Fb.mean_f[:], AF.Square),
                        inc="ACT")
        t_var4 = sch.op("dve", lambda E: E.scalar_tensor_tensor(
            Fb.msqv_f[:], Fb.pHb[:], 1.0 / DM, Fb.msqv_f[:], ALU.mult, ALU.subtract),
            waits=tmax(t_msq4, t_st4), inc="DVE")
        sch.op("act", lambda E: E.activation(
            Fb.rstd_f[:], Fb.msqv_f[:], AF.Ln, bias=C.eps5[:]),
            waits=tmax(t_var4))
        t_rstd4 = sch.op("act", lambda E: E.activation(
            Fb.rstd_f[:], Fb.rstd_f[:], AF.Exp, scale=-0.5), inc="ACT")
        st.pH_free = [t_mean4, t_var4]
        st.stat_free = t_rstd4
        if debug:
            sch.op("sp", lambda E, tf0=tf0: E.dma_start(
                out=D.dbg["d_acc"].rearrange("(c p) t -> p c t", p=128)[:, :, tf0:tf0 + TF],
                in_=Fb.acc[:].rearrange("p (c t) -> p c t", c=NDC)),
                waits=tmax(t_acc_all), inc="LX")
        t_out = None
        for dc in range(NDC):
            b = dc % 2
            t_l = sch.op("sp", lambda E, dc=dc, b=b, tf0=tf0: E.dma_start(
                out=Fb.y1_scr[:, b * TF:(b + 1) * TF],
                in_=D.y1_sp[dc * 128:(dc + 1) * 128, tf0:tf0 + TF]),
                waits=tmax(st.y1_free[b], *t_spill_all), inc=("SYA" if b == 0 else "SYB"))
            sch.op("dve", lambda E, dc=dc: E.tensor_sub(
                Fb.ln_f[:], Fb.acc[:, dc * TF:(dc + 1) * TF], Fb.mean_f[:]),
                waits=tmax(t_rstd4, t_mean4))
            sch.op("dve", lambda E: E.tensor_mul(Fb.ln_f[:], Fb.ln_f[:], Fb.rstd_f[:]))
            t_out = sch.op("dve", lambda E, dc=dc, b=b: E.tensor_add(
                Fb.acc[:, dc * TF:(dc + 1) * TF], Fb.ln_f[:],
                Fb.y1_scr[:, b * TF:(b + 1) * TF]),
                waits=tmax(t_l), inc="DVE")
            st.y1_free[b] = t_out
        t_store = sch.op("sp", lambda E, tf0=tf0: E.dma_start(
            out=D.out_fm.rearrange("(c p) t -> p c t", p=128)[:, :, tf0:tf0 + TF],
            in_=Fb.acc[:].rearrange("p (c t) -> p c t", c=NDC)),
            waits=tmax(t_out), inc="ST")
        st.acc_free = t_store

    sch.join("sp", [("ST", sch.cnt["ST"])])



def _host_layout(inputs):
    x = np.asarray(inputs["L0_attn_x"], dtype=np.float32)
    wq = np.asarray(inputs["L0_attn_wq"], dtype=np.float32)
    wk = np.asarray(inputs["L0_attn_wk"], dtype=np.float32)
    wv = np.asarray(inputs["L0_attn_wv"], dtype=np.float32)
    wo = np.asarray(inputs["L0_attn_wo"], dtype=np.float32)
    gq = np.asarray(inputs["L0_attn_gamma_q"], dtype=np.float32)
    gk = np.asarray(inputs["L0_attn_gamma_k"], dtype=np.float32)
    sw1 = np.asarray(inputs["L0_ffn_shared_w1"], dtype=np.float32)
    sb1 = np.asarray(inputs["L0_ffn_shared_b1"], dtype=np.float32)
    sw2 = np.asarray(inputs["L0_ffn_shared_w2"], dtype=np.float32)
    sb2 = np.asarray(inputs["L0_ffn_shared_b2"], dtype=np.float32)
    rw1 = np.asarray(inputs["L0_ffn_routed_w1"], dtype=np.float32)
    rb1 = np.asarray(inputs["L0_ffn_routed_b1"], dtype=np.float32)
    rw2 = np.asarray(inputs["L0_ffn_routed_w2"], dtype=np.float32)
    rb2 = np.asarray(inputs["L0_ffn_routed_b2"], dtype=np.float32)
    cent = np.asarray(inputs["L0_ffn_centroids"], dtype=np.float32)
    bias = np.asarray(inputs["L0_ffn_bias"], dtype=np.float32)

    com = {}
    com["wq_sl"] = np.ascontiguousarray(
        wq.reshape(H, 128, NDC, 128).transpose(0, 3, 2, 1))
    com["wk_sl"] = np.ascontiguousarray(
        wk.reshape(HK, 128, NDC, 128).transpose(0, 3, 2, 1))
    com["wvT"] = np.ascontiguousarray(wv.T)
    com["wo_sl"] = np.ascontiguousarray(
        wo.reshape(NDC, 128, NDC, 128).transpose(0, 3, 2, 1))
    w1_all = np.concatenate([sw1, rw1], axis=0)
    com["w1_sl"] = np.ascontiguousarray(
        w1_all.reshape(NE, NDC, 128, NFC, 128).transpose(0, 3, 2, 1, 4))
    w2_all = np.concatenate([sw2, rw2], axis=0)
    com["w2_sl"] = np.ascontiguousarray(
        w2_all.reshape(NE, NFC, 128, NDC, 128).transpose(0, 3, 2, 1, 4))
    b1_all = np.concatenate([sb1, rb1], axis=0)
    com["b1c"] = np.ascontiguousarray(b1_all.reshape(NE, NFC, 128).transpose(2, 0, 1)
                                      .reshape(128, NE * NFC))
    com["b2c"] = np.ascontiguousarray(sb2.reshape(2, NDC, 128).transpose(2, 0, 1)
                                      .reshape(128, 2 * NDC))
    com["b2r"] = np.ascontiguousarray(rb2)
    com["cT"] = np.ascontiguousarray(cent.T)
    com["bias_bc"] = np.ascontiguousarray(np.tile(bias[None, :], (128, 1)))
    com["gq"] = np.ascontiguousarray(gq.reshape(128, 1))
    com["gk"] = np.ascontiguousarray(gk.reshape(128, 1))

    inv_freq = (1.0 / RBASE ** (np.arange(0, DH, 2, dtype=np.float64) / DH))
    ang = np.arange(T, dtype=np.float64)[:, None] * inv_freq[None, :]
    cos = np.cos(ang)
    sin = np.sin(ang)
    com["cos_t"] = np.ascontiguousarray(
        np.concatenate([cos, cos], axis=1).T.astype(np.float32))
    com["sin_t"] = np.ascontiguousarray(
        np.concatenate([-sin, sin], axis=1).T.astype(np.float32))
    com["ones_c"] = np.ones((128, 128), np.float32)
    com["ident_c"] = np.eye(128, dtype=np.float32)
    il = np.arange(128)[:, None]
    jl = np.arange(128)[None, :]
    com["mcaus"] = np.where(jl <= il, 0.0, -1e9).astype(np.float32)
    com["medge"] = np.where(jl >= il, 0.0, -1e9).astype(np.float32)
    sel = np.zeros((NRT, NRT * 128), np.float32)
    for j in range(NRT):
        sel[j, j * 128:(j + 1) * 128] = 1.0
    com["sel_c"] = sel
    com["identu_c"] = np.eye(128, dtype=np.uint8)
    kk = np.arange(128)[:, None]
    mm = np.arange(128)[None, :]
    com["p64_c"] = (np.abs(kk - mm) == 64).astype(np.float32)

    in_maps = []
    for b in range(8):
        m = dict(com)
        m["x_fm"] = np.ascontiguousarray(x[b].T)
        in_maps.append(m)
    return in_maps


_NC_CACHE = {}


def kernel(**inputs):
    in_maps = _host_layout(inputs)
    key = False
    if key not in _NC_CACHE:
        _NC_CACHE[key] = build_program(debug=False)
    nc = _NC_CACHE[key]
    res = run_bass_kernel_spmd(nc, in_maps, core_ids=list(range(8)))
    out = np.stack([np.asarray(r["out_fm"]).T for r in res.results], axis=0)
    return np.ascontiguousarray(out.astype(np.float32))


# version 21
# speedup vs baseline: 1.0815x; 1.0146x over previous
import sys
import numpy as np

try:
    import concourse.bass as bass
except ImportError:
    sys.path.insert(0, "/opt/trn_rl_repo")
    import concourse.bass as bass
import concourse.mybir as mybir
from concourse.bass_utils import run_bass_kernel_spmd
from contextlib import ExitStack

F32 = mybir.dt.float32
BF16 = mybir.dt.bfloat16
AF = mybir.ActivationFunctionType
ALU = mybir.AluOpType
AX = mybir.AxisListType

T = 2048
DM = 2048
NDC = 16
H = 16
HK = 4
GQ = 4
DH = 128
DFF = 1408
NFC = 11
NE = 18
NRT = 16
WINDOW = 1024
EPS_QK = 1e-6
EPS_LN = 1e-5
RBASE = 10000.0
TA = 512
NTI = T // TA
TF = 1024
NTF = T // TF
INV_SQRT_DH = float(1.0 / np.sqrt(DH))

DMA_SEMS = ("LW", "LX", "ST", "SX", "SKV", "SQA", "SQB", "SOA", "SOB", "S1A", "S1B", "S1C", "S2A", "S2B", "S2C", "SYA", "SYB", "SXB", "STA", "STB", "STC", "STD")
OWN_SEM = {"pe": "PE", "act": "ACT", "dve": "DVE", "pool": "PL", "sp": None}


class Sched:
    def __init__(self):
        self.cur = None

    def start(self, engname, eng, sems):
        self.cur = engname
        self.eng = eng
        self.sems = sems
        self.cnt = {k: 0 for k in sems}
        self.lastwait = {k: 0 for k in sems}

    def op(self, engname, fn, waits=(), inc=None):
        tok = None
        if inc is not None:
            self.cnt[inc] += 16 if inc in DMA_SEMS else 1
            tok = (inc, self.cnt[inc])
        if engname == self.cur:
            own = OWN_SEM.get(engname)
            for w in waits:
                if w is None:
                    continue
                s, v = w
                if s == own or v <= self.lastwait[s]:
                    continue
                self.eng.wait_ge(self.sems[s], v)
                self.lastwait[s] = v
            ins = fn(self.eng)
            if inc is not None:
                ins.then_inc(self.sems[inc], 16 if inc in DMA_SEMS else 1)
        return tok

    def join(self, engname, waits):
        if engname == self.cur:
            own = OWN_SEM.get(engname)
            for w in waits:
                if w is None:
                    continue
                s, v = w
                if s == own or v <= self.lastwait[s]:
                    continue
                self.eng.wait_ge(self.sems[s], v)
                self.lastwait[s] = v


def tmax(*toks):
    best = {}
    for t in toks:
        if t is None:
            continue
        s, v = t
        if v > best.get(s, -1):
            best[s] = v
    return [(s, v) for s, v in best.items()]


class NS:
    pass


def build_program(debug=False):
    nc = bass.Bass("TRN2", target_bir_lowering=False)

    D = NS()
    D.x_fm = nc.dram_tensor("x_fm", [DM, T], F32, kind="ExternalInput").ap()
    D.wq_sl = nc.dram_tensor("wq_sl", [H, 128, NDC, 128], F32, kind="ExternalInput").ap()
    D.wk_sl = nc.dram_tensor("wk_sl", [HK, 128, NDC, 128], F32, kind="ExternalInput").ap()
    D.wvT = nc.dram_tensor("wvT", [DM, HK * DH], F32, kind="ExternalInput").ap()
    D.wo_sl = nc.dram_tensor("wo_sl", [NDC, 128, NDC, 128], F32, kind="ExternalInput").ap()
    D.w1_sl = nc.dram_tensor("w1_sl", [NE, NFC, 128, NDC, 128], F32, kind="ExternalInput").ap()
    D.w2_sl = nc.dram_tensor("w2_sl", [NE, NDC, 128, NFC, 128], F32, kind="ExternalInput").ap()
    D.b1c = nc.dram_tensor("b1c", [128, NE * NFC], F32, kind="ExternalInput").ap()
    D.b2c = nc.dram_tensor("b2c", [128, 2 * NDC], F32, kind="ExternalInput").ap()
    D.b2r = nc.dram_tensor("b2r", [NRT, DM], F32, kind="ExternalInput").ap()
    D.cT = nc.dram_tensor("cT", [DM, NRT], F32, kind="ExternalInput").ap()
    D.bias_bc = nc.dram_tensor("bias_bc", [128, NRT], F32, kind="ExternalInput").ap()
    D.gq = nc.dram_tensor("gq", [128, 1], F32, kind="ExternalInput").ap()
    D.gk = nc.dram_tensor("gk", [128, 1], F32, kind="ExternalInput").ap()
    D.cos_t = nc.dram_tensor("cos_t", [128, T], F32, kind="ExternalInput").ap()
    D.sin_t = nc.dram_tensor("sin_t", [128, T], F32, kind="ExternalInput").ap()
    D.ones_c = nc.dram_tensor("ones_c", [128, 128], F32, kind="ExternalInput").ap()
    D.ident_c = nc.dram_tensor("ident_c", [128, 128], F32, kind="ExternalInput").ap()
    D.mcaus = nc.dram_tensor("mcaus", [128, 128], F32, kind="ExternalInput").ap()
    D.medge = nc.dram_tensor("medge", [128, 128], F32, kind="ExternalInput").ap()
    D.sel_c = nc.dram_tensor("sel_c", [NRT, NRT * 128], F32, kind="ExternalInput").ap()
    D.p64_c = nc.dram_tensor("p64_c", [128, 128], F32, kind="ExternalInput").ap()
    D.identu_c = nc.dram_tensor("identu_c", [128, 128], mybir.dt.uint8, kind="ExternalInput").ap()
    D.y1_sp = nc.dram_tensor("y1_spill", [DM, T], F32, kind="Internal").ap()
    D.out_fm = nc.dram_tensor("out_fm", [DM, T], F32, kind="ExternalOutput").ap()
    D.dbg = {}
    if debug:
        for nm, shape in (("d_lnx", (DM, T)), ("d_q", (DH, T)), ("d_k", (HK * DH, T)),
                          ("d_v", (T, HK * DH)), ("d_att", (DM, T)), ("d_y1", (DM, T)),
                          ("d_u", (DM, T)), ("d_g", (T, NRT)), ("d_h0", (DFF, TF)),
                          ("d_y0", (DH, T)),
                          ("d_acc", (DM, T))):
            D.dbg[nm] = nc.dram_tensor(nm, list(shape), F32, kind="ExternalOutput").ap()

    with ExitStack() as ctx:
        e = ctx.enter_context
        C = NS()
        C.cos_s = e(nc.sbuf_tensor("cos_s", [128, T], BF16))
        C.sin_s = e(nc.sbuf_tensor("sin_s", [128, T], BF16))
        C.mcaus_bf = e(nc.sbuf_tensor("mcaus_bf", [128, 128], BF16))
        C.medge_bf = e(nc.sbuf_tensor("medge_bf", [128, 128], BF16))
        C.id_bf = e(nc.sbuf_tensor("id_bf", [128, 128], BF16))
        C.id_f = e(nc.sbuf_tensor("id_f", [128, 128], F32))
        C.ones_bf = e(nc.sbuf_tensor("ones_bf", [128, 128], BF16))
        C.p64_bf = e(nc.sbuf_tensor("p64_bf", [128, 128], BF16))
        C.gq_s = e(nc.sbuf_tensor("gq_s", [128, 1], F32))
        C.gk_s = e(nc.sbuf_tensor("gk_s", [128, 1], F32))
        C.eps6 = e(nc.sbuf_tensor("eps6", [128, 1], F32))
        C.eps5 = e(nc.sbuf_tensor("eps5", [128, 1], F32))
        C.b1c_s = e(nc.sbuf_tensor("b1c_s", [128, NE * NFC], F32))
        C.b2c_s = e(nc.sbuf_tensor("b2c_s", [128, 2 * NDC], F32))
        C.b2sh_s = e(nc.sbuf_tensor("b2sh_s", [128, NDC], F32))
        C.b2r_bf = e(nc.sbuf_tensor("b2r_bf", [NRT, DM], BF16))
        C.sel_bf = e(nc.sbuf_tensor("sel_bf", [NRT, NRT * 128], BF16))
        C.cT_bf = e(nc.sbuf_tensor("cT_bf", [128, NDC * NRT], BF16))
        C.bias_bc_s = e(nc.sbuf_tensor("bias_bc_s", [128, NRT], F32))
        C.nmax = e(nc.sbuf_tensor("nmax", [128, 1], F32))
        C.sums = e(nc.sbuf_tensor("sums", [128, 1], F32))
        C.lnsum = e(nc.sbuf_tensor("lnsum", [128, 1], F32))
        C.bias2 = e(nc.sbuf_tensor("bias2", [128, 1], F32))
        C.sc16 = e(nc.sbuf_tensor("sc16", [128, NRT], F32))
        C.mx8 = e(nc.sbuf_tensor("mx8", [128, 8], F32))
        C.msk16 = e(nc.sbuf_tensor("msk16", [128, NRT], F32))
        C.g16 = e(nc.sbuf_tensor("g16", [128, NRT], F32))
        C.g_all = e(nc.sbuf_tensor("g_all", [128, 8 * NRT], F32))
        C.rs1 = e(nc.sbuf_tensor("rs1", [128, 1], F32))
        C.thr1 = e(nc.sbuf_tensor("thr1", [128, 1], F32))
        C.rsB = e(nc.sbuf_tensor("rsB", [128, 1], F32))
        C.rsC = e(nc.sbuf_tensor("rsC", [128, 1], F32))
        C.rsb_bf = e(nc.sbuf_tensor("rsb_bf", [128, 2], BF16))
        C.rs2 = e(nc.sbuf_tensor("rs2", [128, 2], F32))
        C.rep_s = e(nc.sbuf_tensor("rep_s", [128, 2 * 128], F32))
        C.negC = e(nc.sbuf_tensor("negC", [128, 1], F32))
        C.mqA = e(nc.sbuf_tensor("mqA", [128, 1], F32))
        C.gbc_bf = e(nc.sbuf_tensor("gbc_bf", [128, 128], BF16))
        C.eps35 = e(nc.sbuf_tensor("eps35", [128, 1], F32))
        C.diag_bf = e(nc.sbuf_tensor("diag_bf", [128, 2 * 128], BF16))
        C.id_u8 = e(nc.sbuf_tensor("id_u8", [128, 128], mybir.dt.uint8))

        sems = {}
        for s in ("LW", "LX", "ST", "PE", "ACT", "DVE", "PL", "SX", "SKV", "SQA", "SQB", "SOA", "SOB", "S1A", "S1B", "S1C", "S2A", "S2B", "S2C", "SYA", "SYB", "SXB", "STA", "STB", "STC", "STD"):
            sems[s] = e(nc.semaphore(s))

        block = e(nc.Block())
        sch = Sched()
        state = {}

        with ExitStack() as actx:
            ae = actx.enter_context
            A = NS()
            A.x_bf = ae(nc.sbuf_tensor("x_bf", [128, 2 * NDC * TA], BF16))
            A.sq_scr = ae(nc.sbuf_tensor("sq_scr", [128, 2 * TA], BF16))
            A.y1s = ae(nc.sbuf_tensor("y1s", [128, 4 * TA], F32))
            A.mean2_s = ae(nc.sbuf_tensor("mean2_s", [128, TA], F32))
            A.msqv2_s = ae(nc.sbuf_tensor("msqv2_s", [128, TA], F32))
            A.rstd2_s = ae(nc.sbuf_tensor("rstd2_s", [128, TA], F32))
            A.ln_bf = ae(nc.sbuf_tensor("ln_bf", [128, NDC * TA], BF16))
            A.ln_scr = ae(nc.sbuf_tensor("ln_scr", [128, TA], F32))
            A.mean_s = ae(nc.sbuf_tensor("mean_s", [128, TA], F32))
            A.msqv_s = ae(nc.sbuf_tensor("msqv_s", [128, TA], F32))
            A.rstd_s = ae(nc.sbuf_tensor("rstd_s", [128, TA], F32))
            A.k_full = ae(nc.sbuf_tensor("k_full", [128, HK * T], BF16))
            A.v_full = ae(nc.sbuf_tensor("v_full", [128, NDC * TA], BF16))
            A.q_ro = ae(nc.sbuf_tensor("q_ro", [128, TA], F32))
            A.q_fp = ae(nc.sbuf_tensor("q_fp", [128, 2 * TA], BF16))
            A.q_sq = ae(nc.sbuf_tensor("q_sq", [128, TA], BF16))
            A.rn_s = ae(nc.sbuf_tensor("rn_s", [128, TA], F32))
            A.q_bf = ae(nc.sbuf_tensor("q_bf", [128, TA], BF16))
            A.p_bf = ae(nc.sbuf_tensor("p_bf", [128, 2 * 1152], BF16))
            A.pT_sb = ae(nc.sbuf_tensor("pT_sb", [128, 1152], BF16))
            A.y_att = ae(nc.sbuf_tensor("y_att", [128, H * TA], BF16))
            A.att_bf = ae(nc.sbuf_tensor("att_bf", [128, NDC * TA], BF16))
            A.wbig = ae(nc.sbuf_tensor("wbig", [128, 4 * DM], BF16))
            A.wq_sb = ae(nc.sbuf_tensor("wq_sb", [128, 2 * DM], BF16))
            A.wo_sb = ae(nc.sbuf_tensor("wo_sb", [128, 2 * DM], BF16))
            A.pA = ae(nc.psum_tensor("pA", [128, 512], F32))
            A.pB = ae(nc.psum_tensor("pB", [128, 512], F32))
            A.pQ = ae(nc.psum_tensor("pQ", [128, 512], F32))
            A.pS = ae(nc.psum_tensor("pS", [128, 1536], F32))
            A.pTT = ae(nc.psum_tensor("pTT", [128, 2048], BF16))

            def attn_pass(engname, eng):
                sch.start(engname, eng, sems)
                ct = _sched_consts(sch, nc, D, C)
                af = _sched_attn(sch, nc, D, C, A, ct, debug)
                state[engname] = (ct, af, dict(sch.cnt), dict(sch.lastwait))

            block.tensor(lambda eng: attn_pass("pe", eng))
            block.scalar(lambda eng: attn_pass("act", eng))
            block.vector(lambda eng: attn_pass("dve", eng))
            block.gpsimd(lambda eng: attn_pass("pool", eng))
            block.sync(lambda eng: attn_pass("sp", eng))

        with ExitStack() as fctx:
            fe = fctx.enter_context
            Fb = NS()
            Fb.u_bf = fe(nc.sbuf_tensor("u_bf", [128, NDC * TF], BF16))
            Fb.acc = fe(nc.sbuf_tensor("acc", [128, NDC * TF], F32))
            Fb.h_bf = fe(nc.sbuf_tensor("h_bf", [128, NFC * TF], BF16))
            Fb.G_bf = fe(nc.sbuf_tensor("G_bf", [128, TF], BF16))
            Fb.gT_bf = fe(nc.sbuf_tensor("gT_bf", [NRT, TF], BF16))
            Fb.y1_scr = fe(nc.sbuf_tensor("y1_scr", [128, 2 * TF], F32))
            Fb.cast_f = fe(nc.sbuf_tensor("cast_f", [128, 2 * TF], BF16))
            Fb.sq_f = fe(nc.sbuf_tensor("sq_f", [128, TF], BF16))
            Fb.ln_f = fe(nc.sbuf_tensor("ln_f", [128, TF], F32))
            Fb.mean_f = fe(nc.sbuf_tensor("mean_f", [128, TF], F32))
            Fb.msqv_f = fe(nc.sbuf_tensor("msqv_f", [128, TF], F32))
            Fb.rstd_f = fe(nc.sbuf_tensor("rstd_f", [128, TF], F32))
            Fb.w1_sb = fe(nc.sbuf_tensor("w1_sb", [128, 3 * NDC * 128], BF16))
            Fb.w2_sb = fe(nc.sbuf_tensor("w2_sb", [128, 2 * NFC * 128], BF16))
            Fb.pHa = fe(nc.psum_tensor("pHa", [128, TF], F32))
            Fb.pHb = fe(nc.psum_tensor("pHb", [128, TF], F32))
            Fb.pYa = fe(nc.psum_tensor("pYa", [128, TF], F32))
            Fb.pYb = fe(nc.psum_tensor("pYb", [128, TF], F32))

            def ffn_pass(engname, eng):
                ct, af, cnt, lastwait = state[engname]
                sch.start(engname, eng, sems)
                sch.cnt = cnt
                sch.lastwait = lastwait
                _sched_ffn(sch, nc, D, C, Fb, ct, af, debug)

            block.tensor(lambda eng: ffn_pass("pe", eng))
            block.scalar(lambda eng: ffn_pass("act", eng))
            block.vector(lambda eng: ffn_pass("dve", eng))
            block.gpsimd(lambda eng: ffn_pass("pool", eng))
            block.sync(lambda eng: ffn_pass("sp", eng))

    return nc


def _sched_consts(sch, nc, D, C):
    ct = {}
    t = None
    for dst, src in ((C.gq_s, D.gq), (C.gk_s, D.gk),
                     (C.b1c_s, D.b1c), (C.bias_bc_s, D.bias_bc), (C.id_f, D.ident_c)):
        t = sch.op("sp", lambda E, d=dst, s=src: E.dma_start(out=d[:], in_=s), inc="LX")
    sch.op("sp", lambda E: E.dma_start(out=C.id_u8[:], in_=D.identu_c), inc="LX")
    t = sch.op("sp", lambda E: E.dma_start(
        out=C.b2c_s[:].rearrange("p (n c) -> p n c", n=2),
        in_=D.b2c.rearrange("p (n c) -> p n c", n=2)), inc="LX")
    ct["lx"] = t
    t = None
    for dst, src in ((C.ones_bf, D.ones_c), (C.id_bf, D.ident_c), (C.b2r_bf, D.b2r),
                     (C.sel_bf, D.sel_c), (C.cos_s, D.cos_t), (C.sin_s, D.sin_t),
                     (C.mcaus_bf, D.mcaus), (C.medge_bf, D.medge), (C.p64_bf, D.p64_c)):
        t = sch.op("pool", lambda E, d=dst, s=src: E.dma_start(out=d[:], in_=s), inc="LW")
    t = sch.op("pool", lambda E: E.dma_start(
        out=C.cT_bf[:].rearrange("p (c j) -> p c j", c=NDC),
        in_=D.cT.rearrange("(c p) j -> p c j", p=128)), inc="LW")
    ct["lw"] = t
    sch.op("dve", lambda E: E.memset(C.eps6[:], EPS_QK))
    sch.op("dve", lambda E: E.memset(C.diag_bf[:], 0.0))
    sch.op("dve", lambda E: E.memset(C.eps35[:], 1e-35))
    ct["eps"] = sch.op("dve", lambda E: E.memset(C.eps5[:], EPS_LN), inc="DVE")
    ct["b2sh"] = sch.op("dve", lambda E: E.tensor_add(
        C.b2sh_s[:], C.b2c_s[:, 0:NDC], C.b2c_s[:, NDC:2 * NDC]),
        waits=[ct["lx"]], inc="DVE")
    return ct


def _sched_attn(sch, nc, D, C, A, ct, debug):
    st = NS()
    st.wbig_free = None
    st.wq_free = [None, None]
    st.wo_free = [None, None]
    st.scr_free = [None, None]
    st.pA_free = ct["eps"]
    st.pB_free = None
    st.pQ_free = None
    st.pS_free = None
    st.pTT_free = None
    st.pbf_free = None
    st.pT_free = []
    st.rep_free = None
    st.repslot_free = [None, None]
    st.lnbf_free = None
    st.yatt_free = None
    st.attbf_free = []
    st.stat_free = []
    st.qfp_free = [None, None]
    st.xbf_free = [[], []]
    st.xld = [None, None]
    st.y1s_free = []
    st.qro_free = None
    st.proj_tok = None
    st.diag_free = [None, None]
    st.tr_free = [None, None]
    st.pbf_par_free = [None, None]
    st.vts = [None] * NDC
    t_spill = None

    def rope_start(h, par):
        t_ev = sch.op("act", lambda E, par=par: E.copy(
            A.q_fp[:, par * TA:(par + 1) * TA], A.pQ[:]),
            waits=tmax(st.proj_tok, st.qfp_free[par]), inc="ACT")
        st.pQ_free = t_ev
        return t_ev

    def rope_finish(t_ev, par, gamma_s, scale2, dst_fn, ti0):
        t_sw = sch.op("pe", lambda E, par=par: E.matmul(
            A.pA[:], C.p64_bf[:], A.q_fp[:, par * TA:(par + 1) * TA],
            start=True, stop=True),
            waits=tmax(t_ev, st.pA_free), inc="PE")
        sch.op("dve", lambda E, par=par: E.tensor_mul(
            A.ln_scr[:], A.q_fp[:, par * TA:(par + 1) * TA], C.cos_s[:, ti0:ti0 + TA]),
            waits=tmax(t_ev))
        sch.op("dve", lambda E: E.tensor_mul(
            A.q_ro[:], A.pA[:], C.sin_s[:, ti0:ti0 + TA]), waits=tmax(t_sw))
        t_rope = sch.op("dve", lambda E: E.tensor_add(
            A.q_ro[:], A.q_ro[:], A.ln_scr[:]), inc="DVE")
        st.pA_free = t_rope
        st.qfp_free[par] = t_rope
        t_sq = sch.op("dve", lambda E: E.tensor_mul(A.q_sq[:], A.q_ro[:], A.q_ro[:]),
                      inc="DVE")
        t_ss = sch.op("pe", lambda E: E.matmul(
            A.pB[:], C.ones_bf[:], A.q_sq[:], start=True, stop=True),
            waits=tmax(t_sq, st.pB_free), inc="PE")
        t_s1 = sch.op("act", lambda E: E.activation(A.rn_s[:], A.pB[:], AF.Sqrt),
                      waits=tmax(t_ss), inc="ACT")
        st.pB_free = t_s1
        sch.op("act", lambda E: E.activation(A.rn_s[:], A.rn_s[:], AF.Ln, bias=C.eps6[:]))
        t_rn = sch.op("act", lambda E: E.activation(A.rn_s[:], A.rn_s[:], AF.Exp, scale=-1.0),
                      inc="ACT")
        sch.op("dve", lambda E: E.tensor_mul(A.ln_scr[:], A.q_ro[:], A.rn_s[:]),
               waits=tmax(t_rn))
        if scale2 is None:
            t_fin = sch.op("dve", lambda E: E.tensor_scalar(
                dst_fn(), A.ln_scr[:], gamma_s[:], None, ALU.mult), inc="DVE")
        else:
            t_fin = sch.op("dve", lambda E: E.tensor_scalar(
                dst_fn(), A.ln_scr[:], gamma_s[:], scale2, ALU.mult, ALU.mult),
                inc="DVE")
        st.qro_free = t_fin
        return t_fin

    t_b1 = sch.op("dve", lambda E: E.tensor_copy(C.gbc_bf[:], C.gq_s[:].to_broadcast([128, 128])),
                  waits=tmax(ct["lx"]), inc="DVE")
    t_t1 = sch.op("pe", lambda E: E.transpose(A.pTT[:, 0:128], C.gbc_bf[:], C.id_bf[:]),
                  waits=tmax(t_b1, ct["lw"]), inc="PE")
    t_ev1 = sch.op("act", lambda E: E.copy(A.ln_scr[:, 0:128], A.pTT[:, 0:128]),
                   waits=tmax(t_t1), inc="ACT")
    t_m1 = sch.op("dve", lambda E: E.reduce_max(C.mqA[:], A.ln_scr[:, 0:128], AX.X,
                                                apply_absolute_value=True),
                  waits=tmax(t_ev1), inc="DVE")
    t_b2 = sch.op("dve", lambda E: E.tensor_copy(C.gbc_bf[:], C.gk_s[:].to_broadcast([128, 128])),
                  waits=tmax(t_t1), inc="DVE")
    t_t2 = sch.op("pe", lambda E: E.transpose(A.pTT[:, 128:256], C.gbc_bf[:], C.id_bf[:]),
                  waits=tmax(t_b2, t_ev1), inc="PE")
    t_ev2 = sch.op("act", lambda E: E.copy(A.ln_scr[:, 128:256], A.pTT[:, 128:256]),
                   waits=tmax(t_t2), inc="ACT")
    sch.op("dve", lambda E: E.reduce_max(C.negC[:], A.ln_scr[:, 128:256], AX.X,
                                         apply_absolute_value=True),
           waits=tmax(t_ev2))
    sch.op("dve", lambda E: E.tensor_mul(C.negC[:], C.negC[:], C.mqA[:]))
    t_negC = sch.op("dve", lambda E: E.tensor_scalar(
        C.negC[:], C.negC[:], -INV_SQRT_DH, None, ALU.mult), inc="DVE")
    t_m2 = t_negC
    st.pTT_free = t_ev2
    st.stat_free = [t_m2]

    for i in range(NTI):
        ti0 = i * TA
        par = i % 2
        if i == 0:
            st.xld[0] = sch.op("pool", lambda E: E.dma_start(
                out=A.x_bf[:, 0:NDC * TA].rearrange("p (c t) -> p c t", c=NDC),
                in_=D.x_fm.rearrange("(c p) t -> p c t", p=128)[:, :, 0:TA]),
                inc="SX")
        if i + 1 < NTI:
            pnxt = (i + 1) % 2
            st.xld[pnxt] = sch.op("pool", lambda E, i=i, pnxt=pnxt: E.dma_start(
                out=A.x_bf[:, pnxt * NDC * TA:(pnxt + 1) * NDC * TA].rearrange("p (c t) -> p c t", c=NDC),
                in_=D.x_fm.rearrange("(c p) t -> p c t", p=128)[:, :, (i + 1) * TA:(i + 2) * TA]),
                waits=tmax(*st.xbf_free[pnxt]), inc=("SX" if pnxt == 0 else "SXB"))
        t_x = st.xld[par]
        xb0 = par * NDC * TA
        t_wv = sch.op("pool", lambda E: E.dma_start(
            out=A.wbig[:, 0:NDC * 512].rearrange("p (c v) -> p c v", c=NDC),
            in_=D.wvT.rearrange("(c p) v -> p c v", p=128)),
            waits=tmax(st.wbig_free), inc="SKV")
        t_stats = None
        for dc in range(NDC):
            b = dc % 2
            t_cast = sch.op("act", lambda E, dc=dc, b=b, xb0=xb0: E.activation(
                A.sq_scr[:, b * TA:(b + 1) * TA], A.x_bf[:, xb0 + dc * TA: xb0 + (dc + 1) * TA],
                AF.Square), waits=tmax(t_x, st.scr_free[b]), inc="ACT")
            sch.op("pe", lambda E, b=b, dc=dc, xb0=xb0: E.matmul(
                A.pA[:], C.ones_bf[:], A.x_bf[:, xb0 + dc * TA: xb0 + (dc + 1) * TA],
                start=(dc == 0), stop=(dc == NDC - 1)),
                waits=tmax(t_cast,
                           st.pA_free if dc == 0 else None,
                           st.pB_free if dc == 0 else None,
                           ct["lw"] if dc == 0 else None))
            t_stats = sch.op("pe", lambda E, b=b, dc=dc: E.matmul(
                A.pB[:], C.ones_bf[:], A.sq_scr[:, b * TA:(b + 1) * TA],
                start=(dc == 0), stop=(dc == NDC - 1)), inc="PE")
            st.scr_free[b] = t_stats
        t_mean = sch.op("act", lambda E: E.mul(A.mean_s[:], A.pA[:], 1.0 / DM),
                        waits=tmax(t_stats, *st.stat_free), inc="ACT")
        t_msq = sch.op("act", lambda E: E.activation(A.msqv_s[:], A.mean_s[:], AF.Square),
                       inc="ACT")
        t_var = sch.op("dve", lambda E: E.scalar_tensor_tensor(
            A.msqv_s[:], A.pB[:], 1.0 / DM, A.msqv_s[:], ALU.mult, ALU.subtract),
            waits=tmax(t_msq, t_stats), inc="DVE")
        sch.op("act", lambda E: E.activation(
            A.rstd_s[:], A.msqv_s[:], AF.Ln, bias=C.eps5[:]),
            waits=tmax(t_var, ct["eps"]))
        t_rstd = sch.op("act", lambda E: E.activation(
            A.rstd_s[:], A.rstd_s[:], AF.Exp, scale=-0.5), inc="ACT")
        st.pA_free = t_mean
        st.pB_free = t_var
        t_ln = None
        t_ln_pl = None
        t_ln_dc = [None] * NDC
        for dc in range(NDC):
            if dc % 3 == 2:
                sch.op("pool", lambda E, dc=dc, xb0=xb0: E.tensor_sub(
                    A.q_ro[:], A.x_bf[:, xb0 + dc * TA: xb0 + (dc + 1) * TA], A.mean_s[:]),
                    waits=tmax(t_rstd, t_mean, st.lnbf_free, st.qro_free))
                t_ln_pl = sch.op("pool", lambda E, dc=dc: E.tensor_mul(
                    A.ln_bf[:, dc * TA:(dc + 1) * TA], A.q_ro[:], A.rstd_s[:]),
                    inc="PL")
                t_ln_dc[dc] = t_ln_pl
            else:
                sch.op("dve", lambda E, dc=dc, xb0=xb0: E.tensor_sub(
                    A.ln_scr[:], A.x_bf[:, xb0 + dc * TA: xb0 + (dc + 1) * TA], A.mean_s[:]),
                    waits=tmax(t_rstd, t_mean, st.lnbf_free))
                t_ln = sch.op("dve", lambda E, dc=dc: E.tensor_mul(
                    A.ln_bf[:, dc * TA:(dc + 1) * TA], A.ln_scr[:], A.rstd_s[:]),
                    inc="DVE")
                t_ln_dc[dc] = t_ln
        if debug:
            sch.op("pool", lambda E, ti0=ti0: E.dma_start(
                out=D.dbg["d_lnx"].rearrange("(c p) t -> p c t", p=128)[:, :, ti0:ti0 + TA],
                in_=A.ln_bf[:].rearrange("p (c t) -> p c t", c=NDC)),
                waits=tmax(t_ln), inc="LW")
        st.stat_free = [t_ln, t_ln_pl]
        t_vp = None
        for ts_ in range(4):
            gt = 4 * i + ts_

            def vproj(E, ts_=ts_):
                last = None
                for dc in range(NDC):
                    last = E.matmul(
                        A.pQ[:], A.ln_bf[:, dc * TA + ts_ * 128: dc * TA + ts_ * 128 + 128],
                        A.wbig[:, dc * 512:(dc + 1) * 512],
                        start=(dc == 0), stop=(dc == NDC - 1))
                return last
            t_vp = sch.op("pe", vproj,
                          waits=tmax(t_wv, st.pQ_free, t_ln, t_ln_pl,
                                     *([t_ln_dc[dc] for dc in range(NDC)] if ts_ == 0 else [])),
                          inc="PE")
            t_ve = sch.op("act", lambda E, gt=gt: E.copy(
                A.v_full[:, gt * 512:(gt + 1) * 512], A.pQ[:]),
                waits=tmax(t_vp), inc="ACT")
            st.vts[gt] = t_ve
            st.pQ_free = t_ve
        t_wk = sch.op("pool", lambda E: E.dma_start(
            out=A.wbig[:].rearrange("p (k c q) -> p k c q", k=HK, c=NDC),
            in_=D.wk_sl.rearrange("k p c q -> p k c q")),
            waits=tmax(t_vp), inc="SKV")
        t_kfin = None
        for hk in range(HK):
            def kproj(E, hk=hk):
                last = None
                for dc in range(NDC):
                    last = E.matmul(
                        A.pQ[:], A.wbig[:, hk * DM + dc * 128: hk * DM + (dc + 1) * 128],
                        A.ln_bf[:, dc * TA:(dc + 1) * TA],
                        start=(dc == 0), stop=(dc == NDC - 1))
                return last
            st.proj_tok = sch.op("pe", kproj,
                                 waits=tmax(t_ln, t_ln_pl, t_wk, st.pQ_free), inc="PE")
            t_ev = rope_start(hk, hk % 2)
            t_kfin = rope_finish(
                t_ev, hk % 2, C.gk_s, None,
                lambda hk=hk, ti0=ti0: A.k_full[:, hk * T + ti0: hk * T + ti0 + TA], ti0)
        if debug:
            sch.op("pool", lambda E, ti0=ti0: E.dma_start(
                out=D.dbg["d_k"].rearrange("(c p) t -> p c t", p=128)[:, :, ti0:ti0 + TA],
                in_=A.k_full[:].rearrange("p (c t) -> p c t", c=HK)[:, :, ti0:ti0 + TA]),
                waits=tmax(t_kfin), inc="LW")
            sch.op("pool", lambda E, i=i: E.dma_start(
                out=D.dbg["d_v"].rearrange("(c p) v -> p c v", p=128)[:, 4 * i:4 * i + 4, :],
                in_=A.v_full[:].rearrange("p (c v) -> p c v", c=NDC)[:, 4 * i:4 * i + 4, :]),
                waits=tmax(st.vts[4 * i + 3]), inc="LW")
        t_wq = [None] * H

        def load_wq(h):
            return sch.op("pool", lambda E, h=h: E.dma_start(
                out=A.wq_sb[:, (h % 2) * DM:(h % 2 + 1) * DM].rearrange("p (c q) -> p c q", c=NDC),
                in_=D.wq_sl[h]),
                waits=tmax(st.wq_free[h % 2]), inc=("SQA" if h % 2 == 0 else "SQB"))

        t_wq[0] = load_wq(0)
        t_wq[1] = load_wq(1)
        t_qfin = [None] * H
        pend = None

        def q_dst(h, ti0=ti0):
            return lambda h=h: A.wbig[:, h * TA:(h + 1) * TA]

        for h in range(H):
            def qproj(E, h=h):
                last = None
                for dc in range(NDC):
                    last = E.matmul(
                        A.pQ[:], A.wq_sb[:, (h % 2) * DM + dc * 128:(h % 2) * DM + (dc + 1) * 128],
                        A.ln_bf[:, dc * TA:(dc + 1) * TA],
                        start=(dc == 0), stop=(dc == NDC - 1))
                return last
            st.proj_tok = sch.op("pe", qproj,
                                 waits=tmax(t_wq[h], st.pQ_free, t_ln, t_ln_pl,
                                            st.wbig_free if h == 0 else None),
                                 inc="PE")
            t_qp = st.proj_tok
            st.wq_free[h % 2] = t_qp
            if h + 2 < H:
                t_wq[h + 2] = load_wq(h + 2)
            t_ev = rope_start(h, h % 2)
            if pend is not None:
                ph, pev = pend
                t_qfin[ph] = rope_finish(pev, ph % 2, C.gq_s, INV_SQRT_DH, q_dst(ph), ti0)
            pend = (h, t_ev)
        ph, pev = pend
        t_qfin[ph] = rope_finish(pev, ph % 2, C.gq_s, INV_SQRT_DH, q_dst(ph), ti0)
        if debug:
            sch.op("pool", lambda E, ti0=ti0: E.dma_start(
                out=D.dbg["d_q"][:, ti0:ti0 + TA], in_=A.wbig[:, 0:TA]),
                waits=tmax(t_qfin[0]), inc="LW")

        core = []
        for h in range(H):
            for sub in range(4):
                core.append((h, sub))
        pend_tp = None
        t_e1 = None
        t_sc_last = None
        for idx, (h, sub) in enumerate(core):
            hk = h // GQ
            gsub = 4 * i + sub
            c_lo = max(0, gsub - 8)
            nch = gsub - c_lo + 1
            W = 128 * nch

            def scmm(E, h=h, sub=sub, c_lo=c_lo, W=W, hk=hk, has_edge=(gsub >= 8)):
                last = None
                for off in range(0, W, 512):
                    n = min(512, W - off)
                    has_mask = (has_edge and off == 0) or (W - 128 >= off and W - 128 < off + n)
                    last = E.matmul(
                        A.pS[:, off:off + n],
                        A.wbig[:, h * TA + sub * 128: h * TA + sub * 128 + 128],
                        A.k_full[:, hk * T + c_lo * 128 + off: hk * T + c_lo * 128 + off + n],
                        start=True, stop=not has_mask)
                    if has_edge and off == 0:
                        last = E.matmul(A.pS[:, 0:128], C.id_bf[:], C.medge_bf[:],
                                        start=False,
                                        stop=not (W - 128 < off + n))
                    if W - 128 >= off and W - 128 < off + n:
                        last = E.matmul(A.pS[:, W - 128:W], C.id_bf[:], C.mcaus_bf[:],
                                        start=False, stop=True)
                return last
            t_sc = sch.op("pe", scmm, waits=tmax(t_qfin[h], st.pS_free, t_kfin, ct["lw"]),
                          inc="PE")
            t_sc_last = t_sc
            par = idx % 2
            t_e1 = sch.op("act", lambda E, W=W, par=par: E.activation(
                A.p_bf[:, par * 1152: par * 1152 + W], A.pS[:, 0:W], AF.Exp,
                bias=C.negC[:]),
                waits=tmax(t_sc, t_negC, st.pbf_par_free[par]), inc="ACT")
            st.pS_free = t_e1
            t_diag = t_e1
            def emit_tp(ph, psub, pnch, pc_lo, phk, pW, t_pe1, t_pdiag, ppar):
                def trs(E, nch=pnch, par=ppar):
                    last = None
                    for c in range(nch):
                        last = E.transpose(A.pTT[:, c * 128:(c + 1) * 128],
                                           A.p_bf[:, par * 1152 + c * 128: par * 1152 + (c + 1) * 128],
                                           C.id_bf[:])
                    return last
                t_tr = sch.op("pe", trs, waits=tmax(t_pe1, st.pTT_free), inc="PE")
                st.pbf_par_free[ppar] = t_tr
                st.tr_free[ppar] = t_tr
                t_pte = sch.op("dve", lambda E, W=pW: E.tensor_copy(A.pT_sb[:, 0:W], A.pTT[:, 0:W]),
                               waits=tmax(t_tr, *st.pT_free), inc="DVE")
                st.pTT_free = t_pte

                def summ(E, nch=pnch):
                    last = None
                    for c in range(nch):
                        last = E.matmul(
                            A.pQ[:, 0:128], C.ones_bf[:],
                            A.pT_sb[:, c * 128:(c + 1) * 128],
                            start=(c == 0), stop=(c == nch - 1))
                    return last
                t_summ = sch.op("pe", summ,
                                waits=tmax(t_pte, st.rep_free, st.pQ_free), inc="PE")
                sch.op("act", lambda E, par=ppar: E.activation(
                    C.rep_s[:, par * 128:(par + 1) * 128], A.pQ[:, 0:128], AF.Ln,
                    bias=C.eps35[:]),
                    waits=tmax(t_summ, st.repslot_free[ppar]))
                t_rep = sch.op("act", lambda E, par=ppar: E.activation(
                    C.rep_s[:, par * 128:(par + 1) * 128],
                    C.rep_s[:, par * 128:(par + 1) * 128], AF.Exp, scale=-1.0),
                    inc="ACT")
                st.rep_free = t_rep
                st.pQ_free = t_rep

                pvd = A.pB if ppar == 0 else A.pA
                def pv(E, nch=pnch, c_lo=pc_lo, hk=phk, pvd=pvd):
                    last = None
                    for c in range(nch):
                        last = E.matmul(
                            pvd[:, 0:128],
                            A.v_full[:, (c_lo + c) * 512 + hk * 128: (c_lo + c) * 512 + hk * 128 + 128],
                            A.pT_sb[:, c * 128:(c + 1) * 128],
                            start=(c == 0), stop=(c == nch - 1))
                    return last
                t_pv = sch.op("pe", pv,
                              waits=tmax(t_pte,
                                         st.pB_free if ppar == 0 else st.pA_free,
                                         st.vts[4 * i + psub]), inc="PE")
                st.pT_free = [t_pv, t_summ]
                t_ye = sch.op("dve", lambda E, h=ph, sub=psub, par=ppar, pvd=pvd: E.tensor_mul(
                    A.y_att[:, h * TA + sub * 128: h * TA + sub * 128 + 128],
                    pvd[:, 0:128], C.rep_s[:, par * 128:(par + 1) * 128]),
                    waits=tmax(t_pv, t_rep,
                               st.yatt_free if (ph == 0 and psub == 0) else None),
                    inc="DVE")
                if ppar == 0:
                    st.pB_free = t_ye
                else:
                    st.pA_free = t_ye
                st.repslot_free[ppar] = t_ye
                return t_ye
            if pend_tp is not None:
                t_ye = emit_tp(*pend_tp)
            pend_tp = (h, sub, nch, c_lo, hk, W, t_e1, t_diag, par)
        t_ye = emit_tp(*pend_tp)
        t_ye_last = t_ye
        st.wbig_free = t_sc_last
        if debug:
            sch.op("pool", lambda E, ti0=ti0: E.dma_start(
                out=D.dbg["d_y0"][:, ti0:ti0 + TA], in_=A.y_att[:, 0:TA]),
                waits=tmax(t_ye), inc="LW")
        t_wo = {}

        def load_wo(dcp):
            t_wo[dcp] = sch.op("pool", lambda E, dcp=dcp: E.dma_start(
                out=A.wo_sb[:, (dcp % 2) * DM:(dcp % 2 + 1) * DM].rearrange("p (c q) -> p c q", c=NDC),
                in_=D.wo_sl[dcp]),
                waits=tmax(st.wo_free[dcp % 2]), inc=("SOA" if dcp % 2 == 0 else "SOB"))

        load_wo(0)
        load_wo(1)
        t_stats2 = None
        t_wmm = None
        for dcp in range(NDC):
            def womm(E, dcp=dcp):
                last = None
                for ec in range(NDC):
                    last = E.matmul(
                        A.pQ[:], A.wo_sb[:, (dcp % 2) * DM + ec * 128:(dcp % 2) * DM + (ec + 1) * 128],
                        A.y_att[:, ec * TA:(ec + 1) * TA],
                        start=(ec == 0), stop=(ec == NDC - 1))
                return last
            t_wmm = sch.op("pe", womm,
                           waits=tmax(t_wo[dcp], t_ye_last, st.pQ_free), inc="PE")
            st.wo_free[dcp % 2] = t_wmm
            if dcp + 2 < NDC:
                load_wo(dcp + 2)
            t_ae = sch.op("act", lambda E, dcp=dcp: E.copy(
                A.att_bf[:, dcp * TA:(dcp + 1) * TA], A.pQ[:]),
                waits=tmax(t_wmm, *(st.attbf_free if dcp == 0 else [])), inc="ACT")
            st.pQ_free = t_ae
            b = dcp % 2
            t_sqc = sch.op("act", lambda E, dcp=dcp, b=b: E.activation(
                A.sq_scr[:, b * TA:(b + 1) * TA], A.att_bf[:, dcp * TA:(dcp + 1) * TA],
                AF.Square),
                waits=tmax(st.scr_free[b]), inc="ACT")
            sch.op("pe", lambda E, dcp=dcp: E.matmul(
                A.pA[:], C.ones_bf[:], A.att_bf[:, dcp * TA:(dcp + 1) * TA],
                start=(dcp == 0), stop=(dcp == NDC - 1)),
                waits=tmax(t_sqc,
                           st.pA_free if dcp == 0 else None,
                           st.pB_free if dcp == 0 else None))
            t_stats2 = sch.op("pe", lambda E, dcp=dcp, b=b: E.matmul(
                A.pB[:], C.ones_bf[:], A.sq_scr[:, b * TA:(b + 1) * TA],
                start=(dcp == 0), stop=(dcp == NDC - 1)), inc="PE")
            st.scr_free[b] = t_stats2
        if debug:
            sch.op("pool", lambda E, ti0=ti0: E.dma_start(
                out=D.dbg["d_att"].rearrange("(c p) t -> p c t", p=128)[:, :, ti0:ti0 + TA],
                in_=A.att_bf[:].rearrange("p (c t) -> p c t", c=NDC)),
                waits=tmax(t_stats2), inc="LW")
        t_mean2 = sch.op("act", lambda E: E.mul(A.mean2_s[:], A.pA[:], 1.0 / DM),
                         waits=tmax(t_stats2, t_ln, t_ln_pl), inc="ACT")
        t_msq2 = sch.op("act", lambda E: E.activation(A.msqv2_s[:], A.mean2_s[:], AF.Square),
                        inc="ACT")
        t_var2 = sch.op("dve", lambda E: E.scalar_tensor_tensor(
            A.msqv2_s[:], A.pB[:], 1.0 / DM, A.msqv2_s[:], ALU.mult, ALU.subtract),
            waits=tmax(t_msq2, t_stats2), inc="DVE")
        sch.op("act", lambda E: E.activation(
            A.rstd2_s[:], A.msqv2_s[:], AF.Ln, bias=C.eps5[:]),
            waits=tmax(t_var2))
        t_rstd2 = sch.op("act", lambda E: E.activation(
            A.rstd2_s[:], A.rstd2_s[:], AF.Exp, scale=-0.5), inc="ACT")
        st.pA_free = t_mean2
        st.pB_free = t_var2
        t_res = None
        t_res_pl = None
        t_sp_dc = [None] * NDC
        for dc in range(NDC):
            b = dc % 4
            warslot = t_sp_dc[dc - 4] if dc >= 4 else (st.y1s_free[dc] if dc < len(st.y1s_free) else None)
            if dc % 2 == 1:
                sch.op("pool", lambda E, dc=dc: E.tensor_sub(
                    A.q_ro[:], A.att_bf[:, dc * TA:(dc + 1) * TA], A.mean2_s[:]),
                    waits=tmax(t_rstd2, t_mean2, st.qro_free))
                sch.op("pool", lambda E: E.tensor_mul(A.q_ro[:], A.q_ro[:], A.rstd2_s[:]))
                t_res_pl = sch.op("pool", lambda E, dc=dc, b=b, xb0=xb0: E.tensor_add(
                    A.y1s[:, b * TA:(b + 1) * TA], A.q_ro[:],
                    A.x_bf[:, xb0 + dc * TA: xb0 + (dc + 1) * TA]),
                    waits=tmax(warslot), inc="PL")
                t_last = t_res_pl
            else:
                sch.op("dve", lambda E, dc=dc: E.tensor_sub(
                    A.ln_scr[:], A.att_bf[:, dc * TA:(dc + 1) * TA], A.mean2_s[:]),
                    waits=tmax(t_rstd2, t_mean2))
                sch.op("dve", lambda E: E.tensor_mul(A.ln_scr[:], A.ln_scr[:], A.rstd2_s[:]))
                t_res = sch.op("dve", lambda E, dc=dc, b=b, xb0=xb0: E.tensor_add(
                    A.y1s[:, b * TA:(b + 1) * TA], A.ln_scr[:],
                    A.x_bf[:, xb0 + dc * TA: xb0 + (dc + 1) * TA]),
                    waits=tmax(warslot), inc="DVE")
                t_last = t_res
            t_sp_dc[dc] = sch.op("sp", lambda E, dc=dc, b=b, ti0=ti0: E.dma_start(
                out=D.y1_sp[dc * 128:(dc + 1) * 128, ti0:ti0 + TA],
                in_=A.y1s[:, b * TA:(b + 1) * TA]),
                waits=tmax(t_last), inc=("STA", "STB", "STC", "STD")[b])
        st.y1s_free = [t_sp_dc[NDC - 4], t_sp_dc[NDC - 3], t_sp_dc[NDC - 2], t_sp_dc[NDC - 1]]
        t_spill = st.y1s_free
        st.xbf_free[par] = [t_res, t_res_pl]
        st.attbf_free = [t_res, t_res_pl]
        st.qro_free = t_res_pl
        st.lnbf_free = t_qp
        st.yatt_free = t_wmm

    return {"pe": ("PE", sch.cnt["PE"]), "act": ("ACT", sch.cnt["ACT"]),
            "dve": ("DVE", sch.cnt["DVE"]), "spill": t_spill}


def _sched_ffn(sch, nc, D, C, Fb, ct, af, debug):
    t_spill_all = af["spill"]
    if not isinstance(t_spill_all, list):
        t_spill_all = [t_spill_all]
    allf = [af["pe"], af["act"], af["dve"]]
    for en in ("pe", "act", "dve", "pool", "sp"):
        sch.join(en, tmax(*allf))

    st = NS()
    st.y1_free = [None, None]
    st.cast_free = [None, None]
    st.u_free = None
    st.acc_free = None
    st.h_free = [None] * NFC
    st.pH_free = [None, None]
    st.pY_free = [None, None]
    st.G_free = None
    st.gT_free = None
    st.stat_free = None
    st.w1free = [None, None, None]
    st.w2free = [None, None]
    st.gall_free = [None] * 8
    st.sqf_free = None
    st.g16_free = None
    t_w1 = {}
    t_w2 = {}

    def load_w1(j, fc):
        idx = (j * NFC + fc) % 3
        t_w1[(j, fc)] = sch.op("pool", lambda E, j=j, fc=fc, idx=idx: E.dma_start(
            out=Fb.w1_sb[:, idx * DM:(idx + 1) * DM].rearrange("p (c q) -> p c q", c=NDC),
            in_=D.w1_sl[j, fc]),
            waits=tmax(st.w1free[idx]), inc=("S1A", "S1B", "S1C")[idx])

    def load_w2(j, dcp):
        idx = (j * NDC + dcp) % 2
        t_w2[(j, dcp)] = sch.op("pool", lambda E, j=j, dcp=dcp, idx=idx: E.dma_start(
            out=Fb.w2_sb[:, idx * NFC * 128:(idx + 1) * NFC * 128].rearrange("p (c q) -> p c q", c=NFC),
            in_=D.w2_sl[j, dcp]),
            waits=tmax(st.w2free[idx]), inc=("S2A", "S2B")[idx])

    for f in range(NTF):
        tf0 = f * TF
        load_w1(0, 0)
        load_w1(0, 1)
        load_w1(0, 2)
        load_w2(0, 0)
        load_w2(0, 1)
        t_st3 = None
        for dc in range(NDC):
            b = dc % 2
            t_l = sch.op("sp", lambda E, dc=dc, b=b, tf0=tf0: E.dma_start(
                out=Fb.y1_scr[:, b * TF:(b + 1) * TF],
                in_=D.y1_sp[dc * 128:(dc + 1) * 128, tf0:tf0 + TF]),
                waits=tmax(st.y1_free[b], *t_spill_all), inc=("SYA" if b == 0 else "SYB"))
            sch.op("act", lambda E, b=b: E.copy(
                Fb.cast_f[:, b * TF:(b + 1) * TF], Fb.y1_scr[:, b * TF:(b + 1) * TF]),
                waits=tmax(t_l, st.cast_free[b]))
            t_cf = sch.op("act", lambda E, b=b: E.activation(
                Fb.sq_f[:], Fb.y1_scr[:, b * TF:(b + 1) * TF],
                AF.Square), waits=tmax(st.sqf_free), inc="ACT")
            st.y1_free[b] = t_cf

            def stats3(E, dc=dc, b=b):
                last = None
                for hh in range(2):
                    last = E.matmul(Fb.pHa[:, hh * 512:(hh + 1) * 512], C.ones_bf[:],
                                    Fb.cast_f[:, b * TF + hh * 512: b * TF + (hh + 1) * 512],
                                    start=(dc == 0), stop=(dc == NDC - 1))
                for hh in range(2):
                    last = E.matmul(Fb.pHb[:, hh * 512:(hh + 1) * 512], C.ones_bf[:],
                                    Fb.sq_f[:, hh * 512:(hh + 1) * 512],
                                    start=(dc == 0), stop=(dc == NDC - 1))
                return last
            t_st3 = sch.op("pe", stats3,
                           waits=tmax(t_cf,
                                      st.pH_free[0] if dc == 0 else None,
                                      st.pH_free[1] if dc == 0 else None),
                           inc="PE")
            st.cast_free[b] = t_st3
            st.sqf_free = t_st3
        t_mean3 = sch.op("act", lambda E: E.mul(Fb.mean_f[:], Fb.pHa[:], 1.0 / DM),
                         waits=tmax(t_st3, st.stat_free), inc="ACT")
        t_msq3 = sch.op("act", lambda E: E.activation(Fb.msqv_f[:], Fb.mean_f[:], AF.Square),
                        inc="ACT")
        t_var3 = sch.op("dve", lambda E: E.scalar_tensor_tensor(
            Fb.msqv_f[:], Fb.pHb[:], 1.0 / DM, Fb.msqv_f[:], ALU.mult, ALU.subtract),
            waits=tmax(t_msq3, t_st3), inc="DVE")
        sch.op("act", lambda E: E.activation(
            Fb.rstd_f[:], Fb.msqv_f[:], AF.Ln, bias=C.eps5[:]),
            waits=tmax(t_var3))
        t_rstd3 = sch.op("act", lambda E: E.activation(
            Fb.rstd_f[:], Fb.rstd_f[:], AF.Exp, scale=-0.5), inc="ACT")
        st.pH_free = [t_mean3, t_var3]
        t_u = None
        t_u_pl = None
        for dc in range(NDC):
            b = dc % 2
            t_l = sch.op("sp", lambda E, dc=dc, b=b, tf0=tf0: E.dma_start(
                out=Fb.y1_scr[:, b * TF:(b + 1) * TF],
                in_=D.y1_sp[dc * 128:(dc + 1) * 128, tf0:tf0 + TF]),
                waits=tmax(st.y1_free[b], *t_spill_all), inc=("SYA" if b == 0 else "SYB"))
            if dc % 3 == 2:
                lnp = Fb.cast_f[:].bitcast(F32)
                sch.op("pool", lambda E, b=b, lnp=lnp: E.tensor_sub(
                    lnp, Fb.y1_scr[:, b * TF:(b + 1) * TF], Fb.mean_f[:]),
                    waits=tmax(t_l, t_rstd3, t_mean3, st.u_free, st.cast_free[0],
                               st.cast_free[1]))
                t_u_pl = sch.op("pool", lambda E, dc=dc, lnp=lnp: E.tensor_mul(
                    Fb.u_bf[:, dc * TF:(dc + 1) * TF], lnp, Fb.rstd_f[:]),
                    inc="PL")
                st.y1_free[b] = t_u_pl
            else:
                sch.op("dve", lambda E, b=b: E.tensor_sub(
                    Fb.ln_f[:], Fb.y1_scr[:, b * TF:(b + 1) * TF], Fb.mean_f[:]),
                    waits=tmax(t_l, t_rstd3, t_mean3, st.u_free))
                t_u = sch.op("dve", lambda E, dc=dc: E.tensor_mul(
                    Fb.u_bf[:, dc * TF:(dc + 1) * TF], Fb.ln_f[:], Fb.rstd_f[:]),
                    inc="DVE")
                st.y1_free[b] = t_u
        if debug:
            sch.op("pool", lambda E, tf0=tf0: E.dma_start(
                out=D.dbg["d_u"].rearrange("(c p) t -> p c t", p=128)[:, :, tf0:tf0 + TF],
                in_=Fb.u_bf[:].rearrange("p (c t) -> p c t", c=NDC)),
                waits=tmax(t_u), inc="LW")
        t_rmm = [None] * 8
        t_sg_prev = None
        t_msk_prev = None
        t_gmul_prev = None
        t_ln_prev = None
        t_final_prev = None
        t_gfin = [None] * 8
        for ts_ in range(8):
            def rmm(E, ts_=ts_):
                last = None
                for dc in range(NDC):
                    last = E.matmul(
                        Fb.pYa[:, ts_ * 64: ts_ * 64 + NRT],
                        Fb.u_bf[:, dc * TF + ts_ * 128: dc * TF + ts_ * 128 + 128],
                        C.cT_bf[:, dc * NRT:(dc + 1) * NRT],
                        start=(dc == 0), stop=(dc == NDC - 1))
                return last
            t_rmm[ts_] = sch.op("pe", rmm,
                                waits=tmax(t_u, t_u_pl, st.pY_free[0] if ts_ == 0 else None,
                                           ct["lw"]),
                                inc="PE")
        for ts_ in range(8):
            t_sg = sch.op("act", lambda E, ts_=ts_: E.activation(
                C.sc16[:], Fb.pYa[:, ts_ * 64: ts_ * 64 + NRT], AF.Sigmoid),
                waits=tmax(t_rmm[7], t_gmul_prev, t_msk_prev), inc="ACT")
            t_bias = sch.op("pool", lambda E: E.tensor_add(C.sc16[:], C.sc16[:], C.bias_bc_s[:]),
                            waits=tmax(t_sg, ct["lx"]), inc="PL")
            t_mx = sch.op("dve", lambda E: E.max(out=C.mx8[:], in_=C.sc16[:]),
                          waits=tmax(t_bias), inc="DVE")
            t_thr = sch.op("pool", lambda E: E.tensor_copy(C.thr1[:], C.mx8[:, 3:4]),
                           waits=tmax(t_mx), inc="PL")
            t_msk = sch.op("dve", lambda E: E.tensor_tensor(
                C.msk16[:], C.sc16[:], C.thr1[:].to_broadcast([128, NRT]), ALU.is_ge),
                waits=tmax(t_thr), inc="DVE")
            t_msk_prev = t_msk
            t_gmul = sch.op("pool", lambda E, ts_=ts_: E.tensor_mul(
                C.g_all[:, ts_ * NRT:(ts_ + 1) * NRT], C.sc16[:], C.msk16[:]),
                waits=tmax(t_msk, st.gall_free[ts_], t_final_prev), inc="PL")
            t_gmul_prev = t_gmul
            t_red = sch.op("dve", lambda E, ts_=ts_: E.reduce_sum(
                C.rs1[:], C.g_all[:, ts_ * NRT:(ts_ + 1) * NRT], AX.X),
                waits=tmax(t_gmul), inc="DVE")
            t_rsc = sch.op("pool", lambda E: E.tensor_copy(C.rsB[:], C.rs1[:]),
                           waits=tmax(t_red, t_ln_prev), inc="PL")
            t_eps = sch.op("dve", lambda E: E.tensor_scalar_max(C.rsB[:], C.rsB[:], 1e-8),
                           waits=tmax(t_rsc), inc="DVE")
            t_ln = sch.op("act", lambda E: E.activation(C.rsC[:], C.rsB[:], AF.Ln),
                          waits=tmax(t_eps, t_final_prev), inc="ACT")
            t_ln_prev = t_ln
            t_exp = sch.op("act", lambda E: E.activation(C.rsC[:], C.rsC[:], AF.Exp, scale=-1.0),
                           inc="ACT")
            t_final = sch.op("dve", lambda E, ts_=ts_: E.tensor_tensor(
                C.g_all[:, ts_ * NRT:(ts_ + 1) * NRT], C.g_all[:, ts_ * NRT:(ts_ + 1) * NRT],
                C.rsC[:].to_broadcast([128, NRT]), ALU.mult),
                waits=tmax(t_exp), inc="DVE")
            t_final_prev = t_final
            t_gfin[ts_] = t_final
            if debug:
                t_gd = sch.op("sp", lambda E, ts_=ts_, tf0=tf0: E.dma_start(
                    out=D.dbg["d_g"][tf0 + ts_ * 128: tf0 + (ts_ + 1) * 128, :],
                    in_=C.g_all[:, ts_ * NRT:(ts_ + 1) * NRT]),
                    waits=tmax(t_final), inc="LX")
        st.pY_free[0] = t_sg
        t_hm = None
        t_ym = None
        t_hev = [None] * NFC
        t_ac17 = [None] * NDC
        for j in range(NE):
            routed = j >= 2
            jr = j - 2
            t_ge = None
            if routed:
                def gmm(E, jr=jr):
                    last = None
                    for hh in range(2):
                        last = E.matmul(Fb.pHa[:, hh * 512:(hh + 1) * 512],
                                        C.sel_bf[:, jr * 128:(jr + 1) * 128],
                                        Fb.gT_bf[:, hh * 512:(hh + 1) * 512],
                                        start=True, stop=True)
                    return last
                t_gm = sch.op("pe", gmm, waits=tmax(t_gT, st.pH_free[0]), inc="PE")
                t_ge = sch.op("act", lambda E: E.copy(Fb.G_bf[:], Fb.pHa[:]),
                              waits=tmax(t_gm, st.G_free), inc="ACT")
                st.pH_free[0] = t_ge
            for fc in range(NFC):
                pH = Fb.pHa if fc % 2 == 0 else Fb.pHb
                pidx = fc % 2
                widx = (j * NFC + fc) % 3

                def hmm(E, pH=pH, widx=widx):
                    last = None
                    for dc in range(NDC):
                        for hh in range(2):
                            last = E.matmul(
                                pH[:, hh * 512:(hh + 1) * 512],
                                Fb.w1_sb[:, widx * DM + dc * 128: widx * DM + (dc + 1) * 128],
                                Fb.u_bf[:, dc * TF + hh * 512: dc * TF + (hh + 1) * 512],
                                start=(dc == 0), stop=(dc == NDC - 1))
                    return last
                t_hm = sch.op("pe", hmm,
                              waits=tmax(t_w1[(j, fc)], t_u, t_u_pl, st.pH_free[pidx]), inc="PE")
                st.w1free[widx] = t_hm
                nj, nfc = (j, fc + 3) if fc + 3 < NFC else (j + 1, fc + 3 - NFC)
                if nj < NE:
                    load_w1(nj, nfc)
                if j == 0 and fc < 8:
                    ts_ = fc
                    t_gtr = sch.op("pe", lambda E, ts_=ts_: E.transpose(
                        Fb.pYb[0:NRT, ts_ * 128:(ts_ + 1) * 128],
                        C.g_all[:, ts_ * NRT:(ts_ + 1) * NRT], C.id_f[:]),
                        waits=tmax(t_gfin[ts_], st.pY_free[1] if ts_ == 0 else None),
                        inc="PE")
                    st.gall_free[ts_] = t_gtr
                    if ts_ == 7:
                        t_gT = sch.op("act", lambda E: E.copy(Fb.gT_bf[:], Fb.pYb[0:NRT, :]),
                                      waits=tmax(t_gtr, st.gT_free), inc="ACT")
                        st.pY_free[1] = t_gT
                t_he = sch.op("act", lambda E, fc=fc, pH=pH, j=j: E.activation(
                    Fb.h_bf[:, fc * TF:(fc + 1) * TF], pH[:], AF.Relu,
                    bias=C.b1c_s[:, j * NFC + fc: j * NFC + fc + 1]),
                    waits=tmax(t_hm, st.h_free[fc], ct["lx"]), inc="ACT")
                st.pH_free[pidx] = t_he
                t_hev[fc] = t_he
                if routed:
                    t_hg = sch.op("dve", lambda E, fc=fc: E.tensor_mul(
                        Fb.h_bf[:, fc * TF:(fc + 1) * TF],
                        Fb.h_bf[:, fc * TF:(fc + 1) * TF], Fb.G_bf[:]),
                        waits=tmax(t_he, t_ge), inc="DVE")
                    t_hev[fc] = t_hg
            if routed:
                st.G_free = t_hev[NFC - 1]
            if debug and j == 0 and f == 0:
                sch.op("pool", lambda E: E.dma_start(
                    out=D.dbg["d_h0"].rearrange("(c p) t -> p c t", p=128),
                    in_=Fb.h_bf[:].rearrange("p (c t) -> p c t", c=NFC)),
                    waits=tmax(t_hev[NFC - 1]), inc="LW")
            for dcp in range(NDC):
                pY = Fb.pYa if dcp % 2 == 0 else Fb.pYb
                pidx = dcp % 2
                widx = (j * NDC + dcp) % 2

                def ymm(E, dcp=dcp, pY=pY, widx=widx, j=j):
                    last = None
                    islast = (j == NE - 1)
                    for fc in range(NFC):
                        for hh in range(2):
                            last = E.matmul(
                                pY[:, hh * 512:(hh + 1) * 512],
                                Fb.w2_sb[:, widx * NFC * 128 + fc * 128: widx * NFC * 128 + (fc + 1) * 128],
                                Fb.h_bf[:, fc * TF + hh * 512: fc * TF + (hh + 1) * 512],
                                start=(fc == 0), stop=(fc == NFC - 1 and not islast))
                    if islast:
                        for hh in range(2):
                            last = E.matmul(
                                pY[:, hh * 512:(hh + 1) * 512],
                                C.b2r_bf[:, dcp * 128:(dcp + 1) * 128],
                                Fb.gT_bf[:, hh * 512:(hh + 1) * 512],
                                start=False, stop=True)
                    return last
                t_ym = sch.op("pe", ymm,
                              waits=tmax(t_hev[NFC - 1], t_w2[(j, dcp)], st.pY_free[pidx]),
                              inc="PE")
                st.w2free[widx] = t_ym
                nj, ndcp = (j, dcp + 2) if dcp + 2 < NDC else (j + 1, dcp + 2 - NDC)
                if nj < NE:
                    load_w2(nj, ndcp)
                if j == 0:
                    def acc0(E, dcp=dcp, pY=pY):
                        E.tensor_tensor(Fb.acc[:, dcp * TF:(dcp + 1) * TF], pY[:],
                                        C.b2sh_s[:, dcp:dcp + 1].to_broadcast([128, TF]),
                                        ALU.add)
                        return E.tensor_add(Fb.acc[:, dcp * TF:(dcp + 1) * TF],
                                            Fb.acc[:, dcp * TF:(dcp + 1) * TF],
                                            Fb.u_bf[:, dcp * TF:(dcp + 1) * TF])
                    t_ac = sch.op("dve", acc0,
                                  waits=tmax(t_ym, ct["b2sh"], st.acc_free), inc="DVE")
                else:
                    t_ac = sch.op("dve", lambda E, dcp=dcp, pY=pY: E.tensor_add(
                        Fb.acc[:, dcp * TF:(dcp + 1) * TF],
                        Fb.acc[:, dcp * TF:(dcp + 1) * TF], pY[:]),
                        waits=tmax(t_ym), inc="DVE")
                st.pY_free[pidx] = t_ac
                if j == NE - 1:
                    t_ac17[dcp] = t_ac
            for fc in range(NFC):
                st.h_free[fc] = t_ym
        st.u_free = t_hm
        st.gT_free = t_ym
        t_acc_all = t_ac
        t_st4 = None
        for dc in range(NDC):
            b = dc % 2
            sch.op("act", lambda E, dc=dc, b=b: E.copy(
                Fb.cast_f[:, b * TF:(b + 1) * TF], Fb.acc[:, dc * TF:(dc + 1) * TF]),
                waits=tmax(t_ac17[dc], st.cast_free[b], t_u_pl))
            t_c4 = sch.op("act", lambda E, dc=dc, b=b: E.activation(
                Fb.sq_f[:], Fb.acc[:, dc * TF:(dc + 1) * TF],
                AF.Square), waits=tmax(st.sqf_free), inc="ACT")

            def stats4(E, dc=dc, b=b):
                last = None
                for hh in range(2):
                    last = E.matmul(Fb.pHa[:, hh * 512:(hh + 1) * 512], C.ones_bf[:],
                                    Fb.cast_f[:, b * TF + hh * 512: b * TF + (hh + 1) * 512],
                                    start=(dc == 0), stop=(dc == NDC - 1))
                for hh in range(2):
                    last = E.matmul(Fb.pHb[:, hh * 512:(hh + 1) * 512], C.ones_bf[:],
                                    Fb.sq_f[:, hh * 512:(hh + 1) * 512],
                                    start=(dc == 0), stop=(dc == NDC - 1))
                return last
            t_st4 = sch.op("pe", stats4,
                           waits=tmax(t_c4,
                                      st.pH_free[0] if dc == 0 else None,
                                      st.pH_free[1] if dc == 0 else None),
                           inc="PE")
            st.cast_free[b] = t_st4
            st.sqf_free = t_st4
        t_mean4 = sch.op("act", lambda E: E.mul(Fb.mean_f[:], Fb.pHa[:], 1.0 / DM),
                         waits=tmax(t_st4, t_u, t_u_pl), inc="ACT")
        t_msq4 = sch.op("act", lambda E: E.activation(Fb.msqv_f[:], Fb.mean_f[:], AF.Square),
                        inc="ACT")
        t_var4 = sch.op("dve", lambda E: E.scalar_tensor_tensor(
            Fb.msqv_f[:], Fb.pHb[:], 1.0 / DM, Fb.msqv_f[:], ALU.mult, ALU.subtract),
            waits=tmax(t_msq4, t_st4), inc="DVE")
        sch.op("act", lambda E: E.activation(
            Fb.rstd_f[:], Fb.msqv_f[:], AF.Ln, bias=C.eps5[:]),
            waits=tmax(t_var4))
        t_rstd4 = sch.op("act", lambda E: E.activation(
            Fb.rstd_f[:], Fb.rstd_f[:], AF.Exp, scale=-0.5), inc="ACT")
        st.pH_free = [t_mean4, t_var4]
        st.stat_free = t_rstd4
        if debug:
            sch.op("sp", lambda E, tf0=tf0: E.dma_start(
                out=D.dbg["d_acc"].rearrange("(c p) t -> p c t", p=128)[:, :, tf0:tf0 + TF],
                in_=Fb.acc[:].rearrange("p (c t) -> p c t", c=NDC)),
                waits=tmax(t_acc_all), inc="LX")
        t_out = None
        for dc in range(NDC):
            b = dc % 2
            t_l = sch.op("sp", lambda E, dc=dc, b=b, tf0=tf0: E.dma_start(
                out=Fb.y1_scr[:, b * TF:(b + 1) * TF],
                in_=D.y1_sp[dc * 128:(dc + 1) * 128, tf0:tf0 + TF]),
                waits=tmax(st.y1_free[b], *t_spill_all), inc=("SYA" if b == 0 else "SYB"))
            sch.op("dve", lambda E, dc=dc: E.tensor_sub(
                Fb.ln_f[:], Fb.acc[:, dc * TF:(dc + 1) * TF], Fb.mean_f[:]),
                waits=tmax(t_rstd4, t_mean4))
            sch.op("dve", lambda E: E.tensor_mul(Fb.ln_f[:], Fb.ln_f[:], Fb.rstd_f[:]))
            t_out = sch.op("dve", lambda E, dc=dc, b=b: E.tensor_add(
                Fb.acc[:, dc * TF:(dc + 1) * TF], Fb.ln_f[:],
                Fb.y1_scr[:, b * TF:(b + 1) * TF]),
                waits=tmax(t_l), inc="DVE")
            st.y1_free[b] = t_out
        t_store = sch.op("sp", lambda E, tf0=tf0: E.dma_start(
            out=D.out_fm.rearrange("(c p) t -> p c t", p=128)[:, :, tf0:tf0 + TF],
            in_=Fb.acc[:].rearrange("p (c t) -> p c t", c=NDC)),
            waits=tmax(t_out), inc="ST")
        st.acc_free = t_store

    sch.join("sp", [("ST", sch.cnt["ST"])])



def _host_layout(inputs):
    x = np.asarray(inputs["L0_attn_x"], dtype=np.float32)
    wq = np.asarray(inputs["L0_attn_wq"], dtype=np.float32)
    wk = np.asarray(inputs["L0_attn_wk"], dtype=np.float32)
    wv = np.asarray(inputs["L0_attn_wv"], dtype=np.float32)
    wo = np.asarray(inputs["L0_attn_wo"], dtype=np.float32)
    gq = np.asarray(inputs["L0_attn_gamma_q"], dtype=np.float32)
    gk = np.asarray(inputs["L0_attn_gamma_k"], dtype=np.float32)
    sw1 = np.asarray(inputs["L0_ffn_shared_w1"], dtype=np.float32)
    sb1 = np.asarray(inputs["L0_ffn_shared_b1"], dtype=np.float32)
    sw2 = np.asarray(inputs["L0_ffn_shared_w2"], dtype=np.float32)
    sb2 = np.asarray(inputs["L0_ffn_shared_b2"], dtype=np.float32)
    rw1 = np.asarray(inputs["L0_ffn_routed_w1"], dtype=np.float32)
    rb1 = np.asarray(inputs["L0_ffn_routed_b1"], dtype=np.float32)
    rw2 = np.asarray(inputs["L0_ffn_routed_w2"], dtype=np.float32)
    rb2 = np.asarray(inputs["L0_ffn_routed_b2"], dtype=np.float32)
    cent = np.asarray(inputs["L0_ffn_centroids"], dtype=np.float32)
    bias = np.asarray(inputs["L0_ffn_bias"], dtype=np.float32)

    com = {}
    com["wq_sl"] = np.ascontiguousarray(
        wq.reshape(H, 128, NDC, 128).transpose(0, 3, 2, 1))
    com["wk_sl"] = np.ascontiguousarray(
        wk.reshape(HK, 128, NDC, 128).transpose(0, 3, 2, 1))
    com["wvT"] = np.ascontiguousarray(wv.T)
    com["wo_sl"] = np.ascontiguousarray(
        wo.reshape(NDC, 128, NDC, 128).transpose(0, 3, 2, 1))
    w1_all = np.concatenate([sw1, rw1], axis=0)
    com["w1_sl"] = np.ascontiguousarray(
        w1_all.reshape(NE, NDC, 128, NFC, 128).transpose(0, 3, 2, 1, 4))
    w2_all = np.concatenate([sw2, rw2], axis=0)
    com["w2_sl"] = np.ascontiguousarray(
        w2_all.reshape(NE, NFC, 128, NDC, 128).transpose(0, 3, 2, 1, 4))
    b1_all = np.concatenate([sb1, rb1], axis=0)
    com["b1c"] = np.ascontiguousarray(b1_all.reshape(NE, NFC, 128).transpose(2, 0, 1)
                                      .reshape(128, NE * NFC))
    com["b2c"] = np.ascontiguousarray(sb2.reshape(2, NDC, 128).transpose(2, 0, 1)
                                      .reshape(128, 2 * NDC))
    com["b2r"] = np.ascontiguousarray(rb2)
    com["cT"] = np.ascontiguousarray(cent.T)
    com["bias_bc"] = np.ascontiguousarray(np.tile(bias[None, :], (128, 1)))
    com["gq"] = np.ascontiguousarray(gq.reshape(128, 1))
    com["gk"] = np.ascontiguousarray(gk.reshape(128, 1))

    inv_freq = (1.0 / RBASE ** (np.arange(0, DH, 2, dtype=np.float64) / DH))
    ang = np.arange(T, dtype=np.float64)[:, None] * inv_freq[None, :]
    cos = np.cos(ang)
    sin = np.sin(ang)
    com["cos_t"] = np.ascontiguousarray(
        np.concatenate([cos, cos], axis=1).T.astype(np.float32))
    com["sin_t"] = np.ascontiguousarray(
        np.concatenate([-sin, sin], axis=1).T.astype(np.float32))
    com["ones_c"] = np.ones((128, 128), np.float32)
    com["ident_c"] = np.eye(128, dtype=np.float32)
    il = np.arange(128)[:, None]
    jl = np.arange(128)[None, :]
    com["mcaus"] = np.where(jl <= il, 0.0, -1e9).astype(np.float32)
    com["medge"] = np.where(jl >= il, 0.0, -1e9).astype(np.float32)
    sel = np.zeros((NRT, NRT * 128), np.float32)
    for j in range(NRT):
        sel[j, j * 128:(j + 1) * 128] = 1.0
    com["sel_c"] = sel
    com["identu_c"] = np.eye(128, dtype=np.uint8)
    kk = np.arange(128)[:, None]
    mm = np.arange(128)[None, :]
    com["p64_c"] = (np.abs(kk - mm) == 64).astype(np.float32)

    in_maps = []
    for b in range(8):
        m = dict(com)
        m["x_fm"] = np.ascontiguousarray(x[b].T)
        in_maps.append(m)
    return in_maps


_NC_CACHE = {}


def kernel(**inputs):
    in_maps = _host_layout(inputs)
    key = False
    if key not in _NC_CACHE:
        _NC_CACHE[key] = build_program(debug=False)
    nc = _NC_CACHE[key]
    res = run_bass_kernel_spmd(nc, in_maps, core_ids=list(range(8)))
    out = np.stack([np.asarray(r["out_fm"]).T for r in res.results], axis=0)
    return np.ascontiguousarray(out.astype(np.float32))


# version 22 (final)
# speedup vs baseline: 1.1017x; 1.0186x over previous
import sys
import numpy as np

try:
    import concourse.bass as bass
except ImportError:
    sys.path.insert(0, "/opt/trn_rl_repo")
    import concourse.bass as bass
import concourse.mybir as mybir
from concourse.bass_utils import run_bass_kernel_spmd
from contextlib import ExitStack

F32 = mybir.dt.float32
BF16 = mybir.dt.bfloat16
AF = mybir.ActivationFunctionType
ALU = mybir.AluOpType
AX = mybir.AxisListType

T = 2048
DM = 2048
NDC = 16
H = 16
HK = 4
GQ = 4
DH = 128
DFF = 1408
NFC = 11
NE = 18
NRT = 16
WINDOW = 1024
EPS_QK = 1e-6
EPS_LN = 1e-5
RBASE = 10000.0
TA = 512
NTI = T // TA
TF = 1024
NTF = T // TF
INV_SQRT_DH = float(1.0 / np.sqrt(DH))

DMA_SEMS = ("LW", "LX", "ST", "SX", "SKV", "SQA", "SQB", "SOA", "SOB", "S1A", "S1B", "S1C", "S2A", "S2B", "S2C", "SYA", "SYB", "SXB", "STA", "STB", "STC", "STD")
OWN_SEM = {"pe": "PE", "act": "ACT", "dve": "DVE", "pool": "PL", "sp": None}


class Sched:
    def __init__(self):
        self.cur = None

    def start(self, engname, eng, sems):
        self.cur = engname
        self.eng = eng
        self.sems = sems
        self.cnt = {k: 0 for k in sems}
        self.lastwait = {k: 0 for k in sems}

    def op(self, engname, fn, waits=(), inc=None):
        tok = None
        if inc is not None:
            self.cnt[inc] += 16 if inc in DMA_SEMS else 1
            tok = (inc, self.cnt[inc])
        if engname == self.cur:
            own = OWN_SEM.get(engname)
            for w in waits:
                if w is None:
                    continue
                s, v = w
                if s == own or v <= self.lastwait[s]:
                    continue
                self.eng.wait_ge(self.sems[s], v)
                self.lastwait[s] = v
            ins = fn(self.eng)
            if inc is not None:
                ins.then_inc(self.sems[inc], 16 if inc in DMA_SEMS else 1)
        return tok

    def join(self, engname, waits):
        if engname == self.cur:
            own = OWN_SEM.get(engname)
            for w in waits:
                if w is None:
                    continue
                s, v = w
                if s == own or v <= self.lastwait[s]:
                    continue
                self.eng.wait_ge(self.sems[s], v)
                self.lastwait[s] = v


def tmax(*toks):
    best = {}
    for t in toks:
        if t is None:
            continue
        s, v = t
        if v > best.get(s, -1):
            best[s] = v
    return [(s, v) for s, v in best.items()]


class NS:
    pass


def build_program(debug=False):
    nc = bass.Bass("TRN2", target_bir_lowering=False)

    D = NS()
    D.x_fm = nc.dram_tensor("x_fm", [DM, T], F32, kind="ExternalInput").ap()
    D.wq_sl = nc.dram_tensor("wq_sl", [H, 128, NDC, 128], F32, kind="ExternalInput").ap()
    D.wk_sl = nc.dram_tensor("wk_sl", [HK, 128, NDC, 128], F32, kind="ExternalInput").ap()
    D.wvT = nc.dram_tensor("wvT", [DM, HK * DH], F32, kind="ExternalInput").ap()
    D.wo_sl = nc.dram_tensor("wo_sl", [NDC, 128, NDC, 128], F32, kind="ExternalInput").ap()
    D.w1_sl = nc.dram_tensor("w1_sl", [NE, NFC, 128, NDC, 128], F32, kind="ExternalInput").ap()
    D.w2_sl = nc.dram_tensor("w2_sl", [NE, NDC, 128, NFC, 128], F32, kind="ExternalInput").ap()
    D.b1c = nc.dram_tensor("b1c", [128, NE * NFC], F32, kind="ExternalInput").ap()
    D.b2c = nc.dram_tensor("b2c", [128, 2 * NDC], F32, kind="ExternalInput").ap()
    D.b2r = nc.dram_tensor("b2r", [NRT, DM], F32, kind="ExternalInput").ap()
    D.cT = nc.dram_tensor("cT", [DM, NRT], F32, kind="ExternalInput").ap()
    D.bias_bc = nc.dram_tensor("bias_bc", [128, NRT], F32, kind="ExternalInput").ap()
    D.gq = nc.dram_tensor("gq", [128, 1], F32, kind="ExternalInput").ap()
    D.gk = nc.dram_tensor("gk", [128, 1], F32, kind="ExternalInput").ap()
    D.cos_t = nc.dram_tensor("cos_t", [128, T], F32, kind="ExternalInput").ap()
    D.sin_t = nc.dram_tensor("sin_t", [128, T], F32, kind="ExternalInput").ap()
    D.ones_c = nc.dram_tensor("ones_c", [128, 128], F32, kind="ExternalInput").ap()
    D.ident_c = nc.dram_tensor("ident_c", [128, 128], F32, kind="ExternalInput").ap()
    D.mcaus = nc.dram_tensor("mcaus", [128, 128], F32, kind="ExternalInput").ap()
    D.medge = nc.dram_tensor("medge", [128, 128], F32, kind="ExternalInput").ap()
    D.sel_c = nc.dram_tensor("sel_c", [NRT, NRT * 128], F32, kind="ExternalInput").ap()
    D.p64_c = nc.dram_tensor("p64_c", [128, 128], F32, kind="ExternalInput").ap()
    D.identu_c = nc.dram_tensor("identu_c", [128, 128], mybir.dt.uint8, kind="ExternalInput").ap()
    D.y1_sp = nc.dram_tensor("y1_spill", [DM, T], F32, kind="Internal").ap()
    D.out_fm = nc.dram_tensor("out_fm", [DM, T], F32, kind="ExternalOutput").ap()
    D.dbg = {}
    if debug:
        for nm, shape in (("d_lnx", (DM, T)), ("d_q", (DH, T)), ("d_k", (HK * DH, T)),
                          ("d_v", (T, HK * DH)), ("d_att", (DM, T)), ("d_y1", (DM, T)),
                          ("d_u", (DM, T)), ("d_g", (T, NRT)), ("d_h0", (DFF, TF)),
                          ("d_y0", (DH, T)),
                          ("d_acc", (DM, T))):
            D.dbg[nm] = nc.dram_tensor(nm, list(shape), F32, kind="ExternalOutput").ap()

    with ExitStack() as ctx:
        e = ctx.enter_context
        C = NS()
        C.cos_s = e(nc.sbuf_tensor("cos_s", [128, T], BF16))
        C.sin_s = e(nc.sbuf_tensor("sin_s", [128, T], BF16))
        C.mcaus_bf = e(nc.sbuf_tensor("mcaus_bf", [128, 128], BF16))
        C.medge_bf = e(nc.sbuf_tensor("medge_bf", [128, 128], BF16))
        C.id_bf = e(nc.sbuf_tensor("id_bf", [128, 128], BF16))
        C.id_f = e(nc.sbuf_tensor("id_f", [128, 128], F32))
        C.ones_bf = e(nc.sbuf_tensor("ones_bf", [128, 128], BF16))
        C.p64_bf = e(nc.sbuf_tensor("p64_bf", [128, 128], BF16))
        C.gq_s = e(nc.sbuf_tensor("gq_s", [128, 1], F32))
        C.gk_s = e(nc.sbuf_tensor("gk_s", [128, 1], F32))
        C.eps6 = e(nc.sbuf_tensor("eps6", [128, 1], F32))
        C.eps5 = e(nc.sbuf_tensor("eps5", [128, 1], F32))
        C.b1c_s = e(nc.sbuf_tensor("b1c_s", [128, NE * NFC], F32))
        C.b2c_s = e(nc.sbuf_tensor("b2c_s", [128, 2 * NDC], F32))
        C.b2sh_s = e(nc.sbuf_tensor("b2sh_s", [128, NDC], F32))
        C.b2r_bf = e(nc.sbuf_tensor("b2r_bf", [NRT, DM], BF16))
        C.sel_bf = e(nc.sbuf_tensor("sel_bf", [NRT, NRT * 128], BF16))
        C.cT_bf = e(nc.sbuf_tensor("cT_bf", [128, NDC * NRT], BF16))
        C.bias_bc_s = e(nc.sbuf_tensor("bias_bc_s", [128, NRT], F32))
        C.nmax = e(nc.sbuf_tensor("nmax", [128, 1], F32))
        C.sums = e(nc.sbuf_tensor("sums", [128, 1], F32))
        C.lnsum = e(nc.sbuf_tensor("lnsum", [128, 1], F32))
        C.bias2 = e(nc.sbuf_tensor("bias2", [128, 1], F32))
        C.sc16 = e(nc.sbuf_tensor("sc16", [128, NRT], F32))
        C.mx8 = e(nc.sbuf_tensor("mx8", [128, 8], F32))
        C.msk16 = e(nc.sbuf_tensor("msk16", [128, NRT], F32))
        C.g16 = e(nc.sbuf_tensor("g16", [128, NRT], F32))
        C.g_all = e(nc.sbuf_tensor("g_all", [128, 8 * NRT], F32))
        C.rs1 = e(nc.sbuf_tensor("rs1", [128, 1], F32))
        C.thr1 = e(nc.sbuf_tensor("thr1", [128, 1], F32))
        C.rsB = e(nc.sbuf_tensor("rsB", [128, 1], F32))
        C.rsC = e(nc.sbuf_tensor("rsC", [128, 1], F32))
        C.rsb_bf = e(nc.sbuf_tensor("rsb_bf", [128, 2], BF16))
        C.rs2 = e(nc.sbuf_tensor("rs2", [128, 2], F32))
        C.rep_s = e(nc.sbuf_tensor("rep_s", [128, 2 * 128], F32))
        C.negC = e(nc.sbuf_tensor("negC", [128, 1], F32))
        C.mqA = e(nc.sbuf_tensor("mqA", [128, 1], F32))
        C.gbc_bf = e(nc.sbuf_tensor("gbc_bf", [128, 128], BF16))
        C.eps35 = e(nc.sbuf_tensor("eps35", [128, 1], F32))
        C.diag_bf = e(nc.sbuf_tensor("diag_bf", [128, 2 * 128], BF16))
        C.id_u8 = e(nc.sbuf_tensor("id_u8", [128, 128], mybir.dt.uint8))

        sems = {}
        for s in ("LW", "LX", "ST", "PE", "ACT", "DVE", "PL", "SX", "SKV", "SQA", "SQB", "SOA", "SOB", "S1A", "S1B", "S1C", "S2A", "S2B", "S2C", "SYA", "SYB", "SXB", "STA", "STB", "STC", "STD"):
            sems[s] = e(nc.semaphore(s))

        block = e(nc.Block())
        sch = Sched()
        state = {}

        with ExitStack() as actx:
            ae = actx.enter_context
            A = NS()
            A.x_bf = ae(nc.sbuf_tensor("x_bf", [128, 2 * NDC * TA], BF16))
            A.sq_scr = ae(nc.sbuf_tensor("sq_scr", [128, 2 * TA], BF16))
            A.y1s = ae(nc.sbuf_tensor("y1s", [128, 4 * TA], F32))
            A.mean2_s = ae(nc.sbuf_tensor("mean2_s", [128, TA], F32))
            A.msqv2_s = ae(nc.sbuf_tensor("msqv2_s", [128, TA], F32))
            A.rstd2_s = ae(nc.sbuf_tensor("rstd2_s", [128, TA], F32))
            A.ln_bf = ae(nc.sbuf_tensor("ln_bf", [128, NDC * TA], BF16))
            A.ln_scr = ae(nc.sbuf_tensor("ln_scr", [128, TA], F32))
            A.mean_s = ae(nc.sbuf_tensor("mean_s", [128, TA], F32))
            A.msqv_s = ae(nc.sbuf_tensor("msqv_s", [128, TA], F32))
            A.rstd_s = ae(nc.sbuf_tensor("rstd_s", [128, TA], F32))
            A.k_full = ae(nc.sbuf_tensor("k_full", [128, HK * T], BF16))
            A.v_full = ae(nc.sbuf_tensor("v_full", [128, NDC * TA], BF16))
            A.q_ro = ae(nc.sbuf_tensor("q_ro", [128, TA], F32))
            A.q_fp = ae(nc.sbuf_tensor("q_fp", [128, 2 * TA], BF16))
            A.q_sq = ae(nc.sbuf_tensor("q_sq", [128, TA], BF16))
            A.rn_s = ae(nc.sbuf_tensor("rn_s", [128, TA], F32))
            A.q_bf = ae(nc.sbuf_tensor("q_bf", [128, TA], BF16))
            A.p_bf = ae(nc.sbuf_tensor("p_bf", [128, 2 * 1152], BF16))
            A.pT_sb = ae(nc.sbuf_tensor("pT_sb", [128, 1152], BF16))
            A.y_att = ae(nc.sbuf_tensor("y_att", [128, H * TA], BF16))
            A.att_bf = ae(nc.sbuf_tensor("att_bf", [128, NDC * TA], BF16))
            A.wbig = ae(nc.sbuf_tensor("wbig", [128, 4 * DM], BF16))
            A.wq_sb = ae(nc.sbuf_tensor("wq_sb", [128, 2 * DM], BF16))
            A.wo_sb = ae(nc.sbuf_tensor("wo_sb", [128, 2 * DM], BF16))
            A.pA = ae(nc.psum_tensor("pA", [128, 512], F32))
            A.pB = ae(nc.psum_tensor("pB", [128, 512], F32))
            A.pQ = ae(nc.psum_tensor("pQ", [128, 512], F32))
            A.pS = ae(nc.psum_tensor("pS", [128, 1536], F32))
            A.pTT = ae(nc.psum_tensor("pTT", [128, 2048], BF16))

            def attn_pass(engname, eng):
                sch.start(engname, eng, sems)
                ct = _sched_consts(sch, nc, D, C)
                af = _sched_attn(sch, nc, D, C, A, ct, debug)
                state[engname] = (ct, af, dict(sch.cnt), dict(sch.lastwait))

            block.tensor(lambda eng: attn_pass("pe", eng))
            block.scalar(lambda eng: attn_pass("act", eng))
            block.vector(lambda eng: attn_pass("dve", eng))
            block.gpsimd(lambda eng: attn_pass("pool", eng))
            block.sync(lambda eng: attn_pass("sp", eng))

        with ExitStack() as fctx:
            fe = fctx.enter_context
            Fb = NS()
            Fb.u_bf = fe(nc.sbuf_tensor("u_bf", [128, NDC * TF], BF16))
            Fb.acc = fe(nc.sbuf_tensor("acc", [128, NDC * TF], F32))
            Fb.h_bf = fe(nc.sbuf_tensor("h_bf", [128, NFC * TF], BF16))
            Fb.G_bf = fe(nc.sbuf_tensor("G_bf", [128, TF], BF16))
            Fb.gT_bf = fe(nc.sbuf_tensor("gT_bf", [NRT, TF], BF16))
            Fb.y1_scr = fe(nc.sbuf_tensor("y1_scr", [128, 2 * TF], F32))
            Fb.cast_f = fe(nc.sbuf_tensor("cast_f", [128, 2 * TF], BF16))
            Fb.sq_f = fe(nc.sbuf_tensor("sq_f", [128, TF], BF16))
            Fb.ln_f = fe(nc.sbuf_tensor("ln_f", [128, TF], F32))
            Fb.mean_f = fe(nc.sbuf_tensor("mean_f", [128, TF], F32))
            Fb.msqv_f = fe(nc.sbuf_tensor("msqv_f", [128, TF], F32))
            Fb.rstd_f = fe(nc.sbuf_tensor("rstd_f", [128, TF], F32))
            Fb.w1_sb = fe(nc.sbuf_tensor("w1_sb", [128, 3 * NDC * 128], BF16))
            Fb.w2_sb = fe(nc.sbuf_tensor("w2_sb", [128, 2 * NFC * 128], BF16))
            Fb.pHa = fe(nc.psum_tensor("pHa", [128, TF], F32))
            Fb.pHb = fe(nc.psum_tensor("pHb", [128, TF], F32))
            Fb.pYa = fe(nc.psum_tensor("pYa", [128, TF], F32))
            Fb.pYb = fe(nc.psum_tensor("pYb", [128, TF], F32))

            def ffn_pass(engname, eng):
                ct, af, cnt, lastwait = state[engname]
                sch.start(engname, eng, sems)
                sch.cnt = cnt
                sch.lastwait = lastwait
                _sched_ffn(sch, nc, D, C, Fb, ct, af, debug)

            block.tensor(lambda eng: ffn_pass("pe", eng))
            block.scalar(lambda eng: ffn_pass("act", eng))
            block.vector(lambda eng: ffn_pass("dve", eng))
            block.gpsimd(lambda eng: ffn_pass("pool", eng))
            block.sync(lambda eng: ffn_pass("sp", eng))

    return nc


def _sched_consts(sch, nc, D, C):
    ct = {}
    t = None
    for dst, src in ((C.gq_s, D.gq), (C.gk_s, D.gk),
                     (C.b1c_s, D.b1c), (C.bias_bc_s, D.bias_bc), (C.id_f, D.ident_c)):
        t = sch.op("sp", lambda E, d=dst, s=src: E.dma_start(out=d[:], in_=s), inc="LX")
    sch.op("sp", lambda E: E.dma_start(out=C.id_u8[:], in_=D.identu_c), inc="LX")
    t = sch.op("sp", lambda E: E.dma_start(
        out=C.b2c_s[:].rearrange("p (n c) -> p n c", n=2),
        in_=D.b2c.rearrange("p (n c) -> p n c", n=2)), inc="LX")
    ct["lx"] = t
    t = None
    for dst, src in ((C.ones_bf, D.ones_c), (C.id_bf, D.ident_c), (C.b2r_bf, D.b2r),
                     (C.sel_bf, D.sel_c), (C.cos_s, D.cos_t), (C.sin_s, D.sin_t),
                     (C.mcaus_bf, D.mcaus), (C.medge_bf, D.medge), (C.p64_bf, D.p64_c)):
        t = sch.op("pool", lambda E, d=dst, s=src: E.dma_start(out=d[:], in_=s), inc="LW")
    t = sch.op("pool", lambda E: E.dma_start(
        out=C.cT_bf[:].rearrange("p (c j) -> p c j", c=NDC),
        in_=D.cT.rearrange("(c p) j -> p c j", p=128)), inc="LW")
    ct["lw"] = t
    sch.op("dve", lambda E: E.memset(C.eps6[:], EPS_QK))
    sch.op("dve", lambda E: E.memset(C.diag_bf[:], 0.0))
    sch.op("dve", lambda E: E.memset(C.eps35[:], 1e-35))
    ct["eps"] = sch.op("dve", lambda E: E.memset(C.eps5[:], EPS_LN), inc="DVE")
    ct["b2sh"] = sch.op("dve", lambda E: E.tensor_add(
        C.b2sh_s[:], C.b2c_s[:, 0:NDC], C.b2c_s[:, NDC:2 * NDC]),
        waits=[ct["lx"]], inc="DVE")
    return ct


def _sched_attn(sch, nc, D, C, A, ct, debug):
    st = NS()
    st.wbig_free = None
    st.wq_free = [None, None]
    st.wo_free = [None, None]
    st.scr_free = [None, None]
    st.pA_free = ct["eps"]
    st.pB_free = None
    st.pQ_free = None
    st.pS_free = None
    st.pTT_free = None
    st.pbf_free = None
    st.pT_free = []
    st.rep_free = None
    st.repslot_free = [None, None]
    st.lnbf_free = None
    st.yatt_free = None
    st.attbf_free = []
    st.stat_free = []
    st.qfp_free = [None, None]
    st.xbf_free = [[], []]
    st.xld = [None, None]
    st.y1s_free = []
    st.qro_free = None
    st.proj_tok = None
    st.diag_free = [None, None]
    st.tr_free = [None, None]
    st.pbf_par_free = [None, None]
    st.vts = [None] * NDC
    t_spill = None

    def rope_start(h, par):
        t_ev = sch.op("act", lambda E, par=par: E.copy(
            A.q_fp[:, par * TA:(par + 1) * TA], A.pQ[:]),
            waits=tmax(st.proj_tok, st.qfp_free[par]), inc="ACT")
        st.pQ_free = t_ev
        return t_ev

    def rope_finish(t_ev, par, gamma_s, scale2, dst_fn, ti0):
        t_sw = sch.op("pe", lambda E, par=par: E.matmul(
            A.pA[:], C.p64_bf[:], A.q_fp[:, par * TA:(par + 1) * TA],
            start=True, stop=True),
            waits=tmax(t_ev, st.pA_free), inc="PE")
        sch.op("dve", lambda E, par=par: E.tensor_mul(
            A.ln_scr[:], A.q_fp[:, par * TA:(par + 1) * TA], C.cos_s[:, ti0:ti0 + TA]),
            waits=tmax(t_ev))
        sch.op("dve", lambda E: E.tensor_mul(
            A.q_ro[:], A.pA[:], C.sin_s[:, ti0:ti0 + TA]), waits=tmax(t_sw))
        t_rope = sch.op("dve", lambda E: E.tensor_add(
            A.q_ro[:], A.q_ro[:], A.ln_scr[:]), inc="DVE")
        st.pA_free = t_rope
        st.qfp_free[par] = t_rope
        t_sq = sch.op("dve", lambda E: E.tensor_mul(A.q_sq[:], A.q_ro[:], A.q_ro[:]),
                      inc="DVE")
        t_ss = sch.op("pe", lambda E: E.matmul(
            A.pB[:], C.ones_bf[:], A.q_sq[:], start=True, stop=True),
            waits=tmax(t_sq, st.pB_free), inc="PE")
        t_s1 = sch.op("act", lambda E: E.activation(A.rn_s[:], A.pB[:], AF.Sqrt),
                      waits=tmax(t_ss), inc="ACT")
        st.pB_free = t_s1
        sch.op("act", lambda E: E.activation(A.rn_s[:], A.rn_s[:], AF.Ln, bias=C.eps6[:]))
        t_rn = sch.op("act", lambda E: E.activation(A.rn_s[:], A.rn_s[:], AF.Exp, scale=-1.0),
                      inc="ACT")
        sch.op("dve", lambda E: E.tensor_mul(A.ln_scr[:], A.q_ro[:], A.rn_s[:]),
               waits=tmax(t_rn))
        if scale2 is None:
            t_fin = sch.op("dve", lambda E: E.tensor_scalar(
                dst_fn(), A.ln_scr[:], gamma_s[:], None, ALU.mult), inc="DVE")
        else:
            t_fin = sch.op("dve", lambda E: E.tensor_scalar(
                dst_fn(), A.ln_scr[:], gamma_s[:], scale2, ALU.mult, ALU.mult),
                inc="DVE")
        st.qro_free = t_fin
        return t_fin

    t_b1 = sch.op("dve", lambda E: E.tensor_copy(C.gbc_bf[:], C.gq_s[:].to_broadcast([128, 128])),
                  waits=tmax(ct["lx"]), inc="DVE")
    t_t1 = sch.op("pe", lambda E: E.transpose(A.pTT[:, 0:128], C.gbc_bf[:], C.id_bf[:]),
                  waits=tmax(t_b1, ct["lw"]), inc="PE")
    t_ev1 = sch.op("act", lambda E: E.copy(A.ln_scr[:, 0:128], A.pTT[:, 0:128]),
                   waits=tmax(t_t1), inc="ACT")
    t_m1 = sch.op("dve", lambda E: E.reduce_max(C.mqA[:], A.ln_scr[:, 0:128], AX.X,
                                                apply_absolute_value=True),
                  waits=tmax(t_ev1), inc="DVE")
    t_b2 = sch.op("dve", lambda E: E.tensor_copy(C.gbc_bf[:], C.gk_s[:].to_broadcast([128, 128])),
                  waits=tmax(t_t1), inc="DVE")
    t_t2 = sch.op("pe", lambda E: E.transpose(A.pTT[:, 128:256], C.gbc_bf[:], C.id_bf[:]),
                  waits=tmax(t_b2, t_ev1), inc="PE")
    t_ev2 = sch.op("act", lambda E: E.copy(A.ln_scr[:, 128:256], A.pTT[:, 128:256]),
                   waits=tmax(t_t2), inc="ACT")
    sch.op("dve", lambda E: E.reduce_max(C.negC[:], A.ln_scr[:, 128:256], AX.X,
                                         apply_absolute_value=True),
           waits=tmax(t_ev2))
    sch.op("dve", lambda E: E.tensor_mul(C.negC[:], C.negC[:], C.mqA[:]))
    t_negC = sch.op("dve", lambda E: E.tensor_scalar(
        C.negC[:], C.negC[:], -INV_SQRT_DH, None, ALU.mult), inc="DVE")
    t_m2 = t_negC
    st.pTT_free = t_ev2
    st.stat_free = [t_m2]

    for i in range(NTI):
        ti0 = i * TA
        par = i % 2
        if i == 0:
            st.xld[0] = sch.op("pool", lambda E: E.dma_start(
                out=A.x_bf[:, 0:NDC * TA].rearrange("p (c t) -> p c t", c=NDC),
                in_=D.x_fm.rearrange("(c p) t -> p c t", p=128)[:, :, 0:TA]),
                inc="SX")
        if i + 1 < NTI:
            pnxt = (i + 1) % 2
            st.xld[pnxt] = sch.op("pool", lambda E, i=i, pnxt=pnxt: E.dma_start(
                out=A.x_bf[:, pnxt * NDC * TA:(pnxt + 1) * NDC * TA].rearrange("p (c t) -> p c t", c=NDC),
                in_=D.x_fm.rearrange("(c p) t -> p c t", p=128)[:, :, (i + 1) * TA:(i + 2) * TA]),
                waits=tmax(*st.xbf_free[pnxt]), inc=("SX" if pnxt == 0 else "SXB"))
        t_x = st.xld[par]
        xb0 = par * NDC * TA
        t_wv = sch.op("pool", lambda E: E.dma_start(
            out=A.wbig[:, 0:NDC * 512].rearrange("p (c v) -> p c v", c=NDC),
            in_=D.wvT.rearrange("(c p) v -> p c v", p=128)),
            waits=tmax(st.wbig_free), inc="SKV")
        t_stats = None
        for dc in range(NDC):
            b = dc % 2
            t_cast = sch.op("act", lambda E, dc=dc, b=b, xb0=xb0: E.activation(
                A.sq_scr[:, b * TA:(b + 1) * TA], A.x_bf[:, xb0 + dc * TA: xb0 + (dc + 1) * TA],
                AF.Square), waits=tmax(t_x, st.scr_free[b]), inc="ACT")
            sch.op("pe", lambda E, b=b, dc=dc, xb0=xb0: E.matmul(
                A.pA[:], C.ones_bf[:], A.x_bf[:, xb0 + dc * TA: xb0 + (dc + 1) * TA],
                start=(dc == 0), stop=(dc == NDC - 1)),
                waits=tmax(t_cast,
                           st.pA_free if dc == 0 else None,
                           st.pB_free if dc == 0 else None,
                           ct["lw"] if dc == 0 else None))
            t_stats = sch.op("pe", lambda E, b=b, dc=dc: E.matmul(
                A.pB[:], C.ones_bf[:], A.sq_scr[:, b * TA:(b + 1) * TA],
                start=(dc == 0), stop=(dc == NDC - 1)), inc="PE")
            st.scr_free[b] = t_stats
        t_mean = sch.op("act", lambda E: E.mul(A.mean_s[:], A.pA[:], 1.0 / DM),
                        waits=tmax(t_stats, *st.stat_free), inc="ACT")
        t_msq = sch.op("act", lambda E: E.activation(A.msqv_s[:], A.mean_s[:], AF.Square),
                       inc="ACT")
        t_var = sch.op("dve", lambda E: E.scalar_tensor_tensor(
            A.msqv_s[:], A.pB[:], 1.0 / DM, A.msqv_s[:], ALU.mult, ALU.subtract),
            waits=tmax(t_msq, t_stats), inc="DVE")
        sch.op("act", lambda E: E.activation(
            A.rstd_s[:], A.msqv_s[:], AF.Ln, bias=C.eps5[:]),
            waits=tmax(t_var, ct["eps"]))
        t_rstd = sch.op("act", lambda E: E.activation(
            A.rstd_s[:], A.rstd_s[:], AF.Exp, scale=-0.5), inc="ACT")
        st.pA_free = t_mean
        st.pB_free = t_var
        t_ln = None
        t_ln_pl = None
        t_ln_dc = [None] * NDC
        for dc in range(NDC):
            if dc % 3 == 2:
                sch.op("pool", lambda E, dc=dc, xb0=xb0: E.tensor_sub(
                    A.q_ro[:], A.x_bf[:, xb0 + dc * TA: xb0 + (dc + 1) * TA], A.mean_s[:]),
                    waits=tmax(t_rstd, t_mean, st.lnbf_free, st.qro_free))
                t_ln_pl = sch.op("pool", lambda E, dc=dc: E.tensor_mul(
                    A.ln_bf[:, dc * TA:(dc + 1) * TA], A.q_ro[:], A.rstd_s[:]),
                    inc="PL")
                t_ln_dc[dc] = t_ln_pl
            else:
                sch.op("dve", lambda E, dc=dc, xb0=xb0: E.tensor_sub(
                    A.ln_scr[:], A.x_bf[:, xb0 + dc * TA: xb0 + (dc + 1) * TA], A.mean_s[:]),
                    waits=tmax(t_rstd, t_mean, st.lnbf_free))
                t_ln = sch.op("dve", lambda E, dc=dc: E.tensor_mul(
                    A.ln_bf[:, dc * TA:(dc + 1) * TA], A.ln_scr[:], A.rstd_s[:]),
                    inc="DVE")
                t_ln_dc[dc] = t_ln
        if debug:
            sch.op("pool", lambda E, ti0=ti0: E.dma_start(
                out=D.dbg["d_lnx"].rearrange("(c p) t -> p c t", p=128)[:, :, ti0:ti0 + TA],
                in_=A.ln_bf[:].rearrange("p (c t) -> p c t", c=NDC)),
                waits=tmax(t_ln), inc="LW")
        st.stat_free = [t_ln, t_ln_pl]
        t_vp = None
        for ts_ in range(4):
            gt = 4 * i + ts_

            def vproj(E, ts_=ts_):
                last = None
                for dc in range(NDC):
                    last = E.matmul(
                        A.pQ[:], A.ln_bf[:, dc * TA + ts_ * 128: dc * TA + ts_ * 128 + 128],
                        A.wbig[:, dc * 512:(dc + 1) * 512],
                        start=(dc == 0), stop=(dc == NDC - 1))
                return last
            t_vp = sch.op("pe", vproj,
                          waits=tmax(t_wv, st.pQ_free, t_ln, t_ln_pl,
                                     *([t_ln_dc[dc] for dc in range(NDC)] if ts_ == 0 else [])),
                          inc="PE")
            t_ve = sch.op("act", lambda E, gt=gt: E.copy(
                A.v_full[:, gt * 512:(gt + 1) * 512], A.pQ[:]),
                waits=tmax(t_vp), inc="ACT")
            st.vts[gt] = t_ve
            st.pQ_free = t_ve
        t_wk = sch.op("pool", lambda E: E.dma_start(
            out=A.wbig[:].rearrange("p (k c q) -> p k c q", k=HK, c=NDC),
            in_=D.wk_sl.rearrange("k p c q -> p k c q")),
            waits=tmax(t_vp), inc="SKV")
        t_kfin = None
        for hk in range(HK):
            def kproj(E, hk=hk):
                last = None
                for dc in range(NDC):
                    last = E.matmul(
                        A.pQ[:], A.wbig[:, hk * DM + dc * 128: hk * DM + (dc + 1) * 128],
                        A.ln_bf[:, dc * TA:(dc + 1) * TA],
                        start=(dc == 0), stop=(dc == NDC - 1))
                return last
            st.proj_tok = sch.op("pe", kproj,
                                 waits=tmax(t_ln, t_ln_pl, t_wk, st.pQ_free), inc="PE")
            t_ev = rope_start(hk, hk % 2)
            t_kfin = rope_finish(
                t_ev, hk % 2, C.gk_s, None,
                lambda hk=hk, ti0=ti0: A.k_full[:, hk * T + ti0: hk * T + ti0 + TA], ti0)
        if debug:
            sch.op("pool", lambda E, ti0=ti0: E.dma_start(
                out=D.dbg["d_k"].rearrange("(c p) t -> p c t", p=128)[:, :, ti0:ti0 + TA],
                in_=A.k_full[:].rearrange("p (c t) -> p c t", c=HK)[:, :, ti0:ti0 + TA]),
                waits=tmax(t_kfin), inc="LW")
            sch.op("pool", lambda E, i=i: E.dma_start(
                out=D.dbg["d_v"].rearrange("(c p) v -> p c v", p=128)[:, 4 * i:4 * i + 4, :],
                in_=A.v_full[:].rearrange("p (c v) -> p c v", c=NDC)[:, 4 * i:4 * i + 4, :]),
                waits=tmax(st.vts[4 * i + 3]), inc="LW")
        t_wq = [None] * H

        def load_wq(h):
            return sch.op("pool", lambda E, h=h: E.dma_start(
                out=A.wq_sb[:, (h % 2) * DM:(h % 2 + 1) * DM].rearrange("p (c q) -> p c q", c=NDC),
                in_=D.wq_sl[h]),
                waits=tmax(st.wq_free[h % 2]), inc=("SQA" if h % 2 == 0 else "SQB"))

        t_wq[0] = load_wq(0)
        t_wq[1] = load_wq(1)
        t_qfin = [None] * H
        pend = None

        def q_dst(h, ti0=ti0):
            return lambda h=h: A.wbig[:, h * TA:(h + 1) * TA]

        for h in range(H):
            def qproj(E, h=h):
                last = None
                for dc in range(NDC):
                    last = E.matmul(
                        A.pQ[:], A.wq_sb[:, (h % 2) * DM + dc * 128:(h % 2) * DM + (dc + 1) * 128],
                        A.ln_bf[:, dc * TA:(dc + 1) * TA],
                        start=(dc == 0), stop=(dc == NDC - 1))
                return last
            st.proj_tok = sch.op("pe", qproj,
                                 waits=tmax(t_wq[h], st.pQ_free, t_ln, t_ln_pl,
                                            st.wbig_free if h == 0 else None),
                                 inc="PE")
            t_qp = st.proj_tok
            st.wq_free[h % 2] = t_qp
            if h + 2 < H:
                t_wq[h + 2] = load_wq(h + 2)
            t_ev = rope_start(h, h % 2)
            if pend is not None:
                ph, pev = pend
                t_qfin[ph] = rope_finish(pev, ph % 2, C.gq_s, INV_SQRT_DH, q_dst(ph), ti0)
            pend = (h, t_ev)
        ph, pev = pend
        t_qfin[ph] = rope_finish(pev, ph % 2, C.gq_s, INV_SQRT_DH, q_dst(ph), ti0)
        if debug:
            sch.op("pool", lambda E, ti0=ti0: E.dma_start(
                out=D.dbg["d_q"][:, ti0:ti0 + TA], in_=A.wbig[:, 0:TA]),
                waits=tmax(t_qfin[0]), inc="LW")

        core = []
        for h in range(H):
            for sub in range(4):
                core.append((h, sub))
        pend_tp = None
        t_e1 = None
        t_sc_last = None
        for idx, (h, sub) in enumerate(core):
            hk = h // GQ
            gsub = 4 * i + sub
            c_lo = max(0, gsub - 8)
            nch = gsub - c_lo + 1
            W = 128 * nch

            def scmm(E, h=h, sub=sub, c_lo=c_lo, W=W, hk=hk, has_edge=(gsub >= 8)):
                last = None
                for off in range(0, W, 512):
                    n = min(512, W - off)
                    has_mask = (has_edge and off == 0) or (W - 128 >= off and W - 128 < off + n)
                    last = E.matmul(
                        A.pS[:, off:off + n],
                        A.wbig[:, h * TA + sub * 128: h * TA + sub * 128 + 128],
                        A.k_full[:, hk * T + c_lo * 128 + off: hk * T + c_lo * 128 + off + n],
                        start=True, stop=not has_mask)
                    if has_edge and off == 0:
                        last = E.matmul(A.pS[:, 0:128], C.id_bf[:], C.medge_bf[:],
                                        start=False,
                                        stop=not (W - 128 < off + n))
                    if W - 128 >= off and W - 128 < off + n:
                        last = E.matmul(A.pS[:, W - 128:W], C.id_bf[:], C.mcaus_bf[:],
                                        start=False, stop=True)
                return last
            t_sc = sch.op("pe", scmm, waits=tmax(t_qfin[h], st.pS_free, t_kfin, ct["lw"]),
                          inc="PE")
            t_sc_last = t_sc
            par = idx % 2
            t_e1 = sch.op("act", lambda E, W=W, par=par: E.activation(
                A.p_bf[:, par * 1152: par * 1152 + W], A.pS[:, 0:W], AF.Exp,
                bias=C.negC[:]),
                waits=tmax(t_sc, t_negC, st.pbf_par_free[par]), inc="ACT")
            st.pS_free = t_e1
            t_diag = t_e1
            def emit_tp(ph, psub, pnch, pc_lo, phk, pW, t_pe1, t_pdiag, ppar):
                def trs(E, nch=pnch, par=ppar):
                    last = None
                    for c in range(nch):
                        last = E.transpose(A.pTT[:, c * 128:(c + 1) * 128],
                                           A.p_bf[:, par * 1152 + c * 128: par * 1152 + (c + 1) * 128],
                                           C.id_bf[:])
                    return last
                t_tr = sch.op("pe", trs, waits=tmax(t_pe1, st.pTT_free), inc="PE")
                st.pbf_par_free[ppar] = t_tr
                st.tr_free[ppar] = t_tr
                t_pte = sch.op("dve", lambda E, W=pW: E.tensor_copy(A.pT_sb[:, 0:W], A.pTT[:, 0:W]),
                               waits=tmax(t_tr, *st.pT_free), inc="DVE")
                st.pTT_free = t_pte

                pvd = A.pB if ppar == 0 else A.pA
                def pv(E, nch=pnch, c_lo=pc_lo, hk=phk, pvd=pvd):
                    last = None
                    for c in range(nch):
                        last = E.matmul(
                            pvd[:, 0:128],
                            A.v_full[:, (c_lo + c) * 512 + hk * 128: (c_lo + c) * 512 + hk * 128 + 128],
                            A.pT_sb[:, c * 128:(c + 1) * 128],
                            start=(c == 0), stop=(c == nch - 1))
                    return last
                t_pv = sch.op("pe", pv,
                              waits=tmax(t_pte,
                                         st.pB_free if ppar == 0 else st.pA_free,
                                         st.vts[4 * i + psub]), inc="PE")

                def summ(E, nch=pnch):
                    last = None
                    for c in range(nch):
                        last = E.matmul(
                            A.pQ[:, 0:128], C.ones_bf[:],
                            A.pT_sb[:, c * 128:(c + 1) * 128],
                            start=(c == 0), stop=(c == nch - 1))
                    return last
                t_summ = sch.op("pe", summ,
                                waits=tmax(st.rep_free, st.pQ_free), inc="PE")
                sch.op("act", lambda E, par=ppar: E.activation(
                    C.rep_s[:, par * 128:(par + 1) * 128], A.pQ[:, 0:128], AF.Ln,
                    bias=C.eps35[:]),
                    waits=tmax(t_summ, st.repslot_free[ppar]))
                t_rep = sch.op("act", lambda E, par=ppar: E.activation(
                    C.rep_s[:, par * 128:(par + 1) * 128],
                    C.rep_s[:, par * 128:(par + 1) * 128], AF.Exp, scale=-1.0),
                    inc="ACT")
                st.rep_free = t_rep
                st.pQ_free = t_rep
                st.pT_free = [t_pv, t_summ]
                t_ye = sch.op("dve", lambda E, h=ph, sub=psub, par=ppar, pvd=pvd: E.tensor_mul(
                    A.y_att[:, h * TA + sub * 128: h * TA + sub * 128 + 128],
                    pvd[:, 0:128], C.rep_s[:, par * 128:(par + 1) * 128]),
                    waits=tmax(t_pv, t_rep,
                               st.yatt_free if (ph == 0 and psub == 0) else None),
                    inc="DVE")
                if ppar == 0:
                    st.pB_free = t_ye
                else:
                    st.pA_free = t_ye
                st.repslot_free[ppar] = t_ye
                return t_ye
            if pend_tp is not None:
                t_ye = emit_tp(*pend_tp)
            pend_tp = (h, sub, nch, c_lo, hk, W, t_e1, t_diag, par)
        t_ye = emit_tp(*pend_tp)
        t_ye_last = t_ye
        st.wbig_free = t_sc_last
        if debug:
            sch.op("pool", lambda E, ti0=ti0: E.dma_start(
                out=D.dbg["d_y0"][:, ti0:ti0 + TA], in_=A.y_att[:, 0:TA]),
                waits=tmax(t_ye), inc="LW")
        t_wo = {}

        def load_wo(dcp):
            t_wo[dcp] = sch.op("pool", lambda E, dcp=dcp: E.dma_start(
                out=A.wo_sb[:, (dcp % 2) * DM:(dcp % 2 + 1) * DM].rearrange("p (c q) -> p c q", c=NDC),
                in_=D.wo_sl[dcp]),
                waits=tmax(st.wo_free[dcp % 2]), inc=("SOA" if dcp % 2 == 0 else "SOB"))

        load_wo(0)
        load_wo(1)
        t_stats2 = None
        t_wmm = None
        for dcp in range(NDC):
            def womm(E, dcp=dcp):
                last = None
                for ec in range(NDC):
                    last = E.matmul(
                        A.pQ[:], A.wo_sb[:, (dcp % 2) * DM + ec * 128:(dcp % 2) * DM + (ec + 1) * 128],
                        A.y_att[:, ec * TA:(ec + 1) * TA],
                        start=(ec == 0), stop=(ec == NDC - 1))
                return last
            t_wmm = sch.op("pe", womm,
                           waits=tmax(t_wo[dcp], t_ye_last, st.pQ_free), inc="PE")
            st.wo_free[dcp % 2] = t_wmm
            if dcp + 2 < NDC:
                load_wo(dcp + 2)
            t_ae = sch.op("act", lambda E, dcp=dcp: E.copy(
                A.att_bf[:, dcp * TA:(dcp + 1) * TA], A.pQ[:]),
                waits=tmax(t_wmm, *(st.attbf_free if dcp == 0 else [])), inc="ACT")
            st.pQ_free = t_ae
            b = dcp % 2
            t_sqc = sch.op("act", lambda E, dcp=dcp, b=b: E.activation(
                A.sq_scr[:, b * TA:(b + 1) * TA], A.att_bf[:, dcp * TA:(dcp + 1) * TA],
                AF.Square),
                waits=tmax(st.scr_free[b]), inc="ACT")
            sch.op("pe", lambda E, dcp=dcp: E.matmul(
                A.pA[:], C.ones_bf[:], A.att_bf[:, dcp * TA:(dcp + 1) * TA],
                start=(dcp == 0), stop=(dcp == NDC - 1)),
                waits=tmax(t_sqc,
                           st.pA_free if dcp == 0 else None,
                           st.pB_free if dcp == 0 else None))
            t_stats2 = sch.op("pe", lambda E, dcp=dcp, b=b: E.matmul(
                A.pB[:], C.ones_bf[:], A.sq_scr[:, b * TA:(b + 1) * TA],
                start=(dcp == 0), stop=(dcp == NDC - 1)), inc="PE")
            st.scr_free[b] = t_stats2
        if debug:
            sch.op("pool", lambda E, ti0=ti0: E.dma_start(
                out=D.dbg["d_att"].rearrange("(c p) t -> p c t", p=128)[:, :, ti0:ti0 + TA],
                in_=A.att_bf[:].rearrange("p (c t) -> p c t", c=NDC)),
                waits=tmax(t_stats2), inc="LW")
        t_mean2 = sch.op("act", lambda E: E.mul(A.mean2_s[:], A.pA[:], 1.0 / DM),
                         waits=tmax(t_stats2, t_ln, t_ln_pl), inc="ACT")
        t_msq2 = sch.op("act", lambda E: E.activation(A.msqv2_s[:], A.mean2_s[:], AF.Square),
                        inc="ACT")
        t_var2 = sch.op("dve", lambda E: E.scalar_tensor_tensor(
            A.msqv2_s[:], A.pB[:], 1.0 / DM, A.msqv2_s[:], ALU.mult, ALU.subtract),
            waits=tmax(t_msq2, t_stats2), inc="DVE")
        sch.op("act", lambda E: E.activation(
            A.rstd2_s[:], A.msqv2_s[:], AF.Ln, bias=C.eps5[:]),
            waits=tmax(t_var2))
        t_rstd2 = sch.op("act", lambda E: E.activation(
            A.rstd2_s[:], A.rstd2_s[:], AF.Exp, scale=-0.5), inc="ACT")
        st.pA_free = t_mean2
        st.pB_free = t_var2
        t_res = None
        t_res_pl = None
        t_sp_dc = [None] * NDC
        for dc in range(NDC):
            b = dc % 4
            warslot = t_sp_dc[dc - 4] if dc >= 4 else (st.y1s_free[dc] if dc < len(st.y1s_free) else None)
            if dc % 2 == 1:
                sch.op("pool", lambda E, dc=dc: E.tensor_sub(
                    A.q_ro[:], A.att_bf[:, dc * TA:(dc + 1) * TA], A.mean2_s[:]),
                    waits=tmax(t_rstd2, t_mean2, st.qro_free))
                sch.op("pool", lambda E: E.tensor_mul(A.q_ro[:], A.q_ro[:], A.rstd2_s[:]))
                t_res_pl = sch.op("pool", lambda E, dc=dc, b=b, xb0=xb0: E.tensor_add(
                    A.y1s[:, b * TA:(b + 1) * TA], A.q_ro[:],
                    A.x_bf[:, xb0 + dc * TA: xb0 + (dc + 1) * TA]),
                    waits=tmax(warslot), inc="PL")
                t_last = t_res_pl
            else:
                sch.op("dve", lambda E, dc=dc: E.tensor_sub(
                    A.ln_scr[:], A.att_bf[:, dc * TA:(dc + 1) * TA], A.mean2_s[:]),
                    waits=tmax(t_rstd2, t_mean2))
                sch.op("dve", lambda E: E.tensor_mul(A.ln_scr[:], A.ln_scr[:], A.rstd2_s[:]))
                t_res = sch.op("dve", lambda E, dc=dc, b=b, xb0=xb0: E.tensor_add(
                    A.y1s[:, b * TA:(b + 1) * TA], A.ln_scr[:],
                    A.x_bf[:, xb0 + dc * TA: xb0 + (dc + 1) * TA]),
                    waits=tmax(warslot), inc="DVE")
                t_last = t_res
            t_sp_dc[dc] = sch.op("sp", lambda E, dc=dc, b=b, ti0=ti0: E.dma_start(
                out=D.y1_sp[dc * 128:(dc + 1) * 128, ti0:ti0 + TA],
                in_=A.y1s[:, b * TA:(b + 1) * TA]),
                waits=tmax(t_last), inc=("STA", "STB", "STC", "STD")[b])
        st.y1s_free = [t_sp_dc[NDC - 4], t_sp_dc[NDC - 3], t_sp_dc[NDC - 2], t_sp_dc[NDC - 1]]
        t_spill = st.y1s_free
        st.xbf_free[par] = [t_res, t_res_pl]
        st.attbf_free = [t_res, t_res_pl]
        st.qro_free = t_res_pl
        st.lnbf_free = t_qp
        st.yatt_free = t_wmm

    return {"pe": ("PE", sch.cnt["PE"]), "act": ("ACT", sch.cnt["ACT"]),
            "dve": ("DVE", sch.cnt["DVE"]), "spill": t_spill}


def _sched_ffn(sch, nc, D, C, Fb, ct, af, debug):
    t_spill_all = af["spill"]
    if not isinstance(t_spill_all, list):
        t_spill_all = [t_spill_all]
    allf = [af["pe"], af["act"], af["dve"]]
    for en in ("pe", "act", "dve", "pool", "sp"):
        sch.join(en, tmax(*allf))

    st = NS()
    st.y1_free = [None, None]
    st.cast_free = [None, None]
    st.u_free = None
    st.acc_free = None
    st.h_free = [None] * NFC
    st.pH_free = [None, None]
    st.pY_free = [None, None]
    st.G_free = None
    st.gT_free = None
    st.stat_free = None
    st.w1free = [None, None, None]
    st.w2free = [None, None]
    st.gall_free = [None] * 8
    st.sqf_free = None
    st.g16_free = None
    t_w1 = {}
    t_w2 = {}

    def load_w1(j, fc):
        idx = (j * NFC + fc) % 3
        t_w1[(j, fc)] = sch.op("pool", lambda E, j=j, fc=fc, idx=idx: E.dma_start(
            out=Fb.w1_sb[:, idx * DM:(idx + 1) * DM].rearrange("p (c q) -> p c q", c=NDC),
            in_=D.w1_sl[j, fc]),
            waits=tmax(st.w1free[idx]), inc=("S1A", "S1B", "S1C")[idx])

    def load_w2(j, dcp):
        idx = (j * NDC + dcp) % 2
        t_w2[(j, dcp)] = sch.op("pool", lambda E, j=j, dcp=dcp, idx=idx: E.dma_start(
            out=Fb.w2_sb[:, idx * NFC * 128:(idx + 1) * NFC * 128].rearrange("p (c q) -> p c q", c=NFC),
            in_=D.w2_sl[j, dcp]),
            waits=tmax(st.w2free[idx]), inc=("S2A", "S2B")[idx])

    for f in range(NTF):
        tf0 = f * TF
        load_w1(0, 0)
        load_w1(0, 1)
        load_w1(0, 2)
        load_w2(0, 0)
        load_w2(0, 1)
        t_st3 = None
        for dc in range(NDC):
            b = dc % 2
            t_l = sch.op("sp", lambda E, dc=dc, b=b, tf0=tf0: E.dma_start(
                out=Fb.y1_scr[:, b * TF:(b + 1) * TF],
                in_=D.y1_sp[dc * 128:(dc + 1) * 128, tf0:tf0 + TF]),
                waits=tmax(st.y1_free[b], *t_spill_all), inc=("SYA" if b == 0 else "SYB"))
            sch.op("act", lambda E, b=b: E.copy(
                Fb.cast_f[:, b * TF:(b + 1) * TF], Fb.y1_scr[:, b * TF:(b + 1) * TF]),
                waits=tmax(t_l, st.cast_free[b]))
            t_cf = sch.op("act", lambda E, b=b: E.activation(
                Fb.sq_f[:], Fb.y1_scr[:, b * TF:(b + 1) * TF],
                AF.Square), waits=tmax(st.sqf_free), inc="ACT")
            st.y1_free[b] = t_cf

            def stats3(E, dc=dc, b=b):
                last = None
                for hh in range(2):
                    last = E.matmul(Fb.pHa[:, hh * 512:(hh + 1) * 512], C.ones_bf[:],
                                    Fb.cast_f[:, b * TF + hh * 512: b * TF + (hh + 1) * 512],
                                    start=(dc == 0), stop=(dc == NDC - 1))
                for hh in range(2):
                    last = E.matmul(Fb.pHb[:, hh * 512:(hh + 1) * 512], C.ones_bf[:],
                                    Fb.sq_f[:, hh * 512:(hh + 1) * 512],
                                    start=(dc == 0), stop=(dc == NDC - 1))
                return last
            t_st3 = sch.op("pe", stats3,
                           waits=tmax(t_cf,
                                      st.pH_free[0] if dc == 0 else None,
                                      st.pH_free[1] if dc == 0 else None),
                           inc="PE")
            st.cast_free[b] = t_st3
            st.sqf_free = t_st3
        t_mean3 = sch.op("act", lambda E: E.mul(Fb.mean_f[:], Fb.pHa[:], 1.0 / DM),
                         waits=tmax(t_st3, st.stat_free), inc="ACT")
        t_msq3 = sch.op("act", lambda E: E.activation(Fb.msqv_f[:], Fb.mean_f[:], AF.Square),
                        inc="ACT")
        t_var3 = sch.op("dve", lambda E: E.scalar_tensor_tensor(
            Fb.msqv_f[:], Fb.pHb[:], 1.0 / DM, Fb.msqv_f[:], ALU.mult, ALU.subtract),
            waits=tmax(t_msq3, t_st3), inc="DVE")
        sch.op("act", lambda E: E.activation(
            Fb.rstd_f[:], Fb.msqv_f[:], AF.Ln, bias=C.eps5[:]),
            waits=tmax(t_var3))
        t_rstd3 = sch.op("act", lambda E: E.activation(
            Fb.rstd_f[:], Fb.rstd_f[:], AF.Exp, scale=-0.5), inc="ACT")
        st.pH_free = [t_mean3, t_var3]
        t_u = None
        t_u_pl = None
        for dc in range(NDC):
            b = dc % 2
            t_l = sch.op("sp", lambda E, dc=dc, b=b, tf0=tf0: E.dma_start(
                out=Fb.y1_scr[:, b * TF:(b + 1) * TF],
                in_=D.y1_sp[dc * 128:(dc + 1) * 128, tf0:tf0 + TF]),
                waits=tmax(st.y1_free[b], *t_spill_all), inc=("SYA" if b == 0 else "SYB"))
            if dc % 3 == 2:
                lnp = Fb.cast_f[:].bitcast(F32)
                sch.op("pool", lambda E, b=b, lnp=lnp: E.tensor_sub(
                    lnp, Fb.y1_scr[:, b * TF:(b + 1) * TF], Fb.mean_f[:]),
                    waits=tmax(t_l, t_rstd3, t_mean3, st.u_free, st.cast_free[0],
                               st.cast_free[1]))
                t_u_pl = sch.op("pool", lambda E, dc=dc, lnp=lnp: E.tensor_mul(
                    Fb.u_bf[:, dc * TF:(dc + 1) * TF], lnp, Fb.rstd_f[:]),
                    inc="PL")
                st.y1_free[b] = t_u_pl
            else:
                sch.op("dve", lambda E, b=b: E.tensor_sub(
                    Fb.ln_f[:], Fb.y1_scr[:, b * TF:(b + 1) * TF], Fb.mean_f[:]),
                    waits=tmax(t_l, t_rstd3, t_mean3, st.u_free))
                t_u = sch.op("dve", lambda E, dc=dc: E.tensor_mul(
                    Fb.u_bf[:, dc * TF:(dc + 1) * TF], Fb.ln_f[:], Fb.rstd_f[:]),
                    inc="DVE")
                st.y1_free[b] = t_u
        if debug:
            sch.op("pool", lambda E, tf0=tf0: E.dma_start(
                out=D.dbg["d_u"].rearrange("(c p) t -> p c t", p=128)[:, :, tf0:tf0 + TF],
                in_=Fb.u_bf[:].rearrange("p (c t) -> p c t", c=NDC)),
                waits=tmax(t_u), inc="LW")
        t_rmm = [None] * 8
        t_sg_prev = None
        t_msk_prev = None
        t_gmul_prev = None
        t_ln_prev = None
        t_final_prev = None
        t_gfin = [None] * 8
        for ts_ in range(8):
            def rmm(E, ts_=ts_):
                last = None
                for dc in range(NDC):
                    last = E.matmul(
                        Fb.pYa[:, ts_ * 64: ts_ * 64 + NRT],
                        Fb.u_bf[:, dc * TF + ts_ * 128: dc * TF + ts_ * 128 + 128],
                        C.cT_bf[:, dc * NRT:(dc + 1) * NRT],
                        start=(dc == 0), stop=(dc == NDC - 1))
                return last
            t_rmm[ts_] = sch.op("pe", rmm,
                                waits=tmax(t_u, t_u_pl, st.pY_free[0] if ts_ == 0 else None,
                                           ct["lw"]),
                                inc="PE")
        for ts_ in range(8):
            t_sg = sch.op("act", lambda E, ts_=ts_: E.activation(
                C.sc16[:], Fb.pYa[:, ts_ * 64: ts_ * 64 + NRT], AF.Sigmoid),
                waits=tmax(t_rmm[7], t_gmul_prev, t_msk_prev), inc="ACT")
            t_bias = sch.op("pool", lambda E: E.tensor_add(C.sc16[:], C.sc16[:], C.bias_bc_s[:]),
                            waits=tmax(t_sg, ct["lx"]), inc="PL")
            t_mx = sch.op("dve", lambda E: E.max(out=C.mx8[:], in_=C.sc16[:]),
                          waits=tmax(t_bias), inc="DVE")
            t_thr = sch.op("pool", lambda E: E.tensor_copy(C.thr1[:], C.mx8[:, 3:4]),
                           waits=tmax(t_mx), inc="PL")
            t_msk = sch.op("dve", lambda E: E.tensor_tensor(
                C.msk16[:], C.sc16[:], C.thr1[:].to_broadcast([128, NRT]), ALU.is_ge),
                waits=tmax(t_thr), inc="DVE")
            t_msk_prev = t_msk
            t_gmul = sch.op("pool", lambda E, ts_=ts_: E.tensor_mul(
                C.g_all[:, ts_ * NRT:(ts_ + 1) * NRT], C.sc16[:], C.msk16[:]),
                waits=tmax(t_msk, st.gall_free[ts_], t_final_prev), inc="PL")
            t_gmul_prev = t_gmul
            t_red = sch.op("dve", lambda E, ts_=ts_: E.reduce_sum(
                C.rs1[:], C.g_all[:, ts_ * NRT:(ts_ + 1) * NRT], AX.X),
                waits=tmax(t_gmul), inc="DVE")
            t_rsc = sch.op("pool", lambda E: E.tensor_copy(C.rsB[:], C.rs1[:]),
                           waits=tmax(t_red, t_ln_prev), inc="PL")
            t_eps = sch.op("dve", lambda E: E.tensor_scalar_max(C.rsB[:], C.rsB[:], 1e-8),
                           waits=tmax(t_rsc), inc="DVE")
            t_ln = sch.op("act", lambda E: E.activation(C.rsC[:], C.rsB[:], AF.Ln),
                          waits=tmax(t_eps, t_final_prev), inc="ACT")
            t_ln_prev = t_ln
            t_exp = sch.op("act", lambda E: E.activation(C.rsC[:], C.rsC[:], AF.Exp, scale=-1.0),
                           inc="ACT")
            t_final = sch.op("dve", lambda E, ts_=ts_: E.tensor_tensor(
                C.g_all[:, ts_ * NRT:(ts_ + 1) * NRT], C.g_all[:, ts_ * NRT:(ts_ + 1) * NRT],
                C.rsC[:].to_broadcast([128, NRT]), ALU.mult),
                waits=tmax(t_exp), inc="DVE")
            t_final_prev = t_final
            t_gfin[ts_] = t_final
            if debug:
                t_gd = sch.op("sp", lambda E, ts_=ts_, tf0=tf0: E.dma_start(
                    out=D.dbg["d_g"][tf0 + ts_ * 128: tf0 + (ts_ + 1) * 128, :],
                    in_=C.g_all[:, ts_ * NRT:(ts_ + 1) * NRT]),
                    waits=tmax(t_final), inc="LX")
        st.pY_free[0] = t_sg
        t_hm = None
        t_ym = None
        t_hev = [None] * NFC
        t_ac17 = [None] * NDC
        for j in range(NE):
            routed = j >= 2
            jr = j - 2
            t_ge = None
            if routed:
                def gmm(E, jr=jr):
                    last = None
                    for hh in range(2):
                        last = E.matmul(Fb.pHa[:, hh * 512:(hh + 1) * 512],
                                        C.sel_bf[:, jr * 128:(jr + 1) * 128],
                                        Fb.gT_bf[:, hh * 512:(hh + 1) * 512],
                                        start=True, stop=True)
                    return last
                t_gm = sch.op("pe", gmm, waits=tmax(t_gT, st.pH_free[0]), inc="PE")
                t_ge = sch.op("act", lambda E: E.copy(Fb.G_bf[:], Fb.pHa[:]),
                              waits=tmax(t_gm, st.G_free), inc="ACT")
                st.pH_free[0] = t_ge
            for fc in range(NFC):
                pH = Fb.pHa if fc % 2 == 0 else Fb.pHb
                pidx = fc % 2
                widx = (j * NFC + fc) % 3

                def hmm(E, pH=pH, widx=widx):
                    last = None
                    for dc in range(NDC):
                        for hh in range(2):
                            last = E.matmul(
                                pH[:, hh * 512:(hh + 1) * 512],
                                Fb.w1_sb[:, widx * DM + dc * 128: widx * DM + (dc + 1) * 128],
                                Fb.u_bf[:, dc * TF + hh * 512: dc * TF + (hh + 1) * 512],
                                start=(dc == 0), stop=(dc == NDC - 1))
                    return last
                t_hm = sch.op("pe", hmm,
                              waits=tmax(t_w1[(j, fc)], t_u, t_u_pl, st.pH_free[pidx]), inc="PE")
                st.w1free[widx] = t_hm
                nj, nfc = (j, fc + 3) if fc + 3 < NFC else (j + 1, fc + 3 - NFC)
                if nj < NE:
                    load_w1(nj, nfc)
                if j == 0 and fc < 8:
                    ts_ = fc
                    t_gtr = sch.op("pe", lambda E, ts_=ts_: E.transpose(
                        Fb.pYb[0:NRT, ts_ * 128:(ts_ + 1) * 128],
                        C.g_all[:, ts_ * NRT:(ts_ + 1) * NRT], C.id_f[:]),
                        waits=tmax(t_gfin[ts_], st.pY_free[1] if ts_ == 0 else None),
                        inc="PE")
                    st.gall_free[ts_] = t_gtr
                    if ts_ == 7:
                        t_gT = sch.op("act", lambda E: E.copy(Fb.gT_bf[:], Fb.pYb[0:NRT, :]),
                                      waits=tmax(t_gtr, st.gT_free), inc="ACT")
                        st.pY_free[1] = t_gT
                t_he = sch.op("act", lambda E, fc=fc, pH=pH, j=j: E.activation(
                    Fb.h_bf[:, fc * TF:(fc + 1) * TF], pH[:], AF.Relu,
                    bias=C.b1c_s[:, j * NFC + fc: j * NFC + fc + 1]),
                    waits=tmax(t_hm, st.h_free[fc], ct["lx"]), inc="ACT")
                st.pH_free[pidx] = t_he
                t_hev[fc] = t_he
                if routed:
                    t_hg = sch.op("dve", lambda E, fc=fc: E.tensor_mul(
                        Fb.h_bf[:, fc * TF:(fc + 1) * TF],
                        Fb.h_bf[:, fc * TF:(fc + 1) * TF], Fb.G_bf[:]),
                        waits=tmax(t_he, t_ge), inc="DVE")
                    t_hev[fc] = t_hg
            if routed:
                st.G_free = t_hev[NFC - 1]
            if debug and j == 0 and f == 0:
                sch.op("pool", lambda E: E.dma_start(
                    out=D.dbg["d_h0"].rearrange("(c p) t -> p c t", p=128),
                    in_=Fb.h_bf[:].rearrange("p (c t) -> p c t", c=NFC)),
                    waits=tmax(t_hev[NFC - 1]), inc="LW")
            for dcp in range(NDC):
                pY = Fb.pYa if dcp % 2 == 0 else Fb.pYb
                pidx = dcp % 2
                widx = (j * NDC + dcp) % 2

                def ymm(E, dcp=dcp, pY=pY, widx=widx, j=j):
                    last = None
                    islast = (j == NE - 1)
                    for fc in range(NFC):
                        for hh in range(2):
                            last = E.matmul(
                                pY[:, hh * 512:(hh + 1) * 512],
                                Fb.w2_sb[:, widx * NFC * 128 + fc * 128: widx * NFC * 128 + (fc + 1) * 128],
                                Fb.h_bf[:, fc * TF + hh * 512: fc * TF + (hh + 1) * 512],
                                start=(fc == 0), stop=(fc == NFC - 1 and not islast))
                    if islast:
                        for hh in range(2):
                            last = E.matmul(
                                pY[:, hh * 512:(hh + 1) * 512],
                                C.b2r_bf[:, dcp * 128:(dcp + 1) * 128],
                                Fb.gT_bf[:, hh * 512:(hh + 1) * 512],
                                start=False, stop=True)
                    return last
                t_ym = sch.op("pe", ymm,
                              waits=tmax(t_hev[NFC - 1], t_w2[(j, dcp)], st.pY_free[pidx]),
                              inc="PE")
                st.w2free[widx] = t_ym
                nj, ndcp = (j, dcp + 2) if dcp + 2 < NDC else (j + 1, dcp + 2 - NDC)
                if nj < NE:
                    load_w2(nj, ndcp)
                if j == 0:
                    def acc0(E, dcp=dcp, pY=pY):
                        E.tensor_tensor(Fb.acc[:, dcp * TF:(dcp + 1) * TF], pY[:],
                                        C.b2sh_s[:, dcp:dcp + 1].to_broadcast([128, TF]),
                                        ALU.add)
                        return E.tensor_add(Fb.acc[:, dcp * TF:(dcp + 1) * TF],
                                            Fb.acc[:, dcp * TF:(dcp + 1) * TF],
                                            Fb.u_bf[:, dcp * TF:(dcp + 1) * TF])
                    t_ac = sch.op("dve", acc0,
                                  waits=tmax(t_ym, ct["b2sh"], st.acc_free), inc="DVE")
                else:
                    t_ac = sch.op("dve", lambda E, dcp=dcp, pY=pY: E.tensor_add(
                        Fb.acc[:, dcp * TF:(dcp + 1) * TF],
                        Fb.acc[:, dcp * TF:(dcp + 1) * TF], pY[:]),
                        waits=tmax(t_ym), inc="DVE")
                st.pY_free[pidx] = t_ac
                if j == NE - 1:
                    t_ac17[dcp] = t_ac
            for fc in range(NFC):
                st.h_free[fc] = t_ym
        st.u_free = t_hm
        st.gT_free = t_ym
        t_acc_all = t_ac
        t_st4 = None
        for dc in range(NDC):
            b = dc % 2
            sch.op("act", lambda E, dc=dc, b=b: E.copy(
                Fb.cast_f[:, b * TF:(b + 1) * TF], Fb.acc[:, dc * TF:(dc + 1) * TF]),
                waits=tmax(t_ac17[dc], st.cast_free[b], t_u_pl))
            t_c4 = sch.op("act", lambda E, dc=dc, b=b: E.activation(
                Fb.sq_f[:], Fb.acc[:, dc * TF:(dc + 1) * TF],
                AF.Square), waits=tmax(st.sqf_free), inc="ACT")

            def stats4(E, dc=dc, b=b):
                last = None
                for hh in range(2):
                    last = E.matmul(Fb.pHa[:, hh * 512:(hh + 1) * 512], C.ones_bf[:],
                                    Fb.cast_f[:, b * TF + hh * 512: b * TF + (hh + 1) * 512],
                                    start=(dc == 0), stop=(dc == NDC - 1))
                for hh in range(2):
                    last = E.matmul(Fb.pHb[:, hh * 512:(hh + 1) * 512], C.ones_bf[:],
                                    Fb.sq_f[:, hh * 512:(hh + 1) * 512],
                                    start=(dc == 0), stop=(dc == NDC - 1))
                return last
            t_st4 = sch.op("pe", stats4,
                           waits=tmax(t_c4,
                                      st.pH_free[0] if dc == 0 else None,
                                      st.pH_free[1] if dc == 0 else None),
                           inc="PE")
            st.cast_free[b] = t_st4
            st.sqf_free = t_st4
        t_mean4 = sch.op("act", lambda E: E.mul(Fb.mean_f[:], Fb.pHa[:], 1.0 / DM),
                         waits=tmax(t_st4, t_u, t_u_pl), inc="ACT")
        t_msq4 = sch.op("act", lambda E: E.activation(Fb.msqv_f[:], Fb.mean_f[:], AF.Square),
                        inc="ACT")
        t_var4 = sch.op("dve", lambda E: E.scalar_tensor_tensor(
            Fb.msqv_f[:], Fb.pHb[:], 1.0 / DM, Fb.msqv_f[:], ALU.mult, ALU.subtract),
            waits=tmax(t_msq4, t_st4), inc="DVE")
        sch.op("act", lambda E: E.activation(
            Fb.rstd_f[:], Fb.msqv_f[:], AF.Ln, bias=C.eps5[:]),
            waits=tmax(t_var4))
        t_rstd4 = sch.op("act", lambda E: E.activation(
            Fb.rstd_f[:], Fb.rstd_f[:], AF.Exp, scale=-0.5), inc="ACT")
        st.pH_free = [t_mean4, t_var4]
        st.stat_free = t_rstd4
        if debug:
            sch.op("sp", lambda E, tf0=tf0: E.dma_start(
                out=D.dbg["d_acc"].rearrange("(c p) t -> p c t", p=128)[:, :, tf0:tf0 + TF],
                in_=Fb.acc[:].rearrange("p (c t) -> p c t", c=NDC)),
                waits=tmax(t_acc_all), inc="LX")
        t_out = None
        for dc in range(NDC):
            b = dc % 2
            t_l = sch.op("sp", lambda E, dc=dc, b=b, tf0=tf0: E.dma_start(
                out=Fb.y1_scr[:, b * TF:(b + 1) * TF],
                in_=D.y1_sp[dc * 128:(dc + 1) * 128, tf0:tf0 + TF]),
                waits=tmax(st.y1_free[b], *t_spill_all), inc=("SYA" if b == 0 else "SYB"))
            sch.op("dve", lambda E, dc=dc: E.tensor_sub(
                Fb.ln_f[:], Fb.acc[:, dc * TF:(dc + 1) * TF], Fb.mean_f[:]),
                waits=tmax(t_rstd4, t_mean4))
            sch.op("dve", lambda E: E.tensor_mul(Fb.ln_f[:], Fb.ln_f[:], Fb.rstd_f[:]))
            t_out = sch.op("dve", lambda E, dc=dc, b=b: E.tensor_add(
                Fb.acc[:, dc * TF:(dc + 1) * TF], Fb.ln_f[:],
                Fb.y1_scr[:, b * TF:(b + 1) * TF]),
                waits=tmax(t_l), inc="DVE")
            st.y1_free[b] = t_out
        t_store = sch.op("sp", lambda E, tf0=tf0: E.dma_start(
            out=D.out_fm.rearrange("(c p) t -> p c t", p=128)[:, :, tf0:tf0 + TF],
            in_=Fb.acc[:].rearrange("p (c t) -> p c t", c=NDC)),
            waits=tmax(t_out), inc="ST")
        st.acc_free = t_store

    sch.join("sp", [("ST", sch.cnt["ST"])])



def _host_layout(inputs):
    x = np.asarray(inputs["L0_attn_x"], dtype=np.float32)
    wq = np.asarray(inputs["L0_attn_wq"], dtype=np.float32)
    wk = np.asarray(inputs["L0_attn_wk"], dtype=np.float32)
    wv = np.asarray(inputs["L0_attn_wv"], dtype=np.float32)
    wo = np.asarray(inputs["L0_attn_wo"], dtype=np.float32)
    gq = np.asarray(inputs["L0_attn_gamma_q"], dtype=np.float32)
    gk = np.asarray(inputs["L0_attn_gamma_k"], dtype=np.float32)
    sw1 = np.asarray(inputs["L0_ffn_shared_w1"], dtype=np.float32)
    sb1 = np.asarray(inputs["L0_ffn_shared_b1"], dtype=np.float32)
    sw2 = np.asarray(inputs["L0_ffn_shared_w2"], dtype=np.float32)
    sb2 = np.asarray(inputs["L0_ffn_shared_b2"], dtype=np.float32)
    rw1 = np.asarray(inputs["L0_ffn_routed_w1"], dtype=np.float32)
    rb1 = np.asarray(inputs["L0_ffn_routed_b1"], dtype=np.float32)
    rw2 = np.asarray(inputs["L0_ffn_routed_w2"], dtype=np.float32)
    rb2 = np.asarray(inputs["L0_ffn_routed_b2"], dtype=np.float32)
    cent = np.asarray(inputs["L0_ffn_centroids"], dtype=np.float32)
    bias = np.asarray(inputs["L0_ffn_bias"], dtype=np.float32)

    com = {}
    com["wq_sl"] = np.ascontiguousarray(
        wq.reshape(H, 128, NDC, 128).transpose(0, 3, 2, 1))
    com["wk_sl"] = np.ascontiguousarray(
        wk.reshape(HK, 128, NDC, 128).transpose(0, 3, 2, 1))
    com["wvT"] = np.ascontiguousarray(wv.T)
    com["wo_sl"] = np.ascontiguousarray(
        wo.reshape(NDC, 128, NDC, 128).transpose(0, 3, 2, 1))
    w1_all = np.concatenate([sw1, rw1], axis=0)
    com["w1_sl"] = np.ascontiguousarray(
        w1_all.reshape(NE, NDC, 128, NFC, 128).transpose(0, 3, 2, 1, 4))
    w2_all = np.concatenate([sw2, rw2], axis=0)
    com["w2_sl"] = np.ascontiguousarray(
        w2_all.reshape(NE, NFC, 128, NDC, 128).transpose(0, 3, 2, 1, 4))
    b1_all = np.concatenate([sb1, rb1], axis=0)
    com["b1c"] = np.ascontiguousarray(b1_all.reshape(NE, NFC, 128).transpose(2, 0, 1)
                                      .reshape(128, NE * NFC))
    com["b2c"] = np.ascontiguousarray(sb2.reshape(2, NDC, 128).transpose(2, 0, 1)
                                      .reshape(128, 2 * NDC))
    com["b2r"] = np.ascontiguousarray(rb2)
    com["cT"] = np.ascontiguousarray(cent.T)
    com["bias_bc"] = np.ascontiguousarray(np.tile(bias[None, :], (128, 1)))
    com["gq"] = np.ascontiguousarray(gq.reshape(128, 1))
    com["gk"] = np.ascontiguousarray(gk.reshape(128, 1))

    inv_freq = (1.0 / RBASE ** (np.arange(0, DH, 2, dtype=np.float64) / DH))
    ang = np.arange(T, dtype=np.float64)[:, None] * inv_freq[None, :]
    cos = np.cos(ang)
    sin = np.sin(ang)
    com["cos_t"] = np.ascontiguousarray(
        np.concatenate([cos, cos], axis=1).T.astype(np.float32))
    com["sin_t"] = np.ascontiguousarray(
        np.concatenate([-sin, sin], axis=1).T.astype(np.float32))
    com["ones_c"] = np.ones((128, 128), np.float32)
    com["ident_c"] = np.eye(128, dtype=np.float32)
    il = np.arange(128)[:, None]
    jl = np.arange(128)[None, :]
    com["mcaus"] = np.where(jl <= il, 0.0, -1e9).astype(np.float32)
    com["medge"] = np.where(jl >= il, 0.0, -1e9).astype(np.float32)
    sel = np.zeros((NRT, NRT * 128), np.float32)
    for j in range(NRT):
        sel[j, j * 128:(j + 1) * 128] = 1.0
    com["sel_c"] = sel
    com["identu_c"] = np.eye(128, dtype=np.uint8)
    kk = np.arange(128)[:, None]
    mm = np.arange(128)[None, :]
    com["p64_c"] = (np.abs(kk - mm) == 64).astype(np.float32)

    in_maps = []
    for b in range(8):
        m = dict(com)
        m["x_fm"] = np.ascontiguousarray(x[b].T)
        in_maps.append(m)
    return in_maps


_NC_CACHE = {}


def kernel(**inputs):
    in_maps = _host_layout(inputs)
    key = False
    if key not in _NC_CACHE:
        _NC_CACHE[key] = build_program(debug=False)
    nc = _NC_CACHE[key]
    res = run_bass_kernel_spmd(nc, in_maps, core_ids=list(range(8)))
    out = np.stack([np.asarray(r["out_fm"]).T for r in res.results], axis=0)
    return np.ascontiguousarray(out.astype(np.float32))
